# version 01
# baseline (speedup 1.0000x reference)
import numpy as np
from contextlib import ExitStack

import concourse.bass as bass
import concourse.mybir as mybir
from concourse.bass_utils import run_bass_kernel_spmd

F32 = mybir.dt.float32
BF16 = mybir.dt.bfloat16
I32 = mybir.dt.int32
AF = mybir.ActivationFunctionType
OP = mybir.AluOpType
AX = mybir.AxisListType

NCORES = 8
NB = 2
SEQ = 8192
D = 1024
L = SEQ // NCORES
T = NB * L
DC = D // 128
TC = T // 128
NTS = T // 512
EPS_LN = 1e-5

ESEM = {"scalar": "eva", "vector": "evd", "gpsimd": "evp"}

DEBUG_TAP = None
DEBUG_TRUNC = 8
DEBUG_NOAG = False
DEBUG_AGOPS = 6
DEBUG_L3PAIRS = 99
DEBUG_L3NOMASK = False
DEBUG_L3MASKLVL = 4


def tap(C, stage, ap_fn):
    if DEBUG_TAP != stage:
        return
    C.tapped = True
    for s in ("dve", "evd", "eva", "act", "pool"):
        C.val(s)
    marks = {s: C.cnt[s] for s in ("dve", "evd", "eva", "act", "pool")}
    pe_m = C.val("pe")

    @C.G
    def _(g):
        for s, v in marks.items():
            if v:
                g.wait_ge(C.sem(s), v)
        g.wait_ge(C.sem("pe"), pe_m)
        for dst, src_ap in ap_fn():
            ins = g.dma_start(out=dst, in_=src_ap)
            C.inc(ins, "qg", 16)
        g.wait_ge(C.sem("qg"), C.val("qg"))


class Ctx:
    def __init__(self, nc, blk, es):
        self.nc, self.blk, self.es = nc, blk, es
        self._sems, self.cnt = {}, {}
        self.sb, self.dr, self.ps = {}, {}, {}
        self.wb_last_use = [0, 0]
        self.wb_idx = 0
        self.tapped = False

    def sem(self, name):
        if name not in self._sems:
            self._sems[name] = self.es.enter_context(self.nc.semaphore(name))
            self.cnt[name] = 0
        return self._sems[name]

    def inc(self, ins, name, by=1):
        ins.then_inc(self.sem(name), by)
        self.cnt[name] += by
        return self.cnt[name]

    def val(self, name):
        self.sem(name)
        return self.cnt[name]

    def wait(self, eng, name, v=None):
        eng.wait_ge(self.sem(name), self.cnt[name] if v is None else v)

    @property
    def T(self):
        return self.blk.tensor

    @property
    def V(self):
        return self.blk.vector

    @property
    def A(self):
        return self.blk.scalar

    @property
    def G(self):
        return self.blk.gpsimd

    @property
    def Y(self):
        return self.blk.sync


def esec(C, eng):
    return {"vector": C.V, "scalar": C.A, "gpsimd": C.G}[eng]


def block_barrier(C):
    marks = {s: C.val(s) for s in ("pe", "dve", "evd", "eva", "act", "pool")}
    owns = [(C.T, "pe"), (C.V, None), (C.A, None), (C.G, None)]

    for sec, own in owns:
        @sec
        def _(e, own=own):
            for s, v in marks.items():
                if s != own and v:
                    e.wait_ge(C.sem(s), v)
    return marks


def psum_guard(C, t):
    for s in ("eva", "evd", "evp"):
        if C.val(s) > 0:
            t.wait_ge(C.sem(s), C.cnt[s])


def run_pipe(C, groups, bufs, guard=True):
    n = len(groups)
    nb = len(bufs)
    base = {s: C.val(s) for s in ("eva", "evd", "evp")}
    run = dict(base)
    marks = []
    for g in groups:
        s = ESEM[g["eng"]]
        run[s] += 1
        marks.append((s, run[s]))
    pe0 = C.val("pe")

    @C.T
    def _(t):
        if guard:
            psum_guard(C, t)
        for i, g in enumerate(groups):
            for (sn, v) in g.get("wait", ()):
                t.wait_ge(C.sem(sn), v)
            if i >= nb:
                sn, v = marks[i - nb]
                t.wait_ge(C.sem(sn), v)
            ins = g["mm"](t, bufs[i % nb])
            ins.then_inc(C.sem("pe"), 1)
    C.cnt["pe"] += n
    for i, g in enumerate(groups):
        g["_pe_mark"] = pe0 + i + 1

    for eng in ("scalar", "vector", "gpsimd"):
        idxs = [i for i, g in enumerate(groups) if g["eng"] == eng]
        if not idxs:
            continue

        @esec(C, eng)
        def _(e, idxs=idxs):
            for i in idxs:
                g = groups[i]
                e.wait_ge(C.sem("pe"), g["_pe_mark"])
                ins = g["evac"](e, bufs[i % nb])
                ins.then_inc(C.sem(ESEM[eng]), 1)
    for s in ("eva", "evd", "evp"):
        C.cnt[s] = run[s]


def wslab_load(C, w_hbm, og, ocols, wait_pe=None):
    sb = C.sb
    i = C.wb_idx
    C.wb_idx = 1 - C.wb_idx
    buf = sb["WB0"] if i == 0 else sb["WB1"]
    din = w_hbm.shape[0]
    wp = C.wb_last_use[i] if wait_pe is None else max(wait_pe, C.wb_last_use[i])
    if din >= 128:
        nkc = din // 128
        assert nkc * ocols <= 4096
        dst = buf[:, 0:nkc * ocols].rearrange("p (c o) -> p c o", o=ocols)
        src = w_hbm[:, og:og + ocols].rearrange("(c p) o -> p c o", p=128)
    else:
        dst = buf[0:din, 0:ocols].rearrange("p (c o) -> p c o", o=ocols)
        src = w_hbm[:, og:og + ocols].rearrange("p (c o) -> p c o", o=ocols)

    qsem = "qga" if i == 0 else "qgb"

    @C.G
    def _(g):
        if wp > 0:
            g.wait_ge(C.sem("pe"), wp)
        ins = g.dma_start(out=dst, in_=src)
        C.inc(ins, qsem, 16)
    return dst, (qsem, C.val(qsem)), i


def load_rows_bf(C, dram_1d, lo, hi, row_ap, wait_pe=0):
    @C.G
    def _(g):
        if wait_pe:
            g.wait_ge(C.sem("pe"), wait_pe)
        ins = g.dma_start(out=row_ap,
                          in_=dram_1d[lo:hi].rearrange("(o n) -> o n", o=1))
        C.inc(ins, "qgr", 16)
    return C.val("qgr")


def fm_proj(C, w_hbm, src, dout, evac_fn, evac_eng="scalar", bias_row=None,
            tokens=None, bias_mark=None, extra_wait=None):
    sb = C.sb
    din = w_hbm.shape[0]
    nkc = max(1, din // 128)
    t0, t1 = (0, T) if tokens is None else tokens
    tlen = t1 - t0
    nts = tlen // 512
    ones_row = sb["ONER"][0:1, :]
    groups = []
    slabw = min(512, max(128, (4096 // nkc) // 128 * 128))
    nslab = (dout + slabw - 1) // slabw
    pe_entry = C.val("pe")
    cum = [0]
    for s in range(nslab):
        cum.append(cum[-1] + min(slabw, dout - slabw * s) // 128)
    for s in range(nslab):
        og = slabw * s
        ocols = min(slabw, dout - og)
        wp = pe_entry + cum[s - 1] if s >= 2 else None
        ap, mark, bi = wslab_load(C, w_hbm, og, ocols, wait_pe=wp)
        for j in range(ocols // 128):
            oc = og // 128 + j

            def mm(t, ps, ap=ap, j=j, oc=oc):
                for ts in range(nts):
                    for dc in range(nkc):
                        ins = t.matmul(ps[:, 512 * ts:512 * (ts + 1)],
                                       ap[:, dc, 128 * j:128 * (j + 1)],
                                       src(dc)[:, t0 + 512 * ts:t0 + 512 * (ts + 1)],
                                       start=(dc == 0),
                                       stop=(dc == nkc - 1 and bias_row is None))
                    if bias_row is not None:
                        ins = t.matmul(ps[:, 512 * ts:512 * (ts + 1)],
                                       bias_row[:, 128 * oc:128 * (oc + 1)],
                                       ones_row[:, 0:512],
                                       start=False, stop=True)
                return ins

            w = [mark]
            if bias_mark is not None:
                w.append(("qgr", bias_mark))
            if extra_wait:
                w.extend(extra_wait)
            groups.append({"mm": mm,
                           "evac": (lambda e, ps, oc=oc: evac_fn(e, oc, ps)),
                           "eng": evac_eng,
                           "wait": w})
    run_pipe(C, groups, [C.ps["PA"][:, 0:tlen], C.ps["PB"][:, 0:tlen]])
    C.wb_last_use[0] = C.wb_last_use[1] = C.cnt["pe"]


def ln_fm(C, src_bf, emit_apply, gain_row=None, bias_row=None, rows_qgr=None):
    sb = C.sb
    PA, PB = C.ps["PA"], C.ps["PB"]
    ones_col = sb["ONE"][:, 0:1]
    ones_row = sb["ONER"][0:1, :]
    HBs = [sb["HB"]]

    dve0 = C.val("dve")
    pe_base = C.val("pe")

    @C.V
    def _(v):
        for dc in range(DC):
            v.wait_ge(C.sem("pe"), pe_base + dc)
            ins = v.tensor_tensor(HBs[0][:, :], src_bf(dc), src_bf(dc), OP.mult)
            ins.then_inc(C.sem("dve"), 1)
    C.cnt["dve"] += DC

    @C.T
    def _(t):
        psum_guard(C, t)
        for dc in range(DC):
            t.wait_ge(C.sem("dve"), dve0 + dc + 1)
            for ts in range(NTS):
                t.matmul(PB[0:1, 512 * ts:512 * (ts + 1)], ones_col,
                         src_bf(dc)[:, 512 * ts:512 * (ts + 1)],
                         start=(dc == 0), stop=(dc == DC - 1), skip_group_check=True)
                ins = t.matmul(PB[64:65, 512 * ts:512 * (ts + 1)], ones_col,
                               HBs[0][:, 512 * ts:512 * (ts + 1)],
                               start=(dc == 0), stop=(dc == DC - 1), skip_group_check=True)
            ins.then_inc(C.sem("pe"), 1)
    C.cnt["pe"] += DC
    stats_mark = C.cnt["pe"]

    mu = PB[32:33, 0:T]
    var = PA[0:1, 0:T]
    arow_bf = sb["ABR"][0:1, 0:T]
    brow_bf = sb["ABR"][0:1, T:2 * T]

    @C.V
    def _(v):
        v.wait_ge(C.sem("pe"), stats_mark)
        v.tensor_scalar(mu, PB[0:1, 0:T], 1.0 / D, None, OP.mult)
        ins = v.tensor_scalar(var, PB[64:65, 0:T], 1.0 / D, None, OP.mult)
        ins.then_inc(C.sem("dve"), 1)
    C.cnt["dve"] += 1
    mom_mark = C.cnt["dve"]

    @C.A
    def _(a):
        a.wait_ge(C.sem("dve"), mom_mark)
        ins = a.activation(out=brow_bf, in_=mu, func=AF.Square)
        ins.then_inc(C.sem("act"), 1)
    C.cnt["act"] += 1
    m2_mark = C.cnt["act"]

    @C.V
    def _(v):
        v.wait_ge(C.sem("act"), m2_mark)
        ins = v.tensor_tensor(var, var, brow_bf, OP.subtract)
        ins.then_inc(C.sem("dve"), 1)
    C.cnt["dve"] += 1
    var_mark = C.cnt["dve"]

    @C.A
    def _(a):
        a.wait_ge(C.sem("dve"), var_mark)
        a.activation(out=var, in_=var, func=AF.Ln, bias=sb["EPSC"][0:1, 0:1])
        ins = a.activation(out=arow_bf, in_=var, func=AF.Exp, scale=-0.5)
        ins.then_inc(C.sem("act"), 1)
    C.cnt["act"] += 1
    a_mark = C.cnt["act"]

    @C.V
    def _(v):
        v.wait_ge(C.sem("act"), a_mark)
        ins = v.scalar_tensor_tensor(brow_bf, mu, -1.0, arow_bf, OP.mult, OP.mult)
        ins.then_inc(C.sem("dve"), 1)
    C.cnt["dve"] += 1
    rows_mark = C.cnt["dve"]

    groups = []
    for dc in range(DC):
        def mm(t, ps, dc=dc):
            if gain_row is not None:
                ga = gain_row[:, 128 * dc:128 * (dc + 1)]
            else:
                ga = ones_row[:, 0:128]
            for ts in range(NTS):
                t.matmul(PA[:, 512 * ts:512 * (ts + 1)], ga,
                         arow_bf[:, 512 * ts:512 * (ts + 1)], start=True, stop=True)
            for ts in range(NTS):
                if bias_row is not None:
                    t.matmul(PB[:, 512 * ts:512 * (ts + 1)], ga,
                             brow_bf[:, 512 * ts:512 * (ts + 1)], start=True, stop=False)
                    ins = t.matmul(PB[:, 512 * ts:512 * (ts + 1)],
                                   bias_row[:, 128 * dc:128 * (dc + 1)],
                                   ones_row[:, 0:512], start=False, stop=True)
                else:
                    ins = t.matmul(PB[:, 512 * ts:512 * (ts + 1)], ga,
                                   brow_bf[:, 512 * ts:512 * (ts + 1)], start=True, stop=True)
            return ins

        w = [("dve", rows_mark)]
        if rows_qgr is not None:
            w.append(("qgr", rows_qgr))
        groups.append({"mm": mm,
                       "evac": (lambda e, ps, dc=dc: emit_apply(e, dc, PA, PB)),
                       "eng": "vector",
                       "wait": w})
    run_pipe(C, groups, [PA], guard=False)


def postln_from_W(C):
    sb = C.sb
    S, W, XB = sb["S"], sb["W"], sb["XB"]

    @C.V
    def _(v):
        for dc in range(DC):
            ins = v.tensor_copy(XB[:, dc, :], W[:, dc, :])
            ins.then_inc(C.sem("dve"), 1)
    C.cnt["dve"] += DC

    def apply(v, dc, a_ps, b_ps):
        v.tensor_tensor(S[:, dc, :], W[:, dc, :], a_ps[:, :], OP.mult)
        v.tensor_tensor(S[:, dc, :], S[:, dc, :], b_ps[:, :], OP.add)
        return v.tensor_copy(XB[:, dc, :], S[:, dc, :])

    ln_fm(C, lambda dc: XB[:, dc, :], apply)


def emit_prelude(C):
    sb = C.sb

    @C.Y
    def _(s):
        ins = s.dma_start(out=sb["S"][:, :, :],
                          in_=C.dr["xin"].rearrange("(c p) t -> p c t", p=128))
        C.inc(ins, "qs", 16)

    @C.V
    def _(v):
        v.memset(sb["ONE"][:, :], 1.0)
        v.memset(sb["ONER"][:, :], 1.0)
        v.memset(sb["ONERF"][:, :], 1.0)
        v.memset(sb["ONEF"][:, :], 1.0)
        v.memset(sb["EPSC"][:, :], EPS_LN)
        v.memset(sb["IDN"][:, :], 1.0)
        C.wait(v, "qs")
        for dc in range(DC):
            ins = v.tensor_copy(sb["XB"][:, dc, :], sb["S"][:, dc, :])
        C.inc(ins, "dve")
    m = C.val("dve")

    @C.G
    def _(g):
        g.wait_ge(C.sem("dve"), m)
        g.affine_select(sb["IDN"][:, :], sb["IDN"][:, :], [[-1, 128]],
                        OP.is_equal, 0.0, base=0, channel_multiplier=1)
        ins = g.iota(sb["PIDX"][:, :], [[0, 1]], base=0, channel_multiplier=1,
                     allow_small_or_imprecise_dtypes=True)
        C.inc(ins, "pool")


def emit_epilogue(C):
    sb = C.sb
    m_dve = C.val("dve")
    m_evd = C.val("evd")
    m_eva = C.val("eva")

    @C.G
    def _(g):
        g.wait_ge(C.sem("dve"), m_dve)
        if m_evd:
            g.wait_ge(C.sem("evd"), m_evd)
        if m_eva:
            g.wait_ge(C.sem("eva"), m_eva)
        ins = g.dma_start(out=C.dr["xout"].rearrange("(c p) t -> p c t", p=128),
                          in_=sb["S"][:, :, :])
        C.inc(ins, "qo", 16)
        g.wait_ge(C.sem("qo"), C.val("qo"))


def emit_l0_attn(C):
    block_barrier(C)
    sb = C.sb
    S, W, XB = sb["S"], sb["W"], sb["XB"]
    PA, PB = C.ps["PA"], C.ps["PB"]
    ones_row = sb["ONER"][0:1, :]
    ones_col = sb["ONE"][:, 0:1]
    BR = sb["BIASR"]

    Wflat = W[:, :, :].rearrange("p c t -> p (c t)").bitcast(BF16)
    expk = Wflat[:, 0:16384].rearrange("p (a b) -> p a b", b=1024)
    vtm = Wflat[:, 16384:32768].rearrange("p (a b) -> p a b", b=1024)
    qs = Wflat[:, 0:16384].rearrange("p (a b) -> p a b", b=2048)
    gate = Wflat[:, 16384:32768].rearrange("p (a b) -> p a b", b=2048)

    load_rows_bf(C, C.dr["L0b_qkv"], D, 3 * D, BR[0:1, 0:2048])
    load_rows_bf(C, C.dr["L0b_qkv"], 0, D, BR[0:1, 2048:3072])
    rows0 = load_rows_bf(C, C.dr["L0b_g1"], 0, 64, sb["TB"][0:1, 0:64])

    wq = C.dr["L0W_qkv"]
    pe_entry = C.val("pe")
    groups = []
    for s in range(4):
        og = D + 512 * s
        wp = pe_entry + 16 * (s - 1) if s >= 2 else None
        ap, mark, bi = wslab_load(C, wq, og, 512, wait_pe=wp)
        is_k = s < 2
        for tc in range(TC):
            def mm(t, ps, ap=ap, tc=tc, s=s):
                for dc in range(DC):
                    t.matmul(ps[:, 0:512], XB[:, dc, 128 * tc:128 * (tc + 1)],
                             ap[:, dc, :], start=(dc == 0), stop=False)
                return t.matmul(ps[:, 0:512], ones_row[:, 0:128],
                                BR[0:1, 512 * s:512 * (s + 1)], start=False, stop=True)

            def evac(e, ps, s=s, tc=tc, is_k=is_k):
                if is_k:
                    return e.activation(out=expk[:, tc, 512 * s:512 * (s + 1)],
                                        in_=ps[:, 0:512], func=AF.Exp)
                return e.tensor_copy(vtm[:, tc, 512 * (s - 2):512 * (s - 1)],
                                     ps[:, 0:512])

            groups.append({"mm": mm, "evac": evac,
                           "eng": "scalar" if is_k else "vector",
                           "wait": [mark, ("qgr", rows0)]})
    bufs = [PA[:, 0:512], PA[:, 512:1024], PB[:, 0:512], PB[:, 512:1024]]
    run_pipe(C, groups, bufs)
    C.wb_last_use[0] = C.wb_last_use[1] = C.cnt["pe"]
    phaseA_end = C.cnt["pe"]

    @C.T
    def _(t):
        psum_guard(C, t)
        for b in range(NB):
            for h in range(16):
                r0 = 64 * (h % 2)
                c0 = 64 * (8 * b + h // 2)
                for j in range(8):
                    tc = 8 * b + j
                    ins = t.matmul(PA[r0:r0 + 64, c0:c0 + 64],
                                   expk[:, tc, 64 * h:64 * (h + 1)],
                                   vtm[:, tc, 64 * h:64 * (h + 1)],
                                   start=(j == 0), stop=(j == 7),
                                   skip_group_check=True)
        ins.then_inc(C.sem("pe"), 1)
        for b in range(NB):
            row = PB[64 * b:64 * b + 1, :]
            for j in range(8):
                tc = 8 * b + j
                for ts in range(2):
                    ins = t.matmul(row[:, 512 * ts:512 * (ts + 1)], ones_col,
                                   expk[:, tc, 512 * ts:512 * (ts + 1)],
                                   start=(j == 0), stop=(j == 7),
                                   skip_group_check=True)
        ins.then_inc(C.sem("pe"), 1)
    C.cnt["pe"] += 2
    kv_mark = C.val("pe")

    SM2F = sb["SM2"][:, :].bitcast(F32)
    HBF = sb["HB"][:, :].bitcast(F32)

    @C.V
    def _(v):
        v.wait_ge(C.sem("pe"), kv_mark)
        v.tensor_copy(sb["SM"][:, 0:1024], PA[:, 0:1024])
        v.tensor_copy(SM2F[0:1, 0:1024], PB[0:1, 0:1024])
        ins = v.tensor_copy(HBF[0:1, 0:1024], PB[64:65, 0:1024])
        ins.then_inc(C.sem("evd"), 1)
    C.cnt["evd"] += 1
    kv_evac_mark = C.val("evd")

    @C.G
    def _(g):
        g.wait_ge(C.sem("evd"), kv_evac_mark)
        ins = g.dma_start(out=C.dr["cc0_in"][0:128, :], in_=sb["SM"][:, 0:1024])
        C.inc(ins, "qgc", 16)
        ins = g.dma_start(out=C.dr["cc0_in"][128:129, :],
                          in_=SM2F[0:1, 0:1024])
        C.inc(ins, "qgc", 16)
        ins = g.dma_start(out=C.dr["cc0_in"][129:130, :],
                          in_=HBF[0:1, 0:1024])
        C.inc(ins, "qgc", 16)
        g.wait_ge(C.sem("qgc"), C.val("qgc"))
        ins = g.collective_compute("AllReduce", OP.add,
                                   replica_groups=[list(range(NCORES))],
                                   ins=[C.dr["cc0_in"].opt()],
                                   outs=[C.dr["cc0_out"].opt()])
        C.inc(ins, "cc", 1)
    cc_mark = C.val("cc")

    load_rows_bf(C, C.dr["L0b_g2"], 0, D, BR[0:1, 0:1024],
                 wait_pe=phaseA_end)
    rowsC = load_rows_bf(C, C.dr["L0_ln_g"], 0, D, BR[0:1, 1024:2048],
                         wait_pe=phaseA_end)

    def q_evac(e, oc, ps):
        return e.activation(out=qs[:, oc, :], in_=ps[:, :], func=AF.Silu)

    fm_proj(C, C.dr["L0W_qkv"], lambda dc: XB[:, dc, :], D, q_evac,
            evac_eng="scalar", bias_row=BR[0:1, 2048:3072], bias_mark=rows0)
    q_end = C.cnt["pe"]

    ap1, mark1, bi1 = wslab_load(C, C.dr["L0W_g1"], 0, 64)

    def gh_mm(t, ps):
        for ts in range(NTS):
            for dc in range(DC):
                t.matmul(ps[0:64, 512 * ts:512 * (ts + 1)], ap1[:, dc, :],
                         XB[:, dc, 512 * ts:512 * (ts + 1)],
                         start=(dc == 0), stop=False)
            ins = t.matmul(ps[0:64, 512 * ts:512 * (ts + 1)], sb["TB"][0:1, 0:64],
                           ones_row[:, 0:512], start=False, stop=True)
        return ins

    def gh_evac(e, ps):
        return e.tensor_copy(sb["SM2"][0:64, 0:T], ps[0:64, :])

    run_pipe(C, [{"mm": gh_mm, "evac": gh_evac, "eng": "vector",
                  "wait": [mark1, ("qgr", rows0)]}], [PA])
    C.wb_last_use[bi1] = C.cnt["pe"]
    gh_end = C.cnt["pe"]
    gh_mark = C.cnt["evd"]

    ap2, mark2, bi2 = wslab_load(C, C.dr["L0W_g2"], 0, 512)
    ap2b, mark2b, bi2b = wslab_load(C, C.dr["L0W_g2"], 512, 512)
    groups = []
    for oc in range(8):
        ap = ap2 if oc < 4 else ap2b
        j = oc % 4
        mk = mark2 if oc < 4 else mark2b

        def mm(t, ps, ap=ap, j=j, oc=oc):
            for ts in range(NTS):
                t.matmul(ps[:, 512 * ts:512 * (ts + 1)],
                         ap[0:64, 0, 128 * j:128 * (j + 1)],
                         sb["SM2"][0:64, 512 * ts:512 * (ts + 1)],
                         start=True, stop=False)
                ins = t.matmul(ps[:, 512 * ts:512 * (ts + 1)],
                               BR[0:1, 128 * oc:128 * (oc + 1)],
                               ones_row[:, 0:512], start=False, stop=True)
            return ins

        def evac(e, ps, oc=oc):
            return e.activation(out=gate[:, oc, :], in_=ps[:, :], func=AF.Sigmoid)

        groups.append({"mm": mm, "evac": evac, "eng": "scalar",
                       "wait": [mk, ("qgr", rowsC), ("evd", gh_mark)]})
    run_pipe(C, groups, [PA, PB])
    C.wb_last_use[bi2] = C.wb_last_use[bi2b] = C.cnt["pe"]

    gates_end = C.cnt["pe"]
    load_rows_bf(C, C.dr["L0_ln_b"], 0, D, BR[0:1, 2048:3072],
                 wait_pe=gates_end)
    rowsD = load_rows_bf(C, C.dr["L0b_out"], 0, D, BR[0:1, 0:1024],
                         wait_pe=gates_end)

    @C.G
    def _(g):
        g.wait_ge(C.sem("cc"), cc_mark)
        ins = g.dma_start(out=sb["SM"][:, 0:1024], in_=C.dr["cc0_out"][0:128, :])
        C.inc(ins, "qgc", 16)
        ins = g.dma_start(out=SM2F[0:1, 0:1024],
                          in_=C.dr["cc0_out"][128:129, :])
        C.inc(ins, "qgc", 16)
        ins = g.dma_start(out=HBF[0:1, 0:1024],
                          in_=C.dr["cc0_out"][129:130, :])
        C.inc(ins, "qgc", 16)
    cc_done = C.val("qgc")

    @C.T
    def _(t):
        psum_guard(C, t)
        t.wait_ge(C.sem("qgc"), cc_done)
        for dc in range(DC):
            for b in range(NB):
                srow = SM2F if b == 0 else HBF
                ins = t.matmul(PA[:, (2 * dc + b):(2 * dc + b) + 1],
                               srow[0:1, 128 * dc:128 * (dc + 1)],
                               sb["IDN"][0:1, 0:1], start=True, stop=True)
        ins.then_inc(C.sem("pe"), 1)
    C.cnt["pe"] += 1
    tr_mark = C.val("pe")

    @C.V
    def _(v):
        v.wait_ge(C.sem("pe"), tr_mark)
        ins = v.reciprocal(sb["STAT"][:, 0:16], PA[:, 0:16])
        ins.then_inc(C.sem("evd"), 1)
    C.cnt["evd"] += 1
    recip_mark = C.val("evd")

    @C.V
    def _(v):
        v.wait_ge(C.sem("evd"), recip_mark)
        for dc in range(DC):
            for b in range(NB):
                ins = v.tensor_tensor(
                    qs[:, dc, 1024 * b:1024 * (b + 1)],
                    qs[:, dc, 1024 * b:1024 * (b + 1)],
                    sb["STAT"][:, 2 * dc + b:2 * dc + b + 1].broadcast_to([128, 1024]),
                    OP.mult)
        ins.then_inc(C.sem("dve"), 1)
    C.cnt["dve"] += 1
    qscale_mark = C.val("dve")

    @C.V
    def _(v):
        v.memset(sb["SM2"][:, :], 0.0)
        for b in range(NB):
            for hp in range(8):
                c0 = 64 * (8 * b + hp)
                d0 = 128 * (8 * b + hp)
                v.tensor_copy(sb["SM2"][0:64, d0:d0 + 64],
                              sb["SM"][0:64, c0:c0 + 64])
                ins = v.tensor_copy(sb["SM2"][64:128, d0 + 64:d0 + 128],
                                    sb["SM"][64:128, c0:c0 + 64])
        ins.then_inc(C.sem("dve"), 1)
    C.cnt["dve"] += 1
    kvd_mark = C.val("dve")

    groups = []
    for b in range(NB):
        for hp in range(8):
            d0 = 128 * (8 * b + hp)

            def mm(t, ps, b=b, hp=hp, d0=d0):
                for ts in range(2):
                    ins = t.matmul(ps[:, 512 * ts:512 * (ts + 1)],
                                   sb["SM2"][:, d0:d0 + 128],
                                   qs[:, hp, 1024 * b + 512 * ts:1024 * b + 512 * (ts + 1)],
                                   start=True, stop=True)
                return ins

            def evac(e, ps, b=b, hp=hp):
                return e.tensor_copy(XB[:, hp, 1024 * b:1024 * (b + 1)],
                                     ps[:, 0:1024])

            groups.append({"mm": mm, "evac": evac, "eng": "vector",
                           "wait": [("dve", max(qscale_mark, kvd_mark))]})
    run_pipe(C, groups, [PA[:, 0:1024], PB[:, 0:1024]])

    def apply(v, dc, a_ps, b_ps):
        v.tensor_tensor(sb["HB"][:, :], XB[:, dc, :], a_ps[:, :], OP.mult)
        v.tensor_tensor(sb["HB"][:, :], sb["HB"][:, :], b_ps[:, :], OP.add)
        return v.tensor_tensor(XB[:, dc, :], sb["HB"][:, :], gate[:, dc, :], OP.mult)

    ln_fm(C, lambda dc: XB[:, dc, :], apply,
          gain_row=BR[0:1, 1024:2048], bias_row=BR[0:1, 2048:3072],
          rows_qgr=rowsD)

    def oevac(e, oc, ps):
        return e.tensor_tensor(W[:, oc, :], ps[:, :], S[:, oc, :], OP.add)

    fm_proj(C, C.dr["L0W_out"], lambda dc: XB[:, dc, :], D, oevac,
            evac_eng="vector", bias_row=BR[0:1, 0:1024], bias_mark=rowsD)

    postln_from_W(C)


def affine_ln_W_to_XB(C, gain_row, bias_row, rows_qgr=None):
    sb = C.sb
    W, XB = sb["W"], sb["XB"]

    @C.V
    def _(v):
        for dc in range(DC):
            ins = v.tensor_copy(XB[:, dc, :], W[:, dc, :])
            ins.then_inc(C.sem("dve"), 1)
    C.cnt["dve"] += DC

    def apply(v, dc, a_ps, b_ps):
        v.tensor_tensor(sb["HB"][:, :], XB[:, dc, :], a_ps[:, :], OP.mult)
        return v.tensor_tensor(XB[:, dc, :], sb["HB"][:, :], b_ps[:, :], OP.add)

    ln_fm(C, lambda dc: XB[:, dc, :], apply, gain_row=gain_row,
          bias_row=bias_row, rows_qgr=rows_qgr)


def postln_from_S(C):
    sb = C.sb
    S, XB = sb["S"], sb["XB"]

    @C.V
    def _(v):
        for dc in range(DC):
            ins = v.tensor_copy(XB[:, dc, :], S[:, dc, :])
            ins.then_inc(C.sem("dve"), 1)
    C.cnt["dve"] += DC

    def apply(v, dc, a_ps, b_ps):
        v.tensor_tensor(S[:, dc, :], S[:, dc, :], a_ps[:, :], OP.mult)
        v.tensor_tensor(S[:, dc, :], S[:, dc, :], b_ps[:, :], OP.add)
        return v.tensor_copy(XB[:, dc, :], S[:, dc, :])

    ln_fm(C, lambda dc: XB[:, dc, :], apply)


def emit_l0_ffn(C):
    block_barrier(C)
    sb = C.sb
    S, W, XB = sb["S"], sb["W"], sb["XB"]
    PA, PB = C.ps["PA"], C.ps["PB"]
    ones_row = sb["ONER"][0:1, :]
    BR = sb["BIASR"]
    SM = sb["SM"]
    SM2 = sb["SM2"]
    HB = sb["HB"]

    Wflat = W[:, :, :].rearrange("p c t -> p (c t)").bitcast(BF16)
    qslot = Wflat[:, 0:16384].rearrange("p (a b) -> p a b", b=2048)
    kvslot = Wflat[:, 16384:24576].rearrange("p (a b) -> p a b", b=2048)
    hslot = Wflat[:, :].rearrange("p (a b) -> p a b", b=1024)

    m1 = load_rows_bf(C, C.dr["L0f_ln1g"], 0, D, BR[0:1, 0:1024])
    m2 = load_rows_bf(C, C.dr["L0f_ln1b"], 0, D, BR[0:1, 1024:2048])

    BONES = sb["BONES"]
    BONEST = sb["BONEST"]

    @C.V
    def _(v):
        v.memset(BONES[:, :], 0.0)
        for c in range(8):
            v.memset(BONES[0:64, 16 * c + 2 * c:16 * c + 2 * c + 1], 1.0)
            ins = v.memset(BONES[64:128, 16 * c + 2 * c + 1:16 * c + 2 * c + 2], 1.0)
        ins.then_inc(C.sem("dve"), 1)
    C.cnt["dve"] += 1

    SMI = SM[:, :].bitcast(mybir.dt.int16)

    @C.G
    def _(g):
        ins = g.iota(SMI[0:16, 0:1024].rearrange("p (c h x) -> p c h x", h=2, x=64),
                     [[2, 8], [1, 2], [0, 64]], base=0, channel_multiplier=0,
                     allow_small_or_imprecise_dtypes=True)
        C.inc(ins, "pool", 1)
    io_mark = C.val("pool")

    @C.V
    def _(v):
        v.wait_ge(C.sem("pool"), io_mark)
        ins = v.tensor_scalar(BONEST[0:16, :], SMI[0:16, 0:1024],
                              sb["PIDX"][0:16, 0:1], None, OP.is_equal)
        ins.then_inc(C.sem("dve"), 1)
    C.cnt["dve"] += 1

    def apply_ln1(v, dc, a_ps, b_ps):
        v.tensor_tensor(sb["HB"][:, :], XB[:, dc, :], a_ps[:, :], OP.mult)
        return v.tensor_tensor(XB[:, dc, :], sb["HB"][:, :], b_ps[:, :], OP.add)

    ln_fm(C, lambda dc: XB[:, dc, :], apply_ln1,
          gain_row=BR[0:1, 0:1024], bias_row=BR[0:1, 1024:2048], rows_qgr=m2)
    ln1_end = C.cnt["pe"]

    tap(C, "ln1", lambda: [(C.dr["xtap"][0:D, :].rearrange("(c p) t -> p c t", p=128),
                            sb["XB"][:, :, :])])
    if C.tapped:
        return

    def qkv_evac(e, oc, ps):
        if oc < 8:
            return e.tensor_copy(qslot[:, oc, :], ps[:, :])
        return e.tensor_copy(kvslot[:, oc - 8, :], ps[:, :])

    fm_proj(C, C.dr["L0fW_qkv"], lambda dc: XB[:, dc, :], 1536, qkv_evac,
            evac_eng="vector")

    tap(C, "qkv", lambda: [(C.dr["xtap"][0:D, :].rearrange("(c p) t -> p c t", p=128),
                            Wflat[:, 0:16384].rearrange("p (c t) -> p c t", t=2048)),
                           (C.dr["xtap"][D:D + 512, :].rearrange("(c p) t -> p c t", p=128),
                            Wflat[:, 16384:24576].rearrange("p (c t) -> p c t", t=2048))])
    if C.tapped:
        return

    pe0 = C.val("pe")
    dve0 = C.val("dve")

    @C.V
    def _(v):
        for c in range(8):
            g = c // 2
            kb = 64 * (g % 2)
            kc = g // 2
            if c > 0:
                v.wait_ge(C.sem("pe"), pe0 + c)
            v.tensor_copy(SM2[0:64, 0:1024], kvslot[kb:kb + 64, kc, 1024:2048])
            v.tensor_copy(SM2[64:128, 0:1024], kvslot[kb:kb + 64, kc, 1024:2048])
            v.tensor_tensor(SM2[:, 1024:2048], qslot[:, c, 1024:2048],
                            SM2[:, 0:1024], OP.mult)
            v.tensor_copy(HB[0:64, 0:1024], kvslot[kb:kb + 64, kc, 0:1024])
            v.tensor_copy(HB[64:128, 0:1024], kvslot[kb:kb + 64, kc, 0:1024])
            ins = v.tensor_tensor(HB[:, 1024:2048], qslot[:, c, 1024:2048],
                                  HB[:, 0:1024], OP.mult)
            ins.then_inc(C.sem("dve"), 1)
    C.cnt["dve"] += 8

    @C.T
    def _(t):
        psum_guard(C, t)
        for c in range(8):
            t.wait_ge(C.sem("dve"), dve0 + c + 1)
            for ts in range(2):
                t.matmul(PB[0:16, 512 * ts:512 * (ts + 1)],
                         BONES[:, 16 * c:16 * (c + 1)],
                         SM2[:, 1024 + 512 * ts:1024 + 512 * (ts + 1)],
                         start=(c == 0), stop=(c == 7), skip_group_check=True)
                ins = t.matmul(PB[32:48, 512 * ts:512 * (ts + 1)],
                               BONES[:, 16 * c:16 * (c + 1)],
                               HB[:, 1024 + 512 * ts:1024 + 512 * (ts + 1)],
                               start=(c == 0), stop=(c == 7), skip_group_check=True)
            ins.then_inc(C.sem("pe"), 1)
    C.cnt["pe"] += 8
    sc_mark = C.cnt["pe"]

    @C.V
    def _(v):
        v.wait_ge(C.sem("pe"), sc_mark)
        ins = v.tensor_copy(SM[0:16, 0:1024], PB[0:16, 0:1024])
        ins.then_inc(C.sem("dve"), 1)
    C.cnt["dve"] += 1

    @C.V
    def _(v):
        ins = v.tensor_tensor(SM[32:48, 0:1024], PB[32:48, 0:1024],
                              SM[0:16, 0:1024], OP.subtract)
        ins.then_inc(C.sem("dve"), 1)
    C.cnt["dve"] += 1
    d_mark = C.cnt["dve"]

    @C.A
    def _(a):
        a.wait_ge(C.sem("dve"), d_mark)
        ins = a.activation(out=SM2[0:16, 0:1024], in_=SM[32:48, 0:1024],
                           func=AF.Sigmoid, scale=0.125)
        ins.then_inc(C.sem("act"), 1)
    C.cnt["act"] += 1
    p_mark = C.cnt["act"]

    groups = []
    for c in range(8):
        g = c // 2
        vb = 64 * (g % 2)
        vc = 2 + g // 2

        def mm(t, ps, c=c):
            for ts in range(2):
                ins = t.matmul(ps[:, 512 * ts:512 * (ts + 1)],
                               BONEST[0:16, 128 * c:128 * (c + 1)],
                               SM2[0:16, 512 * ts:512 * (ts + 1)],
                               start=True, stop=True)
            return ins

        def evac(e, ps, c=c, vb=vb, vc=vc):
            e.tensor_copy(HB[0:64, 0:1024], kvslot[vb:vb + 64, vc, 0:1024])
            e.tensor_copy(HB[64:128, 0:1024], kvslot[vb:vb + 64, vc, 0:1024])
            e.tensor_copy(HB[0:64, 1024:2048], kvslot[vb:vb + 64, vc, 1024:2048])
            e.tensor_copy(HB[64:128, 1024:2048], kvslot[vb:vb + 64, vc, 1024:2048])
            e.tensor_copy(qslot[:, c, 0:1024], HB[:, 0:1024])
            e.tensor_tensor(SM[:, 0:1024], HB[:, 0:1024], HB[:, 1024:2048],
                            OP.subtract)
            e.scalar_tensor_tensor(SM[:, 0:1024], ps[:, 0:1024], 1.0,
                                   SM[:, 0:1024], OP.mult, OP.mult)
            return e.tensor_tensor(qslot[:, c, 1024:2048], SM[:, 0:1024],
                                   HB[:, 1024:2048], OP.add)

        groups.append({"mm": mm, "evac": evac, "eng": "vector",
                       "wait": [("act", p_mark)]})
    run_pipe(C, groups, [PA[:, 0:1024], PB[:, 0:1024]])

    tap(C, "attn", lambda: [(C.dr["xtap"][0:D, :].rearrange("(c p) t -> p c t", p=128),
                             Wflat[:, 0:16384].rearrange("p (c t) -> p c t", t=2048)),
                            (C.dr["xtap"][D:D + 16, 0:1024],
                             sb["SM2"][0:16, 0:1024])])
    if C.tapped:
        return

    def pevac(e, oc, ps):
        e.tensor_tensor(XB[:, oc, :], ps[:, :], S[:, oc, :], OP.add)
        return e.tensor_tensor(S[:, oc, :], S[:, oc, :], XB[:, oc, :], OP.add)

    fm_proj(C, C.dr["L0fW_proj"], lambda dc: qslot[:, dc, :], D, pevac,
            evac_eng="vector")
    proj_end = C.cnt["pe"]

    m3 = load_rows_bf(C, C.dr["L0f_ln2g"], 0, D, BR[0:1, 0:1024],
                      wait_pe=ln1_end)
    m4 = load_rows_bf(C, C.dr["L0f_ln2b"], 0, D, BR[0:1, 1024:2048],
                      wait_pe=ln1_end)
    def apply_ln2(v, dc, a_ps, b_ps):
        v.tensor_tensor(sb["HB"][:, :], XB[:, dc, :], a_ps[:, :], OP.mult)
        return v.tensor_tensor(XB[:, dc, :], sb["HB"][:, :], b_ps[:, :], OP.add)

    ln_fm(C, lambda dc: XB[:, dc, :], apply_ln2,
          gain_row=BR[0:1, 0:1024], bias_row=BR[0:1, 1024:2048], rows_qgr=m4)

    tap(C, "ln2", lambda: [(C.dr["xtap"][0:D, :].rearrange("(c p) t -> p c t", p=128),
                            sb["XB"][:, :, :]),
                           (C.dr["xtap"][D:2 * D, :].rearrange("(c p) t -> p c t", p=128),
                            sb["S"][:, :, :])])
    if C.tapped:
        return

    apr, mr, bir = wslab_load(C, C.dr["L0fW_rt"], 0, 8)
    pe0r = C.val("pe")

    @C.T
    def _(t):
        psum_guard(C, t)
        t.wait_ge(C.sem(mr[0]), mr[1])
        for tc in range(TC):
            for dc in range(DC):
                ins = t.matmul(PA[:, 8 * tc:8 * (tc + 1)],
                               XB[:, dc, 128 * tc:128 * (tc + 1)],
                               apr[:, dc, :], start=(dc == 0), stop=(dc == DC - 1),
                               skip_group_check=True)
        ins.then_inc(C.sem("pe"), 1)
    C.cnt["pe"] += 1
    rt_mark = C.cnt["pe"]
    C.wb_last_use[bir] = C.cnt["pe"]

    RT = SM[:, 0:128].rearrange("p (a b) -> p a b", b=8)
    RX = SM[:, 128:256].rearrange("p (a b) -> p a b", b=8)
    RS = SM[:, 256:272].rearrange("p (a b) -> p a b", b=1)
    RS2 = SM[:, 272:288].rearrange("p (a b) -> p a b", b=1)
    dve_r0 = C.val("dve")

    @C.V
    def _(v):
        v.wait_ge(C.sem("pe"), rt_mark)
        v.tensor_copy(SM[:, 0:128], PA[:, 0:128])
        ins = v.reduce_max(RS[:, :, :], RT[:, :, :], axis=AX.X)
        ins.then_inc(C.sem("dve"), 1)
        v.wait_ge(C.sem("dve"), C.cnt["dve"] + 1)
        ins = v.tensor_tensor(RX[:, :, :], RT[:, :, :],
                              RS[:, :, :].broadcast_to([128, 16, 8]), OP.subtract)
        ins.then_inc(C.sem("dve"), 1)
    C.cnt["dve"] += 2
    r1_mark = C.cnt["dve"]

    tap(C, "logits", lambda: [(C.dr["xtap"][0:128, 0:512], sb["SM"][:, 0:512])])

    @C.A
    def _(a):
        a.wait_ge(C.sem("dve"), r1_mark)
        ins = a.activation(out=SM[:, 128:256], in_=SM[:, 128:256], func=AF.Exp)
        ins.then_inc(C.sem("act"), 1)
    C.cnt["act"] += 1
    r2_mark = C.cnt["act"]

    @C.V
    def _(v):
        def edge(ins):
            ins.then_inc(C.sem("dve"), 1)
            C.cnt["dve"] += 1
            v.wait_ge(C.sem("dve"), C.cnt["dve"])

        v.wait_ge(C.sem("act"), r2_mark)
        edge(v.reduce_sum(RS[:, :, :], RX[:, :, :], axis=AX.X))
        edge(v.reciprocal(SM[:, 272:288], SM[:, 256:272]))
        v.tensor_tensor(RX[:, :, :], RX[:, :, :],
                        RS2[:, :, :].broadcast_to([128, 16, 8]), OP.mult)
        edge(v.reduce_max(RS[:, :, :], RX[:, :, :], axis=AX.X))
        v.tensor_tensor(RT[:, :, :], RX[:, :, :],
                        RS[:, :, :].broadcast_to([128, 16, 8]), OP.is_ge)
        v.scalar_tensor_tensor(RT[:, :, :], RT[:, :, :], -1e30, RX[:, :, :],
                               OP.mult, OP.add)
        edge(v.reduce_max(RS2[:, :, :], RT[:, :, :], axis=AX.X))
        v.tensor_tensor(RT[:, :, :], RX[:, :, :],
                        RS2[:, :, :].broadcast_to([128, 16, 8]), OP.is_ge)
        v.tensor_tensor(RT[:, :, :], RT[:, :, :], RX[:, :, :], OP.mult)
        edge(v.tensor_tensor(RS[:, :, 0:1], RS[:, :, 0:1], RS2[:, :, 0:1], OP.add))
        edge(v.reciprocal(SM[:, 272:288], SM[:, 256:272]))
        v.tensor_tensor(RT[:, :, :], RT[:, :, :],
                        RS2[:, :, :].broadcast_to([128, 16, 8]), OP.mult)
        ins = v.tensor_copy(SM[:, 288:416], SM[:, 0:128])
        ins.then_inc(C.sem("dve"), 1)
    C.cnt["dve"] += 1
    rw_mark = C.cnt["dve"]

    tap(C, "rmath", lambda: [(C.dr["xtap"][0:128, 0:512], sb["SM"][:, 0:512])])
    if C.tapped:
        return

    @C.T
    def _(t):
        psum_guard(C, t)
        t.wait_ge(C.sem("dve"), rw_mark)
        for tc in range(TC):
            ins = t.transpose(PB[0:8, 128 * tc:128 * (tc + 1)],
                              SM[:, 288 + 8 * tc:288 + 8 * (tc + 1)],
                              sb["IDN"][:, :])
        ins.then_inc(C.sem("pe"), 1)
    C.cnt["pe"] += 1
    wt_mark = C.cnt["pe"]

    WEB = SM[:, :].bitcast(BF16)

    @C.V
    def _(v):
        v.wait_ge(C.sem("pe"), wt_mark)
        ins = v.tensor_copy(WEB[0:8, 0:2048], PB[0:8, 0:2048])
        ins.then_inc(C.sem("dve"), 1)
    C.cnt["dve"] += 1
    we_mark = C.cnt["dve"]

    HBI = sb["HB"][:, :].bitcast(mybir.dt.int16)
    dve_hb = C.val("evd")

    @C.G
    def _(g):
        g.wait_ge(C.sem("dve"), we_mark)
        ins = g.iota(HBI[0:8, 0:1024], [[1, 8], [0, 128]], base=0,
                     channel_multiplier=0, allow_small_or_imprecise_dtypes=True)
        C.inc(ins, "pool", 1)
    sel_mark = C.val("pool")

    @C.V
    def _(v):
        v.wait_ge(C.sem("pool"), sel_mark)
        ins = v.tensor_scalar(sb["BONEST"][0:8, 0:1024], HBI[0:8, 0:1024],
                              sb["PIDX"][0:8, 0:1], None, OP.is_equal)
        ins.then_inc(C.sem("dve"), 1)
    C.cnt["dve"] += 1
    selv_mark = C.cnt["dve"]

    tap(C, "router", lambda: [(C.dr["xtap"][0:8, :], WEB[0:8, 0:2048]),
                              (C.dr["xtap"][32:160, 0:128], sb["SM"][:, 288:416])])
    if C.tapped:
        return

    for tt in range(2):
        tcol = 1024 * tt
        for e in range(8):
            pe_web = C.val("pe")

            @C.T
            def _(t, e=e):
                psum_guard(C, t)
                t.wait_ge(C.sem("dve"), max(we_mark, selv_mark))
                for ts in range(NTS):
                    ins = t.matmul(PA[:, 512 * ts:512 * (ts + 1)],
                                   sb["BONEST"][0:8, 128 * e:128 * (e + 1)],
                                   WEB[0:8, 512 * ts:512 * (ts + 1)],
                                   start=True, stop=True)
                ins.then_inc(C.sem("pe"), 1)
            C.cnt["pe"] += 1
            web_mark = C.cnt["pe"]

            @C.V
            def _(v):
                v.wait_ge(C.sem("pe"), web_mark)
                ins = v.tensor_copy(SM2[:, :], PA[:, :])
                ins.then_inc(C.sem("evd"), 1)
            C.cnt["evd"] += 1

            w1e = C.dr["L0f_w1"][e]
            groups = []
            pe_entry = C.val("pe")
            for sgi in range(8):
                og = 512 * sgi
                wp = pe_entry + 4 * (sgi - 1) if sgi >= 2 else None
                ap, mark, bi = wslab_load(C, w1e, og, 512, wait_pe=wp)
                for j in range(4):
                    fc = 4 * sgi + j

                    def mm(t, ps, ap=ap, j=j):
                        for ts in range(2):
                            for dc in range(DC):
                                ins = t.matmul(
                                    ps[:, 512 * ts:512 * (ts + 1)],
                                    ap[:, dc, 128 * j:128 * (j + 1)],
                                    XB[:, dc, tcol + 512 * ts:tcol + 512 * (ts + 1)],
                                    start=(dc == 0), stop=(dc == DC - 1))
                        return ins

                    def evac(e_, ps, fc=fc):
                        return e_.activation(out=hslot[:, fc, :], in_=ps[:, 0:1024],
                                             func=AF.Gelu)

                    groups.append({"mm": mm, "evac": evac, "eng": "scalar",
                                   "wait": [mark]})
            run_pipe(C, groups, [PA[:, 0:1024], PB[:, 0:1024]])
            C.wb_last_use[0] = C.wb_last_use[1] = C.cnt["pe"]

            w2e = C.dr["L0f_w2"][e]
            groups = []
            pe_entry = C.val("pe")
            for hc in range(8):
                wp = pe_entry + (hc - 1) if hc >= 2 else None
                ap, mark, bi = wslab_load(C, w2e, 128 * hc, 128, wait_pe=wp)

                def mm(t, ps, ap=ap):
                    for ts in range(2):
                        for fc in range(32):
                            ins = t.matmul(ps[:, 512 * ts:512 * (ts + 1)],
                                           ap[:, fc, :],
                                           hslot[:, fc, 512 * ts:512 * (ts + 1)],
                                           start=(fc == 0), stop=(fc == 31))
                    return ins

                def evac(e_, ps, hc=hc):
                    e_.tensor_tensor(sb["HB"][:, 0:1024], ps[:, 0:1024],
                                     SM2[:, tcol:tcol + 1024], OP.mult)
                    return e_.tensor_tensor(S[:, hc, tcol:tcol + 1024],
                                            S[:, hc, tcol:tcol + 1024],
                                            sb["HB"][:, 0:1024], OP.add)

                groups.append({"mm": mm, "evac": evac, "eng": "vector",
                               "wait": [mark]})
            run_pipe(C, groups, [PA[:, 0:1024], PB[:, 0:1024]])
            C.wb_last_use[0] = C.wb_last_use[1] = C.cnt["pe"]

    postln_from_S(C)


def emit_l1_attn(C):
    block_barrier(C)
    sb = C.sb
    S, W, XB = sb["S"], sb["W"], sb["XB"]
    PA, PB = C.ps["PA"], C.ps["PB"]
    ones_row = sb["ONER"][0:1, :]
    ones_col = sb["ONE"][:, 0:1]
    BR = sb["BIASR"]
    SM = sb["SM"]
    SM2 = sb["SM2"]
    HB = sb["HB"]
    HBI = HB[:, :].bitcast(mybir.dt.int16)
    GAM = sb["STAT"][:, 0:1]
    LNG = sb["STAT"][:, 1:2]

    Wflat = W[:, :, :].rearrange("p c t -> p (c t)").bitcast(BF16)
    Qs = Wflat[:, 0:8192].rearrange("p (a b) -> p a b", b=1024)
    Ks = Wflat[:, 8192:16384].rearrange("p (a b) -> p a b", b=1024)
    KT = Wflat[:, 16384:24576].rearrange("p (a b) -> p a b", b=1024)
    VT = Wflat[:, 24576:32768].rearrange("p (a b) -> p a b", b=1024)

    load_rows_bf(C, C.dr["L1b_q"], 0, D, BR[0:1, 0:1024], wait_pe=C.val("pe"))
    load_rows_bf(C, C.dr["L1b_k"], 0, D, BR[0:1, 1024:2048])
    bm = load_rows_bf(C, C.dr["L1b_v"], 0, D, BR[0:1, 2048:3072])

    @C.G
    def _(g):
        ins = g.dma_start(out=GAM, in_=C.dr["L1_gamma"])
        C.inc(ins, "qg", 16)
        ins = g.dma_start(out=SM[0:1, 8:16], in_=C.dr["L1_dexp"])
        C.inc(ins, "qg", 16)
        ins = g.dma_start(out=SM[0:1, 16:24], in_=C.dr["L1_mask"])
        C.inc(ins, "qg", 16)
    gm = C.val("qg")

    @C.A
    def _(a):
        a.wait_ge(C.sem("qg"), gm)
        ins = a.activation(out=LNG, in_=GAM, func=AF.Ln)
        ins.then_inc(C.sem("act"), 1)
    C.cnt["act"] += 1
    lng_mark = C.cnt["act"]

    @C.G
    def _(g):
        g.wait_ge(C.sem("act"), lng_mark)
        ins = g.iota(HBI[:, 8:16], [[-128, 8]], base=L - 1,
                     channel_multiplier=-1, allow_small_or_imprecise_dtypes=True)
        C.inc(ins, "pool", 1)
    io_m = C.val("pool")

    @C.V
    def _(v):
        v.wait_ge(C.sem("pool"), io_m)
        v.tensor_scalar(sb["STAT"][:, 2:10], HBI[:, 8:16], LNG, None, OP.mult)
        ins = v.tensor_scalar(SM[0:1, 0:8], SM[0:1, 8:16], LNG[0:1, 0:1],
                              None, OP.mult)
        ins.then_inc(C.sem("dve"), 1)
    C.cnt["dve"] += 1
    dv_m = C.cnt["dve"]

    @C.A
    def _(a):
        a.wait_ge(C.sem("dve"), dv_m)
        a.activation(out=sb["STAT"][:, 2:10], in_=sb["STAT"][:, 2:10], func=AF.Exp)
        ins = a.activation(out=SM[0:1, 0:8], in_=SM[0:1, 0:8], func=AF.Exp)
        ins.then_inc(C.sem("act"), 1)
    C.cnt["act"] += 1
    dv_m2 = C.cnt["act"]

    @C.V
    def _(v):
        v.wait_ge(C.sem("act"), dv_m2)
        ins = v.tensor_tensor(SM[0:1, 0:8], SM[0:1, 0:8], SM[0:1, 16:24], OP.mult)
        ins.then_inc(C.sem("dve"), 1)
    C.cnt["dve"] += 1
    coef_m = C.cnt["dve"]

    @C.T
    def _(t):
        psum_guard(C, t)
        t.wait_ge(C.sem("dve"), coef_m)
        for k in range(NCORES):
            ins = t.matmul(PB[:, k:k + 1], sb["ONERF"][0:1, 0:128],
                           SM[0:1, k:k + 1], start=True, stop=True)
        ins.then_inc(C.sem("pe"), 1)
    C.cnt["pe"] += 1
    wk_m = C.cnt["pe"]

    @C.V
    def _(v):
        v.wait_ge(C.sem("pe"), wk_m)
        for k in range(NCORES):
            ins = v.scalar_tensor_tensor(SM2[:, 128 * k:128 * (k + 1)],
                                         sb["IDN"][:, :], 1.0,
                                         PB[:, k:k + 1].broadcast_to([128, 128]),
                                         OP.mult, OP.mult)
        ins.then_inc(C.sem("dve"), 1)
    C.cnt["dve"] += 1
    idist_m = C.cnt["dve"]

    ag_mark_prev = None
    for b in range(NB):
        bc = 1024 * b

        def ev_q(e, oc, ps):
            return e.activation(out=Qs[:, oc, :], in_=ps[:, :], func=AF.Relu)

        fm_proj(C, C.dr["L1W_q"], lambda dc: XB[:, dc, :], D, ev_q,
                evac_eng="scalar", bias_row=BR[0:1, 0:1024], tokens=(bc, bc + 1024),
                bias_mark=bm)

        def ev_k(e, oc, ps):
            return e.activation(out=Ks[:, oc, :], in_=ps[:, :], func=AF.Relu)

        fm_proj(C, C.dr["L1W_k"], lambda dc: XB[:, dc, :], D, ev_k,
                evac_eng="scalar", bias_row=BR[0:1, 1024:2048], tokens=(bc, bc + 1024),
                bias_mark=bm)

        groups = []
        for s in range(2):
            w_hbm = C.dr["L1W_k"] if s == 0 else C.dr["L1W_v"]
            brow = BR[0:1, 1024:2048] if s == 0 else BR[0:1, 2048:3072]
            dstsl = KT if s == 0 else VT
            for half in range(2):
                ap, mark, bi = wslab_load(C, w_hbm, 512 * half, 512)
                for tc in range(8):
                    def mm(t, ps, ap=ap, tc=tc, brow=brow, half=half):
                        for dc in range(DC):
                            t.matmul(ps[:, 0:512],
                                     XB[:, dc, bc + 128 * tc:bc + 128 * (tc + 1)],
                                     ap[:, dc, :], start=(dc == 0), stop=False)
                        return t.matmul(ps[:, 0:512], ones_row[:, 0:128],
                                        brow[:, 512 * half:512 * (half + 1)],
                                        start=False, stop=True)

                    def evac(e, ps, s=s, tc=tc, half=half, dstsl=dstsl):
                        if s == 0:
                            return e.activation(
                                out=dstsl[:, tc, 512 * half:512 * (half + 1)],
                                in_=ps[:, 0:512], func=AF.Relu)
                        return e.tensor_copy(
                            dstsl[:, tc, 512 * half:512 * (half + 1)], ps[:, 0:512])

                    groups.append({"mm": mm, "evac": evac,
                                   "eng": "scalar" if s == 0 else "vector",
                                   "wait": [mark, ("qgr", bm)]})
        bufs = [PA[:, 0:512], PA[:, 512:1024], PB[:, 0:512], PB[:, 512:1024]]
        run_pipe(C, groups, bufs)
        C.wb_last_use[0] = C.wb_last_use[1] = C.cnt["pe"]

        tap(C, "l1_proj", lambda: [(C.dr["xtap"][0:128, 0:1024], sb["SM"][:, 0:1024])])
        if C.tapped:
            return

        act_now = C.val("act")

        @C.V
        def _(v):
            v.wait_ge(C.sem("act"), act_now)
            for tc in range(8):
                ins = v.tensor_scalar(KT[:, tc, :], KT[:, tc, :],
                                      sb["STAT"][:, 2 + tc:3 + tc], None, OP.mult)
            ins.then_inc(C.sem("dve"), 1)
        C.cnt["dve"] += 1
        ktsc_m = C.cnt["dve"]

        pe_m0 = C.val("pe")
        evd_m0 = C.val("evd")
        qg_m0 = C.val("qg")

        @C.T
        def _(t):
            psum_guard(C, t)
            t.wait_ge(C.sem("dve"), ktsc_m)
            for dc in range(DC):
                if dc >= 2:
                    t.wait_ge(C.sem("evd"), evd_m0 + dc - 1)
                ps = [PA, PB][dc % 2]
                for jc in range(8):
                    for half in range(2):
                        t.matmul(ps[:, 512 * half:512 * (half + 1)],
                                 KT[:, jc, 128 * dc:128 * (dc + 1)],
                                 VT[:, jc, 512 * half:512 * (half + 1)],
                                 start=(jc == 0), stop=(jc == 7),
                                 skip_group_check=True)
                    ins = t.matmul(ps[:, 1024:1025],
                                   KT[:, jc, 128 * dc:128 * (dc + 1)],
                                   ones_col, start=(jc == 0), stop=(jc == 7),
                                   skip_group_check=True)
                ins.then_inc(C.sem("pe"), 1)
        C.cnt["pe"] += DC

        @C.V
        def _(v):
            for dc in range(DC):
                v.wait_ge(C.sem("pe"), pe_m0 + dc + 1)
                v.wait_ge(C.sem("qg"), qg_m0 + 16 * dc)
                ins = v.tensor_copy(HB[:, 0:1025], [PA, PB][dc % 2][:, 0:1025])
                ins.then_inc(C.sem("evd"), 1)
        C.cnt["evd"] += DC

        @C.G
        def _(g, b=b, ag_mark_prev=ag_mark_prev):
            if b == 1 and ag_mark_prev is not None:
                g.wait_ge(C.sem("cc"), ag_mark_prev)
            for dc in range(DC):
                g.wait_ge(C.sem("evd"), evd_m0 + dc + 1)
                ins = g.dma_start(out=C.dr["cc1_in"][128 * dc:128 * (dc + 1), 0:1025],
                                  in_=HB[:, 0:1025])
                C.inc(ins, "qg", 16)

        tap(C, "l1_m", lambda: [(C.dr["xtap"][0:128, 0:1024], sb["SM"][:, 0:1024])])
        if C.tapped:
            return

        if not DEBUG_NOAG:
            @C.G
            def _(g):
                g.wait_ge(C.sem("qg"), C.val("qg"))
                ins = g.collective_compute("AllGather", OP.bypass,
                                           replica_groups=[list(range(NCORES))],
                                           ins=[C.dr["cc1_in"].opt()],
                                           outs=[C.dr["cc1_out"].opt()])
                C.inc(ins, "cc", 1)
        ag_mark = C.val("cc")
        ag_mark_prev = ag_mark

        pe_a0 = C.val("pe")
        dve_a0 = C.val("dve")
        act_a0 = C.val("act")
        pool_a0 = C.val("pool")
        for jc in range(DEBUG_TRUNC):
            AGOPS = DEBUG_AGOPS

            @C.G
            def _(g, jc=jc):
                if jc == 0:
                    g.wait_ge(C.sem("evd"), C.val("evd"))
                else:
                    g.wait_ge(C.sem("dve"), dve_a0 + 2 * jc - 1)
                ins = g.iota(HBI[:, 0:1024], [[1, 1024]], base=-128 * jc,
                             channel_multiplier=-1,
                             allow_small_or_imprecise_dtypes=True)
                C.inc(ins, "pool", 1)

            if AGOPS >= 2:
                @C.V
                def _(v, jc=jc):
                    v.wait_ge(C.sem("pool"), pool_a0 + 2 * jc + 1)
                    ins = v.tensor_scalar(SM[:, 0:1024], HBI[:, 0:1024], LNG, 0.0,
                                          OP.mult, OP.min)
                    ins.then_inc(C.sem("dve"), 1)

            if AGOPS >= 3:
                @C.A
                def _(a, jc=jc):
                    a.wait_ge(C.sem("dve"), dve_a0 + 2 * jc + 1)
                    ins = a.activation(out=SM[:, 0:1024], in_=SM[:, 0:1024],
                                       func=AF.Exp)
                    ins.then_inc(C.sem("act"), 1)

            if AGOPS >= 4:
                @C.G
                def _(g, jc=jc):
                    g.wait_ge(C.sem("act"), act_a0 + jc + 1)
                    ins = g.affine_select(SM[:, 0:1024], SM[:, 0:1024], [[1, 1024]],
                                          OP.is_ge, 0.0, base=-128 * jc,
                                          channel_multiplier=-1)
                    C.inc(ins, "pool", 1)

            if AGOPS >= 5:
                @C.T
                def _(t, jc=jc):
                    if jc == 0:
                        psum_guard(C, t)
                    t.wait_ge(C.sem("pool"), pool_a0 + 2 * (jc + 1))
                    if jc >= 2:
                        t.wait_ge(C.sem("dve"), dve_a0 + 2 * jc)
                    ps = [PA, PB][jc % 2]
                    for dc in range(DC):
                        for half in range(2):
                            ins = t.matmul(ps[:, 512 * half:512 * (half + 1)],
                                           Ks[:, dc, 128 * jc:128 * (jc + 1)],
                                           Qs[:, dc, 512 * half:512 * (half + 1)],
                                           start=(dc == 0), stop=(dc == DC - 1),
                                           skip_group_check=True)
                    ins.then_inc(C.sem("pe"), 1)

            if AGOPS >= 6:
                @C.V
                def _(v, jc=jc):
                    v.wait_ge(C.sem("pe"), pe_a0 + jc + 1)
                    ins = v.tensor_tensor(KT[:, jc, :], [PA, PB][jc % 2][:, 0:1024],
                                          SM[:, 0:1024], OP.mult)
                    ins.then_inc(C.sem("dve"), 1)
            if AGOPS >= 2:
                C.cnt["dve"] += 1
            if AGOPS >= 6:
                C.cnt["dve"] += 1
            if AGOPS >= 3:
                C.cnt["act"] += 1
            if AGOPS >= 5:
                C.cnt["pe"] += 1
        agam_end_d = C.cnt["dve"]
        tap(C, "l1_agam", lambda: [(C.dr["xtap"][0:128, 0:1024], sb["SM"][:, 0:1024])])
        if C.tapped:
            return


        @C.G
        def _(g):
            g.wait_ge(C.sem("dve"), agam_end_d)
            ins = g.iota(HBI[0:1, 0:1024], [[1, 1024]], base=1,
                         channel_multiplier=0, allow_small_or_imprecise_dtypes=True)
            C.inc(ins, "pool", 1)
        gq_m = C.val("pool")

        @C.V
        def _(v):
            v.wait_ge(C.sem("pool"), gq_m)
            ins = v.tensor_scalar(SM[0:1, 0:1024], HBI[0:1, 0:1024],
                                  LNG[0:1, 0:1], None, OP.mult)
            ins.then_inc(C.sem("dve"), 1)
        C.cnt["dve"] += 1
        gq_m2 = C.cnt["dve"]

        @C.A
        def _(a):
            a.wait_ge(C.sem("dve"), gq_m2)
            ins = a.activation(out=sb["ABR"][0:1, 0:1024], in_=SM[0:1, 0:1024],
                               func=AF.Exp)
            ins.then_inc(C.sem("act"), 1)
        C.cnt["act"] += 1
        gq_m3 = C.cnt["act"]

        @C.T
        def _(t):
            t.wait_ge(C.sem("act"), gq_m3)
            t.wait_ge(C.sem("evd"), C.val("evd"))
            for half in range(2):
                ins = t.matmul(PA[:, 1024 + 512 * half:1024 + 512 * (half + 1)],
                               ones_row[:, 0:128],
                               sb["ABR"][0:1, 512 * half:512 * (half + 1)],
                               start=True, stop=True)
            ins.then_inc(C.sem("pe"), 1)
        C.cnt["pe"] += 1
        gq_m4 = C.cnt["pe"]

        @C.V
        def _(v):
            v.wait_ge(C.sem("pe"), gq_m4)
            for dc in range(DC):
                ins = v.tensor_tensor(Qs[:, dc, :], Qs[:, dc, :],
                                      PA[:, 1024:2048], OP.mult)
            ins.then_inc(C.sem("dve"), 1)
        C.cnt["dve"] += 1
        qt_m = C.cnt["dve"]
        tap(C, "l1_gq", lambda: [(C.dr["xtap"][0:128, 0:1024], sb["SM"][:, 0:1024])])
        if C.tapped:
            return


        pe_s0 = C.val("pe")
        dve_s0 = C.val("dve")
        sin_tiles = []
        for dc in range(DC):
            for ki in range(NCORES):
                gi = 8 * dc + ki
                i = C.wb_idx
                C.wb_idx = 1 - C.wb_idx
                buf = sb["WB0"] if i == 0 else sb["WB1"]
                qsem = "qga" if i == 0 else "qgb"
                lu = C.wb_last_use[i] if gi < 2 else pe_s0 + gi - 1

                @C.G
                def _(g, buf=buf, lu=lu, ki=ki, dc=dc, qsem=qsem,
                      first=(dc == 0 and ki == 0)):
                    if first:
                        g.wait_ge(C.sem("cc"), ag_mark)
                    if lu > 0:
                        g.wait_ge(C.sem("pe"), lu)
                    ins = g.dma_start(
                        out=buf[:, 0:1025],
                        in_=C.dr["cc1_out"][1024 * ki + 128 * dc:
                                            1024 * ki + 128 * (dc + 1), 0:1025])
                    C.inc(ins, qsem, 16)
                sin_tiles.append((dc, ki, buf, (qsem, C.val(qsem)), i))

        @C.T
        def _(t):
            t.wait_ge(C.sem("dve"), max(idist_m, qt_m))
            for dc in range(DC):
                if dc >= 2:
                    t.wait_ge(C.sem("dve"), dve_s0 + dc - 1)
                ps = [PA, PB][dc % 2]
                for (dc2, ki, buf, mark, bi) in sin_tiles:
                    if dc2 != dc:
                        continue
                    t.wait_ge(C.sem(mark[0]), mark[1])
                    for seg in range(3):
                        lo = [0, 512, 1024][seg]
                        hi = [512, 1024, 1025][seg]
                        ins = t.matmul(ps[:, lo:hi],
                                       SM2[:, 128 * ki:128 * (ki + 1)],
                                       buf[:, lo:hi], start=(ki == 0),
                                       stop=(ki == NCORES - 1),
                                       skip_group_check=True)
                    ins.then_inc(C.sem("pe"), 1)
        C.cnt["pe"] += DC * NCORES
        C.wb_last_use[0] = C.wb_last_use[1] = C.cnt["pe"]

        @C.V
        def _(v):
            for dc in range(DC):
                v.wait_ge(C.sem("pe"), pe_s0 + 8 * (dc + 1))
                v.tensor_copy(sb["ZCOL"][:, dc:dc + 1],
                              [PA, PB][dc % 2][:, 1024:1025])
                ins = v.tensor_copy(Ks[:, dc, :], [PA, PB][dc % 2][:, 0:1024])
                ins.then_inc(C.sem("dve"), 1)
        C.cnt["dve"] += DC
        sin_m = C.cnt["dve"]
        tap(C, "l1_sin", lambda: [(C.dr["xtap"][0:128, 0:1024], sb["SM"][:, 0:1024])])
        if C.tapped:
            return


        @C.T
        def _(t):
            t.wait_ge(C.sem("dve"), max(sin_m, qt_m, agam_end_d))
            for jc in range(8):
                for half in range(2):
                    ins = t.matmul(PB[0:1, 1024 + 512 * half:1024 + 512 * (half + 1)],
                                   ones_col, KT[:, jc, 512 * half:512 * (half + 1)],
                                   start=(jc == 0), stop=False,
                                   skip_group_check=True)
            for dc in range(DC):
                for half in range(2):
                    ins = t.matmul(PB[0:1, 1024 + 512 * half:1024 + 512 * (half + 1)],
                                   sb["ZCOL"][:, dc:dc + 1],
                                   Qs[:, dc, 512 * half:512 * (half + 1)],
                                   start=False, stop=(dc == DC - 1),
                                   skip_group_check=True)
            ins.then_inc(C.sem("pe"), 1)
        C.cnt["pe"] += 1
        den_m = C.cnt["pe"]

        @C.V
        def _(v):
            def edge(ins):
                ins.then_inc(C.sem("dve"), 1)
                C.cnt["dve"] += 1
                v.wait_ge(C.sem("dve"), C.cnt["dve"])

            v.wait_ge(C.sem("pe"), den_m)
            edge(v.tensor_scalar(SM[0:1, 0:1024], PB[0:1, 1024:2048], 1e-6,
                                 None, OP.add))
            edge(v.reciprocal(SM[0:1, 0:1024], SM[0:1, 0:1024]))
            ins = v.tensor_copy(sb["ABR"][0:1, 0:1024], SM[0:1, 0:1024])
            ins.then_inc(C.sem("dve"), 1)
        C.cnt["dve"] += 1
        rec_m = C.cnt["dve"]

        @C.T
        def _(t):
            t.wait_ge(C.sem("dve"), rec_m)
            for half in range(2):
                ins = t.matmul(PB[:, 1024 + 512 * half:1024 + 512 * (half + 1)],
                               ones_row[:, 0:128],
                               sb["ABR"][0:1, 512 * half:512 * (half + 1)],
                               start=True, stop=True)
            ins.then_inc(C.sem("pe"), 1)
        C.cnt["pe"] += 1
        recb_m = C.cnt["pe"]

        @C.V
        def _(v):
            v.wait_ge(C.sem("pe"), recb_m)
            ins = v.tensor_copy(SM2[:, 1024:2048], PB[:, 1024:2048])
            ins.then_inc(C.sem("dve"), 1)
        C.cnt["dve"] += 1
        recsb_m = C.cnt["dve"]
        tap(C, "l1_den", lambda: [(C.dr["xtap"][0:128, 0:1024], sb["SM"][:, 0:1024])])
        if C.tapped:
            return


        groups = []
        for ec in range(DC):
            def mm(t, ps, ec=ec):
                for jc in range(8):
                    for half in range(2):
                        t.matmul(ps[:, 512 * half:512 * (half + 1)],
                                 VT[:, jc, 128 * ec:128 * (ec + 1)],
                                 KT[:, jc, 512 * half:512 * (half + 1)],
                                 start=(jc == 0), stop=False)
                for dc in range(DC):
                    for half in range(2):
                        ins = t.matmul(ps[:, 512 * half:512 * (half + 1)],
                                       Ks[:, dc, 128 * ec:128 * (ec + 1)],
                                       Qs[:, dc, 512 * half:512 * (half + 1)],
                                       start=False, stop=(dc == DC - 1))
                return ins

            def evac(e, ps, ec=ec):
                e.tensor_tensor(HB[:, 0:1024], ps[:, 0:1024], SM2[:, 1024:2048],
                                OP.mult)
                return e.tensor_tensor(XB[:, ec, bc:bc + 1024], HB[:, 0:1024],
                                       S[:, ec, bc:bc + 1024], OP.add)

            groups.append({"mm": mm, "evac": evac, "eng": "vector",
                           "wait": [("dve", max(recsb_m, sin_m, qt_m))]})
        run_pipe(C, groups, [PA[:, 0:1024], PB[:, 0:1024]])

    postln_from_XB(C)


def postln_from_XB(C):
    sb = C.sb
    S, XB = sb["S"], sb["XB"]

    def apply(v, dc, a_ps, b_ps):
        v.tensor_tensor(S[:, dc, :], XB[:, dc, :], a_ps[:, :], OP.mult)
        v.tensor_tensor(S[:, dc, :], S[:, dc, :], b_ps[:, :], OP.add)
        return v.tensor_copy(XB[:, dc, :], S[:, dc, :])

    ln_fm(C, lambda dc: XB[:, dc, :], apply)


def _quantize_weight(C, w_hbm, spill, nume):
    sb = C.sb
    WB0F = sb["WB0"][:, :].bitcast(F32)
    WB1I = sb["WB1"][:, :].bitcast(I32)
    WB1B = sb["WB1"]
    din, dout = w_hbm.shape
    nkc = din // 128
    ocols = 2048 // nkc
    nslab = dout // ocols
    acc = sb["STAT"][:, 24:25]
    tmp = sb["STAT"][:, 25:26]
    scol = sb["STAT"][:, 26:27]

    @C.V
    def _(v):
        ins = v.memset(acc, 0.0)
        ins.then_inc(C.sem("dve"), 1)
    C.cnt["dve"] += 1

    for s in range(nslab):
        @C.G
        def _(g, s=s):
            g.wait_ge(C.sem("dve"), C.val("dve"))
            ins = g.dma_start(
                out=WB0F[:, 0:nkc * ocols].rearrange("p (c o) -> p c o", o=ocols),
                in_=w_hbm[:, s * ocols:(s + 1) * ocols]
                    .rearrange("(c p) o -> p c o", p=128))
            C.inc(ins, "qg", 16)
        mk = C.val("qg")

        @C.A
        def _(a):
            a.wait_ge(C.sem("qg"), mk)
            ins = a.activation(out=WB0F[:, :], in_=WB0F[:, :], func=AF.Abs,
                               accum_out=tmp)
            ins.then_inc(C.sem("act"), 1)
        C.cnt["act"] += 1
        mk2 = C.cnt["act"]

        @C.V
        def _(v):
            v.wait_ge(C.sem("act"), mk2)
            ins = v.tensor_tensor(acc, acc, tmp, OP.add)
            ins.then_inc(C.sem("dve"), 1)
        C.cnt["dve"] += 1
    acc_m = C.cnt["dve"]

    @C.T
    def _(t):
        psum_guard(C, t)
        t.wait_ge(C.sem("dve"), acc_m)
        ins = t.matmul(C.ps["PA"][0:1, 0:1], acc, sb["ONEF"][:, 0:1],
                       start=True, stop=True)
        ins.then_inc(C.sem("pe"), 1)
    C.cnt["pe"] += 1
    ts_m = C.cnt["pe"]

    @C.V
    def _(v):
        def edge(ins):
            ins.then_inc(C.sem("dve"), 1)
            C.cnt["dve"] += 1
            v.wait_ge(C.sem("dve"), C.cnt["dve"])

        v.wait_ge(C.sem("pe"), ts_m)
        edge(v.tensor_scalar(sb["STAT"][0:1, 27:28], C.ps["PA"][0:1, 0:1],
                             1.0 / nume, 1e-5, OP.mult, OP.add))
        edge(v.reciprocal(sb["STAT"][0:1, 28:29], sb["STAT"][0:1, 27:28]))
        v.nop()
    rs_m = C.cnt["dve"]

    @C.T
    def _(t):
        t.wait_ge(C.sem("dve"), rs_m)
        ins = t.matmul(C.ps["PA"][:, 1:2], sb["ONERF"][0:1, 0:128],
                       sb["STAT"][0:1, 28:29], start=True, stop=True)
        ins.then_inc(C.sem("pe"), 1)
    C.cnt["pe"] += 1
    sc_m = C.cnt["pe"]

    @C.V
    def _(v):
        v.wait_ge(C.sem("pe"), sc_m)
        ins = v.tensor_copy(scol, C.ps["PA"][:, 1:2])
        ins.then_inc(C.sem("dve"), 1)
    C.cnt["dve"] += 1
    scol_m = C.cnt["dve"]

    for s in range(nslab):
        @C.G
        def _(g, s=s):
            g.wait_ge(C.sem("dve"), C.val("dve"))
            ins = g.dma_start(
                out=WB0F[:, 0:nkc * ocols].rearrange("p (c o) -> p c o", o=ocols),
                in_=w_hbm[:, s * ocols:(s + 1) * ocols]
                    .rearrange("(c p) o -> p c o", p=128))
            C.inc(ins, "qg", 16)
        mk = C.val("qg")

        @C.V
        def _(v, s=s, mk=mk):
            def edge(ins):
                ins.then_inc(C.sem("dve"), 1)
                C.cnt["dve"] += 1
                v.wait_ge(C.sem("dve"), C.cnt["dve"])

            v.wait_ge(C.sem("qg"), mk)
            if s == 0:
                v.wait_ge(C.sem("dve"), scol_m)
            edge(v.tensor_scalar(WB0F[:, :], WB0F[:, :], scol, None, OP.mult))
            edge(v.tensor_copy(WB1I[:, :], WB0F[:, :]))
            edge(v.tensor_copy(WB0F[:, :], WB1I[:, :]))
            edge(v.tensor_scalar(WB0F[:, :], WB0F[:, :], 1.0, -1.0,
                                 OP.min, OP.max))
            ins = v.tensor_copy(WB1B[:, 0:2048], WB0F[:, :])
            ins.then_inc(C.sem("dve"), 1)
        C.cnt["dve"] += 1
        mk2 = C.cnt["dve"]

        @C.G
        def _(g, s=s):
            g.wait_ge(C.sem("dve"), mk2)
            ins = g.dma_start(
                out=spill[:, s * ocols:(s + 1) * ocols]
                    .rearrange("(c p) o -> p c o", p=128),
                in_=WB1B[:, 0:2048].rearrange("p (c o) -> p c o", o=ocols))
            C.inc(ins, "qg", 16)
    C.wb_last_use[0] = C.wb_last_use[1] = C.cnt["pe"]
    return C.val("qg")


def emit_l1_ffn(C):
    block_barrier(C)
    sb = C.sb
    S, W, XB = sb["S"], sb["W"], sb["XB"]
    BR = sb["BIASR"]

    q1_done = _quantize_weight(C, C.dr["L1fW1"], C.dr["qsp1"], D * 4096)
    q2_done = _quantize_weight(C, C.dr["L1fW2"], C.dr["qsp2"], D * 4096)

    Wflat = W[:, :, :].rearrange("p c t -> p (c t)").bitcast(BF16)
    hslot = Wflat[:, :].rearrange("p (a b) -> p a b", b=1024)

    @C.G
    def _(g):
        g.wait_ge(C.sem("qg"), max(q1_done, q2_done))

    for tt in range(2):
        tcol = 1024 * tt
        mb1a = load_rows_bf(C, C.dr["L1fb1"], 0, 2048, BR[0:1, 0:2048],
                            wait_pe=C.val("pe"))
        fm_proj(C, C.dr["qsp1"][:, 0:2048], lambda dc: XB[:, dc, :], 2048,
                lambda e, oc, ps: e.activation(out=hslot[:, oc, :],
                                               in_=ps[:, :], func=AF.Relu),
                evac_eng="scalar", bias_row=BR[0:1, 0:2048],
                tokens=(tcol, tcol + 1024), bias_mark=mb1a)
        mb1b = load_rows_bf(C, C.dr["L1fb1"], 2048, 4096, BR[0:1, 0:2048],
                            wait_pe=C.val("pe"))
        fm_proj(C, C.dr["qsp1"][:, 2048:4096], lambda dc: XB[:, dc, :], 2048,
                lambda e, oc, ps: e.activation(out=hslot[:, 16 + oc, :],
                                               in_=ps[:, :], func=AF.Relu),
                evac_eng="scalar", bias_row=BR[0:1, 0:2048],
                tokens=(tcol, tcol + 1024), bias_mark=mb1b)
        mb2 = load_rows_bf(C, C.dr["L1fb2"], 0, D, BR[0:1, 2048:3072],
                           wait_pe=C.val("pe"))

        def ev_o(e, oc, ps, tcol=tcol):
            return e.tensor_tensor(XB[:, oc, tcol:tcol + 1024], ps[:, :],
                                   S[:, oc, tcol:tcol + 1024], OP.add)

        fm_proj(C, C.dr["qsp2"], lambda fc: hslot[:, fc, :], D, ev_o,
                evac_eng="vector", bias_row=BR[0:1, 2048:3072],
                tokens=(0, 1024), bias_mark=mb2)

    postln_from_XB(C)


def emit_l2_attn(C):
    block_barrier(C)
    sb = C.sb
    S, W, XB = sb["S"], sb["W"], sb["XB"]
    PA, PB = C.ps["PA"], C.ps["PB"]
    ones_row = sb["ONER"][0:1, :]
    SM = sb["SM"]
    SM2 = sb["SM2"]
    HB = sb["HB"]
    RP = sb["BONES"]

    Wflat = W[:, :, :].rearrange("p c t -> p (c t)").bitcast(BF16)
    Qr = [Wflat[:, 0:8192].rearrange("p (a b) -> p a b", b=1024),
          Wflat[:, 8192:16384].rearrange("p (a b) -> p a b", b=1024)]
    KS = Wflat[:, 16384:24576].rearrange("p (a b) -> p a b", b=1024)
    VS = Wflat[:, 24576:32768].rearrange("p (a b) -> p a b", b=1024)
    ATN = Wflat[:, 16384:32768].rearrange("p (a b) -> p a b", b=2048)

    @C.G
    def _(g):
        ins = g.dma_start(out=RP[:, :], in_=C.dr["L2_ropeP"])
        C.inc(ins, "qg", 16)
        ins = g.dma_start(out=SM2[:, 0:1024], in_=C.dr["L2_cos"])
        C.inc(ins, "qg", 16)
        ins = g.dma_start(out=SM2[:, 1024:2048], in_=C.dr["L2_sin"])
        C.inc(ins, "qg", 16)
    const_m = C.val("qg")
    CC = SM2[:, 0:1024]
    SN = SM2[:, 1024:2048]

    def rope_inplace(slot, wait_extra):
        pe0 = C.val("pe")
        dve0 = C.val("dve")

        @C.T
        def _(t):
            psum_guard(C, t)
            t.wait_ge(C.sem("qg"), const_m)
            for (sn, v) in wait_extra:
                t.wait_ge(C.sem(sn), v)
            for dc in range(DC):
                if dc >= 2:
                    t.wait_ge(C.sem("dve"), dve0 + 3 * (dc - 1))
                ps = [PA, PB][dc % 2]
                for half in range(2):
                    ins = t.matmul(ps[:, 512 * half:512 * (half + 1)], RP[:, :],
                                   slot[:, dc, 512 * half:512 * (half + 1)],
                                   start=True, stop=True)
                ins.then_inc(C.sem("pe"), 1)
        C.cnt["pe"] += DC

        @C.V
        def _(v):
            for dc in range(DC):
                v.wait_ge(C.sem("pe"), pe0 + dc + 1)
                v.tensor_tensor(HB[:, 0:1024], slot[:, dc, :], CC, OP.mult)
                v.tensor_tensor(HB[:, 1024:2048], [PA, PB][dc % 2][:, 0:1024],
                                SN, OP.mult)
                ins = v.tensor_tensor(slot[:, dc, :], HB[:, 0:1024],
                                      HB[:, 1024:2048], OP.add)
                ins.then_inc(C.sem("dve"), 3)
        C.cnt["dve"] += 3 * DC

    for b in range(NB):
        fm_proj(C, C.dr["L2W_q"], lambda dc: XB[:, dc, :], D,
                lambda e, oc, ps, b=b: e.tensor_copy(Qr[b][:, oc, :], ps[:, :]),
                evac_eng="vector", tokens=(1024 * b, 1024 * b + 1024))
        rope_inplace(Qr[b], [])

    tap(C, "l2_rope", lambda: [
        (C.dr["xtap"][0:D, 0:1024].rearrange("(c p) t -> p c t", p=128),
         Wflat[:, 0:8192].rearrange("p (c t) -> p c t", t=1024)),
        (C.dr["xtap"][D:2 * D, 0:1024].rearrange("(c p) t -> p c t", p=128),
         Wflat[:, 8192:16384].rearrange("p (c t) -> p c t", t=1024))])
    if C.tapped:
        return

    kv_marks = []
    for b in range(NB):
        stg_w = [("qg", kv_marks[-1])] if kv_marks else []
        fm_proj(C, C.dr["L2W_k"], lambda dc: XB[:, dc, :], D,
                lambda e, oc, ps: e.tensor_copy(KS[:, oc, :], ps[:, :]),
                evac_eng="vector", tokens=(1024 * b, 1024 * b + 1024),
                extra_wait=stg_w)
        rope_inplace(KS, [])
        krot_done = C.cnt["dve"]

        groups = []
        for half in range(2):
            ap, mark, bi = wslab_load(C, C.dr["L2W_v"], 512 * half, 512)
            for tc in range(8):
                def mm(t, ps, ap=ap, tc=tc, b=b):
                    for dc in range(DC):
                        ins = t.matmul(ps[:, 0:512],
                                       XB[:, dc, 1024 * b + 128 * tc:1024 * b + 128 * (tc + 1)],
                                       ap[:, dc, :], start=(dc == 0), stop=(dc == DC - 1))
                    return ins

                def evac(e, ps, tc=tc, half=half):
                    return e.tensor_copy(VS[:, tc, 512 * half:512 * (half + 1)],
                                         ps[:, 0:512])

                groups.append({"mm": mm, "evac": evac, "eng": "vector",
                               "wait": [mark] + stg_w})
        run_pipe(C, groups, [PA[:, 0:512], PA[:, 512:1024],
                             PB[:, 0:512], PB[:, 512:1024]])
        C.wb_last_use[0] = C.wb_last_use[1] = C.cnt["pe"]
        v_done = C.cnt["evd"]

        @C.G
        def _(g, b=b, krot_done=krot_done, v_done=v_done):
            g.wait_ge(C.sem("dve"), krot_done)
            g.wait_ge(C.sem("evd"), v_done)
            ins = g.dma_start(
                out=C.dr["cc2_in"][1024 * b:1024 * (b + 1), 0:1024]
                    .rearrange("(c p) t -> p c t", p=128),
                in_=KS[:, :, :])
            C.inc(ins, "qg", 16)
            ins = g.dma_start(
                out=C.dr["cc2_in"][2048 + 1024 * b:2048 + 1024 * (b + 1), 0:1024]
                    .rearrange("(c p) t -> p c t", p=128),
                in_=VS[:, :, :])
            C.inc(ins, "qg", 16)

        kv_marks.append(C.val("qg"))

    @C.G
    def _(g):
        g.wait_ge(C.sem("qg"), kv_marks[-1])
        ins = g.collective_compute("AllGather", OP.bypass,
                                   replica_groups=[list(range(NCORES))],
                                   ins=[C.dr["cc2_in"].opt()],
                                   outs=[C.dr["cc2_out"].opt()])
        C.inc(ins, "cc", 1)
    ag_m = C.val("cc")

    scale = 1.0 / 8.0
    for b in range(NB):
        for hp in range(8):
            h2 = [2 * hp, 2 * hp + 1]
            tiles = []
            pe_now = C.val("pe")

            @C.V
            def _(v):
                v.wait_ge(C.sem("pe"), pe_now)
                for bufm in (sb["WB0"], sb["WB1"]):
                    for j in range(2):
                        vv = bufm[:, 1024 + 520 * j:1024 + 520 * j + 520] \
                            .rearrange("p (c o) -> p c o", o=65)
                        ins = v.memset(vv[:, :, 64:65], 1.0)
                ins.then_inc(C.sem("dve"), 1)
            C.cnt["dve"] += 1
            ones_m = C.cnt["dve"]

            pe_f0 = C.val("pe")
            for r in range(NCORES):
                i = C.wb_idx
                C.wb_idx = 1 - C.wb_idx
                buf = sb["WB0"] if i == 0 else sb["WB1"]
                lu = C.wb_last_use[i] if r < 2 else pe_f0 + 32 * (r - 2) + 34

                qsem = "qga" if i == 0 else "qgb"

                @C.G
                def _(g, r=r, b=b, hp=hp, buf=buf, lu=lu, qsem=qsem,
                      first=(b == 0 and hp == 0 and r == 0)):
                    if first:
                        g.wait_ge(C.sem("cc"), ag_m)
                    if lu > 0:
                        g.wait_ge(C.sem("pe"), lu)
                    base = 4096 * r
                    ins = g.dma_start(
                        out=buf[:, 0:1024],
                        in_=C.dr["cc2_out"][base + 1024 * b + 128 * hp:
                                            base + 1024 * b + 128 * (hp + 1), 0:1024])
                    C.inc(ins, qsem, 16)
                    for j, h in enumerate([2 * hp, 2 * hp + 1]):
                        vv = buf[:, 1024 + 520 * j:1024 + 520 * j + 520] \
                            .rearrange("p (c o) -> p c o", o=65)
                        ins = g.dma_start(
                            out=vv[:, :, 0:64],
                            in_=C.dr["cc2_out"][base + 2048 + 1024 * b:
                                                base + 2048 + 1024 * (b + 1),
                                                64 * h:64 * (h + 1)]
                                .rearrange("(c p) o -> p c o", p=128))
                        C.inc(ins, qsem, 16)
                tiles.append((buf, (qsem, C.val(qsem)), i))

            dve_f0 = C.val("dve")
            eva_f0 = C.val("eva")
            pe0f = C.val("pe")
            ngrp = NCORES * 8 * 2
            stm = {}
            avm = {}
            run_pe = pe0f
            order = []
            for r in range(NCORES):
                for tkc in range(8):
                    for j, h in enumerate(h2):
                        order.append((r, tkc, j, h))

            @C.T
            def _(t):
                nonlocal_run = [pe0f]

                def emit_st(gi):
                    r, tkc, j, h = order[gi]
                    buf, mark, bi = tiles[r]
                    t.wait_ge(C.sem(mark[0]), mark[1])
                    if gi >= 2:
                        t.wait_ge(C.sem("eva"), eva_f0 + gi - 1)
                    ps = [PB[:, 0:1024], PB[:, 1024:2048]][gi % 2]
                    for half in range(2):
                        ins = t.matmul(
                            ps[:, 512 * half:512 * (half + 1)],
                            buf[64 * (h % 2):64 * (h % 2) + 64,
                                128 * tkc:128 * (tkc + 1)],
                            Qr[b][64 * (h % 2):64 * (h % 2) + 64, hp,
                                  512 * half:512 * (half + 1)],
                            start=True, stop=True, skip_group_check=True)
                    nonlocal_run[0] += 1
                    ins.then_inc(C.sem("pe"), 1)
                    stm[gi] = nonlocal_run[0]

                def emit_av(gi):
                    r, tkc, j, h = order[gi]
                    buf, mark, bi = tiles[r]
                    t.wait_ge(C.sem("eva"), eva_f0 + gi + 1)
                    if gi == 0:
                        t.wait_ge(C.sem("dve"), ones_m)
                    hbslot = HB[:, 1024 * (gi % 2):1024 * (gi % 2) + 1024]
                    vv = buf[:, 1024 + 520 * j:1024 + 520 * j + 520] \
                        .rearrange("p (c o) -> p c o", o=65)
                    avp = PA[0:65, 1024 * j:1024 * j + 1024]
                    for half in range(2):
                        ins = t.matmul(
                            avp[:, 512 * half:512 * (half + 1)],
                            vv[:, tkc, :],
                            hbslot[:, 512 * half:512 * (half + 1)],
                            start=(gi < 2), stop=(gi >= ngrp - 2),
                            skip_group_check=True)
                    nonlocal_run[0] += 1
                    ins.then_inc(C.sem("pe"), 1)
                    avm[gi] = nonlocal_run[0]

                psum_guard(C, t)
                t.wait_ge(C.sem("dve"), dve_f0)
                for gi in range(ngrp):
                    emit_st(gi)
                    if gi >= 2:
                        emit_av(gi - 2)
                emit_av(ngrp - 2)
                emit_av(ngrp - 1)
            C.cnt["pe"] += 2 * ngrp

            @C.A
            def _(a):
                for gi in range(ngrp):
                    a.wait_ge(C.sem("pe"), stm[gi])
                    if gi >= 2:
                        a.wait_ge(C.sem("pe"), avm[gi - 2])
                    ps = [PB[:, 0:1024], PB[:, 1024:2048]][gi % 2]
                    hbslot = HB[:, 1024 * (gi % 2):1024 * (gi % 2) + 1024]
                    ins = a.activation(out=hbslot, in_=ps[:, :], func=AF.Exp,
                                       scale=scale)
                    ins.then_inc(C.sem("eva"), 1)
            C.cnt["eva"] += ngrp
            av_done = C.cnt["pe"]
            C.wb_last_use[0] = C.wb_last_use[1] = av_done

            for j, h in enumerate(h2):
                avp = PA[0:65, 1024 * j:1024 * j + 1024]

                @C.V
                def _(v, j=j):
                    def edge(ins):
                        ins.then_inc(C.sem("dve"), 1)
                        C.cnt["dve"] += 1
                        v.wait_ge(C.sem("dve"), C.cnt["dve"])

                    v.wait_ge(C.sem("pe"), av_done)
                    avp2 = PA[0:65, 1024 * j:1024 * j + 1024]
                    edge(v.tensor_copy(SM[0:1, 0:1024], avp2[64:65, 0:1024]))
                    edge(v.reciprocal(SM[0:1, 0:1024], SM[0:1, 0:1024]))
                    ins = v.tensor_copy(sb["ABR"][0:1, 0:1024], SM[0:1, 0:1024])
                    ins.then_inc(C.sem("dve"), 1)
                C.cnt["dve"] += 1
                rc_m = C.cnt["dve"]

                @C.T
                def _(t):
                    t.wait_ge(C.sem("dve"), rc_m)
                    for half in range(2):
                        ins = t.matmul(
                            PB[0:64, 512 * half:512 * (half + 1)],
                            ones_row[:, 0:64],
                            sb["ABR"][0:1, 512 * half:512 * (half + 1)],
                            start=True, stop=True)
                    ins.then_inc(C.sem("pe"), 1)
                C.cnt["pe"] += 1
                rb_m = C.cnt["pe"]

                @C.V
                def _(v, j=j, h=h, b=b, hp=hp):
                    v.wait_ge(C.sem("pe"), rb_m)
                    v.tensor_copy(HB[0:64, 0:1024], PB[0:64, 0:1024])
                    avp2 = PA[0:65, 1024 * j:1024 * j + 1024]
                    ins = v.tensor_tensor(
                        ATN[64 * (h % 2):64 * (h % 2) + 64, hp,
                            1024 * b:1024 * (b + 1)],
                        avp2[0:64, 0:1024], HB[0:64, 0:1024], OP.mult)
                    ins.then_inc(C.sem("dve"), 1)
                C.cnt["dve"] += 1

    tap(C, "l2_atn", lambda: [
        (C.dr["xtap"][0:D, :].rearrange("(c p) t -> p c t", p=128),
         Wflat[:, 16384:32768].rearrange("p (c t) -> p c t", t=2048))])
    if C.tapped:
        return

    def ev_o(e, oc, ps):
        return e.tensor_tensor(XB[:, oc, :], ps[:, :], S[:, oc, :], OP.add)

    fm_proj(C, C.dr["L2W_o"], lambda dc: ATN[:, dc, :], D, ev_o,
            evac_eng="vector")

    postln_from_XB(C)


def emit_l2_ffn(C):
    block_barrier(C)
    sb = C.sb
    S, W, XB = sb["S"], sb["W"], sb["XB"]
    PA, PB = C.ps["PA"], C.ps["PB"]
    ones_row = sb["ONER"][0:1, :]
    SM = sb["SM"]
    SM2 = sb["SM2"]
    hslot = W[:, :, :].rearrange("p c t -> p (c t)").bitcast(BF16) \
        .rearrange("p (a b) -> p a b", b=1024)

    apr, mr, bir = wslab_load(C, C.dr["L2fW_rt"], 0, 8,
                              wait_pe=C.val("pe"))

    @C.T
    def _(t):
        psum_guard(C, t)
        t.wait_ge(C.sem(mr[0]), mr[1])
        for tc in range(TC):
            for dc in range(DC):
                ins = t.matmul(PA[:, 8 * tc:8 * (tc + 1)],
                               XB[:, dc, 128 * tc:128 * (tc + 1)],
                               apr[:, dc, :], start=(dc == 0), stop=(dc == DC - 1),
                               skip_group_check=True)
        ins.then_inc(C.sem("pe"), 1)
    C.cnt["pe"] += 1
    rt_mark = C.cnt["pe"]
    C.wb_last_use[bir] = C.cnt["pe"]

    RT = SM[:, 0:128].rearrange("p (a b) -> p a b", b=8)
    RX = SM[:, 128:256].rearrange("p (a b) -> p a b", b=8)
    RS = SM[:, 256:272].rearrange("p (a b) -> p a b", b=1)
    RS2 = SM[:, 272:288].rearrange("p (a b) -> p a b", b=1)

    @C.V
    def _(v):
        v.wait_ge(C.sem("pe"), rt_mark)
        v.tensor_copy(SM[:, 0:128], PA[:, 0:128])
        ins = v.reduce_max(RS[:, :, :], RT[:, :, :], axis=AX.X)
        ins.then_inc(C.sem("dve"), 1)
        v.wait_ge(C.sem("dve"), C.cnt["dve"] + 1)
        ins = v.tensor_tensor(RX[:, :, :], RT[:, :, :],
                              RS[:, :, :].broadcast_to([128, 16, 8]), OP.subtract)
        ins.then_inc(C.sem("dve"), 1)
    C.cnt["dve"] += 2
    r1_mark = C.cnt["dve"]

    @C.A
    def _(a):
        a.wait_ge(C.sem("dve"), r1_mark)
        ins = a.activation(out=SM[:, 128:256], in_=SM[:, 128:256], func=AF.Exp)
        ins.then_inc(C.sem("act"), 1)
    C.cnt["act"] += 1
    r2_mark = C.cnt["act"]

    @C.V
    def _(v):
        def edge(ins):
            ins.then_inc(C.sem("dve"), 1)
            C.cnt["dve"] += 1
            v.wait_ge(C.sem("dve"), C.cnt["dve"])

        v.wait_ge(C.sem("act"), r2_mark)
        edge(v.reduce_sum(RS[:, :, :], RX[:, :, :], axis=AX.X))
        edge(v.reciprocal(SM[:, 272:288], SM[:, 256:272]))
        v.tensor_tensor(RX[:, :, :], RX[:, :, :],
                        RS2[:, :, :].broadcast_to([128, 16, 8]), OP.mult)
        edge(v.reduce_max(RS[:, :, :], RX[:, :, :], axis=AX.X))
        v.tensor_tensor(RT[:, :, :], RX[:, :, :],
                        RS[:, :, :].broadcast_to([128, 16, 8]), OP.is_ge)
        v.scalar_tensor_tensor(RT[:, :, :], RT[:, :, :], -1e30, RX[:, :, :],
                               OP.mult, OP.add)
        edge(v.reduce_max(RS2[:, :, :], RT[:, :, :], axis=AX.X))
        v.tensor_tensor(RT[:, :, :], RX[:, :, :],
                        RS2[:, :, :].broadcast_to([128, 16, 8]), OP.is_ge)
        v.tensor_tensor(RT[:, :, :], RT[:, :, :], RX[:, :, :], OP.mult)
        ins = v.tensor_copy(SM[:, 288:416], SM[:, 0:128])
        ins.then_inc(C.sem("dve"), 1)
    C.cnt["dve"] += 1
    rw_mark = C.cnt["dve"]

    @C.T
    def _(t):
        psum_guard(C, t)
        t.wait_ge(C.sem("dve"), rw_mark)
        for tc in range(TC):
            ins = t.transpose(PB[0:8, 128 * tc:128 * (tc + 1)],
                              SM[:, 288 + 8 * tc:288 + 8 * (tc + 1)],
                              sb["IDN"][:, :])
        ins.then_inc(C.sem("pe"), 1)
    C.cnt["pe"] += 1
    wt_mark = C.cnt["pe"]

    WEB = SM[:, :].bitcast(BF16)

    @C.V
    def _(v):
        v.wait_ge(C.sem("pe"), wt_mark)
        ins = v.tensor_copy(WEB[0:8, 0:2048], PB[0:8, 0:2048])
        ins.then_inc(C.sem("dve"), 1)
    C.cnt["dve"] += 1
    we_mark = C.cnt["dve"]

    HBI = sb["HB"][:, :].bitcast(mybir.dt.int16)

    @C.G
    def _(g):
        g.wait_ge(C.sem("dve"), we_mark)
        ins = g.iota(HBI[0:8, 0:1024], [[1, 8], [0, 128]], base=0,
                     channel_multiplier=0, allow_small_or_imprecise_dtypes=True)
        C.inc(ins, "pool", 1)
    sel_mark = C.val("pool")

    @C.V
    def _(v):
        v.wait_ge(C.sem("pool"), sel_mark)
        ins = v.tensor_scalar(sb["BONEST"][0:8, 0:1024], HBI[0:8, 0:1024],
                              sb["PIDX"][0:8, 0:1], None, OP.is_equal)
        ins.then_inc(C.sem("dve"), 1)
    C.cnt["dve"] += 1
    selv_mark = C.cnt["dve"]

    for tt in range(2):
        tcol = 1024 * tt
        for e in range(8):
            @C.T
            def _(t, e=e):
                psum_guard(C, t)
                t.wait_ge(C.sem("dve"), max(we_mark, selv_mark))
                for ts in range(NTS):
                    ins = t.matmul(PA[:, 512 * ts:512 * (ts + 1)],
                                   sb["BONEST"][0:8, 128 * e:128 * (e + 1)],
                                   WEB[0:8, 512 * ts:512 * (ts + 1)],
                                   start=True, stop=True)
                ins.then_inc(C.sem("pe"), 1)
            C.cnt["pe"] += 1
            web_mark = C.cnt["pe"]

            @C.V
            def _(v):
                v.wait_ge(C.sem("pe"), web_mark)
                ins = v.tensor_copy(SM2[:, :], PA[:, :])
                ins.then_inc(C.sem("evd"), 1)
            C.cnt["evd"] += 1

            w1e = C.dr["L2f_w1"][e]
            groups = []
            pe_entry = C.val("pe")
            for sgi in range(8):
                og = 512 * sgi
                wp = pe_entry + 4 * (sgi - 1) if sgi >= 2 else None
                ap, mark, bi = wslab_load(C, w1e, og, 512, wait_pe=wp)
                for j in range(4):
                    fc = 4 * sgi + j

                    def mm(t, ps, ap=ap, j=j):
                        for ts in range(2):
                            for dc in range(DC):
                                ins = t.matmul(
                                    ps[:, 512 * ts:512 * (ts + 1)],
                                    ap[:, dc, 128 * j:128 * (j + 1)],
                                    XB[:, dc, tcol + 512 * ts:tcol + 512 * (ts + 1)],
                                    start=(dc == 0), stop=(dc == DC - 1))
                        return ins

                    def evac(e_, ps, fc=fc):
                        return e_.activation(out=hslot[:, fc, :], in_=ps[:, 0:1024],
                                             func=AF.Gelu)

                    groups.append({"mm": mm, "evac": evac, "eng": "scalar",
                                   "wait": [mark]})
            run_pipe(C, groups, [PA[:, 0:1024], PB[:, 0:1024]])
            C.wb_last_use[0] = C.wb_last_use[1] = C.cnt["pe"]

            w2e = C.dr["L2f_w2"][e]
            groups = []
            pe_entry = C.val("pe")
            for hc in range(8):
                wp = pe_entry + (hc - 1) if hc >= 2 else None
                ap, mark, bi = wslab_load(C, w2e, 128 * hc, 128, wait_pe=wp)

                def mm(t, ps, ap=ap):
                    for ts in range(2):
                        for fc in range(32):
                            ins = t.matmul(ps[:, 512 * ts:512 * (ts + 1)],
                                           ap[:, fc, :],
                                           hslot[:, fc, 512 * ts:512 * (ts + 1)],
                                           start=(fc == 0), stop=(fc == 31))
                    return ins

                def evac(e_, ps, hc=hc):
                    e_.tensor_tensor(sb["HB"][:, 0:1024], ps[:, 0:1024],
                                     SM2[:, tcol:tcol + 1024], OP.mult)
                    return e_.tensor_tensor(S[:, hc, tcol:tcol + 1024],
                                            S[:, hc, tcol:tcol + 1024],
                                            sb["HB"][:, 0:1024], OP.add)

                groups.append({"mm": mm, "evac": evac, "eng": "vector",
                               "wait": [mark]})
            run_pipe(C, groups, [PA[:, 0:1024], PB[:, 0:1024]])
            C.wb_last_use[0] = C.wb_last_use[1] = C.cnt["pe"]

    postln_from_S(C)


def emit_l3_attn(C):
    block_barrier(C)
    sb = C.sb
    S, W, XB = sb["S"], sb["W"], sb["XB"]
    PA, PB = C.ps["PA"], C.ps["PB"]
    ones_row = sb["ONER"][0:1, :]
    SM = sb["SM"]
    SM2 = sb["SM2"]
    HB = sb["HB"]
    BR = sb["BIASR"]
    HBI2 = SM2[:, :].bitcast(mybir.dt.int16)

    Wflat = W[:, :, :].rearrange("p c t -> p (c t)").bitcast(BF16)
    Qr = [Wflat[:, 0:8192].rearrange("p (a b) -> p a b", b=1024),
          Wflat[:, 8192:16384].rearrange("p (a b) -> p a b", b=1024)]
    KS = Wflat[:, 16384:24576].rearrange("p (a b) -> p a b", b=1024)
    VS = Wflat[:, 24576:32768].rearrange("p (a b) -> p a b", b=1024)
    ATN = Wflat[:, 16384:32768].rearrange("p (a b) -> p a b", b=2048)

    mb_qkv = load_rows_bf(C, C.dr["L3b_qkv"], 0, 3 * D, BR[0:1, 0:3072],
                          wait_pe=C.val("pe"))

    @C.G
    def _(g):
        ins = g.dma_start(out=sb["STAT"][:, 0:16], in_=C.dr["L3_ab"])
        C.inc(ins, "qg", 16)
    ab_m = C.val("qg")

    for b in range(NB):
        fm_proj(C, C.dr["L3W_qkv"], lambda dc: XB[:, dc, :], D,
                lambda e, oc, ps, b=b: e.tensor_copy(Qr[b][:, oc, :], ps[:, :]),
                evac_eng="vector", bias_row=BR[0:1, 0:1024],
                tokens=(1024 * b, 1024 * b + 1024), bias_mark=mb_qkv)

    kv_marks = []
    for b in range(NB):
        stg_w = [("qg", kv_marks[-1])] if kv_marks else []
        fm_proj(C, C.dr["L3W_qkv"][:, D:2 * D],
                lambda dc: XB[:, dc, :], D,
                lambda e, oc, ps: e.tensor_copy(KS[:, oc, :], ps[:, :]),
                evac_eng="vector", bias_row=BR[0:1, 1024:2048],
                tokens=(1024 * b, 1024 * b + 1024), bias_mark=mb_qkv,
                extra_wait=stg_w)
        k_done = C.cnt["evd"]

        groups = []
        for half in range(2):
            ap, mark, bi = wslab_load(C, C.dr["L3W_qkv"], 2048 + 512 * half, 512)
            for tc in range(8):
                def mm(t, ps, ap=ap, tc=tc, b=b, half=half):
                    for dc in range(DC):
                        t.matmul(ps[:, 0:512],
                                 XB[:, dc, 1024 * b + 128 * tc:1024 * b + 128 * (tc + 1)],
                                 ap[:, dc, :], start=(dc == 0), stop=False)
                    return t.matmul(ps[:, 0:512], ones_row[:, 0:128],
                                    BR[0:1, 2048 + 512 * half:2048 + 512 * (half + 1)],
                                    start=False, stop=True)

                def evac(e, ps, tc=tc, half=half):
                    return e.tensor_copy(VS[:, tc, 512 * half:512 * (half + 1)],
                                         ps[:, 0:512])

                groups.append({"mm": mm, "evac": evac, "eng": "vector",
                               "wait": [mark, ("qg", ab_m), ("qgr", mb_qkv)]
                                       + stg_w})
        run_pipe(C, groups, [PA[:, 0:512], PA[:, 512:1024],
                             PB[:, 0:512], PB[:, 512:1024]])
        C.wb_last_use[0] = C.wb_last_use[1] = C.cnt["pe"]
        v_done = C.cnt["evd"]

        @C.G
        def _(g, b=b, k_done=k_done, v_done=v_done):
            g.wait_ge(C.sem("evd"), max(k_done, v_done))
            ins = g.dma_start(
                out=C.dr["cc3_in"][1024 * b:1024 * (b + 1), 0:1024]
                    .rearrange("(c p) t -> p c t", p=128),
                in_=KS[:, :, :])
            C.inc(ins, "qg", 16)
            ins = g.dma_start(
                out=C.dr["cc3_in"][2048 + 1024 * b:2048 + 1024 * (b + 1), 0:1024]
                    .rearrange("(c p) t -> p c t", p=128),
                in_=VS[:, :, :])
            C.inc(ins, "qg", 16)
        kv_marks.append(C.val("qg"))

    @C.G
    def _(g):
        g.wait_ge(C.sem("qg"), kv_marks[-1])
        ins = g.collective_compute("AllGather", OP.bypass,
                                   replica_groups=[list(range(NCORES))],
                                   ins=[C.dr["cc3_in"].opt()],
                                   outs=[C.dr["cc3_out"].opt()])
        C.inc(ins, "cc", 1)
    ag_m = C.val("cc")

    scale = 0.125
    npair = 0
    for b in range(NB):
        for hp in range(8):
            npair += 1
            if npair > DEBUG_L3PAIRS:
                continue
            h2 = [2 * hp, 2 * hp + 1]
            pe_now = C.val("pe")

            @C.V
            def _(v):
                v.wait_ge(C.sem("pe"), pe_now)
                for bufm in (sb["WB0"], sb["WB1"]):
                    for j in range(2):
                        vv = bufm[:, 1024 + 520 * j:1024 + 520 * j + 520] \
                            .rearrange("p (c o) -> p c o", o=65)
                        ins = v.memset(vv[:, :, 64:65], 1.0)
                ins.then_inc(C.sem("dve"), 1)
            C.cnt["dve"] += 1
            ones_m = C.cnt["dve"]

            tiles = []
            pe_f0 = C.val("pe")
            dve_f0 = C.val("dve")
            pool_f0 = C.val("pool")
            SMI = SM[:, :].bitcast(mybir.dt.int16)

            def emit_iotas(lo_gg, hi_gg):
                if DEBUG_L3NOMASK:
                    return

                @C.G
                def _(g):
                    for gg in range(lo_gg, hi_gg):
                        tkc = gg % 8
                        if DEBUG_L3MASKLVL == 0:
                            g.wait_ge(C.sem("dve"), dve_f0)
                        else:
                            g.wait_ge(C.sem("dve"),
                                      dve_f0 + gg if gg >= 1 else dve_f0)
                        ins = g.iota(SMI[:, 0:1024],
                                     [[1, 1024]], base=-128 * tkc,
                                     channel_multiplier=-1,
                                     allow_small_or_imprecise_dtypes=True)
                        C.inc(ins, "pool", 1)

            for r in range(NCORES):
                if r >= 2:
                    emit_iotas(8 * (r - 2), 8 * (r - 1))
                i = C.wb_idx
                C.wb_idx = 1 - C.wb_idx
                buf = sb["WB0"] if i == 0 else sb["WB1"]
                lu = C.wb_last_use[i] if r < 2 else pe_f0 + 32 * (r - 2) + 34

                qsem = "qga" if i == 0 else "qgb"

                @C.G
                def _(g, r=r, b=b, hp=hp, buf=buf, lu=lu, qsem=qsem,
                      first=(b == 0 and hp == 0 and r == 0)):
                    if first:
                        g.wait_ge(C.sem("cc"), ag_m)
                    if lu > 0:
                        g.wait_ge(C.sem("pe"), lu)
                    base = 4096 * r
                    ins = g.dma_start(
                        out=buf[:, 0:1024],
                        in_=C.dr["cc3_out"][base + 1024 * b + 128 * hp:
                                            base + 1024 * b + 128 * (hp + 1), 0:1024])
                    C.inc(ins, qsem, 16)
                    for j, h in enumerate([2 * hp, 2 * hp + 1]):
                        vv = buf[:, 1024 + 520 * j:1024 + 520 * j + 520] \
                            .rearrange("p (c o) -> p c o", o=65)
                        ins = g.dma_start(
                            out=vv[:, :, 0:64],
                            in_=C.dr["cc3_out"][base + 2048 + 1024 * b:
                                                base + 2048 + 1024 * (b + 1),
                                                64 * h:64 * (h + 1)]
                                .rearrange("(c p) o -> p c o", p=128))
                        C.inc(ins, qsem, 16)
                tiles.append((buf, (qsem, C.val(qsem)), i))
            emit_iotas(8 * (NCORES - 2), 8 * NCORES)

            eva_f0 = C.val("eva")
            pe0f = C.val("pe")
            act_f0 = C.val("act")
            ngrp = NCORES * 8 * 2
            stm = {}
            avm = {}
            mkm = {}
            order = []
            for r in range(NCORES):
                for tkc in range(8):
                    for j, h in enumerate(h2):
                        order.append((r, tkc, j, h))

            @C.T
            def _(t):
                run = [pe0f]

                def emit_st(gi):
                    r, tkc, j, h = order[gi]
                    buf, mark, bi = tiles[r]
                    t.wait_ge(C.sem(mark[0]), mark[1])
                    if gi >= 2:
                        t.wait_ge(C.sem("act"), act_f0 + gi - 1)
                    ps = [PB[:, 0:1024], PB[:, 1024:2048]][gi % 2]
                    for half in range(2):
                        ins = t.matmul(
                            ps[:, 512 * half:512 * (half + 1)],
                            buf[64 * (h % 2):64 * (h % 2) + 64,
                                128 * tkc:128 * (tkc + 1)],
                            Qr[b][64 * (h % 2):64 * (h % 2) + 64, hp,
                                  512 * half:512 * (half + 1)],
                            start=True, stop=True, skip_group_check=True)
                    run[0] += 1
                    ins.then_inc(C.sem("pe"), 1)
                    stm[gi] = run[0]

                def emit_av(gi):
                    r, tkc, j, h = order[gi]
                    buf, mark, bi = tiles[r]
                    t.wait_ge(C.sem("eva"), eva_f0 + gi + 1)
                    if gi == 0:
                        t.wait_ge(C.sem("dve"), ones_m)
                    hbslot = HB[:, 1024 * (gi % 2):1024 * (gi % 2) + 1024]
                    vv = buf[:, 1024 + 520 * j:1024 + 520 * j + 520] \
                        .rearrange("p (c o) -> p c o", o=65)
                    avp = PA[0:65, 1024 * j:1024 * j + 1024]
                    for half in range(2):
                        ins = t.matmul(
                            avp[:, 512 * half:512 * (half + 1)],
                            vv[:, tkc, :],
                            hbslot[:, 512 * half:512 * (half + 1)],
                            start=(gi < 2), stop=(gi >= ngrp - 2),
                            skip_group_check=True)
                    run[0] += 1
                    ins.then_inc(C.sem("pe"), 1)
                    avm[gi] = run[0]

                psum_guard(C, t)
                t.wait_ge(C.sem("dve"), dve_f0)
                for gi in range(ngrp):
                    emit_st(gi)
                    if gi >= 2:
                        emit_av(gi - 2)
                emit_av(ngrp - 2)
                emit_av(ngrp - 1)
            C.cnt["pe"] += 2 * ngrp

            @C.A
            def _(a):
                for gi in range(ngrp):
                    a.wait_ge(C.sem("pe"), stm[gi])
                    if gi >= 2:
                        a.wait_ge(C.sem("pe"), avm[gi - 2])
                    ps = [PB[:, 0:1024], PB[:, 1024:2048]][gi % 2]
                    hbslot = HB[:, 1024 * (gi % 2):1024 * (gi % 2) + 1024]
                    ins = a.activation(out=hbslot, in_=ps[:, :], func=AF.Exp,
                                       scale=scale)
                    ins.then_inc(C.sem("act"), 1)
            C.cnt["act"] += ngrp

            @C.V
            def _(v):
                if DEBUG_L3NOMASK or DEBUG_L3MASKLVL == 0:
                    for gi in range(ngrp):
                        v.wait_ge(C.sem("act"), act_f0 + gi + 1)
                        hbslot = HB[:, 1024 * (gi % 2):1024 * (gi % 2) + 1024]
                        ins = v.tensor_scalar(hbslot, hbslot, 1.0, None, OP.mult)
                        ins.then_inc(C.sem("eva"), 1)
                    return
                for gi in range(ngrp):
                    r, tkc, j, h = order[gi]
                    gg = gi // 2
                    if gi % 2 == 0:
                        v.wait_ge(C.sem("pool"), pool_f0 + gg + 1)
                        if DEBUG_L3MASKLVL >= 3:
                            v.tensor_scalar(
                                SM2[:, 1024 * (gg % 2):1024 * (gg % 2) + 1024],
                                SMI[:, 0:1024],
                                0.0, None, OP.is_ge)
                        else:
                            v.memset(SM2[:, 1024 * (gg % 2):1024 * (gg % 2) + 1024], 1.0)
                        if DEBUG_L3MASKLVL >= 4:
                            ins = v.tensor_scalar(
                                SM2[:, 1024 * (gg % 2):1024 * (gg % 2) + 1024],
                                SM2[:, 1024 * (gg % 2):1024 * (gg % 2) + 1024],
                                sb["STAT"][:, 2 * r + 1:2 * r + 2],
                                sb["STAT"][:, 2 * r:2 * r + 1], OP.mult, OP.add)
                        else:
                            ins = v.tensor_scalar(
                                SM2[:, 1024 * (gg % 2):1024 * (gg % 2) + 1024],
                                SM2[:, 1024 * (gg % 2):1024 * (gg % 2) + 1024],
                                1.0, None, OP.mult)
                        ins.then_inc(C.sem("dve"), 1)
                    v.wait_ge(C.sem("act"), act_f0 + gi + 1)
                    hbslot = HB[:, 1024 * (gi % 2):1024 * (gi % 2) + 1024]
                    if DEBUG_L3MASKLVL >= 2:
                        ins = v.tensor_tensor(
                            hbslot, hbslot,
                            SM2[:, 1024 * (gg % 2):1024 * (gg % 2) + 1024], OP.mult)
                    else:
                        ins = v.tensor_scalar(hbslot, hbslot, 1.0, None, OP.mult)
                    ins.then_inc(C.sem("eva"), 1)
            if not DEBUG_L3NOMASK and DEBUG_L3MASKLVL > 0:
                C.cnt["dve"] += ngrp // 2
            C.cnt["eva"] += ngrp
            av_done = C.cnt["pe"]
            C.wb_last_use[0] = C.wb_last_use[1] = av_done

            for j, h in enumerate(h2):
                @C.V
                def _(v, j=j):
                    def edge(ins):
                        ins.then_inc(C.sem("dve"), 1)
                        C.cnt["dve"] += 1
                        v.wait_ge(C.sem("dve"), C.cnt["dve"])

                    v.wait_ge(C.sem("pe"), av_done)
                    avp2 = PA[0:65, 1024 * j:1024 * j + 1024]
                    edge(v.tensor_copy(SM[0:1, 0:1024], avp2[64:65, 0:1024]))
                    edge(v.reciprocal(SM[0:1, 0:1024], SM[0:1, 0:1024]))
                    ins = v.tensor_copy(sb["ABR"][0:1, 0:1024], SM[0:1, 0:1024])
                    ins.then_inc(C.sem("dve"), 1)
                C.cnt["dve"] += 1
                rc_m = C.cnt["dve"]

                @C.T
                def _(t):
                    t.wait_ge(C.sem("dve"), rc_m)
                    for half in range(2):
                        ins = t.matmul(
                            PB[0:64, 512 * half:512 * (half + 1)],
                            ones_row[:, 0:64],
                            sb["ABR"][0:1, 512 * half:512 * (half + 1)],
                            start=True, stop=True)
                    ins.then_inc(C.sem("pe"), 1)
                C.cnt["pe"] += 1
                rb_m = C.cnt["pe"]

                @C.V
                def _(v, j=j, h=h, b=b, hp=hp):
                    v.wait_ge(C.sem("pe"), rb_m)
                    v.tensor_copy(HB[0:64, 0:1024], PB[0:64, 0:1024])
                    avp2 = PA[0:65, 1024 * j:1024 * j + 1024]
                    ins = v.tensor_tensor(
                        ATN[64 * (h % 2):64 * (h % 2) + 64, hp,
                            1024 * b:1024 * (b + 1)],
                        avp2[0:64, 0:1024], HB[0:64, 0:1024], OP.mult)
                    ins.then_inc(C.sem("dve"), 1)
                C.cnt["dve"] += 1

    mb_out = load_rows_bf(C, C.dr["L3b_out"], 0, D, BR[0:1, 0:1024],
                          wait_pe=C.val("pe"))

    def ev_o(e, oc, ps):
        return e.tensor_tensor(XB[:, oc, :], ps[:, :], S[:, oc, :], OP.add)

    fm_proj(C, C.dr["L3W_out"], lambda dc: ATN[:, dc, :], D, ev_o,
            evac_eng="vector", bias_row=BR[0:1, 0:1024], bias_mark=mb_out)

    postln_from_XB(C)




def emit_l3_ffn(C):
    block_barrier(C)
    sb = C.sb
    S, W, XB = sb["S"], sb["W"], sb["XB"]
    PA, PB = C.ps["PA"], C.ps["PB"]
    SM = sb["SM"]
    HB = sb["HB"]

    Wflat = W[:, :, :].rearrange("p c t -> p (c t)").bitcast(BF16)
    xrslot = Wflat[:, 0:8192].rearrange("p (a b) -> p a b", b=1024)
    xkslot = Wflat[:, 8192:16384].rearrange("p (a b) -> p a b", b=1024)
    kslot = Wflat[:, 16384:24576].rearrange("p (a b) -> p a b", b=1024)
    rslot = Wflat[:, 24576:32768].rearrange("p (a b) -> p a b", b=1024)

    PUB = SM[:, 0:16]
    GATH = SM[:, 16:144].rearrange("p (r j) -> p r j", j=16)
    BND = SM[:, 144:160]
    MUR = SM[:, 160:168]
    MUK = SM[:, 168:176]
    SELC = SM[:, 176:184]

    @C.V
    def _(v):
        for b in range(NB):
            for dc in range(DC):
                ins = v.tensor_copy(PUB[:, 8 * b + dc:8 * b + dc + 1],
                                    S[:, dc, 1024 * b + 1023:1024 * b + 1024])
        ins.then_inc(C.sem("dve"), 1)
    C.cnt["dve"] += 1
    pub_m = C.cnt["dve"]

    @C.G
    def _(g):
        ins = g.dma_start(out=MUR[:, :], in_=C.dr["L3f_mur"])
        C.inc(ins, "qg", 16)
        ins = g.dma_start(out=MUK[:, :], in_=C.dr["L3f_muk"])
        C.inc(ins, "qg", 16)
        ins = g.dma_start(out=SM[:, 176:184], in_=C.dr["L3f_sel"])
        C.inc(ins, "qg", 16)
    const_m = C.val("qg")

    @C.G
    def _(g):
        g.wait_ge(C.sem("dve"), pub_m)
        ins = g.dma_start(out=C.dr["cc7_in"], in_=PUB[:, :])
        C.inc(ins, "qgc", 16)
        g.wait_ge(C.sem("qgc"), C.val("qgc"))
        ins = g.collective_compute("AllGather", OP.bypass,
                                   replica_groups=[list(range(NCORES))],
                                   ins=[C.dr["cc7_in"].opt()],
                                   outs=[C.dr["cc7_out"].opt()])
        C.inc(ins, "cc", 1)
    ag_m = C.val("cc")

    @C.G
    def _(g):
        g.wait_ge(C.sem("cc"), ag_m)
        ins = g.dma_start(out=GATH[:, :, :],
                          in_=C.dr["cc7_out"].rearrange("(r p) j -> p r j", p=128))
        C.inc(ins, "qgc", 16)
    rb_m = C.val("qgc")

    @C.V
    def _(v):
        v.wait_ge(C.sem("qgc"), rb_m)
        v.wait_ge(C.sem("qg"), const_m)
        v.tensor_scalar(BND[:, :], GATH[:, 0, :], SELC[:, 0:1], None, OP.mult)
        for r in range(1, NCORES):
            ins = v.scalar_tensor_tensor(BND[:, :], GATH[:, r, :],
                                         SELC[:, r:r + 1], BND[:, :],
                                         OP.mult, OP.add)
        ins.then_inc(C.sem("dve"), 1)
    C.cnt["dve"] += 1
    bnd_m = C.cnt["dve"]

    for b in range(NB):
        bc = 1024 * b
        pe_b_gate = C.val("pe")
        evd_b_gate = C.val("evd")

        @C.V
        def _(v, b=b, pe_b_gate=pe_b_gate):
            if b == 0:
                v.wait_ge(C.sem("dve"), bnd_m)
            else:
                v.wait_ge(C.sem("pe"), pe_b_gate)
            for dc in range(DC):
                v.tensor_tensor(HB[:, 1:1024], S[:, dc, bc + 1:bc + 1024],
                                S[:, dc, bc:bc + 1023], OP.subtract)
                v.tensor_tensor(HB[:, 0:1], S[:, dc, bc:bc + 1],
                                BND[:, 8 * b + dc:8 * b + dc + 1], OP.subtract)
                v.scalar_tensor_tensor(xrslot[:, dc, 1:1024], HB[:, 1:1024],
                                       MUR[:, dc:dc + 1], S[:, dc, bc:bc + 1023],
                                       OP.mult, OP.add)
                v.scalar_tensor_tensor(xrslot[:, dc, 0:1], HB[:, 0:1],
                                       MUR[:, dc:dc + 1],
                                       BND[:, 8 * b + dc:8 * b + dc + 1],
                                       OP.mult, OP.add)
                v.scalar_tensor_tensor(xkslot[:, dc, 1:1024], HB[:, 1:1024],
                                       MUK[:, dc:dc + 1], S[:, dc, bc:bc + 1023],
                                       OP.mult, OP.add)
                ins = v.scalar_tensor_tensor(xkslot[:, dc, 0:1], HB[:, 0:1],
                                             MUK[:, dc:dc + 1],
                                             BND[:, 8 * b + dc:8 * b + dc + 1],
                                             OP.mult, OP.add)
            ins.then_inc(C.sem("dve"), 1)
        C.cnt["dve"] += 1
        xr_m = C.cnt["dve"]

        rk_wait = [("dve", xr_m)]
        if b == 1:
            rk_wait.append(("evd", evd_b_gate))

        fm_proj(C, C.dr["L3fW_r"], lambda dc: xrslot[:, dc, :], D,
                lambda e, oc, ps: e.activation(out=rslot[:, oc, :], in_=ps[:, :],
                                               func=AF.Sigmoid),
                evac_eng="scalar", tokens=(0, 1024), extra_wait=rk_wait)

        def k_evac(e, oc, ps):
            e.activation(out=HB[:, 1024:2048], in_=ps[:, :], func=AF.Relu)
            return e.activation(out=kslot[:, oc, :], in_=HB[:, 1024:2048],
                                func=AF.Square)

        fm_proj(C, C.dr["L3fW_k"], lambda dc: xkslot[:, dc, :], D, k_evac,
                evac_eng="scalar", tokens=(0, 1024), extra_wait=rk_wait)
        eva_rk = C.val("eva")

        def v_evac(e, oc, ps, b=b):
            e.tensor_tensor(HB[:, 0:1024], ps[:, :], rslot[:, oc, :], OP.mult)
            return e.tensor_tensor(XB[:, oc, 1024 * b:1024 * (b + 1)],
                                   HB[:, 0:1024],
                                   S[:, oc, 1024 * b:1024 * (b + 1)], OP.add)

        fm_proj(C, C.dr["L3fW_v"], lambda dc: kslot[:, dc, :], D, v_evac,
                evac_eng="vector", tokens=(0, 1024),
                extra_wait=[("eva", eva_rk)])

    postln_from_XB(C)


BLOCK_DECLS = {
    0: [("L0W_qkv", [D, 3 * D], F32), ("L0b_qkv", [3 * D], F32),
        ("L0W_g1", [D, 64], F32), ("L0b_g1", [64], F32),
        ("L0W_g2", [64, D], F32), ("L0b_g2", [D], F32),
        ("L0W_out", [D, D], F32), ("L0b_out", [D], F32),
        ("L0_ln_g", [D], F32), ("L0_ln_b", [D], F32)],
    1: [("L0fW_qkv", [D, 1536], F32), ("L0fW_proj", [D, D], F32),
        ("L0fW_rt", [D, 8], F32),
        ("L0f_w1", [8, D, 4096], F32), ("L0f_w2", [8, 4096, D], F32),
        ("L0f_ln1g", [D], F32), ("L0f_ln1b", [D], F32),
        ("L0f_ln2g", [D], F32), ("L0f_ln2b", [D], F32)],
    2: [("L1W_q", [D, D], F32), ("L1b_q", [D], F32),
        ("L1W_k", [D, D], F32), ("L1b_k", [D], F32),
        ("L1W_v", [D, D], F32), ("L1b_v", [D], F32),
        ("L1_gamma", [128, 1], F32), ("L1_dexp", [1, 8], F32),
        ("L1_mask", [1, 8], F32)],
    3: [("L1fW1", [D, 4096], F32), ("L1fb1", [4096], F32),
        ("L1fW2", [4096, D], F32), ("L1fb2", [D], F32)],
    4: [("L2W_q", [D, D], F32), ("L2W_k", [D, D], F32),
        ("L2W_v", [D, D], F32), ("L2W_o", [D, D], F32),
        ("L2_ropeP", [128, 128], F32), ("L2_cos", [128, 1024], F32),
        ("L2_sin", [128, 1024], F32)],
    5: [("L2fW_rt", [D, 8], F32),
        ("L2f_w1", [8, D, 4096], F32), ("L2f_w2", [8, 4096, D], F32)],
    6: [("L3W_qkv", [D, 3 * D], F32), ("L3b_qkv", [3 * D], F32),
        ("L3W_out", [D, D], F32), ("L3b_out", [D], F32),
        ("L3_ab", [128, 16], F32)],
    7: [("L3fW_r", [D, D], F32), ("L3fW_k", [D, D], F32),
        ("L3fW_v", [D, D], F32), ("L3f_mur", [128, 8], F32),
        ("L3f_muk", [128, 8], F32), ("L3f_sel", [128, 8], F32)],
}
BLOCK_EMITTERS = {0: emit_l0_attn, 1: emit_l0_ffn, 2: emit_l1_attn,
                  3: emit_l1_ffn, 4: emit_l2_attn, 5: emit_l2_ffn,
                  6: emit_l3_attn, 7: emit_l3_ffn}


def build_program(block_ids):
    es = ExitStack()
    nc = bass.Bass("TRN2", target_bir_lowering=False)
    dram = {}

    def din(name, shape, dtype=F32):
        dram[name] = nc.dram_tensor(name, list(shape), dtype, kind="ExternalInput").ap()

    def dint(name, shape, dtype=F32):
        dram[name] = nc.dram_tensor(name, list(shape), dtype).ap()

    din("xin", [D, T])
    dram["xout"] = nc.dram_tensor("xout", [D, T], F32, kind="ExternalOutput").ap()
    tap_shape = [2 * D, T] if DEBUG_TAP else [1, 1]
    dram["xtap"] = nc.dram_tensor("xtap", tap_shape, F32, kind="ExternalOutput").ap()
    for b in block_ids:
        for (name, shape, dt) in BLOCK_DECLS[b]:
            din(name, shape, dt)
    if 0 in block_ids:
        dint("cc0_in", [130, 1024])
        dint("cc0_out", [130, 1024])
    if 2 in block_ids:
        dint("cc1_in", [1024, 1032], BF16)
        dint("cc1_out", [NCORES * 1024, 1032], BF16)
    if 3 in block_ids:
        dint("qsp1", [D, 4096], BF16)
        dint("qsp2", [4096, D], BF16)
    if 4 in block_ids:
        dint("cc2_in", [4096, 1032], BF16)
        dint("cc2_out", [NCORES * 4096, 1032], BF16)
    if 7 in block_ids:
        dint("cc7_in", [128, 16], F32)
        dint("cc7_out", [NCORES * 128, 16], F32)
    if 6 in block_ids:
        dint("cc3_in", [4096, 1024], BF16)
        dint("cc3_out", [NCORES * 4096, 1024], BF16)

    with es:
        with nc.Block() as blk:
            C = Ctx(nc, blk, es)
            C.dr = dram

            def SB(name, shape, dtype):
                C.sb[name] = es.enter_context(nc.sbuf_tensor(name, list(shape), dtype))

            def PSUM(name, shape, dtype=F32):
                C.ps[name] = es.enter_context(nc.psum_tensor(name, list(shape), dtype))

            SB("S", [128, DC, T], F32)
            SB("W", [128, DC, T], F32)
            SB("XB", [128, DC, T], BF16)
            SB("WB0", [128, 4096], BF16)
            SB("WB1", [128, 4096], BF16)
            SB("SM", [128, 1024], F32)
            SB("SM2", [128, 2048], BF16)
            SB("HB", [128, 2048], BF16)
            SB("IDN", [128, 128], F32)
            SB("ONER", [1, 512], BF16)
            SB("ABR", [1, 4096], BF16)
            SB("BIASR", [1, 3072], BF16)
            SB("TB", [1, 128], BF16)
            SB("BONES", [128, 128], BF16)
            SB("BONEST", [16, 1024], BF16)
            SB("PIDX", [128, 1], F32)
            SB("ONE", [128, 8], BF16)
            SB("ONERF", [1, 128], F32)
            SB("ONEF", [128, 1], F32)
            SB("ZCOL", [128, 8], BF16)
            SB("EPSC", [1, 8], F32)
            SB("STAT", [128, 32], F32)
            PSUM("PA", [128, 2048], F32)
            PSUM("PB", [128, 2048], F32)

            emit_prelude(C)
            for b in block_ids:
                BLOCK_EMITTERS[b](C)
                if C.tapped:
                    break
            emit_epilogue(C)
    return nc


def _weight_maps(inputs, block_ids):
    t = lambda a: np.ascontiguousarray(np.asarray(a).T)
    m = {}
    if 1 in block_ids:
        wq = np.asarray(inputs["L0_ffn_qkv_weight"]).T
        perm = np.empty(1536, np.int64)
        for g in range(4):
            for j in range(4):
                perm[(4 * g + j) * 64:(4 * g + j + 1) * 64] = \
                    np.arange(g * 384 + j * 64, g * 384 + (j + 1) * 64)
            perm[1024 + g * 64:1024 + (g + 1) * 64] = \
                np.arange(g * 384 + 256, g * 384 + 320)
            perm[1280 + g * 64:1280 + (g + 1) * 64] = \
                np.arange(g * 384 + 320, g * 384 + 384)
        m["L0fW_qkv"] = np.ascontiguousarray(wq[:, perm])
        m["L0fW_proj"] = t(inputs["L0_ffn_proj_weight"])
        m["L0fW_rt"] = np.ascontiguousarray(np.asarray(inputs["L0_ffn_router_weight"]))
        m["L0f_w1"] = np.asarray(inputs["L0_ffn_moe_w1"])
        m["L0f_w2"] = np.asarray(inputs["L0_ffn_moe_w2"])
        m["L0f_ln1g"] = np.asarray(inputs["L0_ffn_ln1_weight"])
        m["L0f_ln1b"] = np.asarray(inputs["L0_ffn_ln1_bias"])
        m["L0f_ln2g"] = np.asarray(inputs["L0_ffn_ln2_weight"])
        m["L0f_ln2b"] = np.asarray(inputs["L0_ffn_ln2_bias"])
    if 2 in block_ids:
        m["L1W_q"] = t(inputs["L1_attn_Wq"])
        m["L1b_q"] = np.asarray(inputs["L1_attn_bq"])
        m["L1W_k"] = t(inputs["L1_attn_Wk"])
        m["L1b_k"] = np.asarray(inputs["L1_attn_bk"])
        m["L1W_v"] = t(inputs["L1_attn_Wv"])
        m["L1b_v"] = np.asarray(inputs["L1_attn_bv"])
        m["L1_gamma"] = np.broadcast_to(
            np.asarray(inputs["L1_attn_gamma"], dtype=np.float32).reshape(1, 1),
            (128, 1)).copy()
    if 4 in block_ids:
        m["L2W_q"] = np.ascontiguousarray(np.asarray(inputs["L2_attn_Wq"]))
        m["L2W_k"] = np.ascontiguousarray(np.asarray(inputs["L2_attn_Wk"]))
        m["L2W_v"] = np.ascontiguousarray(np.asarray(inputs["L2_attn_Wv"]))
        m["L2W_o"] = np.ascontiguousarray(np.asarray(inputs["L2_attn_Wo"]))
        P = np.zeros((128, 128), np.float32)
        for i in range(128):
            P[i ^ 1, i] = 1.0
        m["L2_ropeP"] = P
    if 6 in block_ids:
        m["L3W_qkv"] = np.ascontiguousarray(np.asarray(inputs["L3_attn_w_qkv"]))
        m["L3b_qkv"] = np.asarray(inputs["L3_attn_b_qkv"])
        m["L3W_out"] = np.ascontiguousarray(np.asarray(inputs["L3_attn_w_out"]))
        m["L3b_out"] = np.asarray(inputs["L3_attn_b_out"])
    if 7 in block_ids:
        m["L3fW_r"] = t(inputs["L3_ffn_W_r"])
        m["L3fW_k"] = t(inputs["L3_ffn_W_k"])
        m["L3fW_v"] = t(inputs["L3_ffn_W_v"])
        m["L3f_mur"] = np.ascontiguousarray(
            np.asarray(inputs["L3_ffn_mu_r"], np.float32).reshape(8, 128).T)
        m["L3f_muk"] = np.ascontiguousarray(
            np.asarray(inputs["L3_ffn_mu_k"], np.float32).reshape(8, 128).T)
    if 5 in block_ids:
        m["L2fW_rt"] = t(inputs["L2_ffn_router_w"])
        m["L2f_w1"] = np.ascontiguousarray(
            np.transpose(np.asarray(inputs["L2_ffn_w1"]), (0, 2, 1)))
        m["L2f_w2"] = np.ascontiguousarray(
            np.transpose(np.asarray(inputs["L2_ffn_w2"]), (0, 2, 1)))
    if 3 in block_ids:
        m["L1fW1"] = t(inputs["L1_ffn_W1"])
        m["L1fb1"] = np.asarray(inputs["L1_ffn_b1"])
        m["L1fW2"] = t(inputs["L1_ffn_W2"])
        m["L1fb2"] = np.asarray(inputs["L1_ffn_b2"])
    if 0 in block_ids:
        m["L0W_qkv"] = t(inputs["L0_attn_w_qkv"])
        m["L0b_qkv"] = np.asarray(inputs["L0_attn_b_qkv"])
        m["L0W_g1"] = t(inputs["L0_attn_w_g1"])
        m["L0b_g1"] = np.asarray(inputs["L0_attn_b_g1"])
        m["L0W_g2"] = t(inputs["L0_attn_w_g2"])
        m["L0b_g2"] = np.asarray(inputs["L0_attn_b_g2"])
        m["L0W_out"] = t(inputs["L0_attn_w_out"])
        m["L0b_out"] = np.asarray(inputs["L0_attn_b_out"])
        m["L0_ln_g"] = np.asarray(inputs["L0_attn_ln_g"])
        m["L0_ln_b"] = np.asarray(inputs["L0_attn_ln_b"])
    return m


def shard_stream(x_full):
    out = []
    for c in range(NCORES):
        sl = x_full[:, c * L:(c + 1) * L, :]
        fm = np.transpose(sl, (2, 0, 1)).reshape(D, T)
        out.append(np.ascontiguousarray(fm, dtype=np.float32))
    return out


def unshard_stream(cores):
    x = np.empty((NB, SEQ, D), dtype=np.float32)
    for c in range(NCORES):
        fm = np.asarray(cores[c]).reshape(D, NB, L)
        x[:, c * L:(c + 1) * L, :] = np.transpose(fm, (1, 2, 0))
    return x


def _rope_tables(c):
    pos = np.arange(c * L, (c + 1) * L, dtype=np.float64)
    i = np.arange(32, dtype=np.float64)
    coss, sins = [], []
    for s in (1.0, 4.0, 16.0):
        freqs = 1.0 / (10000.0 * s) ** (2.0 * i / 64.0)
        ang = pos[:, None] * freqs[None, :]
        coss.append(np.cos(ang))
        sins.append(np.sin(ang))
    cos = np.mean(coss, 0)
    sin = np.mean(sins, 0)
    CC = np.zeros((128, L), np.float32)
    SN = np.zeros((128, L), np.float32)
    for p in range(128):
        ii = (p % 64) // 2
        CC[p] = cos[:, ii]
        SN[p] = sin[:, ii] * (-1.0 if p % 2 == 0 else 1.0)
    return CC, SN


def _percore_maps(block_ids, c):
    m = {}
    if 4 in block_ids:
        CC, SN = _rope_tables(c)
        m["L2_cos"] = CC
        m["L2_sin"] = SN
    if 7 in block_ids:
        sel = np.zeros((128, 8), np.float32)
        if c > 0:
            sel[:, c - 1] = 1.0
        m["L3f_sel"] = sel
    if 6 in block_ids:
        ab = np.zeros((128, 16), np.float32)
        for r in range(8):
            if r < c:
                ab[:, 2 * r] = 1.0
            elif r == c:
                ab[:, 2 * r + 1] = 1.0
        m["L3_ab"] = ab
    if 2 in block_ids:
        dexp = np.zeros((1, 8), np.float32)
        mask = np.zeros((1, 8), np.float32)
        for k in range(8):
            if k < c:
                dexp[0, k] = 1024.0 * (c - 1 - k)
                mask[0, k] = 1.0
        m["L1_dexp"] = dexp
        m["L1_mask"] = mask
    return m


def run_blocks(block_ids, stream_percore, inputs):
    nc = build_program(block_ids)
    wmap = _weight_maps(inputs, block_ids)
    in_maps = [dict(wmap, xin=stream_percore[c], **_percore_maps(block_ids, c))
               for c in range(NCORES)]
    res = run_bass_kernel_spmd(nc, in_maps, core_ids=list(range(NCORES)))
    return [res.results[c]["xout"] for c in range(NCORES)]


def kernel(**inputs):
    streams = shard_stream(np.asarray(inputs["L0_attn_x"], dtype=np.float32))
    streams = run_blocks(list(range(8)), streams, inputs)
    return unshard_stream(streams)


# version 02
# speedup vs baseline: 1.0024x; 1.0024x over previous
import numpy as np
from contextlib import ExitStack

import concourse.bass as bass
import concourse.mybir as mybir
from concourse.bass_utils import run_bass_kernel_spmd

F32 = mybir.dt.float32
BF16 = mybir.dt.bfloat16
I32 = mybir.dt.int32
AF = mybir.ActivationFunctionType
OP = mybir.AluOpType
AX = mybir.AxisListType

NCORES = 8
NB = 2
SEQ = 8192
D = 1024
L = SEQ // NCORES
T = NB * L
DC = D // 128
TC = T // 128
NTS = T // 512
EPS_LN = 1e-5

ESEM = {"scalar": "eva", "vector": "evd", "gpsimd": "evp"}

DEBUG_TAP = None
DEBUG_TRUNC = 8
DEBUG_NOAG = False
DEBUG_AGOPS = 6
DEBUG_L3PAIRS = 99
DEBUG_L3NOMASK = False
DEBUG_L3MASKLVL = 4


def tap(C, stage, ap_fn):
    if DEBUG_TAP != stage:
        return
    C.tapped = True
    for s in ("dve", "evd", "eva", "act", "pool"):
        C.val(s)
    marks = {s: C.cnt[s] for s in ("dve", "evd", "eva", "act", "pool")}
    pe_m = C.val("pe")

    @C.G
    def _(g):
        for s, v in marks.items():
            if v:
                g.wait_ge(C.sem(s), v)
        g.wait_ge(C.sem("pe"), pe_m)
        for dst, src_ap in ap_fn():
            ins = g.dma_start(out=dst, in_=src_ap)
            C.inc(ins, "qg", 16)
        g.wait_ge(C.sem("qg"), C.val("qg"))


class Ctx:
    def __init__(self, nc, blk, es):
        self.nc, self.blk, self.es = nc, blk, es
        self._sems, self.cnt = {}, {}
        self.sb, self.dr, self.ps = {}, {}, {}
        self.wb_last_use = [0, 0]
        self.wb_idx = 0
        self.tapped = False

    def sem(self, name):
        if name not in self._sems:
            self._sems[name] = self.es.enter_context(self.nc.semaphore(name))
            self.cnt[name] = 0
        return self._sems[name]

    def inc(self, ins, name, by=1):
        ins.then_inc(self.sem(name), by)
        self.cnt[name] += by
        return self.cnt[name]

    def val(self, name):
        self.sem(name)
        return self.cnt[name]

    def wait(self, eng, name, v=None):
        eng.wait_ge(self.sem(name), self.cnt[name] if v is None else v)

    @property
    def T(self):
        return self.blk.tensor

    @property
    def V(self):
        return self.blk.vector

    @property
    def A(self):
        return self.blk.scalar

    @property
    def G(self):
        return self.blk.gpsimd

    @property
    def Y(self):
        return self.blk.sync


def esec(C, eng):
    return {"vector": C.V, "scalar": C.A, "gpsimd": C.G}[eng]


def block_barrier(C):
    marks = {s: C.val(s) for s in ("pe", "dve", "evd", "eva", "act", "pool")}
    owns = [(C.T, "pe"), (C.V, None), (C.A, None), (C.G, None)]

    for sec, own in owns:
        @sec
        def _(e, own=own):
            for s, v in marks.items():
                if s != own and v:
                    e.wait_ge(C.sem(s), v)
    return marks


def psum_guard(C, t):
    for s in ("eva", "evd", "evp"):
        if C.val(s) > 0:
            t.wait_ge(C.sem(s), C.cnt[s])


def run_pipe(C, groups, bufs, guard=True):
    n = len(groups)
    nb = len(bufs)
    base = {s: C.val(s) for s in ("eva", "evd", "evp")}
    run = dict(base)
    marks = []
    for g in groups:
        s = ESEM[g["eng"]]
        run[s] += 1
        marks.append((s, run[s]))
    pe0 = C.val("pe")

    @C.T
    def _(t):
        if guard:
            psum_guard(C, t)
        for i, g in enumerate(groups):
            for (sn, v) in g.get("wait", ()):
                t.wait_ge(C.sem(sn), v)
            if i >= nb:
                sn, v = marks[i - nb]
                t.wait_ge(C.sem(sn), v)
            ins = g["mm"](t, bufs[i % nb])
            ins.then_inc(C.sem("pe"), 1)
    C.cnt["pe"] += n
    for i, g in enumerate(groups):
        g["_pe_mark"] = pe0 + i + 1

    for eng in ("scalar", "vector", "gpsimd"):
        idxs = [i for i, g in enumerate(groups) if g["eng"] == eng]
        if not idxs:
            continue

        @esec(C, eng)
        def _(e, idxs=idxs):
            for i in idxs:
                g = groups[i]
                e.wait_ge(C.sem("pe"), g["_pe_mark"])
                ins = g["evac"](e, bufs[i % nb])
                ins.then_inc(C.sem(ESEM[eng]), 1)
    for s in ("eva", "evd", "evp"):
        C.cnt[s] = run[s]


def wslab_load(C, w_hbm, og, ocols, wait_pe=None):
    sb = C.sb
    i = C.wb_idx
    C.wb_idx = 1 - C.wb_idx
    buf = sb["WB0"] if i == 0 else sb["WB1"]
    din = w_hbm.shape[0]
    wp = C.wb_last_use[i] if wait_pe is None else max(wait_pe, C.wb_last_use[i])
    if din >= 128:
        nkc = din // 128
        assert nkc * ocols <= 4096
        dst = buf[:, 0:nkc * ocols].rearrange("p (c o) -> p c o", o=ocols)
        src = w_hbm[:, og:og + ocols].rearrange("(c p) o -> p c o", p=128)
    else:
        dst = buf[0:din, 0:ocols].rearrange("p (c o) -> p c o", o=ocols)
        src = w_hbm[:, og:og + ocols].rearrange("p (c o) -> p c o", o=ocols)

    qsem = "qga" if i == 0 else "qgb"

    @C.G
    def _(g):
        if wp > 0:
            g.wait_ge(C.sem("pe"), wp)
        ins = g.dma_start(out=dst, in_=src)
        C.inc(ins, qsem, 16)
    return dst, (qsem, C.val(qsem)), i


def load_rows_bf(C, dram_1d, lo, hi, row_ap, wait_pe=0):
    @C.G
    def _(g):
        if wait_pe:
            g.wait_ge(C.sem("pe"), wait_pe)
        ins = g.dma_start(out=row_ap,
                          in_=dram_1d[lo:hi].rearrange("(o n) -> o n", o=1))
        C.inc(ins, "qgr", 16)
    return C.val("qgr")


def fm_proj(C, w_hbm, src, dout, evac_fn, evac_eng="scalar", bias_row=None,
            tokens=None, bias_mark=None, extra_wait=None):
    sb = C.sb
    din = w_hbm.shape[0]
    nkc = max(1, din // 128)
    t0, t1 = (0, T) if tokens is None else tokens
    tlen = t1 - t0
    nts = tlen // 512
    ones_row = sb["ONER"][0:1, :]
    groups = []
    slabw = min(512, max(128, (4096 // nkc) // 128 * 128))
    nslab = (dout + slabw - 1) // slabw
    pe_entry = C.val("pe")
    cum = [0]
    for s in range(nslab):
        cum.append(cum[-1] + min(slabw, dout - slabw * s) // 128)
    for s in range(nslab):
        og = slabw * s
        ocols = min(slabw, dout - og)
        wp = pe_entry + cum[s - 1] if s >= 2 else None
        ap, mark, bi = wslab_load(C, w_hbm, og, ocols, wait_pe=wp)
        for j in range(ocols // 128):
            oc = og // 128 + j

            def mm(t, ps, ap=ap, j=j, oc=oc):
                for ts in range(nts):
                    for dc in range(nkc):
                        ins = t.matmul(ps[:, 512 * ts:512 * (ts + 1)],
                                       ap[:, dc, 128 * j:128 * (j + 1)],
                                       src(dc)[:, t0 + 512 * ts:t0 + 512 * (ts + 1)],
                                       start=(dc == 0),
                                       stop=(dc == nkc - 1 and bias_row is None))
                    if bias_row is not None:
                        ins = t.matmul(ps[:, 512 * ts:512 * (ts + 1)],
                                       bias_row[:, 128 * oc:128 * (oc + 1)],
                                       ones_row[:, 0:512],
                                       start=False, stop=True)
                return ins

            w = [mark]
            if bias_mark is not None:
                w.append(("qgr", bias_mark))
            if extra_wait:
                w.extend(extra_wait)
            groups.append({"mm": mm,
                           "evac": (lambda e, ps, oc=oc: evac_fn(e, oc, ps)),
                           "eng": evac_eng,
                           "wait": w})
    run_pipe(C, groups, [C.ps["PA"][:, 0:tlen], C.ps["PB"][:, 0:tlen]])
    C.wb_last_use[0] = C.wb_last_use[1] = C.cnt["pe"]


def ln_fm(C, src_bf, emit_apply, gain_row=None, bias_row=None, rows_qgr=None):
    sb = C.sb
    PA, PB = C.ps["PA"], C.ps["PB"]
    ones_col = sb["ONE"][:, 0:1]
    ones_row = sb["ONER"][0:1, :]
    HBs = [sb["HB"]]

    dve0 = C.val("dve")
    pe_base = C.val("pe")

    @C.V
    def _(v):
        for dc in range(DC):
            v.wait_ge(C.sem("pe"), pe_base + dc)
            ins = v.tensor_tensor(HBs[0][:, :], src_bf(dc), src_bf(dc), OP.mult)
            ins.then_inc(C.sem("dve"), 1)
    C.cnt["dve"] += DC

    @C.T
    def _(t):
        psum_guard(C, t)
        for dc in range(DC):
            t.wait_ge(C.sem("dve"), dve0 + dc + 1)
            for ts in range(NTS):
                t.matmul(PB[0:1, 512 * ts:512 * (ts + 1)], ones_col,
                         src_bf(dc)[:, 512 * ts:512 * (ts + 1)],
                         start=(dc == 0), stop=(dc == DC - 1), skip_group_check=True)
                ins = t.matmul(PB[64:65, 512 * ts:512 * (ts + 1)], ones_col,
                               HBs[0][:, 512 * ts:512 * (ts + 1)],
                               start=(dc == 0), stop=(dc == DC - 1), skip_group_check=True)
            ins.then_inc(C.sem("pe"), 1)
    C.cnt["pe"] += DC
    stats_mark = C.cnt["pe"]

    mu = PB[32:33, 0:T]
    var = PA[0:1, 0:T]
    arow_bf = sb["ABR"][0:1, 0:T]
    brow_bf = sb["ABR"][0:1, T:2 * T]

    @C.V
    def _(v):
        v.wait_ge(C.sem("pe"), stats_mark)
        v.tensor_scalar(mu, PB[0:1, 0:T], 1.0 / D, None, OP.mult)
        ins = v.tensor_scalar(var, PB[64:65, 0:T], 1.0 / D, None, OP.mult)
        ins.then_inc(C.sem("dve"), 1)
    C.cnt["dve"] += 1
    mom_mark = C.cnt["dve"]

    @C.A
    def _(a):
        a.wait_ge(C.sem("dve"), mom_mark)
        ins = a.activation(out=brow_bf, in_=mu, func=AF.Square)
        ins.then_inc(C.sem("act"), 1)
    C.cnt["act"] += 1
    m2_mark = C.cnt["act"]

    @C.V
    def _(v):
        v.wait_ge(C.sem("act"), m2_mark)
        ins = v.tensor_tensor(var, var, brow_bf, OP.subtract)
        ins.then_inc(C.sem("dve"), 1)
    C.cnt["dve"] += 1
    var_mark = C.cnt["dve"]

    @C.A
    def _(a):
        a.wait_ge(C.sem("dve"), var_mark)
        a.activation(out=var, in_=var, func=AF.Ln, bias=sb["EPSC"][0:1, 0:1])
        ins = a.activation(out=arow_bf, in_=var, func=AF.Exp, scale=-0.5)
        ins.then_inc(C.sem("act"), 1)
    C.cnt["act"] += 1
    a_mark = C.cnt["act"]

    @C.V
    def _(v):
        v.wait_ge(C.sem("act"), a_mark)
        ins = v.scalar_tensor_tensor(brow_bf, mu, -1.0, arow_bf, OP.mult, OP.mult)
        ins.then_inc(C.sem("dve"), 1)
    C.cnt["dve"] += 1
    rows_mark = C.cnt["dve"]

    groups = []
    for dc in range(DC):
        def mm(t, ps, dc=dc):
            if gain_row is not None:
                ga = gain_row[:, 128 * dc:128 * (dc + 1)]
            else:
                ga = ones_row[:, 0:128]
            for ts in range(NTS):
                t.matmul(PA[:, 512 * ts:512 * (ts + 1)], ga,
                         arow_bf[:, 512 * ts:512 * (ts + 1)], start=True, stop=True)
            for ts in range(NTS):
                if bias_row is not None:
                    t.matmul(PB[:, 512 * ts:512 * (ts + 1)], ga,
                             brow_bf[:, 512 * ts:512 * (ts + 1)], start=True, stop=False)
                    ins = t.matmul(PB[:, 512 * ts:512 * (ts + 1)],
                                   bias_row[:, 128 * dc:128 * (dc + 1)],
                                   ones_row[:, 0:512], start=False, stop=True)
                else:
                    ins = t.matmul(PB[:, 512 * ts:512 * (ts + 1)], ga,
                                   brow_bf[:, 512 * ts:512 * (ts + 1)], start=True, stop=True)
            return ins

        w = [("dve", rows_mark)]
        if rows_qgr is not None:
            w.append(("qgr", rows_qgr))
        groups.append({"mm": mm,
                       "evac": (lambda e, ps, dc=dc: emit_apply(e, dc, PA, PB)),
                       "eng": "vector",
                       "wait": w})
    run_pipe(C, groups, [PA], guard=False)


def postln_from_W(C):
    sb = C.sb
    S, W, XB = sb["S"], sb["W"], sb["XB"]

    @C.V
    def _(v):
        for dc in range(DC):
            ins = v.tensor_copy(XB[:, dc, :], W[:, dc, :])
            ins.then_inc(C.sem("dve"), 1)
    C.cnt["dve"] += DC

    def apply(v, dc, a_ps, b_ps):
        v.tensor_tensor(S[:, dc, :], W[:, dc, :], a_ps[:, :], OP.mult)
        v.tensor_tensor(S[:, dc, :], S[:, dc, :], b_ps[:, :], OP.add)
        return v.tensor_copy(XB[:, dc, :], S[:, dc, :])

    ln_fm(C, lambda dc: XB[:, dc, :], apply)


def emit_prelude(C):
    sb = C.sb

    @C.Y
    def _(s):
        ins = s.dma_start(out=sb["S"][:, :, :],
                          in_=C.dr["xin"].rearrange("(c p) t -> p c t", p=128))
        C.inc(ins, "qs", 16)

    @C.V
    def _(v):
        v.memset(sb["ONE"][:, :], 1.0)
        v.memset(sb["ONER"][:, :], 1.0)
        v.memset(sb["ONERF"][:, :], 1.0)
        v.memset(sb["ONEF"][:, :], 1.0)
        v.memset(sb["EPSC"][:, :], EPS_LN)
        v.memset(sb["IDN"][:, :], 1.0)
        C.wait(v, "qs")
        for dc in range(DC):
            ins = v.tensor_copy(sb["XB"][:, dc, :], sb["S"][:, dc, :])
        C.inc(ins, "dve")
    m = C.val("dve")

    @C.G
    def _(g):
        g.wait_ge(C.sem("dve"), m)
        g.affine_select(sb["IDN"][:, :], sb["IDN"][:, :], [[-1, 128]],
                        OP.is_equal, 0.0, base=0, channel_multiplier=1)
        ins = g.iota(sb["PIDX"][:, :], [[0, 1]], base=0, channel_multiplier=1,
                     allow_small_or_imprecise_dtypes=True)
        C.inc(ins, "pool")


def emit_epilogue(C):
    sb = C.sb
    m_dve = C.val("dve")
    m_evd = C.val("evd")
    m_eva = C.val("eva")

    @C.G
    def _(g):
        g.wait_ge(C.sem("dve"), m_dve)
        if m_evd:
            g.wait_ge(C.sem("evd"), m_evd)
        if m_eva:
            g.wait_ge(C.sem("eva"), m_eva)
        ins = g.dma_start(out=C.dr["xout"].rearrange("(c p) t -> p c t", p=128),
                          in_=sb["S"][:, :, :])
        C.inc(ins, "qo", 16)
        g.wait_ge(C.sem("qo"), C.val("qo"))


def emit_l0_attn(C):
    block_barrier(C)
    sb = C.sb
    S, W, XB = sb["S"], sb["W"], sb["XB"]
    PA, PB = C.ps["PA"], C.ps["PB"]
    ones_row = sb["ONER"][0:1, :]
    ones_col = sb["ONE"][:, 0:1]
    BR = sb["BIASR"]

    Wflat = W[:, :, :].rearrange("p c t -> p (c t)").bitcast(BF16)
    expk = Wflat[:, 0:16384].rearrange("p (a b) -> p a b", b=1024)
    vtm = Wflat[:, 16384:32768].rearrange("p (a b) -> p a b", b=1024)
    qs = Wflat[:, 0:16384].rearrange("p (a b) -> p a b", b=2048)
    gate = Wflat[:, 16384:32768].rearrange("p (a b) -> p a b", b=2048)

    load_rows_bf(C, C.dr["L0b_qkv"], D, 3 * D, BR[0:1, 0:2048])
    load_rows_bf(C, C.dr["L0b_qkv"], 0, D, BR[0:1, 2048:3072])
    rows0 = load_rows_bf(C, C.dr["L0b_g1"], 0, 64, sb["TB"][0:1, 0:64])

    wq = C.dr["L0W_qkv"]
    pe_entry = C.val("pe")
    groups = []
    for s in range(4):
        og = D + 512 * s
        wp = pe_entry + 16 * (s - 1) if s >= 2 else None
        ap, mark, bi = wslab_load(C, wq, og, 512, wait_pe=wp)
        is_k = s < 2
        for tc in range(TC):
            def mm(t, ps, ap=ap, tc=tc, s=s):
                for dc in range(DC):
                    t.matmul(ps[:, 0:512], XB[:, dc, 128 * tc:128 * (tc + 1)],
                             ap[:, dc, :], start=(dc == 0), stop=False)
                return t.matmul(ps[:, 0:512], ones_row[:, 0:128],
                                BR[0:1, 512 * s:512 * (s + 1)], start=False, stop=True)

            def evac(e, ps, s=s, tc=tc, is_k=is_k):
                if is_k:
                    return e.activation(out=expk[:, tc, 512 * s:512 * (s + 1)],
                                        in_=ps[:, 0:512], func=AF.Exp)
                return e.tensor_copy(vtm[:, tc, 512 * (s - 2):512 * (s - 1)],
                                     ps[:, 0:512])

            groups.append({"mm": mm, "evac": evac,
                           "eng": "scalar" if is_k else "vector",
                           "wait": [mark, ("qgr", rows0)]})
    bufs = [PA[:, 0:512], PA[:, 512:1024], PB[:, 0:512], PB[:, 512:1024]]
    run_pipe(C, groups, bufs)
    C.wb_last_use[0] = C.wb_last_use[1] = C.cnt["pe"]
    phaseA_end = C.cnt["pe"]

    @C.T
    def _(t):
        psum_guard(C, t)
        for b in range(NB):
            for h in range(16):
                r0 = 64 * (h % 2)
                c0 = 64 * (8 * b + h // 2)
                for j in range(8):
                    tc = 8 * b + j
                    ins = t.matmul(PA[r0:r0 + 64, c0:c0 + 64],
                                   expk[:, tc, 64 * h:64 * (h + 1)],
                                   vtm[:, tc, 64 * h:64 * (h + 1)],
                                   start=(j == 0), stop=(j == 7),
                                   skip_group_check=True)
        ins.then_inc(C.sem("pe"), 1)
        for b in range(NB):
            row = PB[64 * b:64 * b + 1, :]
            for j in range(8):
                tc = 8 * b + j
                for ts in range(2):
                    ins = t.matmul(row[:, 512 * ts:512 * (ts + 1)], ones_col,
                                   expk[:, tc, 512 * ts:512 * (ts + 1)],
                                   start=(j == 0), stop=(j == 7),
                                   skip_group_check=True)
        ins.then_inc(C.sem("pe"), 1)
    C.cnt["pe"] += 2
    kv_mark = C.val("pe")

    SM2F = sb["SM2"][:, :].bitcast(F32)
    HBF = sb["HB"][:, :].bitcast(F32)

    @C.V
    def _(v):
        v.wait_ge(C.sem("pe"), kv_mark)
        v.tensor_copy(sb["SM"][:, 0:1024], PA[:, 0:1024])
        v.tensor_copy(SM2F[0:1, 0:1024], PB[0:1, 0:1024])
        ins = v.tensor_copy(HBF[0:1, 0:1024], PB[64:65, 0:1024])
        ins.then_inc(C.sem("evd"), 1)
    C.cnt["evd"] += 1
    kv_evac_mark = C.val("evd")

    @C.G
    def _(g):
        g.wait_ge(C.sem("evd"), kv_evac_mark)
        ins = g.dma_start(out=C.dr["cc0_in"][0:128, :], in_=sb["SM"][:, 0:1024])
        C.inc(ins, "qgc", 16)
        ins = g.dma_start(out=C.dr["cc0_in"][128:129, :],
                          in_=SM2F[0:1, 0:1024])
        C.inc(ins, "qgc", 16)
        ins = g.dma_start(out=C.dr["cc0_in"][129:130, :],
                          in_=HBF[0:1, 0:1024])
        C.inc(ins, "qgc", 16)
        g.wait_ge(C.sem("qgc"), C.val("qgc"))
        ins = g.collective_compute("AllReduce", OP.add,
                                   replica_groups=[list(range(NCORES))],
                                   ins=[C.dr["cc0_in"].opt()],
                                   outs=[C.dr["cc0_out"].opt()])
        C.inc(ins, "cc", 1)
    cc_mark = C.val("cc")

    load_rows_bf(C, C.dr["L0b_g2"], 0, D, BR[0:1, 0:1024],
                 wait_pe=phaseA_end)
    rowsC = load_rows_bf(C, C.dr["L0_ln_g"], 0, D, BR[0:1, 1024:2048],
                         wait_pe=phaseA_end)

    def q_evac(e, oc, ps):
        return e.activation(out=qs[:, oc, :], in_=ps[:, :], func=AF.Silu)

    fm_proj(C, C.dr["L0W_qkv"], lambda dc: XB[:, dc, :], D, q_evac,
            evac_eng="scalar", bias_row=BR[0:1, 2048:3072], bias_mark=rows0)
    q_end = C.cnt["pe"]

    ap1, mark1, bi1 = wslab_load(C, C.dr["L0W_g1"], 0, 64)

    def gh_mm(t, ps):
        for ts in range(NTS):
            for dc in range(DC):
                t.matmul(ps[0:64, 512 * ts:512 * (ts + 1)], ap1[:, dc, :],
                         XB[:, dc, 512 * ts:512 * (ts + 1)],
                         start=(dc == 0), stop=False)
            ins = t.matmul(ps[0:64, 512 * ts:512 * (ts + 1)], sb["TB"][0:1, 0:64],
                           ones_row[:, 0:512], start=False, stop=True)
        return ins

    def gh_evac(e, ps):
        return e.tensor_copy(sb["SM2"][0:64, 0:T], ps[0:64, :])

    run_pipe(C, [{"mm": gh_mm, "evac": gh_evac, "eng": "vector",
                  "wait": [mark1, ("qgr", rows0)]}], [PA])
    C.wb_last_use[bi1] = C.cnt["pe"]
    gh_end = C.cnt["pe"]
    gh_mark = C.cnt["evd"]

    ap2, mark2, bi2 = wslab_load(C, C.dr["L0W_g2"], 0, 512)
    ap2b, mark2b, bi2b = wslab_load(C, C.dr["L0W_g2"], 512, 512)
    groups = []
    for oc in range(8):
        ap = ap2 if oc < 4 else ap2b
        j = oc % 4
        mk = mark2 if oc < 4 else mark2b

        def mm(t, ps, ap=ap, j=j, oc=oc):
            for ts in range(NTS):
                t.matmul(ps[:, 512 * ts:512 * (ts + 1)],
                         ap[0:64, 0, 128 * j:128 * (j + 1)],
                         sb["SM2"][0:64, 512 * ts:512 * (ts + 1)],
                         start=True, stop=False)
                ins = t.matmul(ps[:, 512 * ts:512 * (ts + 1)],
                               BR[0:1, 128 * oc:128 * (oc + 1)],
                               ones_row[:, 0:512], start=False, stop=True)
            return ins

        def evac(e, ps, oc=oc):
            return e.activation(out=gate[:, oc, :], in_=ps[:, :], func=AF.Sigmoid)

        groups.append({"mm": mm, "evac": evac, "eng": "scalar",
                       "wait": [mk, ("qgr", rowsC), ("evd", gh_mark)]})
    run_pipe(C, groups, [PA, PB])
    C.wb_last_use[bi2] = C.wb_last_use[bi2b] = C.cnt["pe"]

    gates_end = C.cnt["pe"]
    load_rows_bf(C, C.dr["L0_ln_b"], 0, D, BR[0:1, 2048:3072],
                 wait_pe=gates_end)
    rowsD = load_rows_bf(C, C.dr["L0b_out"], 0, D, BR[0:1, 0:1024],
                         wait_pe=gates_end)

    @C.G
    def _(g):
        g.wait_ge(C.sem("cc"), cc_mark)
        ins = g.dma_start(out=sb["SM"][:, 0:1024], in_=C.dr["cc0_out"][0:128, :])
        C.inc(ins, "qgc", 16)
        ins = g.dma_start(out=SM2F[0:1, 0:1024],
                          in_=C.dr["cc0_out"][128:129, :])
        C.inc(ins, "qgc", 16)
        ins = g.dma_start(out=HBF[0:1, 0:1024],
                          in_=C.dr["cc0_out"][129:130, :])
        C.inc(ins, "qgc", 16)
    cc_done = C.val("qgc")

    @C.T
    def _(t):
        psum_guard(C, t)
        t.wait_ge(C.sem("qgc"), cc_done)
        for dc in range(DC):
            for b in range(NB):
                srow = SM2F if b == 0 else HBF
                ins = t.matmul(PA[:, (2 * dc + b):(2 * dc + b) + 1],
                               srow[0:1, 128 * dc:128 * (dc + 1)],
                               sb["IDN"][0:1, 0:1], start=True, stop=True)
        ins.then_inc(C.sem("pe"), 1)
    C.cnt["pe"] += 1
    tr_mark = C.val("pe")

    @C.V
    def _(v):
        v.wait_ge(C.sem("pe"), tr_mark)
        ins = v.reciprocal(sb["STAT"][:, 0:16], PA[:, 0:16])
        ins.then_inc(C.sem("evd"), 1)
    C.cnt["evd"] += 1
    recip_mark = C.val("evd")

    @C.V
    def _(v):
        v.wait_ge(C.sem("evd"), recip_mark)
        for dc in range(DC):
            for b in range(NB):
                ins = v.tensor_tensor(
                    qs[:, dc, 1024 * b:1024 * (b + 1)],
                    qs[:, dc, 1024 * b:1024 * (b + 1)],
                    sb["STAT"][:, 2 * dc + b:2 * dc + b + 1].broadcast_to([128, 1024]),
                    OP.mult)
        ins.then_inc(C.sem("dve"), 1)
    C.cnt["dve"] += 1
    qscale_mark = C.val("dve")

    @C.V
    def _(v):
        v.memset(sb["SM2"][:, :], 0.0)
        for b in range(NB):
            for hp in range(8):
                c0 = 64 * (8 * b + hp)
                d0 = 128 * (8 * b + hp)
                v.tensor_copy(sb["SM2"][0:64, d0:d0 + 64],
                              sb["SM"][0:64, c0:c0 + 64])
                ins = v.tensor_copy(sb["SM2"][64:128, d0 + 64:d0 + 128],
                                    sb["SM"][64:128, c0:c0 + 64])
        ins.then_inc(C.sem("dve"), 1)
    C.cnt["dve"] += 1
    kvd_mark = C.val("dve")

    groups = []
    for b in range(NB):
        for hp in range(8):
            d0 = 128 * (8 * b + hp)

            def mm(t, ps, b=b, hp=hp, d0=d0):
                for ts in range(2):
                    ins = t.matmul(ps[:, 512 * ts:512 * (ts + 1)],
                                   sb["SM2"][:, d0:d0 + 128],
                                   qs[:, hp, 1024 * b + 512 * ts:1024 * b + 512 * (ts + 1)],
                                   start=True, stop=True)
                return ins

            def evac(e, ps, b=b, hp=hp):
                return e.tensor_copy(XB[:, hp, 1024 * b:1024 * (b + 1)],
                                     ps[:, 0:1024])

            groups.append({"mm": mm, "evac": evac, "eng": "vector",
                           "wait": [("dve", max(qscale_mark, kvd_mark))]})
    run_pipe(C, groups, [PA[:, 0:1024], PB[:, 0:1024]])

    def apply(v, dc, a_ps, b_ps):
        v.tensor_tensor(sb["HB"][:, :], XB[:, dc, :], a_ps[:, :], OP.mult)
        v.tensor_tensor(sb["HB"][:, :], sb["HB"][:, :], b_ps[:, :], OP.add)
        return v.tensor_tensor(XB[:, dc, :], sb["HB"][:, :], gate[:, dc, :], OP.mult)

    ln_fm(C, lambda dc: XB[:, dc, :], apply,
          gain_row=BR[0:1, 1024:2048], bias_row=BR[0:1, 2048:3072],
          rows_qgr=rowsD)

    def oevac(e, oc, ps):
        return e.tensor_tensor(W[:, oc, :], ps[:, :], S[:, oc, :], OP.add)

    fm_proj(C, C.dr["L0W_out"], lambda dc: XB[:, dc, :], D, oevac,
            evac_eng="vector", bias_row=BR[0:1, 0:1024], bias_mark=rowsD)

    postln_from_W(C)


def affine_ln_W_to_XB(C, gain_row, bias_row, rows_qgr=None):
    sb = C.sb
    W, XB = sb["W"], sb["XB"]

    @C.V
    def _(v):
        for dc in range(DC):
            ins = v.tensor_copy(XB[:, dc, :], W[:, dc, :])
            ins.then_inc(C.sem("dve"), 1)
    C.cnt["dve"] += DC

    def apply(v, dc, a_ps, b_ps):
        v.tensor_tensor(sb["HB"][:, :], XB[:, dc, :], a_ps[:, :], OP.mult)
        return v.tensor_tensor(XB[:, dc, :], sb["HB"][:, :], b_ps[:, :], OP.add)

    ln_fm(C, lambda dc: XB[:, dc, :], apply, gain_row=gain_row,
          bias_row=bias_row, rows_qgr=rows_qgr)


def postln_from_S(C):
    sb = C.sb
    S, XB = sb["S"], sb["XB"]

    @C.V
    def _(v):
        for dc in range(DC):
            ins = v.tensor_copy(XB[:, dc, :], S[:, dc, :])
            ins.then_inc(C.sem("dve"), 1)
    C.cnt["dve"] += DC

    def apply(v, dc, a_ps, b_ps):
        v.tensor_tensor(S[:, dc, :], S[:, dc, :], a_ps[:, :], OP.mult)
        v.tensor_tensor(S[:, dc, :], S[:, dc, :], b_ps[:, :], OP.add)
        return v.tensor_copy(XB[:, dc, :], S[:, dc, :])

    ln_fm(C, lambda dc: XB[:, dc, :], apply)


def emit_l0_ffn(C):
    block_barrier(C)
    sb = C.sb
    S, W, XB = sb["S"], sb["W"], sb["XB"]
    PA, PB = C.ps["PA"], C.ps["PB"]
    ones_row = sb["ONER"][0:1, :]
    BR = sb["BIASR"]
    SM = sb["SM"]
    SM2 = sb["SM2"]
    HB = sb["HB"]

    Wflat = W[:, :, :].rearrange("p c t -> p (c t)").bitcast(BF16)
    qslot = Wflat[:, 0:16384].rearrange("p (a b) -> p a b", b=2048)
    kvslot = Wflat[:, 16384:24576].rearrange("p (a b) -> p a b", b=2048)
    hslot = Wflat[:, :].rearrange("p (a b) -> p a b", b=1024)

    m1 = load_rows_bf(C, C.dr["L0f_ln1g"], 0, D, BR[0:1, 0:1024])
    m2 = load_rows_bf(C, C.dr["L0f_ln1b"], 0, D, BR[0:1, 1024:2048])

    BONES = sb["BONES"]
    BONEST = sb["BONEST"]

    @C.V
    def _(v):
        v.memset(BONES[:, :], 0.0)
        for c in range(8):
            v.memset(BONES[0:64, 16 * c + 2 * c:16 * c + 2 * c + 1], 1.0)
            ins = v.memset(BONES[64:128, 16 * c + 2 * c + 1:16 * c + 2 * c + 2], 1.0)
        ins.then_inc(C.sem("dve"), 1)
    C.cnt["dve"] += 1

    SMI = SM[:, :].bitcast(mybir.dt.int16)

    @C.G
    def _(g):
        ins = g.iota(SMI[0:16, 0:1024].rearrange("p (c h x) -> p c h x", h=2, x=64),
                     [[2, 8], [1, 2], [0, 64]], base=0, channel_multiplier=0,
                     allow_small_or_imprecise_dtypes=True)
        C.inc(ins, "pool", 1)
    io_mark = C.val("pool")

    @C.V
    def _(v):
        v.wait_ge(C.sem("pool"), io_mark)
        ins = v.tensor_scalar(BONEST[0:16, :], SMI[0:16, 0:1024],
                              sb["PIDX"][0:16, 0:1], None, OP.is_equal)
        ins.then_inc(C.sem("dve"), 1)
    C.cnt["dve"] += 1

    def apply_ln1(v, dc, a_ps, b_ps):
        v.tensor_tensor(sb["HB"][:, :], XB[:, dc, :], a_ps[:, :], OP.mult)
        return v.tensor_tensor(XB[:, dc, :], sb["HB"][:, :], b_ps[:, :], OP.add)

    ln_fm(C, lambda dc: XB[:, dc, :], apply_ln1,
          gain_row=BR[0:1, 0:1024], bias_row=BR[0:1, 1024:2048], rows_qgr=m2)
    ln1_end = C.cnt["pe"]

    tap(C, "ln1", lambda: [(C.dr["xtap"][0:D, :].rearrange("(c p) t -> p c t", p=128),
                            sb["XB"][:, :, :])])
    if C.tapped:
        return

    def qkv_evac(e, oc, ps):
        if oc < 8:
            return e.tensor_copy(qslot[:, oc, :], ps[:, :])
        return e.tensor_copy(kvslot[:, oc - 8, :], ps[:, :])

    fm_proj(C, C.dr["L0fW_qkv"], lambda dc: XB[:, dc, :], 1536, qkv_evac,
            evac_eng="vector")

    tap(C, "qkv", lambda: [(C.dr["xtap"][0:D, :].rearrange("(c p) t -> p c t", p=128),
                            Wflat[:, 0:16384].rearrange("p (c t) -> p c t", t=2048)),
                           (C.dr["xtap"][D:D + 512, :].rearrange("(c p) t -> p c t", p=128),
                            Wflat[:, 16384:24576].rearrange("p (c t) -> p c t", t=2048))])
    if C.tapped:
        return

    pe0 = C.val("pe")
    dve0 = C.val("dve")

    @C.V
    def _(v):
        for c in range(8):
            g = c // 2
            kb = 64 * (g % 2)
            kc = g // 2
            if c > 0:
                v.wait_ge(C.sem("pe"), pe0 + c)
            v.tensor_copy(SM2[0:64, 0:1024], kvslot[kb:kb + 64, kc, 1024:2048])
            v.tensor_copy(SM2[64:128, 0:1024], kvslot[kb:kb + 64, kc, 1024:2048])
            v.tensor_tensor(SM2[:, 1024:2048], qslot[:, c, 1024:2048],
                            SM2[:, 0:1024], OP.mult)
            v.tensor_copy(HB[0:64, 0:1024], kvslot[kb:kb + 64, kc, 0:1024])
            v.tensor_copy(HB[64:128, 0:1024], kvslot[kb:kb + 64, kc, 0:1024])
            ins = v.tensor_tensor(HB[:, 1024:2048], qslot[:, c, 1024:2048],
                                  HB[:, 0:1024], OP.mult)
            ins.then_inc(C.sem("dve"), 1)
    C.cnt["dve"] += 8

    @C.T
    def _(t):
        psum_guard(C, t)
        for c in range(8):
            t.wait_ge(C.sem("dve"), dve0 + c + 1)
            for ts in range(2):
                t.matmul(PB[0:16, 512 * ts:512 * (ts + 1)],
                         BONES[:, 16 * c:16 * (c + 1)],
                         SM2[:, 1024 + 512 * ts:1024 + 512 * (ts + 1)],
                         start=(c == 0), stop=(c == 7), skip_group_check=True)
                ins = t.matmul(PB[32:48, 512 * ts:512 * (ts + 1)],
                               BONES[:, 16 * c:16 * (c + 1)],
                               HB[:, 1024 + 512 * ts:1024 + 512 * (ts + 1)],
                               start=(c == 0), stop=(c == 7), skip_group_check=True)
            ins.then_inc(C.sem("pe"), 1)
    C.cnt["pe"] += 8
    sc_mark = C.cnt["pe"]

    @C.V
    def _(v):
        v.wait_ge(C.sem("pe"), sc_mark)
        ins = v.tensor_copy(SM[0:16, 0:1024], PB[0:16, 0:1024])
        ins.then_inc(C.sem("dve"), 1)
    C.cnt["dve"] += 1

    @C.V
    def _(v):
        ins = v.tensor_tensor(SM[32:48, 0:1024], PB[32:48, 0:1024],
                              SM[0:16, 0:1024], OP.subtract)
        ins.then_inc(C.sem("dve"), 1)
    C.cnt["dve"] += 1
    d_mark = C.cnt["dve"]

    @C.A
    def _(a):
        a.wait_ge(C.sem("dve"), d_mark)
        ins = a.activation(out=SM2[0:16, 0:1024], in_=SM[32:48, 0:1024],
                           func=AF.Sigmoid, scale=0.125)
        ins.then_inc(C.sem("act"), 1)
    C.cnt["act"] += 1
    p_mark = C.cnt["act"]

    groups = []
    for c in range(8):
        g = c // 2
        vb = 64 * (g % 2)
        vc = 2 + g // 2

        def mm(t, ps, c=c):
            for ts in range(2):
                ins = t.matmul(ps[:, 512 * ts:512 * (ts + 1)],
                               BONEST[0:16, 128 * c:128 * (c + 1)],
                               SM2[0:16, 512 * ts:512 * (ts + 1)],
                               start=True, stop=True)
            return ins

        def evac(e, ps, c=c, vb=vb, vc=vc):
            e.tensor_copy(HB[0:64, 0:1024], kvslot[vb:vb + 64, vc, 0:1024])
            e.tensor_copy(HB[64:128, 0:1024], kvslot[vb:vb + 64, vc, 0:1024])
            e.tensor_copy(HB[0:64, 1024:2048], kvslot[vb:vb + 64, vc, 1024:2048])
            e.tensor_copy(HB[64:128, 1024:2048], kvslot[vb:vb + 64, vc, 1024:2048])
            e.tensor_copy(qslot[:, c, 0:1024], HB[:, 0:1024])
            e.tensor_tensor(SM[:, 0:1024], HB[:, 0:1024], HB[:, 1024:2048],
                            OP.subtract)
            e.scalar_tensor_tensor(SM[:, 0:1024], ps[:, 0:1024], 1.0,
                                   SM[:, 0:1024], OP.mult, OP.mult)
            return e.tensor_tensor(qslot[:, c, 1024:2048], SM[:, 0:1024],
                                   HB[:, 1024:2048], OP.add)

        groups.append({"mm": mm, "evac": evac, "eng": "vector",
                       "wait": [("act", p_mark)]})
    run_pipe(C, groups, [PA[:, 0:1024], PB[:, 0:1024]])

    tap(C, "attn", lambda: [(C.dr["xtap"][0:D, :].rearrange("(c p) t -> p c t", p=128),
                             Wflat[:, 0:16384].rearrange("p (c t) -> p c t", t=2048)),
                            (C.dr["xtap"][D:D + 16, 0:1024],
                             sb["SM2"][0:16, 0:1024])])
    if C.tapped:
        return

    def pevac(e, oc, ps):
        e.tensor_tensor(XB[:, oc, :], ps[:, :], S[:, oc, :], OP.add)
        return e.tensor_tensor(S[:, oc, :], S[:, oc, :], XB[:, oc, :], OP.add)

    fm_proj(C, C.dr["L0fW_proj"], lambda dc: qslot[:, dc, :], D, pevac,
            evac_eng="vector")
    proj_end = C.cnt["pe"]

    m3 = load_rows_bf(C, C.dr["L0f_ln2g"], 0, D, BR[0:1, 0:1024],
                      wait_pe=ln1_end)
    m4 = load_rows_bf(C, C.dr["L0f_ln2b"], 0, D, BR[0:1, 1024:2048],
                      wait_pe=ln1_end)
    def apply_ln2(v, dc, a_ps, b_ps):
        v.tensor_tensor(sb["HB"][:, :], XB[:, dc, :], a_ps[:, :], OP.mult)
        return v.tensor_tensor(XB[:, dc, :], sb["HB"][:, :], b_ps[:, :], OP.add)

    ln_fm(C, lambda dc: XB[:, dc, :], apply_ln2,
          gain_row=BR[0:1, 0:1024], bias_row=BR[0:1, 1024:2048], rows_qgr=m4)

    tap(C, "ln2", lambda: [(C.dr["xtap"][0:D, :].rearrange("(c p) t -> p c t", p=128),
                            sb["XB"][:, :, :]),
                           (C.dr["xtap"][D:2 * D, :].rearrange("(c p) t -> p c t", p=128),
                            sb["S"][:, :, :])])
    if C.tapped:
        return

    apr, mr, bir = wslab_load(C, C.dr["L0fW_rt"], 0, 8)
    pe0r = C.val("pe")

    @C.T
    def _(t):
        psum_guard(C, t)
        t.wait_ge(C.sem(mr[0]), mr[1])
        for tc in range(TC):
            for dc in range(DC):
                ins = t.matmul(PA[:, 8 * tc:8 * (tc + 1)],
                               XB[:, dc, 128 * tc:128 * (tc + 1)],
                               apr[:, dc, :], start=(dc == 0), stop=(dc == DC - 1),
                               skip_group_check=True)
        ins.then_inc(C.sem("pe"), 1)
    C.cnt["pe"] += 1
    rt_mark = C.cnt["pe"]
    C.wb_last_use[bir] = C.cnt["pe"]

    RT = SM[:, 0:128].rearrange("p (a b) -> p a b", b=8)
    RX = SM[:, 128:256].rearrange("p (a b) -> p a b", b=8)
    RS = SM[:, 256:272].rearrange("p (a b) -> p a b", b=1)
    RS2 = SM[:, 272:288].rearrange("p (a b) -> p a b", b=1)
    dve_r0 = C.val("dve")

    @C.V
    def _(v):
        v.wait_ge(C.sem("pe"), rt_mark)
        v.tensor_copy(SM[:, 0:128], PA[:, 0:128])
        ins = v.reduce_max(RS[:, :, :], RT[:, :, :], axis=AX.X)
        ins.then_inc(C.sem("dve"), 1)
        v.wait_ge(C.sem("dve"), C.cnt["dve"] + 1)
        ins = v.tensor_tensor(RX[:, :, :], RT[:, :, :],
                              RS[:, :, :].broadcast_to([128, 16, 8]), OP.subtract)
        ins.then_inc(C.sem("dve"), 1)
    C.cnt["dve"] += 2
    r1_mark = C.cnt["dve"]

    tap(C, "logits", lambda: [(C.dr["xtap"][0:128, 0:512], sb["SM"][:, 0:512])])

    @C.A
    def _(a):
        a.wait_ge(C.sem("dve"), r1_mark)
        ins = a.activation(out=SM[:, 128:256], in_=SM[:, 128:256], func=AF.Exp)
        ins.then_inc(C.sem("act"), 1)
    C.cnt["act"] += 1
    r2_mark = C.cnt["act"]

    @C.V
    def _(v):
        def edge(ins):
            ins.then_inc(C.sem("dve"), 1)
            C.cnt["dve"] += 1
            v.wait_ge(C.sem("dve"), C.cnt["dve"])

        v.wait_ge(C.sem("act"), r2_mark)
        edge(v.reduce_sum(RS[:, :, :], RX[:, :, :], axis=AX.X))
        edge(v.reciprocal(SM[:, 272:288], SM[:, 256:272]))
        v.tensor_tensor(RX[:, :, :], RX[:, :, :],
                        RS2[:, :, :].broadcast_to([128, 16, 8]), OP.mult)
        edge(v.reduce_max(RS[:, :, :], RX[:, :, :], axis=AX.X))
        v.tensor_tensor(RT[:, :, :], RX[:, :, :],
                        RS[:, :, :].broadcast_to([128, 16, 8]), OP.is_ge)
        v.scalar_tensor_tensor(RT[:, :, :], RT[:, :, :], -1e30, RX[:, :, :],
                               OP.mult, OP.add)
        edge(v.reduce_max(RS2[:, :, :], RT[:, :, :], axis=AX.X))
        v.tensor_tensor(RT[:, :, :], RX[:, :, :],
                        RS2[:, :, :].broadcast_to([128, 16, 8]), OP.is_ge)
        v.tensor_tensor(RT[:, :, :], RT[:, :, :], RX[:, :, :], OP.mult)
        edge(v.tensor_tensor(RS[:, :, 0:1], RS[:, :, 0:1], RS2[:, :, 0:1], OP.add))
        edge(v.reciprocal(SM[:, 272:288], SM[:, 256:272]))
        v.tensor_tensor(RT[:, :, :], RT[:, :, :],
                        RS2[:, :, :].broadcast_to([128, 16, 8]), OP.mult)
        ins = v.tensor_copy(SM[:, 288:416], SM[:, 0:128])
        ins.then_inc(C.sem("dve"), 1)
    C.cnt["dve"] += 1
    rw_mark = C.cnt["dve"]

    tap(C, "rmath", lambda: [(C.dr["xtap"][0:128, 0:512], sb["SM"][:, 0:512])])
    if C.tapped:
        return

    @C.T
    def _(t):
        psum_guard(C, t)
        t.wait_ge(C.sem("dve"), rw_mark)
        for tc in range(TC):
            ins = t.transpose(PB[0:8, 128 * tc:128 * (tc + 1)],
                              SM[:, 288 + 8 * tc:288 + 8 * (tc + 1)],
                              sb["IDN"][:, :])
        ins.then_inc(C.sem("pe"), 1)
    C.cnt["pe"] += 1
    wt_mark = C.cnt["pe"]

    WEB = SM[:, :].bitcast(BF16)

    @C.V
    def _(v):
        v.wait_ge(C.sem("pe"), wt_mark)
        ins = v.tensor_copy(WEB[0:8, 0:2048], PB[0:8, 0:2048])
        ins.then_inc(C.sem("dve"), 1)
    C.cnt["dve"] += 1
    we_mark = C.cnt["dve"]

    HBI = sb["HB"][:, :].bitcast(mybir.dt.int16)
    dve_hb = C.val("evd")

    @C.G
    def _(g):
        g.wait_ge(C.sem("dve"), we_mark)
        ins = g.iota(HBI[0:8, 0:1024], [[1, 8], [0, 128]], base=0,
                     channel_multiplier=0, allow_small_or_imprecise_dtypes=True)
        C.inc(ins, "pool", 1)
    sel_mark = C.val("pool")

    @C.V
    def _(v):
        v.wait_ge(C.sem("pool"), sel_mark)
        ins = v.tensor_scalar(sb["BONEST"][0:8, 0:1024], HBI[0:8, 0:1024],
                              sb["PIDX"][0:8, 0:1], None, OP.is_equal)
        ins.then_inc(C.sem("dve"), 1)
    C.cnt["dve"] += 1
    selv_mark = C.cnt["dve"]

    tap(C, "router", lambda: [(C.dr["xtap"][0:8, :], WEB[0:8, 0:2048]),
                              (C.dr["xtap"][32:160, 0:128], sb["SM"][:, 288:416])])
    if C.tapped:
        return

    for tt in range(2):
        tcol = 1024 * tt
        for e in range(8):
            pe_web = C.val("pe")

            @C.T
            def _(t, e=e):
                psum_guard(C, t)
                t.wait_ge(C.sem("dve"), max(we_mark, selv_mark))
                for ts in range(NTS):
                    ins = t.matmul(PA[:, 512 * ts:512 * (ts + 1)],
                                   sb["BONEST"][0:8, 128 * e:128 * (e + 1)],
                                   WEB[0:8, 512 * ts:512 * (ts + 1)],
                                   start=True, stop=True)
                ins.then_inc(C.sem("pe"), 1)
            C.cnt["pe"] += 1
            web_mark = C.cnt["pe"]

            @C.V
            def _(v):
                v.wait_ge(C.sem("pe"), web_mark)
                ins = v.tensor_copy(SM2[:, :], PA[:, :])
                ins.then_inc(C.sem("evd"), 1)
            C.cnt["evd"] += 1

            w1e = C.dr["L0f_w1"][e]
            groups = []
            pe_entry = C.val("pe")
            for sgi in range(8):
                og = 512 * sgi
                wp = pe_entry + 4 * (sgi - 1) if sgi >= 2 else None
                ap, mark, bi = wslab_load(C, w1e, og, 512, wait_pe=wp)
                for j in range(4):
                    fc = 4 * sgi + j

                    def mm(t, ps, ap=ap, j=j):
                        for ts in range(2):
                            for dc in range(DC):
                                ins = t.matmul(
                                    ps[:, 512 * ts:512 * (ts + 1)],
                                    ap[:, dc, 128 * j:128 * (j + 1)],
                                    XB[:, dc, tcol + 512 * ts:tcol + 512 * (ts + 1)],
                                    start=(dc == 0), stop=(dc == DC - 1))
                        return ins

                    def evac(e_, ps, fc=fc):
                        return e_.activation(out=hslot[:, fc, :], in_=ps[:, 0:1024],
                                             func=AF.Gelu)

                    groups.append({"mm": mm, "evac": evac, "eng": "scalar",
                                   "wait": [mark]})
            run_pipe(C, groups, [PA[:, 0:1024], PB[:, 0:1024]])
            C.wb_last_use[0] = C.wb_last_use[1] = C.cnt["pe"]

            w2e = C.dr["L0f_w2"][e]
            groups = []
            pe_entry = C.val("pe")
            for hc in range(8):
                wp = pe_entry + (hc - 1) if hc >= 2 else None
                ap, mark, bi = wslab_load(C, w2e, 128 * hc, 128, wait_pe=wp)

                def mm(t, ps, ap=ap):
                    for ts in range(2):
                        for fc in range(32):
                            ins = t.matmul(ps[:, 512 * ts:512 * (ts + 1)],
                                           ap[:, fc, :],
                                           hslot[:, fc, 512 * ts:512 * (ts + 1)],
                                           start=(fc == 0), stop=(fc == 31))
                    return ins

                def evac(e_, ps, hc=hc):
                    e_.tensor_tensor(sb["HB"][:, 0:1024], ps[:, 0:1024],
                                     SM2[:, tcol:tcol + 1024], OP.mult)
                    return e_.tensor_tensor(S[:, hc, tcol:tcol + 1024],
                                            S[:, hc, tcol:tcol + 1024],
                                            sb["HB"][:, 0:1024], OP.add)

                groups.append({"mm": mm, "evac": evac, "eng": "vector",
                               "wait": [mark]})
            run_pipe(C, groups, [PA[:, 0:1024], PB[:, 0:1024]])
            C.wb_last_use[0] = C.wb_last_use[1] = C.cnt["pe"]

    postln_from_S(C)


def emit_l1_attn(C):
    block_barrier(C)
    sb = C.sb
    S, W, XB = sb["S"], sb["W"], sb["XB"]
    PA, PB = C.ps["PA"], C.ps["PB"]
    ones_row = sb["ONER"][0:1, :]
    ones_col = sb["ONE"][:, 0:1]
    BR = sb["BIASR"]
    SM = sb["SM"]
    SM2 = sb["SM2"]
    HB = sb["HB"]
    HBI = HB[:, :].bitcast(mybir.dt.int16)
    GAM = sb["STAT"][:, 0:1]
    LNG = sb["STAT"][:, 1:2]

    Wflat = W[:, :, :].rearrange("p c t -> p (c t)").bitcast(BF16)
    Qs = Wflat[:, 0:8192].rearrange("p (a b) -> p a b", b=1024)
    Ks = Wflat[:, 8192:16384].rearrange("p (a b) -> p a b", b=1024)
    KT = Wflat[:, 16384:24576].rearrange("p (a b) -> p a b", b=1024)
    VT = Wflat[:, 24576:32768].rearrange("p (a b) -> p a b", b=1024)

    load_rows_bf(C, C.dr["L1b_q"], 0, D, BR[0:1, 0:1024], wait_pe=C.val("pe"))
    load_rows_bf(C, C.dr["L1b_k"], 0, D, BR[0:1, 1024:2048])
    bm = load_rows_bf(C, C.dr["L1b_v"], 0, D, BR[0:1, 2048:3072])

    @C.G
    def _(g):
        ins = g.dma_start(out=GAM, in_=C.dr["L1_gamma"])
        C.inc(ins, "qg", 16)
        ins = g.dma_start(out=SM[0:1, 8:16], in_=C.dr["L1_dexp"])
        C.inc(ins, "qg", 16)
        ins = g.dma_start(out=SM[0:1, 16:24], in_=C.dr["L1_mask"])
        C.inc(ins, "qg", 16)
    gm = C.val("qg")

    @C.A
    def _(a):
        a.wait_ge(C.sem("qg"), gm)
        ins = a.activation(out=LNG, in_=GAM, func=AF.Ln)
        ins.then_inc(C.sem("act"), 1)
    C.cnt["act"] += 1
    lng_mark = C.cnt["act"]

    @C.G
    def _(g):
        g.wait_ge(C.sem("act"), lng_mark)
        ins = g.iota(HBI[:, 8:16], [[-128, 8]], base=L - 1,
                     channel_multiplier=-1, allow_small_or_imprecise_dtypes=True)
        C.inc(ins, "pool", 1)
    io_m = C.val("pool")

    @C.V
    def _(v):
        v.wait_ge(C.sem("pool"), io_m)
        v.tensor_scalar(sb["STAT"][:, 2:10], HBI[:, 8:16], LNG, None, OP.mult)
        ins = v.tensor_scalar(SM[0:1, 0:8], SM[0:1, 8:16], LNG[0:1, 0:1],
                              None, OP.mult)
        ins.then_inc(C.sem("dve"), 1)
    C.cnt["dve"] += 1
    dv_m = C.cnt["dve"]

    @C.A
    def _(a):
        a.wait_ge(C.sem("dve"), dv_m)
        a.activation(out=sb["STAT"][:, 2:10], in_=sb["STAT"][:, 2:10], func=AF.Exp)
        ins = a.activation(out=SM[0:1, 0:8], in_=SM[0:1, 0:8], func=AF.Exp)
        ins.then_inc(C.sem("act"), 1)
    C.cnt["act"] += 1
    dv_m2 = C.cnt["act"]

    @C.V
    def _(v):
        v.wait_ge(C.sem("act"), dv_m2)
        ins = v.tensor_tensor(SM[0:1, 0:8], SM[0:1, 0:8], SM[0:1, 16:24], OP.mult)
        ins.then_inc(C.sem("dve"), 1)
    C.cnt["dve"] += 1
    coef_m = C.cnt["dve"]

    @C.T
    def _(t):
        psum_guard(C, t)
        t.wait_ge(C.sem("dve"), coef_m)
        for k in range(NCORES):
            ins = t.matmul(PB[:, k:k + 1], sb["ONERF"][0:1, 0:128],
                           SM[0:1, k:k + 1], start=True, stop=True)
        ins.then_inc(C.sem("pe"), 1)
    C.cnt["pe"] += 1
    wk_m = C.cnt["pe"]

    @C.V
    def _(v):
        v.wait_ge(C.sem("pe"), wk_m)
        for k in range(NCORES):
            ins = v.scalar_tensor_tensor(SM2[:, 128 * k:128 * (k + 1)],
                                         sb["IDN"][:, :], 1.0,
                                         PB[:, k:k + 1].broadcast_to([128, 128]),
                                         OP.mult, OP.mult)
        ins.then_inc(C.sem("dve"), 1)
    C.cnt["dve"] += 1
    idist_m = C.cnt["dve"]

    ag_mark_prev = None
    for b in range(NB):
        bc = 1024 * b

        def ev_q(e, oc, ps):
            return e.activation(out=Qs[:, oc, :], in_=ps[:, :], func=AF.Relu)

        fm_proj(C, C.dr["L1W_q"], lambda dc: XB[:, dc, :], D, ev_q,
                evac_eng="scalar", bias_row=BR[0:1, 0:1024], tokens=(bc, bc + 1024),
                bias_mark=bm)

        def ev_k(e, oc, ps):
            return e.activation(out=Ks[:, oc, :], in_=ps[:, :], func=AF.Relu)

        fm_proj(C, C.dr["L1W_k"], lambda dc: XB[:, dc, :], D, ev_k,
                evac_eng="scalar", bias_row=BR[0:1, 1024:2048], tokens=(bc, bc + 1024),
                bias_mark=bm)

        groups = []
        for s in range(2):
            w_hbm = C.dr["L1W_k"] if s == 0 else C.dr["L1W_v"]
            brow = BR[0:1, 1024:2048] if s == 0 else BR[0:1, 2048:3072]
            dstsl = KT if s == 0 else VT
            for half in range(2):
                ap, mark, bi = wslab_load(C, w_hbm, 512 * half, 512)
                for tc in range(8):
                    def mm(t, ps, ap=ap, tc=tc, brow=brow, half=half):
                        for dc in range(DC):
                            t.matmul(ps[:, 0:512],
                                     XB[:, dc, bc + 128 * tc:bc + 128 * (tc + 1)],
                                     ap[:, dc, :], start=(dc == 0), stop=False)
                        return t.matmul(ps[:, 0:512], ones_row[:, 0:128],
                                        brow[:, 512 * half:512 * (half + 1)],
                                        start=False, stop=True)

                    def evac(e, ps, s=s, tc=tc, half=half, dstsl=dstsl):
                        if s == 0:
                            return e.activation(
                                out=dstsl[:, tc, 512 * half:512 * (half + 1)],
                                in_=ps[:, 0:512], func=AF.Relu)
                        return e.tensor_copy(
                            dstsl[:, tc, 512 * half:512 * (half + 1)], ps[:, 0:512])

                    groups.append({"mm": mm, "evac": evac,
                                   "eng": "scalar" if s == 0 else "vector",
                                   "wait": [mark, ("qgr", bm)]})
        bufs = [PA[:, 0:512], PA[:, 512:1024], PB[:, 0:512], PB[:, 512:1024]]
        run_pipe(C, groups, bufs)
        C.wb_last_use[0] = C.wb_last_use[1] = C.cnt["pe"]

        tap(C, "l1_proj", lambda: [(C.dr["xtap"][0:128, 0:1024], sb["SM"][:, 0:1024])])
        if C.tapped:
            return

        act_now = C.val("act")

        @C.V
        def _(v):
            v.wait_ge(C.sem("act"), act_now)
            for tc in range(8):
                ins = v.tensor_scalar(KT[:, tc, :], KT[:, tc, :],
                                      sb["STAT"][:, 2 + tc:3 + tc], None, OP.mult)
            ins.then_inc(C.sem("dve"), 1)
        C.cnt["dve"] += 1
        ktsc_m = C.cnt["dve"]

        pe_m0 = C.val("pe")
        evd_m0 = C.val("evd")
        qg_m0 = C.val("qg")

        @C.T
        def _(t):
            psum_guard(C, t)
            t.wait_ge(C.sem("dve"), ktsc_m)
            for dc in range(DC):
                if dc >= 2:
                    t.wait_ge(C.sem("evd"), evd_m0 + dc - 1)
                ps = [PA, PB][dc % 2]
                for jc in range(8):
                    for half in range(2):
                        t.matmul(ps[:, 512 * half:512 * (half + 1)],
                                 KT[:, jc, 128 * dc:128 * (dc + 1)],
                                 VT[:, jc, 512 * half:512 * (half + 1)],
                                 start=(jc == 0), stop=(jc == 7),
                                 skip_group_check=True)
                    ins = t.matmul(ps[:, 1024:1025],
                                   KT[:, jc, 128 * dc:128 * (dc + 1)],
                                   ones_col, start=(jc == 0), stop=(jc == 7),
                                   skip_group_check=True)
                ins.then_inc(C.sem("pe"), 1)
        C.cnt["pe"] += DC

        @C.V
        def _(v):
            for dc in range(DC):
                v.wait_ge(C.sem("pe"), pe_m0 + dc + 1)
                v.wait_ge(C.sem("qg"), qg_m0 + 16 * dc)
                ins = v.tensor_copy(HB[:, 0:1025], [PA, PB][dc % 2][:, 0:1025])
                ins.then_inc(C.sem("evd"), 1)
        C.cnt["evd"] += DC

        @C.G
        def _(g, b=b, ag_mark_prev=ag_mark_prev):
            if b == 1 and ag_mark_prev is not None:
                g.wait_ge(C.sem("cc"), ag_mark_prev)
            for dc in range(DC):
                g.wait_ge(C.sem("evd"), evd_m0 + dc + 1)
                ins = g.dma_start(out=C.dr["cc1_in"][128 * dc:128 * (dc + 1), 0:1025],
                                  in_=HB[:, 0:1025])
                C.inc(ins, "qg", 16)

        tap(C, "l1_m", lambda: [(C.dr["xtap"][0:128, 0:1024], sb["SM"][:, 0:1024])])
        if C.tapped:
            return

        if not DEBUG_NOAG:
            @C.G
            def _(g):
                g.wait_ge(C.sem("qg"), C.val("qg"))
                ins = g.collective_compute("AllGather", OP.bypass,
                                           replica_groups=[list(range(NCORES))],
                                           ins=[C.dr["cc1_in"].opt()],
                                           outs=[C.dr["cc1_out"].opt()])
                C.inc(ins, "cc", 1)
        ag_mark = C.val("cc")
        ag_mark_prev = ag_mark

        pe_a0 = C.val("pe")
        dve_a0 = C.val("dve")
        act_a0 = C.val("act")
        pool_a0 = C.val("pool")
        for jc in range(DEBUG_TRUNC):
            AGOPS = DEBUG_AGOPS

            @C.G
            def _(g, jc=jc):
                if jc == 0:
                    g.wait_ge(C.sem("evd"), C.val("evd"))
                else:
                    g.wait_ge(C.sem("dve"), dve_a0 + 2 * jc - 1)
                ins = g.iota(HBI[:, 0:1024], [[1, 1024]], base=-128 * jc,
                             channel_multiplier=-1,
                             allow_small_or_imprecise_dtypes=True)
                C.inc(ins, "pool", 1)

            if AGOPS >= 2:
                @C.V
                def _(v, jc=jc):
                    v.wait_ge(C.sem("pool"), pool_a0 + 2 * jc + 1)
                    ins = v.tensor_scalar(SM[:, 0:1024], HBI[:, 0:1024], LNG, 0.0,
                                          OP.mult, OP.min)
                    ins.then_inc(C.sem("dve"), 1)

            if AGOPS >= 3:
                @C.A
                def _(a, jc=jc):
                    a.wait_ge(C.sem("dve"), dve_a0 + 2 * jc + 1)
                    ins = a.activation(out=SM[:, 0:1024], in_=SM[:, 0:1024],
                                       func=AF.Exp)
                    ins.then_inc(C.sem("act"), 1)

            if AGOPS >= 4:
                @C.G
                def _(g, jc=jc):
                    g.wait_ge(C.sem("act"), act_a0 + jc + 1)
                    ins = g.affine_select(SM[:, 0:1024], SM[:, 0:1024], [[1, 1024]],
                                          OP.is_ge, 0.0, base=-128 * jc,
                                          channel_multiplier=-1)
                    C.inc(ins, "pool", 1)

            if AGOPS >= 5:
                @C.T
                def _(t, jc=jc):
                    if jc == 0:
                        psum_guard(C, t)
                    t.wait_ge(C.sem("pool"), pool_a0 + 2 * (jc + 1))
                    if jc >= 2:
                        t.wait_ge(C.sem("dve"), dve_a0 + 2 * jc)
                    ps = [PA, PB][jc % 2]
                    for dc in range(DC):
                        for half in range(2):
                            ins = t.matmul(ps[:, 512 * half:512 * (half + 1)],
                                           Ks[:, dc, 128 * jc:128 * (jc + 1)],
                                           Qs[:, dc, 512 * half:512 * (half + 1)],
                                           start=(dc == 0), stop=(dc == DC - 1),
                                           skip_group_check=True)
                    ins.then_inc(C.sem("pe"), 1)

            if AGOPS >= 6:
                @C.V
                def _(v, jc=jc):
                    v.wait_ge(C.sem("pe"), pe_a0 + jc + 1)
                    ins = v.tensor_tensor(KT[:, jc, :], [PA, PB][jc % 2][:, 0:1024],
                                          SM[:, 0:1024], OP.mult)
                    ins.then_inc(C.sem("dve"), 1)
            if AGOPS >= 2:
                C.cnt["dve"] += 1
            if AGOPS >= 6:
                C.cnt["dve"] += 1
            if AGOPS >= 3:
                C.cnt["act"] += 1
            if AGOPS >= 5:
                C.cnt["pe"] += 1
        agam_end_d = C.cnt["dve"]
        tap(C, "l1_agam", lambda: [(C.dr["xtap"][0:128, 0:1024], sb["SM"][:, 0:1024])])
        if C.tapped:
            return


        @C.G
        def _(g):
            g.wait_ge(C.sem("dve"), agam_end_d)
            ins = g.iota(HBI[0:1, 0:1024], [[1, 1024]], base=1,
                         channel_multiplier=0, allow_small_or_imprecise_dtypes=True)
            C.inc(ins, "pool", 1)
        gq_m = C.val("pool")

        @C.V
        def _(v):
            v.wait_ge(C.sem("pool"), gq_m)
            ins = v.tensor_scalar(SM[0:1, 0:1024], HBI[0:1, 0:1024],
                                  LNG[0:1, 0:1], None, OP.mult)
            ins.then_inc(C.sem("dve"), 1)
        C.cnt["dve"] += 1
        gq_m2 = C.cnt["dve"]

        @C.A
        def _(a):
            a.wait_ge(C.sem("dve"), gq_m2)
            ins = a.activation(out=sb["ABR"][0:1, 0:1024], in_=SM[0:1, 0:1024],
                               func=AF.Exp)
            ins.then_inc(C.sem("act"), 1)
        C.cnt["act"] += 1
        gq_m3 = C.cnt["act"]

        @C.T
        def _(t):
            t.wait_ge(C.sem("act"), gq_m3)
            t.wait_ge(C.sem("evd"), C.val("evd"))
            for half in range(2):
                ins = t.matmul(PA[:, 1024 + 512 * half:1024 + 512 * (half + 1)],
                               ones_row[:, 0:128],
                               sb["ABR"][0:1, 512 * half:512 * (half + 1)],
                               start=True, stop=True)
            ins.then_inc(C.sem("pe"), 1)
        C.cnt["pe"] += 1
        gq_m4 = C.cnt["pe"]

        @C.V
        def _(v):
            v.wait_ge(C.sem("pe"), gq_m4)
            for dc in range(DC):
                ins = v.tensor_tensor(Qs[:, dc, :], Qs[:, dc, :],
                                      PA[:, 1024:2048], OP.mult)
            ins.then_inc(C.sem("dve"), 1)
        C.cnt["dve"] += 1
        qt_m = C.cnt["dve"]
        tap(C, "l1_gq", lambda: [(C.dr["xtap"][0:128, 0:1024], sb["SM"][:, 0:1024])])
        if C.tapped:
            return


        pe_s0 = C.val("pe")
        dve_s0 = C.val("dve")
        sin_tiles = []
        for dc in range(DC):
            for ki in range(NCORES):
                gi = 8 * dc + ki
                i = C.wb_idx
                C.wb_idx = 1 - C.wb_idx
                buf = sb["WB0"] if i == 0 else sb["WB1"]
                qsem = "qga" if i == 0 else "qgb"
                lu = C.wb_last_use[i] if gi < 2 else pe_s0 + gi - 1

                @C.G
                def _(g, buf=buf, lu=lu, ki=ki, dc=dc, qsem=qsem,
                      first=(dc == 0 and ki == 0)):
                    if first:
                        g.wait_ge(C.sem("cc"), ag_mark)
                    if lu > 0:
                        g.wait_ge(C.sem("pe"), lu)
                    ins = g.dma_start(
                        out=buf[:, 0:1025],
                        in_=C.dr["cc1_out"][1024 * ki + 128 * dc:
                                            1024 * ki + 128 * (dc + 1), 0:1025])
                    C.inc(ins, qsem, 16)
                sin_tiles.append((dc, ki, buf, (qsem, C.val(qsem)), i))

        @C.T
        def _(t):
            t.wait_ge(C.sem("dve"), max(idist_m, qt_m))
            for dc in range(DC):
                if dc >= 2:
                    t.wait_ge(C.sem("dve"), dve_s0 + dc - 1)
                ps = [PA, PB][dc % 2]
                for (dc2, ki, buf, mark, bi) in sin_tiles:
                    if dc2 != dc:
                        continue
                    t.wait_ge(C.sem(mark[0]), mark[1])
                    for seg in range(3):
                        lo = [0, 512, 1024][seg]
                        hi = [512, 1024, 1025][seg]
                        ins = t.matmul(ps[:, lo:hi],
                                       SM2[:, 128 * ki:128 * (ki + 1)],
                                       buf[:, lo:hi], start=(ki == 0),
                                       stop=(ki == NCORES - 1),
                                       skip_group_check=True)
                    ins.then_inc(C.sem("pe"), 1)
        C.cnt["pe"] += DC * NCORES
        C.wb_last_use[0] = C.wb_last_use[1] = C.cnt["pe"]

        @C.V
        def _(v):
            for dc in range(DC):
                v.wait_ge(C.sem("pe"), pe_s0 + 8 * (dc + 1))
                v.tensor_copy(sb["ZCOL"][:, dc:dc + 1],
                              [PA, PB][dc % 2][:, 1024:1025])
                ins = v.tensor_copy(Ks[:, dc, :], [PA, PB][dc % 2][:, 0:1024])
                ins.then_inc(C.sem("dve"), 1)
        C.cnt["dve"] += DC
        sin_m = C.cnt["dve"]
        tap(C, "l1_sin", lambda: [(C.dr["xtap"][0:128, 0:1024], sb["SM"][:, 0:1024])])
        if C.tapped:
            return


        @C.T
        def _(t):
            t.wait_ge(C.sem("dve"), max(sin_m, qt_m, agam_end_d))
            for jc in range(8):
                for half in range(2):
                    ins = t.matmul(PB[0:1, 1024 + 512 * half:1024 + 512 * (half + 1)],
                                   ones_col, KT[:, jc, 512 * half:512 * (half + 1)],
                                   start=(jc == 0), stop=False,
                                   skip_group_check=True)
            for dc in range(DC):
                for half in range(2):
                    ins = t.matmul(PB[0:1, 1024 + 512 * half:1024 + 512 * (half + 1)],
                                   sb["ZCOL"][:, dc:dc + 1],
                                   Qs[:, dc, 512 * half:512 * (half + 1)],
                                   start=False, stop=(dc == DC - 1),
                                   skip_group_check=True)
            ins.then_inc(C.sem("pe"), 1)
        C.cnt["pe"] += 1
        den_m = C.cnt["pe"]

        @C.V
        def _(v):
            def edge(ins):
                ins.then_inc(C.sem("dve"), 1)
                C.cnt["dve"] += 1
                v.wait_ge(C.sem("dve"), C.cnt["dve"])

            v.wait_ge(C.sem("pe"), den_m)
            edge(v.tensor_scalar(SM[0:1, 0:1024], PB[0:1, 1024:2048], 1e-6,
                                 None, OP.add))
            edge(v.reciprocal(SM[0:1, 0:1024], SM[0:1, 0:1024]))
            ins = v.tensor_copy(sb["ABR"][0:1, 0:1024], SM[0:1, 0:1024])
            ins.then_inc(C.sem("dve"), 1)
        C.cnt["dve"] += 1
        rec_m = C.cnt["dve"]

        @C.T
        def _(t):
            t.wait_ge(C.sem("dve"), rec_m)
            for half in range(2):
                ins = t.matmul(PB[:, 1024 + 512 * half:1024 + 512 * (half + 1)],
                               ones_row[:, 0:128],
                               sb["ABR"][0:1, 512 * half:512 * (half + 1)],
                               start=True, stop=True)
            ins.then_inc(C.sem("pe"), 1)
        C.cnt["pe"] += 1
        recb_m = C.cnt["pe"]

        @C.V
        def _(v):
            v.wait_ge(C.sem("pe"), recb_m)
            ins = v.tensor_copy(SM2[:, 1024:2048], PB[:, 1024:2048])
            ins.then_inc(C.sem("dve"), 1)
        C.cnt["dve"] += 1
        recsb_m = C.cnt["dve"]
        tap(C, "l1_den", lambda: [(C.dr["xtap"][0:128, 0:1024], sb["SM"][:, 0:1024])])
        if C.tapped:
            return


        groups = []
        for ec in range(DC):
            def mm(t, ps, ec=ec):
                for jc in range(8):
                    for half in range(2):
                        t.matmul(ps[:, 512 * half:512 * (half + 1)],
                                 VT[:, jc, 128 * ec:128 * (ec + 1)],
                                 KT[:, jc, 512 * half:512 * (half + 1)],
                                 start=(jc == 0), stop=False)
                for dc in range(DC):
                    for half in range(2):
                        ins = t.matmul(ps[:, 512 * half:512 * (half + 1)],
                                       Ks[:, dc, 128 * ec:128 * (ec + 1)],
                                       Qs[:, dc, 512 * half:512 * (half + 1)],
                                       start=False, stop=(dc == DC - 1))
                return ins

            def evac(e, ps, ec=ec):
                e.tensor_tensor(HB[:, 0:1024], ps[:, 0:1024], SM2[:, 1024:2048],
                                OP.mult)
                return e.tensor_tensor(XB[:, ec, bc:bc + 1024], HB[:, 0:1024],
                                       S[:, ec, bc:bc + 1024], OP.add)

            groups.append({"mm": mm, "evac": evac, "eng": "vector",
                           "wait": [("dve", max(recsb_m, sin_m, qt_m))]})
        run_pipe(C, groups, [PA[:, 0:1024], PB[:, 0:1024]])

    postln_from_XB(C)


def postln_from_XB(C):
    sb = C.sb
    S, XB = sb["S"], sb["XB"]

    def apply(v, dc, a_ps, b_ps):
        v.tensor_tensor(S[:, dc, :], XB[:, dc, :], a_ps[:, :], OP.mult)
        v.tensor_tensor(S[:, dc, :], S[:, dc, :], b_ps[:, :], OP.add)
        return v.tensor_copy(XB[:, dc, :], S[:, dc, :])

    ln_fm(C, lambda dc: XB[:, dc, :], apply)


def _quantize_weight(C, w_hbm, spill, nume):
    sb = C.sb
    WB0F = sb["WB0"][:, :].bitcast(F32)
    WB1I = sb["WB1"][:, :].bitcast(I32)
    WB1B = sb["WB1"]
    din, dout = w_hbm.shape
    nkc = din // 128
    ocols = 2048 // nkc
    nslab = dout // ocols
    acc = sb["STAT"][:, 24:25]
    tmp = sb["STAT"][:, 25:26]
    scol = sb["STAT"][:, 26:27]

    @C.V
    def _(v):
        ins = v.memset(acc, 0.0)
        ins.then_inc(C.sem("dve"), 1)
    C.cnt["dve"] += 1

    for s in range(nslab):
        @C.G
        def _(g, s=s):
            g.wait_ge(C.sem("dve"), C.val("dve"))
            ins = g.dma_start(
                out=WB0F[:, 0:nkc * ocols].rearrange("p (c o) -> p c o", o=ocols),
                in_=w_hbm[:, s * ocols:(s + 1) * ocols]
                    .rearrange("(c p) o -> p c o", p=128))
            C.inc(ins, "qg", 16)
        mk = C.val("qg")

        @C.A
        def _(a):
            a.wait_ge(C.sem("qg"), mk)
            ins = a.activation(out=WB0F[:, :], in_=WB0F[:, :], func=AF.Abs,
                               accum_out=tmp)
            ins.then_inc(C.sem("act"), 1)
        C.cnt["act"] += 1
        mk2 = C.cnt["act"]

        @C.V
        def _(v):
            v.wait_ge(C.sem("act"), mk2)
            ins = v.tensor_tensor(acc, acc, tmp, OP.add)
            ins.then_inc(C.sem("dve"), 1)
        C.cnt["dve"] += 1
    acc_m = C.cnt["dve"]

    @C.T
    def _(t):
        psum_guard(C, t)
        t.wait_ge(C.sem("dve"), acc_m)
        ins = t.matmul(C.ps["PA"][0:1, 0:1], acc, sb["ONEF"][:, 0:1],
                       start=True, stop=True)
        ins.then_inc(C.sem("pe"), 1)
    C.cnt["pe"] += 1
    ts_m = C.cnt["pe"]

    @C.V
    def _(v):
        def edge(ins):
            ins.then_inc(C.sem("dve"), 1)
            C.cnt["dve"] += 1
            v.wait_ge(C.sem("dve"), C.cnt["dve"])

        v.wait_ge(C.sem("pe"), ts_m)
        edge(v.tensor_scalar(sb["STAT"][0:1, 27:28], C.ps["PA"][0:1, 0:1],
                             1.0 / nume, 1e-5, OP.mult, OP.add))
        edge(v.reciprocal(sb["STAT"][0:1, 28:29], sb["STAT"][0:1, 27:28]))
        v.nop()
    rs_m = C.cnt["dve"]

    @C.T
    def _(t):
        t.wait_ge(C.sem("dve"), rs_m)
        ins = t.matmul(C.ps["PA"][:, 1:2], sb["ONERF"][0:1, 0:128],
                       sb["STAT"][0:1, 28:29], start=True, stop=True)
        ins.then_inc(C.sem("pe"), 1)
    C.cnt["pe"] += 1
    sc_m = C.cnt["pe"]

    @C.V
    def _(v):
        v.wait_ge(C.sem("pe"), sc_m)
        ins = v.tensor_copy(scol, C.ps["PA"][:, 1:2])
        ins.then_inc(C.sem("dve"), 1)
    C.cnt["dve"] += 1
    scol_m = C.cnt["dve"]

    for s in range(nslab):
        @C.G
        def _(g, s=s):
            g.wait_ge(C.sem("dve"), C.val("dve"))
            ins = g.dma_start(
                out=WB0F[:, 0:nkc * ocols].rearrange("p (c o) -> p c o", o=ocols),
                in_=w_hbm[:, s * ocols:(s + 1) * ocols]
                    .rearrange("(c p) o -> p c o", p=128))
            C.inc(ins, "qg", 16)
        mk = C.val("qg")

        @C.V
        def _(v, s=s, mk=mk):
            def edge(ins):
                ins.then_inc(C.sem("dve"), 1)
                C.cnt["dve"] += 1
                v.wait_ge(C.sem("dve"), C.cnt["dve"])

            v.wait_ge(C.sem("qg"), mk)
            if s == 0:
                v.wait_ge(C.sem("dve"), scol_m)
            edge(v.tensor_scalar(WB0F[:, :], WB0F[:, :], scol, None, OP.mult))
            edge(v.tensor_copy(WB1I[:, :], WB0F[:, :]))
            edge(v.tensor_copy(WB0F[:, :], WB1I[:, :]))
            edge(v.tensor_scalar(WB0F[:, :], WB0F[:, :], 1.0, -1.0,
                                 OP.min, OP.max))
            ins = v.tensor_copy(WB1B[:, 0:2048], WB0F[:, :])
            ins.then_inc(C.sem("dve"), 1)
        C.cnt["dve"] += 1
        mk2 = C.cnt["dve"]

        @C.G
        def _(g, s=s):
            g.wait_ge(C.sem("dve"), mk2)
            ins = g.dma_start(
                out=spill[:, s * ocols:(s + 1) * ocols]
                    .rearrange("(c p) o -> p c o", p=128),
                in_=WB1B[:, 0:2048].rearrange("p (c o) -> p c o", o=ocols))
            C.inc(ins, "qg", 16)
    C.wb_last_use[0] = C.wb_last_use[1] = C.cnt["pe"]
    return C.val("qg")


def emit_l1_ffn(C):
    block_barrier(C)
    sb = C.sb
    S, W, XB = sb["S"], sb["W"], sb["XB"]
    BR = sb["BIASR"]

    q1_done = _quantize_weight(C, C.dr["L1fW1"], C.dr["qsp1"], D * 4096)
    q2_done = _quantize_weight(C, C.dr["L1fW2"], C.dr["qsp2"], D * 4096)

    Wflat = W[:, :, :].rearrange("p c t -> p (c t)").bitcast(BF16)
    hslot = Wflat[:, :].rearrange("p (a b) -> p a b", b=1024)

    @C.G
    def _(g):
        g.wait_ge(C.sem("qg"), max(q1_done, q2_done))

    for tt in range(2):
        tcol = 1024 * tt
        mb1a = load_rows_bf(C, C.dr["L1fb1"], 0, 2048, BR[0:1, 0:2048],
                            wait_pe=C.val("pe"))
        fm_proj(C, C.dr["qsp1"][:, 0:2048], lambda dc: XB[:, dc, :], 2048,
                lambda e, oc, ps: e.activation(out=hslot[:, oc, :],
                                               in_=ps[:, :], func=AF.Relu),
                evac_eng="scalar", bias_row=BR[0:1, 0:2048],
                tokens=(tcol, tcol + 1024), bias_mark=mb1a)
        mb1b = load_rows_bf(C, C.dr["L1fb1"], 2048, 4096, BR[0:1, 0:2048],
                            wait_pe=C.val("pe"))
        fm_proj(C, C.dr["qsp1"][:, 2048:4096], lambda dc: XB[:, dc, :], 2048,
                lambda e, oc, ps: e.activation(out=hslot[:, 16 + oc, :],
                                               in_=ps[:, :], func=AF.Relu),
                evac_eng="scalar", bias_row=BR[0:1, 0:2048],
                tokens=(tcol, tcol + 1024), bias_mark=mb1b)
        mb2 = load_rows_bf(C, C.dr["L1fb2"], 0, D, BR[0:1, 2048:3072],
                           wait_pe=C.val("pe"))

        def ev_o(e, oc, ps, tcol=tcol):
            return e.tensor_tensor(XB[:, oc, tcol:tcol + 1024], ps[:, :],
                                   S[:, oc, tcol:tcol + 1024], OP.add)

        fm_proj(C, C.dr["qsp2"], lambda fc: hslot[:, fc, :], D, ev_o,
                evac_eng="vector", bias_row=BR[0:1, 2048:3072],
                tokens=(0, 1024), bias_mark=mb2)

    postln_from_XB(C)


def emit_l2_attn(C):
    block_barrier(C)
    sb = C.sb
    S, W, XB = sb["S"], sb["W"], sb["XB"]
    PA, PB = C.ps["PA"], C.ps["PB"]
    ones_row = sb["ONER"][0:1, :]
    SM = sb["SM"]
    SM2 = sb["SM2"]
    HB = sb["HB"]
    RP = sb["BONES"]

    Wflat = W[:, :, :].rearrange("p c t -> p (c t)").bitcast(BF16)
    Qr = [Wflat[:, 0:8192].rearrange("p (a b) -> p a b", b=1024),
          Wflat[:, 8192:16384].rearrange("p (a b) -> p a b", b=1024)]
    KS = Wflat[:, 16384:24576].rearrange("p (a b) -> p a b", b=1024)
    VS = Wflat[:, 24576:32768].rearrange("p (a b) -> p a b", b=1024)
    ATN = Wflat[:, 16384:32768].rearrange("p (a b) -> p a b", b=2048)
    XBf = XB[:, :, :].rearrange("p c t -> p (c t)")

    @C.G
    def _(g):
        ins = g.dma_start(out=RP[:, :], in_=C.dr["L2_ropeP"])
        C.inc(ins, "qg", 16)
        ins = g.dma_start(out=SM2[:, 0:1024], in_=C.dr["L2_cos"])
        C.inc(ins, "qg", 16)
        ins = g.dma_start(out=SM2[:, 1024:2048], in_=C.dr["L2_sin"])
        C.inc(ins, "qg", 16)
    const_m = C.val("qg")
    CC = SM2[:, 0:1024]
    SN = SM2[:, 1024:2048]

    def rope_inplace(slot, wait_extra):
        pe0 = C.val("pe")
        dve0 = C.val("dve")

        @C.T
        def _(t):
            psum_guard(C, t)
            t.wait_ge(C.sem("qg"), const_m)
            for (sn, v) in wait_extra:
                t.wait_ge(C.sem(sn), v)
            for dc in range(DC):
                if dc >= 2:
                    t.wait_ge(C.sem("dve"), dve0 + 3 * (dc - 1))
                ps = [PA, PB][dc % 2]
                for half in range(2):
                    ins = t.matmul(ps[:, 512 * half:512 * (half + 1)], RP[:, :],
                                   slot[:, dc, 512 * half:512 * (half + 1)],
                                   start=True, stop=True)
                ins.then_inc(C.sem("pe"), 1)
        C.cnt["pe"] += DC

        @C.V
        def _(v):
            for dc in range(DC):
                v.wait_ge(C.sem("pe"), pe0 + dc + 1)
                v.tensor_tensor(HB[:, 0:1024], slot[:, dc, :], CC, OP.mult)
                v.tensor_tensor(HB[:, 1024:2048], [PA, PB][dc % 2][:, 0:1024],
                                SN, OP.mult)
                ins = v.tensor_tensor(slot[:, dc, :], HB[:, 0:1024],
                                      HB[:, 1024:2048], OP.add)
                ins.then_inc(C.sem("dve"), 3)
        C.cnt["dve"] += 3 * DC

    for b in range(NB):
        fm_proj(C, C.dr["L2W_q"], lambda dc: XB[:, dc, :], D,
                lambda e, oc, ps, b=b: e.tensor_copy(Qr[b][:, oc, :], ps[:, :]),
                evac_eng="vector", tokens=(1024 * b, 1024 * b + 1024))
        rope_inplace(Qr[b], [])

    tap(C, "l2_rope", lambda: [
        (C.dr["xtap"][0:D, 0:1024].rearrange("(c p) t -> p c t", p=128),
         Wflat[:, 0:8192].rearrange("p (c t) -> p c t", t=1024)),
        (C.dr["xtap"][D:2 * D, 0:1024].rearrange("(c p) t -> p c t", p=128),
         Wflat[:, 8192:16384].rearrange("p (c t) -> p c t", t=1024))])
    if C.tapped:
        return

    kv_marks = []
    for b in range(NB):
        stg_w = [("qg", kv_marks[-1])] if kv_marks else []
        fm_proj(C, C.dr["L2W_k"], lambda dc: XB[:, dc, :], D,
                lambda e, oc, ps: e.tensor_copy(KS[:, oc, :], ps[:, :]),
                evac_eng="vector", tokens=(1024 * b, 1024 * b + 1024),
                extra_wait=stg_w)
        rope_inplace(KS, [])
        krot_done = C.cnt["dve"]

        groups = []
        for half in range(2):
            ap, mark, bi = wslab_load(C, C.dr["L2W_v"], 512 * half, 512)
            for tc in range(8):
                def mm(t, ps, ap=ap, tc=tc, b=b):
                    for dc in range(DC):
                        ins = t.matmul(ps[:, 0:512],
                                       XB[:, dc, 1024 * b + 128 * tc:1024 * b + 128 * (tc + 1)],
                                       ap[:, dc, :], start=(dc == 0), stop=(dc == DC - 1))
                    return ins

                def evac(e, ps, tc=tc, half=half):
                    return e.tensor_copy(VS[:, tc, 512 * half:512 * (half + 1)],
                                         ps[:, 0:512])

                groups.append({"mm": mm, "evac": evac, "eng": "vector",
                               "wait": [mark] + stg_w})
        run_pipe(C, groups, [PA[:, 0:512], PA[:, 512:1024],
                             PB[:, 0:512], PB[:, 512:1024]])
        C.wb_last_use[0] = C.wb_last_use[1] = C.cnt["pe"]
        v_done = C.cnt["evd"]

        @C.G
        def _(g, b=b, krot_done=krot_done, v_done=v_done):
            g.wait_ge(C.sem("dve"), krot_done)
            g.wait_ge(C.sem("evd"), v_done)
            ins = g.dma_start(
                out=C.dr["cc2_in"][1024 * b:1024 * (b + 1), 0:1024]
                    .rearrange("(c p) t -> p c t", p=128),
                in_=KS[:, :, :])
            C.inc(ins, "qg", 16)
            ins = g.dma_start(
                out=C.dr["cc2_in"][2048 + 1024 * b:2048 + 1024 * (b + 1), 0:1024]
                    .rearrange("(c p) t -> p c t", p=128),
                in_=VS[:, :, :])
            C.inc(ins, "qg", 16)

        kv_marks.append(C.val("qg"))

    @C.G
    def _(g):
        g.wait_ge(C.sem("qg"), kv_marks[-1])
        ins = g.collective_compute("AllGather", OP.bypass,
                                   replica_groups=[list(range(NCORES))],
                                   ins=[C.dr["cc2_in"].opt()],
                                   outs=[C.dr["cc2_out"].opt()])
        C.inc(ins, "cc", 1)
    ag_m = C.val("cc")

    scale = 1.0 / 8.0
    for b in range(NB):
        for hp in range(8):
            h2 = [2 * hp, 2 * hp + 1]
            tiles = []
            pe_now = C.val("pe")

            @C.V
            def _(v):
                v.wait_ge(C.sem("pe"), pe_now)
                for bufm in (sb["WB0"], sb["WB1"]):
                    for j in range(2):
                        vv = bufm[:, 1024 + 520 * j:1024 + 520 * j + 520] \
                            .rearrange("p (c o) -> p c o", o=65)
                        ins = v.memset(vv[:, :, 64:65], 1.0)
                ins.then_inc(C.sem("dve"), 1)
            C.cnt["dve"] += 1
            ones_m = C.cnt["dve"]

            pe_f0 = C.val("pe")
            for r in range(NCORES):
                i = C.wb_idx
                C.wb_idx = 1 - C.wb_idx
                buf = sb["WB0"] if i == 0 else sb["WB1"]
                lu = C.wb_last_use[i] if r < 2 else pe_f0 + 32 * (r - 2) + 34

                qsem = "qga" if i == 0 else "qgb"

                @C.G
                def _(g, r=r, b=b, hp=hp, buf=buf, lu=lu, qsem=qsem,
                      first=(b == 0 and hp == 0 and r == 0)):
                    if first:
                        g.wait_ge(C.sem("cc"), ag_m)
                    if lu > 0:
                        g.wait_ge(C.sem("pe"), lu)
                    base = 4096 * r
                    ins = g.dma_start(
                        out=buf[:, 0:1024],
                        in_=C.dr["cc2_out"][base + 1024 * b + 128 * hp:
                                            base + 1024 * b + 128 * (hp + 1), 0:1024])
                    C.inc(ins, qsem, 16)
                    for j, h in enumerate([2 * hp, 2 * hp + 1]):
                        vv = buf[:, 1024 + 520 * j:1024 + 520 * j + 520] \
                            .rearrange("p (c o) -> p c o", o=65)
                        ins = g.dma_start(
                            out=vv[:, :, 0:64],
                            in_=C.dr["cc2_out"][base + 2048 + 1024 * b:
                                                base + 2048 + 1024 * (b + 1),
                                                64 * h:64 * (h + 1)]
                                .rearrange("(c p) o -> p c o", p=128))
                        C.inc(ins, qsem, 16)
                tiles.append((buf, (qsem, C.val(qsem)), i))

            dve_f0 = C.val("dve")
            eva_f0 = C.val("eva")
            pe0f = C.val("pe")
            ngrp = NCORES * 8 * 2
            stm = {}
            avm = {}
            run_pe = pe0f
            order = []
            for r in range(NCORES):
                for tkc in range(8):
                    for j, h in enumerate(h2):
                        order.append((r, tkc, j, h))

            @C.T
            def _(t):
                nonlocal_run = [pe0f]

                def emit_st(gi):
                    r, tkc, j, h = order[gi]
                    buf, mark, bi = tiles[r]
                    t.wait_ge(C.sem(mark[0]), mark[1])
                    if gi >= 2:
                        t.wait_ge(C.sem("eva"), eva_f0 + gi - 1)
                    ps = [PB[:, 0:1024], PB[:, 1024:2048]][gi % 2]
                    for half in range(2):
                        ins = t.matmul(
                            ps[:, 512 * half:512 * (half + 1)],
                            buf[64 * (h % 2):64 * (h % 2) + 64,
                                128 * tkc:128 * (tkc + 1)],
                            Qr[b][64 * (h % 2):64 * (h % 2) + 64, hp,
                                  512 * half:512 * (half + 1)],
                            start=True, stop=True, skip_group_check=True)
                    nonlocal_run[0] += 1
                    ins.then_inc(C.sem("pe"), 1)
                    stm[gi] = nonlocal_run[0]

                def emit_av(gi):
                    r, tkc, j, h = order[gi]
                    buf, mark, bi = tiles[r]
                    t.wait_ge(C.sem("eva"), eva_f0 + gi + 1)
                    if gi == 0:
                        t.wait_ge(C.sem("dve"), ones_m)
                    hbslot = XBf[:, 1024 * (gi % 2):1024 * (gi % 2) + 1024]
                    vv = buf[:, 1024 + 520 * j:1024 + 520 * j + 520] \
                        .rearrange("p (c o) -> p c o", o=65)
                    avp = PA[0:65, 1024 * j:1024 * j + 1024]
                    for half in range(2):
                        ins = t.matmul(
                            avp[:, 512 * half:512 * (half + 1)],
                            vv[:, tkc, :],
                            hbslot[:, 512 * half:512 * (half + 1)],
                            start=(gi < 2), stop=(gi >= ngrp - 2),
                            skip_group_check=True)
                    nonlocal_run[0] += 1
                    ins.then_inc(C.sem("pe"), 1)
                    avm[gi] = nonlocal_run[0]

                psum_guard(C, t)
                t.wait_ge(C.sem("dve"), dve_f0)
                for gi in range(ngrp):
                    emit_st(gi)
                    if gi >= 2:
                        emit_av(gi - 2)
                emit_av(ngrp - 2)
                emit_av(ngrp - 1)
            C.cnt["pe"] += 2 * ngrp

            @C.A
            def _(a):
                for gi in range(ngrp):
                    a.wait_ge(C.sem("pe"), stm[gi])
                    if gi >= 2:
                        a.wait_ge(C.sem("pe"), avm[gi - 2])
                    ps = [PB[:, 0:1024], PB[:, 1024:2048]][gi % 2]
                    hbslot = XBf[:, 1024 * (gi % 2):1024 * (gi % 2) + 1024]
                    ins = a.activation(out=hbslot, in_=ps[:, :], func=AF.Exp,
                                       scale=scale)
                    ins.then_inc(C.sem("eva"), 1)
            C.cnt["eva"] += ngrp
            av_done = C.cnt["pe"]
            C.wb_last_use[0] = C.wb_last_use[1] = av_done

            for j, h in enumerate(h2):
                avp = PA[0:65, 1024 * j:1024 * j + 1024]

                @C.V
                def _(v, j=j):
                    def edge(ins):
                        ins.then_inc(C.sem("dve"), 1)
                        C.cnt["dve"] += 1
                        v.wait_ge(C.sem("dve"), C.cnt["dve"])

                    v.wait_ge(C.sem("pe"), av_done)
                    avp2 = PA[0:65, 1024 * j:1024 * j + 1024]
                    edge(v.tensor_copy(SM[0:1, 0:1024], avp2[64:65, 0:1024]))
                    edge(v.reciprocal(SM[0:1, 0:1024], SM[0:1, 0:1024]))
                    ins = v.tensor_copy(sb["ABR"][0:1, 0:1024], SM[0:1, 0:1024])
                    ins.then_inc(C.sem("dve"), 1)
                C.cnt["dve"] += 1
                rc_m = C.cnt["dve"]

                @C.T
                def _(t):
                    t.wait_ge(C.sem("dve"), rc_m)
                    for half in range(2):
                        ins = t.matmul(
                            PB[0:64, 512 * half:512 * (half + 1)],
                            ones_row[:, 0:64],
                            sb["ABR"][0:1, 512 * half:512 * (half + 1)],
                            start=True, stop=True)
                    ins.then_inc(C.sem("pe"), 1)
                C.cnt["pe"] += 1
                rb_m = C.cnt["pe"]

                @C.V
                def _(v, j=j, h=h, b=b, hp=hp):
                    v.wait_ge(C.sem("pe"), rb_m)
                    v.tensor_copy(HB[0:64, 0:1024], PB[0:64, 0:1024])
                    avp2 = PA[0:65, 1024 * j:1024 * j + 1024]
                    ins = v.tensor_tensor(
                        ATN[64 * (h % 2):64 * (h % 2) + 64, hp,
                            1024 * b:1024 * (b + 1)],
                        avp2[0:64, 0:1024], HB[0:64, 0:1024], OP.mult)
                    ins.then_inc(C.sem("dve"), 1)
                C.cnt["dve"] += 1

    tap(C, "l2_atn", lambda: [
        (C.dr["xtap"][0:D, :].rearrange("(c p) t -> p c t", p=128),
         Wflat[:, 16384:32768].rearrange("p (c t) -> p c t", t=2048))])
    if C.tapped:
        return

    def ev_o(e, oc, ps):
        return e.tensor_tensor(XB[:, oc, :], ps[:, :], S[:, oc, :], OP.add)

    fm_proj(C, C.dr["L2W_o"], lambda dc: ATN[:, dc, :], D, ev_o,
            evac_eng="vector")

    postln_from_XB(C)


def emit_l2_ffn(C):
    block_barrier(C)
    sb = C.sb
    S, W, XB = sb["S"], sb["W"], sb["XB"]
    PA, PB = C.ps["PA"], C.ps["PB"]
    ones_row = sb["ONER"][0:1, :]
    SM = sb["SM"]
    SM2 = sb["SM2"]
    hslot = W[:, :, :].rearrange("p c t -> p (c t)").bitcast(BF16) \
        .rearrange("p (a b) -> p a b", b=1024)

    apr, mr, bir = wslab_load(C, C.dr["L2fW_rt"], 0, 8,
                              wait_pe=C.val("pe"))

    @C.T
    def _(t):
        psum_guard(C, t)
        t.wait_ge(C.sem(mr[0]), mr[1])
        for tc in range(TC):
            for dc in range(DC):
                ins = t.matmul(PA[:, 8 * tc:8 * (tc + 1)],
                               XB[:, dc, 128 * tc:128 * (tc + 1)],
                               apr[:, dc, :], start=(dc == 0), stop=(dc == DC - 1),
                               skip_group_check=True)
        ins.then_inc(C.sem("pe"), 1)
    C.cnt["pe"] += 1
    rt_mark = C.cnt["pe"]
    C.wb_last_use[bir] = C.cnt["pe"]

    RT = SM[:, 0:128].rearrange("p (a b) -> p a b", b=8)
    RX = SM[:, 128:256].rearrange("p (a b) -> p a b", b=8)
    RS = SM[:, 256:272].rearrange("p (a b) -> p a b", b=1)
    RS2 = SM[:, 272:288].rearrange("p (a b) -> p a b", b=1)

    @C.V
    def _(v):
        v.wait_ge(C.sem("pe"), rt_mark)
        v.tensor_copy(SM[:, 0:128], PA[:, 0:128])
        ins = v.reduce_max(RS[:, :, :], RT[:, :, :], axis=AX.X)
        ins.then_inc(C.sem("dve"), 1)
        v.wait_ge(C.sem("dve"), C.cnt["dve"] + 1)
        ins = v.tensor_tensor(RX[:, :, :], RT[:, :, :],
                              RS[:, :, :].broadcast_to([128, 16, 8]), OP.subtract)
        ins.then_inc(C.sem("dve"), 1)
    C.cnt["dve"] += 2
    r1_mark = C.cnt["dve"]

    @C.A
    def _(a):
        a.wait_ge(C.sem("dve"), r1_mark)
        ins = a.activation(out=SM[:, 128:256], in_=SM[:, 128:256], func=AF.Exp)
        ins.then_inc(C.sem("act"), 1)
    C.cnt["act"] += 1
    r2_mark = C.cnt["act"]

    @C.V
    def _(v):
        def edge(ins):
            ins.then_inc(C.sem("dve"), 1)
            C.cnt["dve"] += 1
            v.wait_ge(C.sem("dve"), C.cnt["dve"])

        v.wait_ge(C.sem("act"), r2_mark)
        edge(v.reduce_sum(RS[:, :, :], RX[:, :, :], axis=AX.X))
        edge(v.reciprocal(SM[:, 272:288], SM[:, 256:272]))
        v.tensor_tensor(RX[:, :, :], RX[:, :, :],
                        RS2[:, :, :].broadcast_to([128, 16, 8]), OP.mult)
        edge(v.reduce_max(RS[:, :, :], RX[:, :, :], axis=AX.X))
        v.tensor_tensor(RT[:, :, :], RX[:, :, :],
                        RS[:, :, :].broadcast_to([128, 16, 8]), OP.is_ge)
        v.scalar_tensor_tensor(RT[:, :, :], RT[:, :, :], -1e30, RX[:, :, :],
                               OP.mult, OP.add)
        edge(v.reduce_max(RS2[:, :, :], RT[:, :, :], axis=AX.X))
        v.tensor_tensor(RT[:, :, :], RX[:, :, :],
                        RS2[:, :, :].broadcast_to([128, 16, 8]), OP.is_ge)
        v.tensor_tensor(RT[:, :, :], RT[:, :, :], RX[:, :, :], OP.mult)
        ins = v.tensor_copy(SM[:, 288:416], SM[:, 0:128])
        ins.then_inc(C.sem("dve"), 1)
    C.cnt["dve"] += 1
    rw_mark = C.cnt["dve"]

    @C.T
    def _(t):
        psum_guard(C, t)
        t.wait_ge(C.sem("dve"), rw_mark)
        for tc in range(TC):
            ins = t.transpose(PB[0:8, 128 * tc:128 * (tc + 1)],
                              SM[:, 288 + 8 * tc:288 + 8 * (tc + 1)],
                              sb["IDN"][:, :])
        ins.then_inc(C.sem("pe"), 1)
    C.cnt["pe"] += 1
    wt_mark = C.cnt["pe"]

    WEB = SM[:, :].bitcast(BF16)

    @C.V
    def _(v):
        v.wait_ge(C.sem("pe"), wt_mark)
        ins = v.tensor_copy(WEB[0:8, 0:2048], PB[0:8, 0:2048])
        ins.then_inc(C.sem("dve"), 1)
    C.cnt["dve"] += 1
    we_mark = C.cnt["dve"]

    HBI = sb["HB"][:, :].bitcast(mybir.dt.int16)

    @C.G
    def _(g):
        g.wait_ge(C.sem("dve"), we_mark)
        ins = g.iota(HBI[0:8, 0:1024], [[1, 8], [0, 128]], base=0,
                     channel_multiplier=0, allow_small_or_imprecise_dtypes=True)
        C.inc(ins, "pool", 1)
    sel_mark = C.val("pool")

    @C.V
    def _(v):
        v.wait_ge(C.sem("pool"), sel_mark)
        ins = v.tensor_scalar(sb["BONEST"][0:8, 0:1024], HBI[0:8, 0:1024],
                              sb["PIDX"][0:8, 0:1], None, OP.is_equal)
        ins.then_inc(C.sem("dve"), 1)
    C.cnt["dve"] += 1
    selv_mark = C.cnt["dve"]

    for tt in range(2):
        tcol = 1024 * tt
        for e in range(8):
            @C.T
            def _(t, e=e):
                psum_guard(C, t)
                t.wait_ge(C.sem("dve"), max(we_mark, selv_mark))
                for ts in range(NTS):
                    ins = t.matmul(PA[:, 512 * ts:512 * (ts + 1)],
                                   sb["BONEST"][0:8, 128 * e:128 * (e + 1)],
                                   WEB[0:8, 512 * ts:512 * (ts + 1)],
                                   start=True, stop=True)
                ins.then_inc(C.sem("pe"), 1)
            C.cnt["pe"] += 1
            web_mark = C.cnt["pe"]

            @C.V
            def _(v):
                v.wait_ge(C.sem("pe"), web_mark)
                ins = v.tensor_copy(SM2[:, :], PA[:, :])
                ins.then_inc(C.sem("evd"), 1)
            C.cnt["evd"] += 1

            w1e = C.dr["L2f_w1"][e]
            groups = []
            pe_entry = C.val("pe")
            for sgi in range(8):
                og = 512 * sgi
                wp = pe_entry + 4 * (sgi - 1) if sgi >= 2 else None
                ap, mark, bi = wslab_load(C, w1e, og, 512, wait_pe=wp)
                for j in range(4):
                    fc = 4 * sgi + j

                    def mm(t, ps, ap=ap, j=j):
                        for ts in range(2):
                            for dc in range(DC):
                                ins = t.matmul(
                                    ps[:, 512 * ts:512 * (ts + 1)],
                                    ap[:, dc, 128 * j:128 * (j + 1)],
                                    XB[:, dc, tcol + 512 * ts:tcol + 512 * (ts + 1)],
                                    start=(dc == 0), stop=(dc == DC - 1))
                        return ins

                    def evac(e_, ps, fc=fc):
                        return e_.activation(out=hslot[:, fc, :], in_=ps[:, 0:1024],
                                             func=AF.Gelu)

                    groups.append({"mm": mm, "evac": evac, "eng": "scalar",
                                   "wait": [mark]})
            run_pipe(C, groups, [PA[:, 0:1024], PB[:, 0:1024]])
            C.wb_last_use[0] = C.wb_last_use[1] = C.cnt["pe"]

            w2e = C.dr["L2f_w2"][e]
            groups = []
            pe_entry = C.val("pe")
            for hc in range(8):
                wp = pe_entry + (hc - 1) if hc >= 2 else None
                ap, mark, bi = wslab_load(C, w2e, 128 * hc, 128, wait_pe=wp)

                def mm(t, ps, ap=ap):
                    for ts in range(2):
                        for fc in range(32):
                            ins = t.matmul(ps[:, 512 * ts:512 * (ts + 1)],
                                           ap[:, fc, :],
                                           hslot[:, fc, 512 * ts:512 * (ts + 1)],
                                           start=(fc == 0), stop=(fc == 31))
                    return ins

                def evac(e_, ps, hc=hc):
                    e_.tensor_tensor(sb["HB"][:, 0:1024], ps[:, 0:1024],
                                     SM2[:, tcol:tcol + 1024], OP.mult)
                    return e_.tensor_tensor(S[:, hc, tcol:tcol + 1024],
                                            S[:, hc, tcol:tcol + 1024],
                                            sb["HB"][:, 0:1024], OP.add)

                groups.append({"mm": mm, "evac": evac, "eng": "vector",
                               "wait": [mark]})
            run_pipe(C, groups, [PA[:, 0:1024], PB[:, 0:1024]])
            C.wb_last_use[0] = C.wb_last_use[1] = C.cnt["pe"]

    postln_from_S(C)


def emit_l3_attn(C):
    block_barrier(C)
    sb = C.sb
    S, W, XB = sb["S"], sb["W"], sb["XB"]
    PA, PB = C.ps["PA"], C.ps["PB"]
    ones_row = sb["ONER"][0:1, :]
    SM = sb["SM"]
    SM2 = sb["SM2"]
    HB = sb["HB"]
    BR = sb["BIASR"]
    HBI2 = SM2[:, :].bitcast(mybir.dt.int16)

    Wflat = W[:, :, :].rearrange("p c t -> p (c t)").bitcast(BF16)
    Qr = [Wflat[:, 0:8192].rearrange("p (a b) -> p a b", b=1024),
          Wflat[:, 8192:16384].rearrange("p (a b) -> p a b", b=1024)]
    KS = Wflat[:, 16384:24576].rearrange("p (a b) -> p a b", b=1024)
    VS = Wflat[:, 24576:32768].rearrange("p (a b) -> p a b", b=1024)
    ATN = Wflat[:, 16384:32768].rearrange("p (a b) -> p a b", b=2048)
    XBf = XB[:, :, :].rearrange("p c t -> p (c t)")

    mb_qkv = load_rows_bf(C, C.dr["L3b_qkv"], 0, 3 * D, BR[0:1, 0:3072],
                          wait_pe=C.val("pe"))

    @C.G
    def _(g):
        ins = g.dma_start(out=sb["STAT"][:, 0:16], in_=C.dr["L3_ab"])
        C.inc(ins, "qg", 16)
    ab_m = C.val("qg")

    for b in range(NB):
        fm_proj(C, C.dr["L3W_qkv"], lambda dc: XB[:, dc, :], D,
                lambda e, oc, ps, b=b: e.tensor_copy(Qr[b][:, oc, :], ps[:, :]),
                evac_eng="vector", bias_row=BR[0:1, 0:1024],
                tokens=(1024 * b, 1024 * b + 1024), bias_mark=mb_qkv)

    kv_marks = []
    for b in range(NB):
        stg_w = [("qg", kv_marks[-1])] if kv_marks else []
        fm_proj(C, C.dr["L3W_qkv"][:, D:2 * D],
                lambda dc: XB[:, dc, :], D,
                lambda e, oc, ps: e.tensor_copy(KS[:, oc, :], ps[:, :]),
                evac_eng="vector", bias_row=BR[0:1, 1024:2048],
                tokens=(1024 * b, 1024 * b + 1024), bias_mark=mb_qkv,
                extra_wait=stg_w)
        k_done = C.cnt["evd"]

        groups = []
        for half in range(2):
            ap, mark, bi = wslab_load(C, C.dr["L3W_qkv"], 2048 + 512 * half, 512)
            for tc in range(8):
                def mm(t, ps, ap=ap, tc=tc, b=b, half=half):
                    for dc in range(DC):
                        t.matmul(ps[:, 0:512],
                                 XB[:, dc, 1024 * b + 128 * tc:1024 * b + 128 * (tc + 1)],
                                 ap[:, dc, :], start=(dc == 0), stop=False)
                    return t.matmul(ps[:, 0:512], ones_row[:, 0:128],
                                    BR[0:1, 2048 + 512 * half:2048 + 512 * (half + 1)],
                                    start=False, stop=True)

                def evac(e, ps, tc=tc, half=half):
                    return e.tensor_copy(VS[:, tc, 512 * half:512 * (half + 1)],
                                         ps[:, 0:512])

                groups.append({"mm": mm, "evac": evac, "eng": "vector",
                               "wait": [mark, ("qg", ab_m), ("qgr", mb_qkv)]
                                       + stg_w})
        run_pipe(C, groups, [PA[:, 0:512], PA[:, 512:1024],
                             PB[:, 0:512], PB[:, 512:1024]])
        C.wb_last_use[0] = C.wb_last_use[1] = C.cnt["pe"]
        v_done = C.cnt["evd"]

        @C.G
        def _(g, b=b, k_done=k_done, v_done=v_done):
            g.wait_ge(C.sem("evd"), max(k_done, v_done))
            ins = g.dma_start(
                out=C.dr["cc3_in"][1024 * b:1024 * (b + 1), 0:1024]
                    .rearrange("(c p) t -> p c t", p=128),
                in_=KS[:, :, :])
            C.inc(ins, "qg", 16)
            ins = g.dma_start(
                out=C.dr["cc3_in"][2048 + 1024 * b:2048 + 1024 * (b + 1), 0:1024]
                    .rearrange("(c p) t -> p c t", p=128),
                in_=VS[:, :, :])
            C.inc(ins, "qg", 16)
        kv_marks.append(C.val("qg"))

    @C.G
    def _(g):
        g.wait_ge(C.sem("qg"), kv_marks[-1])
        ins = g.collective_compute("AllGather", OP.bypass,
                                   replica_groups=[list(range(NCORES))],
                                   ins=[C.dr["cc3_in"].opt()],
                                   outs=[C.dr["cc3_out"].opt()])
        C.inc(ins, "cc", 1)
    ag_m = C.val("cc")

    scale = 0.125
    npair = 0
    for b in range(NB):
        for hp in range(8):
            npair += 1
            if npair > DEBUG_L3PAIRS:
                continue
            h2 = [2 * hp, 2 * hp + 1]
            pe_now = C.val("pe")

            @C.V
            def _(v):
                v.wait_ge(C.sem("pe"), pe_now)
                for bufm in (sb["WB0"], sb["WB1"]):
                    for j in range(2):
                        vv = bufm[:, 1024 + 520 * j:1024 + 520 * j + 520] \
                            .rearrange("p (c o) -> p c o", o=65)
                        ins = v.memset(vv[:, :, 64:65], 1.0)
                ins.then_inc(C.sem("dve"), 1)
            C.cnt["dve"] += 1
            ones_m = C.cnt["dve"]

            tiles = []
            pe_f0 = C.val("pe")
            dve_f0 = C.val("dve")
            pool_f0 = C.val("pool")
            SMI = SM[:, :].bitcast(mybir.dt.int16)

            def emit_iotas(lo_gg, hi_gg):
                if DEBUG_L3NOMASK:
                    return

                @C.G
                def _(g):
                    for gg in range(lo_gg, hi_gg):
                        tkc = gg % 8
                        if DEBUG_L3MASKLVL == 0:
                            g.wait_ge(C.sem("dve"), dve_f0)
                        else:
                            g.wait_ge(C.sem("dve"),
                                      dve_f0 + gg if gg >= 1 else dve_f0)
                        ins = g.iota(SMI[:, 0:1024],
                                     [[1, 1024]], base=-128 * tkc,
                                     channel_multiplier=-1,
                                     allow_small_or_imprecise_dtypes=True)
                        C.inc(ins, "pool", 1)

            for r in range(NCORES):
                if r >= 2:
                    emit_iotas(8 * (r - 2), 8 * (r - 1))
                i = C.wb_idx
                C.wb_idx = 1 - C.wb_idx
                buf = sb["WB0"] if i == 0 else sb["WB1"]
                lu = C.wb_last_use[i] if r < 2 else pe_f0 + 32 * (r - 2) + 34

                qsem = "qga" if i == 0 else "qgb"

                @C.G
                def _(g, r=r, b=b, hp=hp, buf=buf, lu=lu, qsem=qsem,
                      first=(b == 0 and hp == 0 and r == 0)):
                    if first:
                        g.wait_ge(C.sem("cc"), ag_m)
                    if lu > 0:
                        g.wait_ge(C.sem("pe"), lu)
                    base = 4096 * r
                    ins = g.dma_start(
                        out=buf[:, 0:1024],
                        in_=C.dr["cc3_out"][base + 1024 * b + 128 * hp:
                                            base + 1024 * b + 128 * (hp + 1), 0:1024])
                    C.inc(ins, qsem, 16)
                    for j, h in enumerate([2 * hp, 2 * hp + 1]):
                        vv = buf[:, 1024 + 520 * j:1024 + 520 * j + 520] \
                            .rearrange("p (c o) -> p c o", o=65)
                        ins = g.dma_start(
                            out=vv[:, :, 0:64],
                            in_=C.dr["cc3_out"][base + 2048 + 1024 * b:
                                                base + 2048 + 1024 * (b + 1),
                                                64 * h:64 * (h + 1)]
                                .rearrange("(c p) o -> p c o", p=128))
                        C.inc(ins, qsem, 16)
                tiles.append((buf, (qsem, C.val(qsem)), i))
            emit_iotas(8 * (NCORES - 2), 8 * NCORES)

            eva_f0 = C.val("eva")
            pe0f = C.val("pe")
            act_f0 = C.val("act")
            ngrp = NCORES * 8 * 2
            stm = {}
            avm = {}
            mkm = {}
            order = []
            for r in range(NCORES):
                for tkc in range(8):
                    for j, h in enumerate(h2):
                        order.append((r, tkc, j, h))

            @C.T
            def _(t):
                run = [pe0f]

                def emit_st(gi):
                    r, tkc, j, h = order[gi]
                    buf, mark, bi = tiles[r]
                    t.wait_ge(C.sem(mark[0]), mark[1])
                    if gi >= 2:
                        t.wait_ge(C.sem("act"), act_f0 + gi - 1)
                    ps = [PB[:, 0:1024], PB[:, 1024:2048]][gi % 2]
                    for half in range(2):
                        ins = t.matmul(
                            ps[:, 512 * half:512 * (half + 1)],
                            buf[64 * (h % 2):64 * (h % 2) + 64,
                                128 * tkc:128 * (tkc + 1)],
                            Qr[b][64 * (h % 2):64 * (h % 2) + 64, hp,
                                  512 * half:512 * (half + 1)],
                            start=True, stop=True, skip_group_check=True)
                    run[0] += 1
                    ins.then_inc(C.sem("pe"), 1)
                    stm[gi] = run[0]

                def emit_av(gi):
                    r, tkc, j, h = order[gi]
                    buf, mark, bi = tiles[r]
                    t.wait_ge(C.sem("eva"), eva_f0 + gi + 1)
                    if gi == 0:
                        t.wait_ge(C.sem("dve"), ones_m)
                    hbslot = XBf[:, 1024 * (gi % 2):1024 * (gi % 2) + 1024]
                    vv = buf[:, 1024 + 520 * j:1024 + 520 * j + 520] \
                        .rearrange("p (c o) -> p c o", o=65)
                    avp = PA[0:65, 1024 * j:1024 * j + 1024]
                    for half in range(2):
                        ins = t.matmul(
                            avp[:, 512 * half:512 * (half + 1)],
                            vv[:, tkc, :],
                            hbslot[:, 512 * half:512 * (half + 1)],
                            start=(gi < 2), stop=(gi >= ngrp - 2),
                            skip_group_check=True)
                    run[0] += 1
                    ins.then_inc(C.sem("pe"), 1)
                    avm[gi] = run[0]

                psum_guard(C, t)
                t.wait_ge(C.sem("dve"), dve_f0)
                for gi in range(ngrp):
                    emit_st(gi)
                    if gi >= 2:
                        emit_av(gi - 2)
                emit_av(ngrp - 2)
                emit_av(ngrp - 1)
            C.cnt["pe"] += 2 * ngrp

            @C.A
            def _(a):
                for gi in range(ngrp):
                    a.wait_ge(C.sem("pe"), stm[gi])
                    if gi >= 2:
                        a.wait_ge(C.sem("pe"), avm[gi - 2])
                    ps = [PB[:, 0:1024], PB[:, 1024:2048]][gi % 2]
                    hbslot = XBf[:, 1024 * (gi % 2):1024 * (gi % 2) + 1024]
                    ins = a.activation(out=hbslot, in_=ps[:, :], func=AF.Exp,
                                       scale=scale)
                    ins.then_inc(C.sem("act"), 1)
            C.cnt["act"] += ngrp

            @C.V
            def _(v):
                if DEBUG_L3NOMASK or DEBUG_L3MASKLVL == 0:
                    for gi in range(ngrp):
                        v.wait_ge(C.sem("act"), act_f0 + gi + 1)
                        hbslot = XBf[:, 1024 * (gi % 2):1024 * (gi % 2) + 1024]
                        ins = v.tensor_scalar(hbslot, hbslot, 1.0, None, OP.mult)
                        ins.then_inc(C.sem("eva"), 1)
                    return
                for gi in range(ngrp):
                    r, tkc, j, h = order[gi]
                    gg = gi // 2
                    if gi % 2 == 0:
                        v.wait_ge(C.sem("pool"), pool_f0 + gg + 1)
                        if DEBUG_L3MASKLVL >= 3:
                            v.tensor_scalar(
                                SM2[:, 1024 * (gg % 2):1024 * (gg % 2) + 1024],
                                SMI[:, 0:1024],
                                0.0, None, OP.is_ge)
                        else:
                            v.memset(SM2[:, 1024 * (gg % 2):1024 * (gg % 2) + 1024], 1.0)
                        if DEBUG_L3MASKLVL >= 4:
                            ins = v.tensor_scalar(
                                SM2[:, 1024 * (gg % 2):1024 * (gg % 2) + 1024],
                                SM2[:, 1024 * (gg % 2):1024 * (gg % 2) + 1024],
                                sb["STAT"][:, 2 * r + 1:2 * r + 2],
                                sb["STAT"][:, 2 * r:2 * r + 1], OP.mult, OP.add)
                        else:
                            ins = v.tensor_scalar(
                                SM2[:, 1024 * (gg % 2):1024 * (gg % 2) + 1024],
                                SM2[:, 1024 * (gg % 2):1024 * (gg % 2) + 1024],
                                1.0, None, OP.mult)
                        ins.then_inc(C.sem("dve"), 1)
                    v.wait_ge(C.sem("act"), act_f0 + gi + 1)
                    hbslot = XBf[:, 1024 * (gi % 2):1024 * (gi % 2) + 1024]
                    if DEBUG_L3MASKLVL >= 2:
                        ins = v.tensor_tensor(
                            hbslot, hbslot,
                            SM2[:, 1024 * (gg % 2):1024 * (gg % 2) + 1024], OP.mult)
                    else:
                        ins = v.tensor_scalar(hbslot, hbslot, 1.0, None, OP.mult)
                    ins.then_inc(C.sem("eva"), 1)
            if not DEBUG_L3NOMASK and DEBUG_L3MASKLVL > 0:
                C.cnt["dve"] += ngrp // 2
            C.cnt["eva"] += ngrp
            av_done = C.cnt["pe"]
            C.wb_last_use[0] = C.wb_last_use[1] = av_done

            for j, h in enumerate(h2):
                @C.V
                def _(v, j=j):
                    def edge(ins):
                        ins.then_inc(C.sem("dve"), 1)
                        C.cnt["dve"] += 1
                        v.wait_ge(C.sem("dve"), C.cnt["dve"])

                    v.wait_ge(C.sem("pe"), av_done)
                    avp2 = PA[0:65, 1024 * j:1024 * j + 1024]
                    edge(v.tensor_copy(SM[0:1, 0:1024], avp2[64:65, 0:1024]))
                    edge(v.reciprocal(SM[0:1, 0:1024], SM[0:1, 0:1024]))
                    ins = v.tensor_copy(sb["ABR"][0:1, 0:1024], SM[0:1, 0:1024])
                    ins.then_inc(C.sem("dve"), 1)
                C.cnt["dve"] += 1
                rc_m = C.cnt["dve"]

                @C.T
                def _(t):
                    t.wait_ge(C.sem("dve"), rc_m)
                    for half in range(2):
                        ins = t.matmul(
                            PB[0:64, 512 * half:512 * (half + 1)],
                            ones_row[:, 0:64],
                            sb["ABR"][0:1, 512 * half:512 * (half + 1)],
                            start=True, stop=True)
                    ins.then_inc(C.sem("pe"), 1)
                C.cnt["pe"] += 1
                rb_m = C.cnt["pe"]

                @C.V
                def _(v, j=j, h=h, b=b, hp=hp):
                    v.wait_ge(C.sem("pe"), rb_m)
                    v.tensor_copy(HB[0:64, 0:1024], PB[0:64, 0:1024])
                    avp2 = PA[0:65, 1024 * j:1024 * j + 1024]
                    ins = v.tensor_tensor(
                        ATN[64 * (h % 2):64 * (h % 2) + 64, hp,
                            1024 * b:1024 * (b + 1)],
                        avp2[0:64, 0:1024], HB[0:64, 0:1024], OP.mult)
                    ins.then_inc(C.sem("dve"), 1)
                C.cnt["dve"] += 1

    mb_out = load_rows_bf(C, C.dr["L3b_out"], 0, D, BR[0:1, 0:1024],
                          wait_pe=C.val("pe"))

    def ev_o(e, oc, ps):
        return e.tensor_tensor(XB[:, oc, :], ps[:, :], S[:, oc, :], OP.add)

    fm_proj(C, C.dr["L3W_out"], lambda dc: ATN[:, dc, :], D, ev_o,
            evac_eng="vector", bias_row=BR[0:1, 0:1024], bias_mark=mb_out)

    postln_from_XB(C)




def emit_l3_ffn(C):
    block_barrier(C)
    sb = C.sb
    S, W, XB = sb["S"], sb["W"], sb["XB"]
    PA, PB = C.ps["PA"], C.ps["PB"]
    SM = sb["SM"]
    HB = sb["HB"]

    Wflat = W[:, :, :].rearrange("p c t -> p (c t)").bitcast(BF16)
    xrslot = Wflat[:, 0:8192].rearrange("p (a b) -> p a b", b=1024)
    xkslot = Wflat[:, 8192:16384].rearrange("p (a b) -> p a b", b=1024)
    kslot = Wflat[:, 16384:24576].rearrange("p (a b) -> p a b", b=1024)
    rslot = Wflat[:, 24576:32768].rearrange("p (a b) -> p a b", b=1024)

    PUB = SM[:, 0:16]
    GATH = SM[:, 16:144].rearrange("p (r j) -> p r j", j=16)
    BND = SM[:, 144:160]
    MUR = SM[:, 160:168]
    MUK = SM[:, 168:176]
    SELC = SM[:, 176:184]

    @C.V
    def _(v):
        for b in range(NB):
            for dc in range(DC):
                ins = v.tensor_copy(PUB[:, 8 * b + dc:8 * b + dc + 1],
                                    S[:, dc, 1024 * b + 1023:1024 * b + 1024])
        ins.then_inc(C.sem("dve"), 1)
    C.cnt["dve"] += 1
    pub_m = C.cnt["dve"]

    @C.G
    def _(g):
        ins = g.dma_start(out=MUR[:, :], in_=C.dr["L3f_mur"])
        C.inc(ins, "qg", 16)
        ins = g.dma_start(out=MUK[:, :], in_=C.dr["L3f_muk"])
        C.inc(ins, "qg", 16)
        ins = g.dma_start(out=SM[:, 176:184], in_=C.dr["L3f_sel"])
        C.inc(ins, "qg", 16)
    const_m = C.val("qg")

    @C.G
    def _(g):
        g.wait_ge(C.sem("dve"), pub_m)
        ins = g.dma_start(out=C.dr["cc7_in"], in_=PUB[:, :])
        C.inc(ins, "qgc", 16)
        g.wait_ge(C.sem("qgc"), C.val("qgc"))
        ins = g.collective_compute("AllGather", OP.bypass,
                                   replica_groups=[list(range(NCORES))],
                                   ins=[C.dr["cc7_in"].opt()],
                                   outs=[C.dr["cc7_out"].opt()])
        C.inc(ins, "cc", 1)
    ag_m = C.val("cc")

    @C.G
    def _(g):
        g.wait_ge(C.sem("cc"), ag_m)
        ins = g.dma_start(out=GATH[:, :, :],
                          in_=C.dr["cc7_out"].rearrange("(r p) j -> p r j", p=128))
        C.inc(ins, "qgc", 16)
    rb_m = C.val("qgc")

    @C.V
    def _(v):
        v.wait_ge(C.sem("qgc"), rb_m)
        v.wait_ge(C.sem("qg"), const_m)
        v.tensor_scalar(BND[:, :], GATH[:, 0, :], SELC[:, 0:1], None, OP.mult)
        for r in range(1, NCORES):
            ins = v.scalar_tensor_tensor(BND[:, :], GATH[:, r, :],
                                         SELC[:, r:r + 1], BND[:, :],
                                         OP.mult, OP.add)
        ins.then_inc(C.sem("dve"), 1)
    C.cnt["dve"] += 1
    bnd_m = C.cnt["dve"]

    for b in range(NB):
        bc = 1024 * b
        pe_b_gate = C.val("pe")
        evd_b_gate = C.val("evd")

        @C.V
        def _(v, b=b, pe_b_gate=pe_b_gate):
            if b == 0:
                v.wait_ge(C.sem("dve"), bnd_m)
            else:
                v.wait_ge(C.sem("pe"), pe_b_gate)
            for dc in range(DC):
                v.tensor_tensor(HB[:, 1:1024], S[:, dc, bc + 1:bc + 1024],
                                S[:, dc, bc:bc + 1023], OP.subtract)
                v.tensor_tensor(HB[:, 0:1], S[:, dc, bc:bc + 1],
                                BND[:, 8 * b + dc:8 * b + dc + 1], OP.subtract)
                v.scalar_tensor_tensor(xrslot[:, dc, 1:1024], HB[:, 1:1024],
                                       MUR[:, dc:dc + 1], S[:, dc, bc:bc + 1023],
                                       OP.mult, OP.add)
                v.scalar_tensor_tensor(xrslot[:, dc, 0:1], HB[:, 0:1],
                                       MUR[:, dc:dc + 1],
                                       BND[:, 8 * b + dc:8 * b + dc + 1],
                                       OP.mult, OP.add)
                v.scalar_tensor_tensor(xkslot[:, dc, 1:1024], HB[:, 1:1024],
                                       MUK[:, dc:dc + 1], S[:, dc, bc:bc + 1023],
                                       OP.mult, OP.add)
                ins = v.scalar_tensor_tensor(xkslot[:, dc, 0:1], HB[:, 0:1],
                                             MUK[:, dc:dc + 1],
                                             BND[:, 8 * b + dc:8 * b + dc + 1],
                                             OP.mult, OP.add)
            ins.then_inc(C.sem("dve"), 1)
        C.cnt["dve"] += 1
        xr_m = C.cnt["dve"]

        rk_wait = [("dve", xr_m)]
        if b == 1:
            rk_wait.append(("evd", evd_b_gate))

        fm_proj(C, C.dr["L3fW_r"], lambda dc: xrslot[:, dc, :], D,
                lambda e, oc, ps: e.activation(out=rslot[:, oc, :], in_=ps[:, :],
                                               func=AF.Sigmoid),
                evac_eng="scalar", tokens=(0, 1024), extra_wait=rk_wait)

        def k_evac(e, oc, ps):
            e.activation(out=HB[:, 1024:2048], in_=ps[:, :], func=AF.Relu)
            return e.activation(out=kslot[:, oc, :], in_=HB[:, 1024:2048],
                                func=AF.Square)

        fm_proj(C, C.dr["L3fW_k"], lambda dc: xkslot[:, dc, :], D, k_evac,
                evac_eng="scalar", tokens=(0, 1024), extra_wait=rk_wait)
        eva_rk = C.val("eva")

        def v_evac(e, oc, ps, b=b):
            e.tensor_tensor(HB[:, 0:1024], ps[:, :], rslot[:, oc, :], OP.mult)
            return e.tensor_tensor(XB[:, oc, 1024 * b:1024 * (b + 1)],
                                   HB[:, 0:1024],
                                   S[:, oc, 1024 * b:1024 * (b + 1)], OP.add)

        fm_proj(C, C.dr["L3fW_v"], lambda dc: kslot[:, dc, :], D, v_evac,
                evac_eng="vector", tokens=(0, 1024),
                extra_wait=[("eva", eva_rk)])

    postln_from_XB(C)


BLOCK_DECLS = {
    0: [("L0W_qkv", [D, 3 * D], F32), ("L0b_qkv", [3 * D], F32),
        ("L0W_g1", [D, 64], F32), ("L0b_g1", [64], F32),
        ("L0W_g2", [64, D], F32), ("L0b_g2", [D], F32),
        ("L0W_out", [D, D], F32), ("L0b_out", [D], F32),
        ("L0_ln_g", [D], F32), ("L0_ln_b", [D], F32)],
    1: [("L0fW_qkv", [D, 1536], F32), ("L0fW_proj", [D, D], F32),
        ("L0fW_rt", [D, 8], F32),
        ("L0f_w1", [8, D, 4096], F32), ("L0f_w2", [8, 4096, D], F32),
        ("L0f_ln1g", [D], F32), ("L0f_ln1b", [D], F32),
        ("L0f_ln2g", [D], F32), ("L0f_ln2b", [D], F32)],
    2: [("L1W_q", [D, D], F32), ("L1b_q", [D], F32),
        ("L1W_k", [D, D], F32), ("L1b_k", [D], F32),
        ("L1W_v", [D, D], F32), ("L1b_v", [D], F32),
        ("L1_gamma", [128, 1], F32), ("L1_dexp", [1, 8], F32),
        ("L1_mask", [1, 8], F32)],
    3: [("L1fW1", [D, 4096], F32), ("L1fb1", [4096], F32),
        ("L1fW2", [4096, D], F32), ("L1fb2", [D], F32)],
    4: [("L2W_q", [D, D], F32), ("L2W_k", [D, D], F32),
        ("L2W_v", [D, D], F32), ("L2W_o", [D, D], F32),
        ("L2_ropeP", [128, 128], F32), ("L2_cos", [128, 1024], F32),
        ("L2_sin", [128, 1024], F32)],
    5: [("L2fW_rt", [D, 8], F32),
        ("L2f_w1", [8, D, 4096], F32), ("L2f_w2", [8, 4096, D], F32)],
    6: [("L3W_qkv", [D, 3 * D], F32), ("L3b_qkv", [3 * D], F32),
        ("L3W_out", [D, D], F32), ("L3b_out", [D], F32),
        ("L3_ab", [128, 16], F32)],
    7: [("L3fW_r", [D, D], F32), ("L3fW_k", [D, D], F32),
        ("L3fW_v", [D, D], F32), ("L3f_mur", [128, 8], F32),
        ("L3f_muk", [128, 8], F32), ("L3f_sel", [128, 8], F32)],
}
BLOCK_EMITTERS = {0: emit_l0_attn, 1: emit_l0_ffn, 2: emit_l1_attn,
                  3: emit_l1_ffn, 4: emit_l2_attn, 5: emit_l2_ffn,
                  6: emit_l3_attn, 7: emit_l3_ffn}


def build_program(block_ids):
    es = ExitStack()
    nc = bass.Bass("TRN2", target_bir_lowering=False)
    dram = {}

    def din(name, shape, dtype=F32):
        dram[name] = nc.dram_tensor(name, list(shape), dtype, kind="ExternalInput").ap()

    def dint(name, shape, dtype=F32):
        dram[name] = nc.dram_tensor(name, list(shape), dtype).ap()

    din("xin", [D, T])
    dram["xout"] = nc.dram_tensor("xout", [D, T], F32, kind="ExternalOutput").ap()
    tap_shape = [2 * D, T] if DEBUG_TAP else [1, 1]
    dram["xtap"] = nc.dram_tensor("xtap", tap_shape, F32, kind="ExternalOutput").ap()
    for b in block_ids:
        for (name, shape, dt) in BLOCK_DECLS[b]:
            din(name, shape, dt)
    if 0 in block_ids:
        dint("cc0_in", [130, 1024])
        dint("cc0_out", [130, 1024])
    if 2 in block_ids:
        dint("cc1_in", [1024, 1032], BF16)
        dint("cc1_out", [NCORES * 1024, 1032], BF16)
    if 3 in block_ids:
        dint("qsp1", [D, 4096], BF16)
        dint("qsp2", [4096, D], BF16)
    if 4 in block_ids:
        dint("cc2_in", [4096, 1032], BF16)
        dint("cc2_out", [NCORES * 4096, 1032], BF16)
    if 7 in block_ids:
        dint("cc7_in", [128, 16], F32)
        dint("cc7_out", [NCORES * 128, 16], F32)
    if 6 in block_ids:
        dint("cc3_in", [4096, 1024], BF16)
        dint("cc3_out", [NCORES * 4096, 1024], BF16)

    with es:
        with nc.Block() as blk:
            C = Ctx(nc, blk, es)
            C.dr = dram

            def SB(name, shape, dtype):
                C.sb[name] = es.enter_context(nc.sbuf_tensor(name, list(shape), dtype))

            def PSUM(name, shape, dtype=F32):
                C.ps[name] = es.enter_context(nc.psum_tensor(name, list(shape), dtype))

            SB("S", [128, DC, T], F32)
            SB("W", [128, DC, T], F32)
            SB("XB", [128, DC, T], BF16)
            SB("WB0", [128, 4096], BF16)
            SB("WB1", [128, 4096], BF16)
            SB("SM", [128, 1024], F32)
            SB("SM2", [128, 2048], BF16)
            SB("HB", [128, 2048], BF16)
            SB("IDN", [128, 128], F32)
            SB("ONER", [1, 512], BF16)
            SB("ABR", [1, 4096], BF16)
            SB("BIASR", [1, 3072], BF16)
            SB("TB", [1, 128], BF16)
            SB("BONES", [128, 128], BF16)
            SB("BONEST", [16, 1024], BF16)
            SB("PIDX", [128, 1], F32)
            SB("ONE", [128, 8], BF16)
            SB("ONERF", [1, 128], F32)
            SB("ONEF", [128, 1], F32)
            SB("ZCOL", [128, 8], BF16)
            SB("EPSC", [1, 8], F32)
            SB("STAT", [128, 32], F32)
            PSUM("PA", [128, 2048], F32)
            PSUM("PB", [128, 2048], F32)

            emit_prelude(C)
            for b in block_ids:
                BLOCK_EMITTERS[b](C)
                if C.tapped:
                    break
            emit_epilogue(C)
    return nc


def _weight_maps(inputs, block_ids):
    t = lambda a: np.ascontiguousarray(np.asarray(a).T)
    m = {}
    if 1 in block_ids:
        wq = np.asarray(inputs["L0_ffn_qkv_weight"]).T
        perm = np.empty(1536, np.int64)
        for g in range(4):
            for j in range(4):
                perm[(4 * g + j) * 64:(4 * g + j + 1) * 64] = \
                    np.arange(g * 384 + j * 64, g * 384 + (j + 1) * 64)
            perm[1024 + g * 64:1024 + (g + 1) * 64] = \
                np.arange(g * 384 + 256, g * 384 + 320)
            perm[1280 + g * 64:1280 + (g + 1) * 64] = \
                np.arange(g * 384 + 320, g * 384 + 384)
        m["L0fW_qkv"] = np.ascontiguousarray(wq[:, perm])
        m["L0fW_proj"] = t(inputs["L0_ffn_proj_weight"])
        m["L0fW_rt"] = np.ascontiguousarray(np.asarray(inputs["L0_ffn_router_weight"]))
        m["L0f_w1"] = np.asarray(inputs["L0_ffn_moe_w1"])
        m["L0f_w2"] = np.asarray(inputs["L0_ffn_moe_w2"])
        m["L0f_ln1g"] = np.asarray(inputs["L0_ffn_ln1_weight"])
        m["L0f_ln1b"] = np.asarray(inputs["L0_ffn_ln1_bias"])
        m["L0f_ln2g"] = np.asarray(inputs["L0_ffn_ln2_weight"])
        m["L0f_ln2b"] = np.asarray(inputs["L0_ffn_ln2_bias"])
    if 2 in block_ids:
        m["L1W_q"] = t(inputs["L1_attn_Wq"])
        m["L1b_q"] = np.asarray(inputs["L1_attn_bq"])
        m["L1W_k"] = t(inputs["L1_attn_Wk"])
        m["L1b_k"] = np.asarray(inputs["L1_attn_bk"])
        m["L1W_v"] = t(inputs["L1_attn_Wv"])
        m["L1b_v"] = np.asarray(inputs["L1_attn_bv"])
        m["L1_gamma"] = np.broadcast_to(
            np.asarray(inputs["L1_attn_gamma"], dtype=np.float32).reshape(1, 1),
            (128, 1)).copy()
    if 4 in block_ids:
        m["L2W_q"] = np.ascontiguousarray(np.asarray(inputs["L2_attn_Wq"]))
        m["L2W_k"] = np.ascontiguousarray(np.asarray(inputs["L2_attn_Wk"]))
        m["L2W_v"] = np.ascontiguousarray(np.asarray(inputs["L2_attn_Wv"]))
        m["L2W_o"] = np.ascontiguousarray(np.asarray(inputs["L2_attn_Wo"]))
        P = np.zeros((128, 128), np.float32)
        for i in range(128):
            P[i ^ 1, i] = 1.0
        m["L2_ropeP"] = P
    if 6 in block_ids:
        m["L3W_qkv"] = np.ascontiguousarray(np.asarray(inputs["L3_attn_w_qkv"]))
        m["L3b_qkv"] = np.asarray(inputs["L3_attn_b_qkv"])
        m["L3W_out"] = np.ascontiguousarray(np.asarray(inputs["L3_attn_w_out"]))
        m["L3b_out"] = np.asarray(inputs["L3_attn_b_out"])
    if 7 in block_ids:
        m["L3fW_r"] = t(inputs["L3_ffn_W_r"])
        m["L3fW_k"] = t(inputs["L3_ffn_W_k"])
        m["L3fW_v"] = t(inputs["L3_ffn_W_v"])
        m["L3f_mur"] = np.ascontiguousarray(
            np.asarray(inputs["L3_ffn_mu_r"], np.float32).reshape(8, 128).T)
        m["L3f_muk"] = np.ascontiguousarray(
            np.asarray(inputs["L3_ffn_mu_k"], np.float32).reshape(8, 128).T)
    if 5 in block_ids:
        m["L2fW_rt"] = t(inputs["L2_ffn_router_w"])
        m["L2f_w1"] = np.ascontiguousarray(
            np.transpose(np.asarray(inputs["L2_ffn_w1"]), (0, 2, 1)))
        m["L2f_w2"] = np.ascontiguousarray(
            np.transpose(np.asarray(inputs["L2_ffn_w2"]), (0, 2, 1)))
    if 3 in block_ids:
        m["L1fW1"] = t(inputs["L1_ffn_W1"])
        m["L1fb1"] = np.asarray(inputs["L1_ffn_b1"])
        m["L1fW2"] = t(inputs["L1_ffn_W2"])
        m["L1fb2"] = np.asarray(inputs["L1_ffn_b2"])
    if 0 in block_ids:
        m["L0W_qkv"] = t(inputs["L0_attn_w_qkv"])
        m["L0b_qkv"] = np.asarray(inputs["L0_attn_b_qkv"])
        m["L0W_g1"] = t(inputs["L0_attn_w_g1"])
        m["L0b_g1"] = np.asarray(inputs["L0_attn_b_g1"])
        m["L0W_g2"] = t(inputs["L0_attn_w_g2"])
        m["L0b_g2"] = np.asarray(inputs["L0_attn_b_g2"])
        m["L0W_out"] = t(inputs["L0_attn_w_out"])
        m["L0b_out"] = np.asarray(inputs["L0_attn_b_out"])
        m["L0_ln_g"] = np.asarray(inputs["L0_attn_ln_g"])
        m["L0_ln_b"] = np.asarray(inputs["L0_attn_ln_b"])
    return m


def shard_stream(x_full):
    out = []
    for c in range(NCORES):
        sl = x_full[:, c * L:(c + 1) * L, :]
        fm = np.transpose(sl, (2, 0, 1)).reshape(D, T)
        out.append(np.ascontiguousarray(fm, dtype=np.float32))
    return out


def unshard_stream(cores):
    x = np.empty((NB, SEQ, D), dtype=np.float32)
    for c in range(NCORES):
        fm = np.asarray(cores[c]).reshape(D, NB, L)
        x[:, c * L:(c + 1) * L, :] = np.transpose(fm, (1, 2, 0))
    return x


def _rope_tables(c):
    pos = np.arange(c * L, (c + 1) * L, dtype=np.float64)
    i = np.arange(32, dtype=np.float64)
    coss, sins = [], []
    for s in (1.0, 4.0, 16.0):
        freqs = 1.0 / (10000.0 * s) ** (2.0 * i / 64.0)
        ang = pos[:, None] * freqs[None, :]
        coss.append(np.cos(ang))
        sins.append(np.sin(ang))
    cos = np.mean(coss, 0)
    sin = np.mean(sins, 0)
    CC = np.zeros((128, L), np.float32)
    SN = np.zeros((128, L), np.float32)
    for p in range(128):
        ii = (p % 64) // 2
        CC[p] = cos[:, ii]
        SN[p] = sin[:, ii] * (-1.0 if p % 2 == 0 else 1.0)
    return CC, SN


def _percore_maps(block_ids, c):
    m = {}
    if 4 in block_ids:
        CC, SN = _rope_tables(c)
        m["L2_cos"] = CC
        m["L2_sin"] = SN
    if 7 in block_ids:
        sel = np.zeros((128, 8), np.float32)
        if c > 0:
            sel[:, c - 1] = 1.0
        m["L3f_sel"] = sel
    if 6 in block_ids:
        ab = np.zeros((128, 16), np.float32)
        for r in range(8):
            if r < c:
                ab[:, 2 * r] = 1.0
            elif r == c:
                ab[:, 2 * r + 1] = 1.0
        m["L3_ab"] = ab
    if 2 in block_ids:
        dexp = np.zeros((1, 8), np.float32)
        mask = np.zeros((1, 8), np.float32)
        for k in range(8):
            if k < c:
                dexp[0, k] = 1024.0 * (c - 1 - k)
                mask[0, k] = 1.0
        m["L1_dexp"] = dexp
        m["L1_mask"] = mask
    return m


def run_blocks(block_ids, stream_percore, inputs):
    nc = build_program(block_ids)
    wmap = _weight_maps(inputs, block_ids)
    in_maps = [dict(wmap, xin=stream_percore[c], **_percore_maps(block_ids, c))
               for c in range(NCORES)]
    res = run_bass_kernel_spmd(nc, in_maps, core_ids=list(range(NCORES)))
    return [res.results[c]["xout"] for c in range(NCORES)]


def kernel(**inputs):
    streams = shard_stream(np.asarray(inputs["L0_attn_x"], dtype=np.float32))
    streams = run_blocks(list(range(8)), streams, inputs)
    return unshard_stream(streams)


# version 03 (final)
# speedup vs baseline: 1.0190x; 1.0166x over previous
import numpy as np
from contextlib import ExitStack

import concourse.bass as bass
import concourse.mybir as mybir
from concourse.bass_utils import run_bass_kernel_spmd

F32 = mybir.dt.float32
BF16 = mybir.dt.bfloat16
I32 = mybir.dt.int32
AF = mybir.ActivationFunctionType
OP = mybir.AluOpType
AX = mybir.AxisListType

NCORES = 8
NB = 2
SEQ = 8192
D = 1024
L = SEQ // NCORES
T = NB * L
DC = D // 128
TC = T // 128
NTS = T // 512
EPS_LN = 1e-5

ESEM = {"scalar": "eva", "vector": "evd", "gpsimd": "evp"}

DEBUG_TAP = None
DEBUG_TRUNC = 8
DEBUG_NOAG = False
DEBUG_AGOPS = 6
DEBUG_L3PAIRS = 99
DEBUG_L3NOMASK = False
DEBUG_L3MASKLVL = 4


def tap(C, stage, ap_fn):
    if DEBUG_TAP != stage:
        return
    C.tapped = True
    for s in ("dve", "evd", "eva", "act", "pool"):
        C.val(s)
    marks = {s: C.cnt[s] for s in ("dve", "evd", "eva", "act", "pool")}
    pe_m = C.val("pe")

    @C.G
    def _(g):
        for s, v in marks.items():
            if v:
                g.wait_ge(C.sem(s), v)
        g.wait_ge(C.sem("pe"), pe_m)
        for dst, src_ap in ap_fn():
            ins = g.dma_start(out=dst, in_=src_ap)
            C.inc(ins, "qg", 16)
        g.wait_ge(C.sem("qg"), C.val("qg"))


class Ctx:
    def __init__(self, nc, blk, es):
        self.nc, self.blk, self.es = nc, blk, es
        self._sems, self.cnt = {}, {}
        self.sb, self.dr, self.ps = {}, {}, {}
        self.wb_last_use = [0, 0]
        self.wb_idx = 0
        self.tapped = False

    def sem(self, name):
        if name not in self._sems:
            self._sems[name] = self.es.enter_context(self.nc.semaphore(name))
            self.cnt[name] = 0
        return self._sems[name]

    def inc(self, ins, name, by=1):
        ins.then_inc(self.sem(name), by)
        self.cnt[name] += by
        return self.cnt[name]

    def val(self, name):
        self.sem(name)
        return self.cnt[name]

    def wait(self, eng, name, v=None):
        eng.wait_ge(self.sem(name), self.cnt[name] if v is None else v)

    @property
    def T(self):
        return self.blk.tensor

    @property
    def V(self):
        return self.blk.vector

    @property
    def A(self):
        return self.blk.scalar

    @property
    def G(self):
        return self.blk.gpsimd

    @property
    def Y(self):
        return self.blk.sync


def esec(C, eng):
    return {"vector": C.V, "scalar": C.A, "gpsimd": C.G}[eng]


def block_barrier(C):
    marks = {s: C.val(s) for s in ("pe", "dve", "evd", "eva", "act", "pool")}
    owns = [(C.T, "pe"), (C.V, None), (C.A, None), (C.G, None)]

    for sec, own in owns:
        @sec
        def _(e, own=own):
            for s, v in marks.items():
                if s != own and v:
                    e.wait_ge(C.sem(s), v)
    return marks


def psum_guard(C, t):
    for s in ("eva", "evd", "evp"):
        if C.val(s) > 0:
            t.wait_ge(C.sem(s), C.cnt[s])


def run_pipe(C, groups, bufs, guard=True):
    n = len(groups)
    nb = len(bufs)
    base = {s: C.val(s) for s in ("eva", "evd", "evp")}
    run = dict(base)
    marks = []
    for g in groups:
        s = ESEM[g["eng"]]
        run[s] += 1
        marks.append((s, run[s]))
    pe0 = C.val("pe")

    @C.T
    def _(t):
        if guard:
            psum_guard(C, t)
        for i, g in enumerate(groups):
            for (sn, v) in g.get("wait", ()):
                t.wait_ge(C.sem(sn), v)
            if i >= nb:
                sn, v = marks[i - nb]
                t.wait_ge(C.sem(sn), v)
            ins = g["mm"](t, bufs[i % nb])
            ins.then_inc(C.sem("pe"), 1)
    C.cnt["pe"] += n
    for i, g in enumerate(groups):
        g["_pe_mark"] = pe0 + i + 1

    for eng in ("scalar", "vector", "gpsimd"):
        idxs = [i for i, g in enumerate(groups) if g["eng"] == eng]
        if not idxs:
            continue

        @esec(C, eng)
        def _(e, idxs=idxs):
            for i in idxs:
                g = groups[i]
                e.wait_ge(C.sem("pe"), g["_pe_mark"])
                ins = g["evac"](e, bufs[i % nb])
                ins.then_inc(C.sem(ESEM[eng]), 1)
    for s in ("eva", "evd", "evp"):
        C.cnt[s] = run[s]


def wslab_load(C, w_hbm, og, ocols, wait_pe=None):
    sb = C.sb
    i = C.wb_idx
    C.wb_idx = 1 - C.wb_idx
    buf = sb["WB0"] if i == 0 else sb["WB1"]
    din = w_hbm.shape[0]
    wp = C.wb_last_use[i] if wait_pe is None else max(wait_pe, C.wb_last_use[i])
    if din >= 128:
        nkc = din // 128
        assert nkc * ocols <= 4096
        dst = buf[:, 0:nkc * ocols].rearrange("p (c o) -> p c o", o=ocols)
        src = w_hbm[:, og:og + ocols].rearrange("(c p) o -> p c o", p=128)
    else:
        dst = buf[0:din, 0:ocols].rearrange("p (c o) -> p c o", o=ocols)
        src = w_hbm[:, og:og + ocols].rearrange("p (c o) -> p c o", o=ocols)

    qsem = "qga" if i == 0 else "qgb"

    @C.G
    def _(g):
        if wp > 0:
            g.wait_ge(C.sem("pe"), wp)
        ins = g.dma_start(out=dst, in_=src)
        C.inc(ins, qsem, 16)
    return dst, (qsem, C.val(qsem)), i


def load_rows_bf(C, dram_1d, lo, hi, row_ap, wait_pe=0):
    @C.G
    def _(g):
        if wait_pe:
            g.wait_ge(C.sem("pe"), wait_pe)
        ins = g.dma_start(out=row_ap,
                          in_=dram_1d[lo:hi].rearrange("(o n) -> o n", o=1))
        C.inc(ins, "qgr", 16)
    return C.val("qgr")


def fm_proj(C, w_hbm, src, dout, evac_fn, evac_eng="scalar", bias_row=None,
            tokens=None, bias_mark=None, extra_wait=None):
    sb = C.sb
    din = w_hbm.shape[0]
    nkc = max(1, din // 128)
    t0, t1 = (0, T) if tokens is None else tokens
    tlen = t1 - t0
    nts = tlen // 512
    ones_row = sb["ONER"][0:1, :]
    groups = []
    slabw = min(512, max(128, (4096 // nkc) // 128 * 128))
    nslab = (dout + slabw - 1) // slabw
    pe_entry = C.val("pe")
    cum = [0]
    for s in range(nslab):
        cum.append(cum[-1] + min(slabw, dout - slabw * s) // 128)
    for s in range(nslab):
        og = slabw * s
        ocols = min(slabw, dout - og)
        wp = pe_entry + cum[s - 1] if s >= 2 else None
        ap, mark, bi = wslab_load(C, w_hbm, og, ocols, wait_pe=wp)
        for j in range(ocols // 128):
            oc = og // 128 + j

            def mm(t, ps, ap=ap, j=j, oc=oc):
                for ts in range(nts):
                    for dc in range(nkc):
                        ins = t.matmul(ps[:, 512 * ts:512 * (ts + 1)],
                                       ap[:, dc, 128 * j:128 * (j + 1)],
                                       src(dc)[:, t0 + 512 * ts:t0 + 512 * (ts + 1)],
                                       start=(dc == 0),
                                       stop=(dc == nkc - 1 and bias_row is None))
                    if bias_row is not None:
                        ins = t.matmul(ps[:, 512 * ts:512 * (ts + 1)],
                                       bias_row[:, 128 * oc:128 * (oc + 1)],
                                       ones_row[:, 0:512],
                                       start=False, stop=True)
                return ins

            w = [mark]
            if bias_mark is not None:
                w.append(("qgr", bias_mark))
            if extra_wait:
                w.extend(extra_wait)
            groups.append({"mm": mm,
                           "evac": (lambda e, ps, oc=oc: evac_fn(e, oc, ps)),
                           "eng": evac_eng,
                           "wait": w})
    run_pipe(C, groups, [C.ps["PA"][:, 0:tlen], C.ps["PB"][:, 0:tlen]])
    C.wb_last_use[0] = C.wb_last_use[1] = C.cnt["pe"]


def ln_fm(C, src_bf, emit_apply, gain_row=None, bias_row=None, rows_qgr=None):
    sb = C.sb
    PA, PB = C.ps["PA"], C.ps["PB"]
    ones_col = sb["ONE"][:, 0:1]
    ones_row = sb["ONER"][0:1, :]
    HBs = [sb["HB"]]

    dve0 = C.val("dve")
    pe_base = C.val("pe")

    @C.V
    def _(v):
        for dc in range(DC):
            v.wait_ge(C.sem("pe"), pe_base + dc)
            ins = v.tensor_tensor(HBs[0][:, :], src_bf(dc), src_bf(dc), OP.mult)
            ins.then_inc(C.sem("dve"), 1)
    C.cnt["dve"] += DC

    @C.T
    def _(t):
        psum_guard(C, t)
        for dc in range(DC):
            t.wait_ge(C.sem("dve"), dve0 + dc + 1)
            for ts in range(NTS):
                t.matmul(PB[0:1, 512 * ts:512 * (ts + 1)], ones_col,
                         src_bf(dc)[:, 512 * ts:512 * (ts + 1)],
                         start=(dc == 0), stop=(dc == DC - 1), skip_group_check=True)
                ins = t.matmul(PB[64:65, 512 * ts:512 * (ts + 1)], ones_col,
                               HBs[0][:, 512 * ts:512 * (ts + 1)],
                               start=(dc == 0), stop=(dc == DC - 1), skip_group_check=True)
            ins.then_inc(C.sem("pe"), 1)
    C.cnt["pe"] += DC
    stats_mark = C.cnt["pe"]

    mu = PB[32:33, 0:T]
    var = PA[0:1, 0:T]
    arow_bf = sb["ABR"][0:1, 0:T]
    brow_bf = sb["ABR"][0:1, T:2 * T]

    @C.V
    def _(v):
        v.wait_ge(C.sem("pe"), stats_mark)
        v.tensor_scalar(mu, PB[0:1, 0:T], 1.0 / D, None, OP.mult)
        ins = v.tensor_scalar(var, PB[64:65, 0:T], 1.0 / D, None, OP.mult)
        ins.then_inc(C.sem("dve"), 1)
    C.cnt["dve"] += 1
    mom_mark = C.cnt["dve"]

    @C.A
    def _(a):
        a.wait_ge(C.sem("dve"), mom_mark)
        ins = a.activation(out=brow_bf, in_=mu, func=AF.Square)
        ins.then_inc(C.sem("act"), 1)
    C.cnt["act"] += 1
    m2_mark = C.cnt["act"]

    @C.V
    def _(v):
        v.wait_ge(C.sem("act"), m2_mark)
        ins = v.tensor_tensor(var, var, brow_bf, OP.subtract)
        ins.then_inc(C.sem("dve"), 1)
    C.cnt["dve"] += 1
    var_mark = C.cnt["dve"]

    @C.A
    def _(a):
        a.wait_ge(C.sem("dve"), var_mark)
        a.activation(out=var, in_=var, func=AF.Ln, bias=sb["EPSC"][0:1, 0:1])
        ins = a.activation(out=arow_bf, in_=var, func=AF.Exp, scale=-0.5)
        ins.then_inc(C.sem("act"), 1)
    C.cnt["act"] += 1
    a_mark = C.cnt["act"]

    @C.V
    def _(v):
        v.wait_ge(C.sem("act"), a_mark)
        ins = v.scalar_tensor_tensor(brow_bf, mu, -1.0, arow_bf, OP.mult, OP.mult)
        ins.then_inc(C.sem("dve"), 1)
    C.cnt["dve"] += 1
    rows_mark = C.cnt["dve"]

    groups = []
    for dc in range(DC):
        def mm(t, ps, dc=dc):
            if gain_row is not None:
                ga = gain_row[:, 128 * dc:128 * (dc + 1)]
            else:
                ga = ones_row[:, 0:128]
            for ts in range(NTS):
                t.matmul(PA[:, 512 * ts:512 * (ts + 1)], ga,
                         arow_bf[:, 512 * ts:512 * (ts + 1)], start=True, stop=True)
            for ts in range(NTS):
                if bias_row is not None:
                    t.matmul(PB[:, 512 * ts:512 * (ts + 1)], ga,
                             brow_bf[:, 512 * ts:512 * (ts + 1)], start=True, stop=False)
                    ins = t.matmul(PB[:, 512 * ts:512 * (ts + 1)],
                                   bias_row[:, 128 * dc:128 * (dc + 1)],
                                   ones_row[:, 0:512], start=False, stop=True)
                else:
                    ins = t.matmul(PB[:, 512 * ts:512 * (ts + 1)], ga,
                                   brow_bf[:, 512 * ts:512 * (ts + 1)], start=True, stop=True)
            return ins

        w = [("dve", rows_mark)]
        if rows_qgr is not None:
            w.append(("qgr", rows_qgr))
        groups.append({"mm": mm,
                       "evac": (lambda e, ps, dc=dc: emit_apply(e, dc, PA, PB)),
                       "eng": "vector",
                       "wait": w})
    run_pipe(C, groups, [PA], guard=False)


def postln_from_W(C):
    sb = C.sb
    S, W, XB = sb["S"], sb["W"], sb["XB"]

    @C.V
    def _(v):
        for dc in range(DC):
            ins = v.tensor_copy(XB[:, dc, :], W[:, dc, :])
            ins.then_inc(C.sem("dve"), 1)
    C.cnt["dve"] += DC

    def apply(v, dc, a_ps, b_ps):
        v.tensor_tensor(S[:, dc, :], W[:, dc, :], a_ps[:, :], OP.mult)
        v.tensor_tensor(S[:, dc, :], S[:, dc, :], b_ps[:, :], OP.add)
        return v.tensor_copy(XB[:, dc, :], S[:, dc, :])

    ln_fm(C, lambda dc: XB[:, dc, :], apply)


def emit_prelude(C):
    sb = C.sb

    @C.Y
    def _(s):
        ins = s.dma_start(out=sb["S"][:, :, :],
                          in_=C.dr["xin"].rearrange("(c p) t -> p c t", p=128))
        C.inc(ins, "qs", 16)

    @C.V
    def _(v):
        v.memset(sb["ONE"][:, :], 1.0)
        v.memset(sb["ONER"][:, :], 1.0)
        v.memset(sb["ONERF"][:, :], 1.0)
        v.memset(sb["ONEF"][:, :], 1.0)
        v.memset(sb["EPSC"][:, :], EPS_LN)
        v.memset(sb["IDN"][:, :], 1.0)
        C.wait(v, "qs")
        for dc in range(DC):
            ins = v.tensor_copy(sb["XB"][:, dc, :], sb["S"][:, dc, :])
        C.inc(ins, "dve")
    m = C.val("dve")

    @C.G
    def _(g):
        g.wait_ge(C.sem("dve"), m)
        g.affine_select(sb["IDN"][:, :], sb["IDN"][:, :], [[-1, 128]],
                        OP.is_equal, 0.0, base=0, channel_multiplier=1)
        ins = g.iota(sb["PIDX"][:, :], [[0, 1]], base=0, channel_multiplier=1,
                     allow_small_or_imprecise_dtypes=True)
        C.inc(ins, "pool")


def emit_epilogue(C):
    sb = C.sb
    m_dve = C.val("dve")
    m_evd = C.val("evd")
    m_eva = C.val("eva")

    @C.G
    def _(g):
        g.wait_ge(C.sem("dve"), m_dve)
        if m_evd:
            g.wait_ge(C.sem("evd"), m_evd)
        if m_eva:
            g.wait_ge(C.sem("eva"), m_eva)
        ins = g.dma_start(out=C.dr["xout"].rearrange("(c p) t -> p c t", p=128),
                          in_=sb["S"][:, :, :])
        C.inc(ins, "qo", 16)
        g.wait_ge(C.sem("qo"), C.val("qo"))


def emit_l0_attn(C):
    block_barrier(C)
    sb = C.sb
    S, W, XB = sb["S"], sb["W"], sb["XB"]
    PA, PB = C.ps["PA"], C.ps["PB"]
    ones_row = sb["ONER"][0:1, :]
    ones_col = sb["ONE"][:, 0:1]
    BR = sb["BIASR"]

    Wflat = W[:, :, :].rearrange("p c t -> p (c t)").bitcast(BF16)
    expk = Wflat[:, 0:16384].rearrange("p (a b) -> p a b", b=1024)
    vtm = Wflat[:, 16384:32768].rearrange("p (a b) -> p a b", b=1024)
    qs = Wflat[:, 0:16384].rearrange("p (a b) -> p a b", b=2048)
    gate = Wflat[:, 16384:32768].rearrange("p (a b) -> p a b", b=2048)

    load_rows_bf(C, C.dr["L0b_qkv"], D, 3 * D, BR[0:1, 0:2048])
    load_rows_bf(C, C.dr["L0b_qkv"], 0, D, BR[0:1, 2048:3072])
    rows0 = load_rows_bf(C, C.dr["L0b_g1"], 0, 64, sb["TB"][0:1, 0:64])

    wq = C.dr["L0W_qkv"]
    pe_entry = C.val("pe")
    groups = []
    for s in range(4):
        og = D + 512 * s
        wp = pe_entry + 16 * (s - 1) if s >= 2 else None
        ap, mark, bi = wslab_load(C, wq, og, 512, wait_pe=wp)
        is_k = s < 2
        for tc in range(TC):
            def mm(t, ps, ap=ap, tc=tc, s=s):
                for dc in range(DC):
                    t.matmul(ps[:, 0:512], XB[:, dc, 128 * tc:128 * (tc + 1)],
                             ap[:, dc, :], start=(dc == 0), stop=False)
                return t.matmul(ps[:, 0:512], ones_row[:, 0:128],
                                BR[0:1, 512 * s:512 * (s + 1)], start=False, stop=True)

            def evac(e, ps, s=s, tc=tc, is_k=is_k):
                if is_k:
                    return e.activation(out=expk[:, tc, 512 * s:512 * (s + 1)],
                                        in_=ps[:, 0:512], func=AF.Exp)
                return e.tensor_copy(vtm[:, tc, 512 * (s - 2):512 * (s - 1)],
                                     ps[:, 0:512])

            groups.append({"mm": mm, "evac": evac,
                           "eng": "scalar" if is_k else "vector",
                           "wait": [mark, ("qgr", rows0)]})
    bufs = [PA[:, 0:512], PA[:, 512:1024], PB[:, 0:512], PB[:, 512:1024]]
    run_pipe(C, groups, bufs)
    C.wb_last_use[0] = C.wb_last_use[1] = C.cnt["pe"]
    phaseA_end = C.cnt["pe"]

    @C.T
    def _(t):
        psum_guard(C, t)
        for b in range(NB):
            for h in range(16):
                r0 = 64 * (h % 2)
                c0 = 64 * (8 * b + h // 2)
                for j in range(8):
                    tc = 8 * b + j
                    ins = t.matmul(PA[r0:r0 + 64, c0:c0 + 64],
                                   expk[:, tc, 64 * h:64 * (h + 1)],
                                   vtm[:, tc, 64 * h:64 * (h + 1)],
                                   start=(j == 0), stop=(j == 7),
                                   skip_group_check=True)
        ins.then_inc(C.sem("pe"), 1)
        for b in range(NB):
            row = PB[64 * b:64 * b + 1, :]
            for j in range(8):
                tc = 8 * b + j
                for ts in range(2):
                    ins = t.matmul(row[:, 512 * ts:512 * (ts + 1)], ones_col,
                                   expk[:, tc, 512 * ts:512 * (ts + 1)],
                                   start=(j == 0), stop=(j == 7),
                                   skip_group_check=True)
        ins.then_inc(C.sem("pe"), 1)
    C.cnt["pe"] += 2
    kv_mark = C.val("pe")

    SM2F = sb["SM2"][:, :].bitcast(F32)
    HBF = sb["HB"][:, :].bitcast(F32)

    @C.V
    def _(v):
        v.wait_ge(C.sem("pe"), kv_mark)
        v.tensor_copy(sb["SM"][:, 0:1024], PA[:, 0:1024])
        v.tensor_copy(SM2F[0:1, 0:1024], PB[0:1, 0:1024])
        ins = v.tensor_copy(HBF[0:1, 0:1024], PB[64:65, 0:1024])
        ins.then_inc(C.sem("evd"), 1)
    C.cnt["evd"] += 1
    kv_evac_mark = C.val("evd")

    @C.G
    def _(g):
        g.wait_ge(C.sem("evd"), kv_evac_mark)
        ins = g.dma_start(out=C.dr["cc0_in"][0:128, :], in_=sb["SM"][:, 0:1024])
        C.inc(ins, "qgc", 16)
        ins = g.dma_start(out=C.dr["cc0_in"][128:129, :],
                          in_=SM2F[0:1, 0:1024])
        C.inc(ins, "qgc", 16)
        ins = g.dma_start(out=C.dr["cc0_in"][129:130, :],
                          in_=HBF[0:1, 0:1024])
        C.inc(ins, "qgc", 16)
        g.wait_ge(C.sem("qgc"), C.val("qgc"))
        ins = g.collective_compute("AllReduce", OP.add,
                                   replica_groups=[list(range(NCORES))],
                                   ins=[C.dr["cc0_in"].opt()],
                                   outs=[C.dr["cc0_out"].opt()])
        C.inc(ins, "cc", 1)
    cc_mark = C.val("cc")

    load_rows_bf(C, C.dr["L0b_g2"], 0, D, BR[0:1, 0:1024],
                 wait_pe=phaseA_end)
    rowsC = load_rows_bf(C, C.dr["L0_ln_g"], 0, D, BR[0:1, 1024:2048],
                         wait_pe=phaseA_end)

    def q_evac(e, oc, ps):
        return e.activation(out=qs[:, oc, :], in_=ps[:, :], func=AF.Silu)

    fm_proj(C, C.dr["L0W_qkv"], lambda dc: XB[:, dc, :], D, q_evac,
            evac_eng="scalar", bias_row=BR[0:1, 2048:3072], bias_mark=rows0)
    q_end = C.cnt["pe"]

    ap1, mark1, bi1 = wslab_load(C, C.dr["L0W_g1"], 0, 64)

    def gh_mm(t, ps):
        for ts in range(NTS):
            for dc in range(DC):
                t.matmul(ps[0:64, 512 * ts:512 * (ts + 1)], ap1[:, dc, :],
                         XB[:, dc, 512 * ts:512 * (ts + 1)],
                         start=(dc == 0), stop=False)
            ins = t.matmul(ps[0:64, 512 * ts:512 * (ts + 1)], sb["TB"][0:1, 0:64],
                           ones_row[:, 0:512], start=False, stop=True)
        return ins

    def gh_evac(e, ps):
        return e.tensor_copy(sb["SM2"][0:64, 0:T], ps[0:64, :])

    run_pipe(C, [{"mm": gh_mm, "evac": gh_evac, "eng": "vector",
                  "wait": [mark1, ("qgr", rows0)]}], [PA])
    C.wb_last_use[bi1] = C.cnt["pe"]
    gh_end = C.cnt["pe"]
    gh_mark = C.cnt["evd"]

    ap2, mark2, bi2 = wslab_load(C, C.dr["L0W_g2"], 0, 512)
    ap2b, mark2b, bi2b = wslab_load(C, C.dr["L0W_g2"], 512, 512)
    groups = []
    for oc in range(8):
        ap = ap2 if oc < 4 else ap2b
        j = oc % 4
        mk = mark2 if oc < 4 else mark2b

        def mm(t, ps, ap=ap, j=j, oc=oc):
            for ts in range(NTS):
                t.matmul(ps[:, 512 * ts:512 * (ts + 1)],
                         ap[0:64, 0, 128 * j:128 * (j + 1)],
                         sb["SM2"][0:64, 512 * ts:512 * (ts + 1)],
                         start=True, stop=False)
                ins = t.matmul(ps[:, 512 * ts:512 * (ts + 1)],
                               BR[0:1, 128 * oc:128 * (oc + 1)],
                               ones_row[:, 0:512], start=False, stop=True)
            return ins

        def evac(e, ps, oc=oc):
            return e.activation(out=gate[:, oc, :], in_=ps[:, :], func=AF.Sigmoid)

        groups.append({"mm": mm, "evac": evac, "eng": "scalar",
                       "wait": [mk, ("qgr", rowsC), ("evd", gh_mark)]})
    run_pipe(C, groups, [PA, PB])
    C.wb_last_use[bi2] = C.wb_last_use[bi2b] = C.cnt["pe"]

    gates_end = C.cnt["pe"]
    load_rows_bf(C, C.dr["L0_ln_b"], 0, D, BR[0:1, 2048:3072],
                 wait_pe=gates_end)
    rowsD = load_rows_bf(C, C.dr["L0b_out"], 0, D, BR[0:1, 0:1024],
                         wait_pe=gates_end)

    @C.G
    def _(g):
        g.wait_ge(C.sem("cc"), cc_mark)
        ins = g.dma_start(out=sb["SM"][:, 0:1024], in_=C.dr["cc0_out"][0:128, :])
        C.inc(ins, "qgc", 16)
        ins = g.dma_start(out=SM2F[0:1, 0:1024],
                          in_=C.dr["cc0_out"][128:129, :])
        C.inc(ins, "qgc", 16)
        ins = g.dma_start(out=HBF[0:1, 0:1024],
                          in_=C.dr["cc0_out"][129:130, :])
        C.inc(ins, "qgc", 16)
    cc_done = C.val("qgc")

    @C.T
    def _(t):
        psum_guard(C, t)
        t.wait_ge(C.sem("qgc"), cc_done)
        for dc in range(DC):
            for b in range(NB):
                srow = SM2F if b == 0 else HBF
                ins = t.matmul(PA[:, (2 * dc + b):(2 * dc + b) + 1],
                               srow[0:1, 128 * dc:128 * (dc + 1)],
                               sb["IDN"][0:1, 0:1], start=True, stop=True)
        ins.then_inc(C.sem("pe"), 1)
    C.cnt["pe"] += 1
    tr_mark = C.val("pe")

    @C.V
    def _(v):
        v.wait_ge(C.sem("pe"), tr_mark)
        ins = v.reciprocal(sb["STAT"][:, 0:16], PA[:, 0:16])
        ins.then_inc(C.sem("evd"), 1)
    C.cnt["evd"] += 1
    recip_mark = C.val("evd")

    @C.V
    def _(v):
        v.wait_ge(C.sem("evd"), recip_mark)
        for dc in range(DC):
            for b in range(NB):
                ins = v.tensor_tensor(
                    qs[:, dc, 1024 * b:1024 * (b + 1)],
                    qs[:, dc, 1024 * b:1024 * (b + 1)],
                    sb["STAT"][:, 2 * dc + b:2 * dc + b + 1].broadcast_to([128, 1024]),
                    OP.mult)
        ins.then_inc(C.sem("dve"), 1)
    C.cnt["dve"] += 1
    qscale_mark = C.val("dve")

    @C.V
    def _(v):
        v.memset(sb["SM2"][:, :], 0.0)
        for b in range(NB):
            for hp in range(8):
                c0 = 64 * (8 * b + hp)
                d0 = 128 * (8 * b + hp)
                v.tensor_copy(sb["SM2"][0:64, d0:d0 + 64],
                              sb["SM"][0:64, c0:c0 + 64])
                ins = v.tensor_copy(sb["SM2"][64:128, d0 + 64:d0 + 128],
                                    sb["SM"][64:128, c0:c0 + 64])
        ins.then_inc(C.sem("dve"), 1)
    C.cnt["dve"] += 1
    kvd_mark = C.val("dve")

    groups = []
    for b in range(NB):
        for hp in range(8):
            d0 = 128 * (8 * b + hp)

            def mm(t, ps, b=b, hp=hp, d0=d0):
                for ts in range(2):
                    ins = t.matmul(ps[:, 512 * ts:512 * (ts + 1)],
                                   sb["SM2"][:, d0:d0 + 128],
                                   qs[:, hp, 1024 * b + 512 * ts:1024 * b + 512 * (ts + 1)],
                                   start=True, stop=True)
                return ins

            def evac(e, ps, b=b, hp=hp):
                return e.tensor_copy(XB[:, hp, 1024 * b:1024 * (b + 1)],
                                     ps[:, 0:1024])

            groups.append({"mm": mm, "evac": evac, "eng": "vector",
                           "wait": [("dve", max(qscale_mark, kvd_mark))]})
    run_pipe(C, groups, [PA[:, 0:1024], PB[:, 0:1024]])

    def apply(v, dc, a_ps, b_ps):
        v.tensor_tensor(sb["HB"][:, :], XB[:, dc, :], a_ps[:, :], OP.mult)
        v.tensor_tensor(sb["HB"][:, :], sb["HB"][:, :], b_ps[:, :], OP.add)
        return v.tensor_tensor(XB[:, dc, :], sb["HB"][:, :], gate[:, dc, :], OP.mult)

    ln_fm(C, lambda dc: XB[:, dc, :], apply,
          gain_row=BR[0:1, 1024:2048], bias_row=BR[0:1, 2048:3072],
          rows_qgr=rowsD)

    def oevac(e, oc, ps):
        return e.tensor_tensor(W[:, oc, :], ps[:, :], S[:, oc, :], OP.add)

    fm_proj(C, C.dr["L0W_out"], lambda dc: XB[:, dc, :], D, oevac,
            evac_eng="vector", bias_row=BR[0:1, 0:1024], bias_mark=rowsD)

    postln_from_W(C)


def affine_ln_W_to_XB(C, gain_row, bias_row, rows_qgr=None):
    sb = C.sb
    W, XB = sb["W"], sb["XB"]

    @C.V
    def _(v):
        for dc in range(DC):
            ins = v.tensor_copy(XB[:, dc, :], W[:, dc, :])
            ins.then_inc(C.sem("dve"), 1)
    C.cnt["dve"] += DC

    def apply(v, dc, a_ps, b_ps):
        v.tensor_tensor(sb["HB"][:, :], XB[:, dc, :], a_ps[:, :], OP.mult)
        return v.tensor_tensor(XB[:, dc, :], sb["HB"][:, :], b_ps[:, :], OP.add)

    ln_fm(C, lambda dc: XB[:, dc, :], apply, gain_row=gain_row,
          bias_row=bias_row, rows_qgr=rows_qgr)


def postln_from_S(C):
    sb = C.sb
    S, XB = sb["S"], sb["XB"]

    @C.V
    def _(v):
        for dc in range(DC):
            ins = v.tensor_copy(XB[:, dc, :], S[:, dc, :])
            ins.then_inc(C.sem("dve"), 1)
    C.cnt["dve"] += DC

    def apply(v, dc, a_ps, b_ps):
        v.tensor_tensor(S[:, dc, :], S[:, dc, :], a_ps[:, :], OP.mult)
        v.tensor_tensor(S[:, dc, :], S[:, dc, :], b_ps[:, :], OP.add)
        return v.tensor_copy(XB[:, dc, :], S[:, dc, :])

    ln_fm(C, lambda dc: XB[:, dc, :], apply)


def emit_l0_ffn(C):
    block_barrier(C)
    sb = C.sb
    S, W, XB = sb["S"], sb["W"], sb["XB"]
    PA, PB = C.ps["PA"], C.ps["PB"]
    ones_row = sb["ONER"][0:1, :]
    BR = sb["BIASR"]
    SM = sb["SM"]
    SM2 = sb["SM2"]
    HB = sb["HB"]

    Wflat = W[:, :, :].rearrange("p c t -> p (c t)").bitcast(BF16)
    qslot = Wflat[:, 0:16384].rearrange("p (a b) -> p a b", b=2048)
    kvslot = Wflat[:, 16384:24576].rearrange("p (a b) -> p a b", b=2048)
    hslot = Wflat[:, :].rearrange("p (a b) -> p a b", b=1024)

    m1 = load_rows_bf(C, C.dr["L0f_ln1g"], 0, D, BR[0:1, 0:1024])
    m2 = load_rows_bf(C, C.dr["L0f_ln1b"], 0, D, BR[0:1, 1024:2048])

    BONES = sb["BONES"]
    BONEST = sb["BONEST"]

    @C.V
    def _(v):
        v.memset(BONES[:, :], 0.0)
        for c in range(8):
            v.memset(BONES[0:64, 16 * c + 2 * c:16 * c + 2 * c + 1], 1.0)
            ins = v.memset(BONES[64:128, 16 * c + 2 * c + 1:16 * c + 2 * c + 2], 1.0)
        ins.then_inc(C.sem("dve"), 1)
    C.cnt["dve"] += 1

    SMI = SM[:, :].bitcast(mybir.dt.int16)

    @C.G
    def _(g):
        ins = g.iota(SMI[0:16, 0:1024].rearrange("p (c h x) -> p c h x", h=2, x=64),
                     [[2, 8], [1, 2], [0, 64]], base=0, channel_multiplier=0,
                     allow_small_or_imprecise_dtypes=True)
        C.inc(ins, "pool", 1)
    io_mark = C.val("pool")

    @C.V
    def _(v):
        v.wait_ge(C.sem("pool"), io_mark)
        ins = v.tensor_scalar(BONEST[0:16, :], SMI[0:16, 0:1024],
                              sb["PIDX"][0:16, 0:1], None, OP.is_equal)
        ins.then_inc(C.sem("dve"), 1)
    C.cnt["dve"] += 1

    def apply_ln1(v, dc, a_ps, b_ps):
        v.tensor_tensor(sb["HB"][:, :], XB[:, dc, :], a_ps[:, :], OP.mult)
        return v.tensor_tensor(XB[:, dc, :], sb["HB"][:, :], b_ps[:, :], OP.add)

    ln_fm(C, lambda dc: XB[:, dc, :], apply_ln1,
          gain_row=BR[0:1, 0:1024], bias_row=BR[0:1, 1024:2048], rows_qgr=m2)
    ln1_end = C.cnt["pe"]

    tap(C, "ln1", lambda: [(C.dr["xtap"][0:D, :].rearrange("(c p) t -> p c t", p=128),
                            sb["XB"][:, :, :])])
    if C.tapped:
        return

    def qkv_evac(e, oc, ps):
        if oc < 8:
            return e.tensor_copy(qslot[:, oc, :], ps[:, :])
        return e.tensor_copy(kvslot[:, oc - 8, :], ps[:, :])

    fm_proj(C, C.dr["L0fW_qkv"], lambda dc: XB[:, dc, :], 1536, qkv_evac,
            evac_eng="vector")

    tap(C, "qkv", lambda: [(C.dr["xtap"][0:D, :].rearrange("(c p) t -> p c t", p=128),
                            Wflat[:, 0:16384].rearrange("p (c t) -> p c t", t=2048)),
                           (C.dr["xtap"][D:D + 512, :].rearrange("(c p) t -> p c t", p=128),
                            Wflat[:, 16384:24576].rearrange("p (c t) -> p c t", t=2048))])
    if C.tapped:
        return

    pe0 = C.val("pe")
    dve0 = C.val("dve")

    @C.V
    def _(v):
        for c in range(8):
            g = c // 2
            kb = 64 * (g % 2)
            kc = g // 2
            if c > 0:
                v.wait_ge(C.sem("pe"), pe0 + c)
            v.tensor_copy(SM2[0:64, 0:1024], kvslot[kb:kb + 64, kc, 1024:2048])
            v.tensor_copy(SM2[64:128, 0:1024], kvslot[kb:kb + 64, kc, 1024:2048])
            v.tensor_tensor(SM2[:, 1024:2048], qslot[:, c, 1024:2048],
                            SM2[:, 0:1024], OP.mult)
            v.tensor_copy(HB[0:64, 0:1024], kvslot[kb:kb + 64, kc, 0:1024])
            v.tensor_copy(HB[64:128, 0:1024], kvslot[kb:kb + 64, kc, 0:1024])
            ins = v.tensor_tensor(HB[:, 1024:2048], qslot[:, c, 1024:2048],
                                  HB[:, 0:1024], OP.mult)
            ins.then_inc(C.sem("dve"), 1)
    C.cnt["dve"] += 8

    @C.T
    def _(t):
        psum_guard(C, t)
        for c in range(8):
            t.wait_ge(C.sem("dve"), dve0 + c + 1)
            for ts in range(2):
                t.matmul(PB[0:16, 512 * ts:512 * (ts + 1)],
                         BONES[:, 16 * c:16 * (c + 1)],
                         SM2[:, 1024 + 512 * ts:1024 + 512 * (ts + 1)],
                         start=(c == 0), stop=(c == 7), skip_group_check=True)
                ins = t.matmul(PB[32:48, 512 * ts:512 * (ts + 1)],
                               BONES[:, 16 * c:16 * (c + 1)],
                               HB[:, 1024 + 512 * ts:1024 + 512 * (ts + 1)],
                               start=(c == 0), stop=(c == 7), skip_group_check=True)
            ins.then_inc(C.sem("pe"), 1)
    C.cnt["pe"] += 8
    sc_mark = C.cnt["pe"]

    @C.V
    def _(v):
        v.wait_ge(C.sem("pe"), sc_mark)
        ins = v.tensor_copy(SM[0:16, 0:1024], PB[0:16, 0:1024])
        ins.then_inc(C.sem("dve"), 1)
    C.cnt["dve"] += 1

    @C.V
    def _(v):
        ins = v.tensor_tensor(SM[32:48, 0:1024], PB[32:48, 0:1024],
                              SM[0:16, 0:1024], OP.subtract)
        ins.then_inc(C.sem("dve"), 1)
    C.cnt["dve"] += 1
    d_mark = C.cnt["dve"]

    @C.A
    def _(a):
        a.wait_ge(C.sem("dve"), d_mark)
        ins = a.activation(out=SM2[0:16, 0:1024], in_=SM[32:48, 0:1024],
                           func=AF.Sigmoid, scale=0.125)
        ins.then_inc(C.sem("act"), 1)
    C.cnt["act"] += 1
    p_mark = C.cnt["act"]

    groups = []
    for c in range(8):
        g = c // 2
        vb = 64 * (g % 2)
        vc = 2 + g // 2

        def mm(t, ps, c=c):
            for ts in range(2):
                ins = t.matmul(ps[:, 512 * ts:512 * (ts + 1)],
                               BONEST[0:16, 128 * c:128 * (c + 1)],
                               SM2[0:16, 512 * ts:512 * (ts + 1)],
                               start=True, stop=True)
            return ins

        def evac(e, ps, c=c, vb=vb, vc=vc):
            e.tensor_copy(HB[0:64, 0:1024], kvslot[vb:vb + 64, vc, 0:1024])
            e.tensor_copy(HB[64:128, 0:1024], kvslot[vb:vb + 64, vc, 0:1024])
            e.tensor_copy(HB[0:64, 1024:2048], kvslot[vb:vb + 64, vc, 1024:2048])
            e.tensor_copy(HB[64:128, 1024:2048], kvslot[vb:vb + 64, vc, 1024:2048])
            e.tensor_copy(qslot[:, c, 0:1024], HB[:, 0:1024])
            e.tensor_tensor(SM[:, 0:1024], HB[:, 0:1024], HB[:, 1024:2048],
                            OP.subtract)
            e.scalar_tensor_tensor(SM[:, 0:1024], ps[:, 0:1024], 1.0,
                                   SM[:, 0:1024], OP.mult, OP.mult)
            return e.tensor_tensor(qslot[:, c, 1024:2048], SM[:, 0:1024],
                                   HB[:, 1024:2048], OP.add)

        groups.append({"mm": mm, "evac": evac, "eng": "vector",
                       "wait": [("act", p_mark)]})
    run_pipe(C, groups, [PA[:, 0:1024], PB[:, 0:1024]])

    tap(C, "attn", lambda: [(C.dr["xtap"][0:D, :].rearrange("(c p) t -> p c t", p=128),
                             Wflat[:, 0:16384].rearrange("p (c t) -> p c t", t=2048)),
                            (C.dr["xtap"][D:D + 16, 0:1024],
                             sb["SM2"][0:16, 0:1024])])
    if C.tapped:
        return

    def pevac(e, oc, ps):
        e.tensor_tensor(XB[:, oc, :], ps[:, :], S[:, oc, :], OP.add)
        return e.tensor_tensor(S[:, oc, :], S[:, oc, :], XB[:, oc, :], OP.add)

    fm_proj(C, C.dr["L0fW_proj"], lambda dc: qslot[:, dc, :], D, pevac,
            evac_eng="vector")
    proj_end = C.cnt["pe"]

    m3 = load_rows_bf(C, C.dr["L0f_ln2g"], 0, D, BR[0:1, 0:1024],
                      wait_pe=ln1_end)
    m4 = load_rows_bf(C, C.dr["L0f_ln2b"], 0, D, BR[0:1, 1024:2048],
                      wait_pe=ln1_end)
    def apply_ln2(v, dc, a_ps, b_ps):
        v.tensor_tensor(sb["HB"][:, :], XB[:, dc, :], a_ps[:, :], OP.mult)
        return v.tensor_tensor(XB[:, dc, :], sb["HB"][:, :], b_ps[:, :], OP.add)

    ln_fm(C, lambda dc: XB[:, dc, :], apply_ln2,
          gain_row=BR[0:1, 0:1024], bias_row=BR[0:1, 1024:2048], rows_qgr=m4)

    tap(C, "ln2", lambda: [(C.dr["xtap"][0:D, :].rearrange("(c p) t -> p c t", p=128),
                            sb["XB"][:, :, :]),
                           (C.dr["xtap"][D:2 * D, :].rearrange("(c p) t -> p c t", p=128),
                            sb["S"][:, :, :])])
    if C.tapped:
        return

    apr, mr, bir = wslab_load(C, C.dr["L0fW_rt"], 0, 8)
    pe0r = C.val("pe")

    @C.T
    def _(t):
        psum_guard(C, t)
        t.wait_ge(C.sem(mr[0]), mr[1])
        for tc in range(TC):
            for dc in range(DC):
                ins = t.matmul(PA[:, 8 * tc:8 * (tc + 1)],
                               XB[:, dc, 128 * tc:128 * (tc + 1)],
                               apr[:, dc, :], start=(dc == 0), stop=(dc == DC - 1),
                               skip_group_check=True)
        ins.then_inc(C.sem("pe"), 1)
    C.cnt["pe"] += 1
    rt_mark = C.cnt["pe"]
    C.wb_last_use[bir] = C.cnt["pe"]

    RT = SM[:, 0:128].rearrange("p (a b) -> p a b", b=8)
    RX = SM[:, 128:256].rearrange("p (a b) -> p a b", b=8)
    RS = SM[:, 256:272].rearrange("p (a b) -> p a b", b=1)
    RS2 = SM[:, 272:288].rearrange("p (a b) -> p a b", b=1)
    dve_r0 = C.val("dve")

    @C.V
    def _(v):
        v.wait_ge(C.sem("pe"), rt_mark)
        v.tensor_copy(SM[:, 0:128], PA[:, 0:128])
        ins = v.reduce_max(RS[:, :, :], RT[:, :, :], axis=AX.X)
        ins.then_inc(C.sem("dve"), 1)
        v.wait_ge(C.sem("dve"), C.cnt["dve"] + 1)
        ins = v.tensor_tensor(RX[:, :, :], RT[:, :, :],
                              RS[:, :, :].broadcast_to([128, 16, 8]), OP.subtract)
        ins.then_inc(C.sem("dve"), 1)
    C.cnt["dve"] += 2
    r1_mark = C.cnt["dve"]

    tap(C, "logits", lambda: [(C.dr["xtap"][0:128, 0:512], sb["SM"][:, 0:512])])

    @C.A
    def _(a):
        a.wait_ge(C.sem("dve"), r1_mark)
        ins = a.activation(out=SM[:, 128:256], in_=SM[:, 128:256], func=AF.Exp)
        ins.then_inc(C.sem("act"), 1)
    C.cnt["act"] += 1
    r2_mark = C.cnt["act"]

    @C.V
    def _(v):
        def edge(ins):
            ins.then_inc(C.sem("dve"), 1)
            C.cnt["dve"] += 1
            v.wait_ge(C.sem("dve"), C.cnt["dve"])

        v.wait_ge(C.sem("act"), r2_mark)
        edge(v.reduce_sum(RS[:, :, :], RX[:, :, :], axis=AX.X))
        edge(v.reciprocal(SM[:, 272:288], SM[:, 256:272]))
        v.tensor_tensor(RX[:, :, :], RX[:, :, :],
                        RS2[:, :, :].broadcast_to([128, 16, 8]), OP.mult)
        edge(v.reduce_max(RS[:, :, :], RX[:, :, :], axis=AX.X))
        v.tensor_tensor(RT[:, :, :], RX[:, :, :],
                        RS[:, :, :].broadcast_to([128, 16, 8]), OP.is_ge)
        v.scalar_tensor_tensor(RT[:, :, :], RT[:, :, :], -1e30, RX[:, :, :],
                               OP.mult, OP.add)
        edge(v.reduce_max(RS2[:, :, :], RT[:, :, :], axis=AX.X))
        v.tensor_tensor(RT[:, :, :], RX[:, :, :],
                        RS2[:, :, :].broadcast_to([128, 16, 8]), OP.is_ge)
        v.tensor_tensor(RT[:, :, :], RT[:, :, :], RX[:, :, :], OP.mult)
        edge(v.tensor_tensor(RS[:, :, 0:1], RS[:, :, 0:1], RS2[:, :, 0:1], OP.add))
        edge(v.reciprocal(SM[:, 272:288], SM[:, 256:272]))
        v.tensor_tensor(RT[:, :, :], RT[:, :, :],
                        RS2[:, :, :].broadcast_to([128, 16, 8]), OP.mult)
        ins = v.tensor_copy(SM[:, 288:416], SM[:, 0:128])
        ins.then_inc(C.sem("dve"), 1)
    C.cnt["dve"] += 1
    rw_mark = C.cnt["dve"]

    tap(C, "rmath", lambda: [(C.dr["xtap"][0:128, 0:512], sb["SM"][:, 0:512])])
    if C.tapped:
        return

    @C.T
    def _(t):
        psum_guard(C, t)
        t.wait_ge(C.sem("dve"), rw_mark)
        for tc in range(TC):
            ins = t.transpose(PB[0:8, 128 * tc:128 * (tc + 1)],
                              SM[:, 288 + 8 * tc:288 + 8 * (tc + 1)],
                              sb["IDN"][:, :])
        ins.then_inc(C.sem("pe"), 1)
    C.cnt["pe"] += 1
    wt_mark = C.cnt["pe"]

    WEB = SM[:, :].bitcast(BF16)

    @C.V
    def _(v):
        v.wait_ge(C.sem("pe"), wt_mark)
        ins = v.tensor_copy(WEB[0:8, 0:2048], PB[0:8, 0:2048])
        ins.then_inc(C.sem("dve"), 1)
    C.cnt["dve"] += 1
    we_mark = C.cnt["dve"]

    HBI = sb["HB"][:, :].bitcast(mybir.dt.int16)
    dve_hb = C.val("evd")

    @C.G
    def _(g):
        g.wait_ge(C.sem("dve"), we_mark)
        ins = g.iota(HBI[0:8, 0:1024], [[1, 8], [0, 128]], base=0,
                     channel_multiplier=0, allow_small_or_imprecise_dtypes=True)
        C.inc(ins, "pool", 1)
    sel_mark = C.val("pool")

    @C.V
    def _(v):
        v.wait_ge(C.sem("pool"), sel_mark)
        ins = v.tensor_scalar(sb["BONEST"][0:8, 0:1024], HBI[0:8, 0:1024],
                              sb["PIDX"][0:8, 0:1], None, OP.is_equal)
        ins.then_inc(C.sem("dve"), 1)
    C.cnt["dve"] += 1
    selv_mark = C.cnt["dve"]

    tap(C, "router", lambda: [(C.dr["xtap"][0:8, :], WEB[0:8, 0:2048]),
                              (C.dr["xtap"][32:160, 0:128], sb["SM"][:, 288:416])])
    if C.tapped:
        return

    for tt in range(2):
        tcol = 1024 * tt
        for e in range(8):
            pe_web = C.val("pe")

            @C.T
            def _(t, e=e):
                psum_guard(C, t)
                t.wait_ge(C.sem("dve"), max(we_mark, selv_mark))
                for ts in range(NTS):
                    ins = t.matmul(PA[:, 512 * ts:512 * (ts + 1)],
                                   sb["BONEST"][0:8, 128 * e:128 * (e + 1)],
                                   WEB[0:8, 512 * ts:512 * (ts + 1)],
                                   start=True, stop=True)
                ins.then_inc(C.sem("pe"), 1)
            C.cnt["pe"] += 1
            web_mark = C.cnt["pe"]

            @C.V
            def _(v):
                v.wait_ge(C.sem("pe"), web_mark)
                ins = v.tensor_copy(SM2[:, :], PA[:, :])
                ins.then_inc(C.sem("evd"), 1)
            C.cnt["evd"] += 1

            w1e = C.dr["L0f_w1"][e]
            groups = []
            pe_entry = C.val("pe")
            for sgi in range(8):
                og = 512 * sgi
                wp = pe_entry + 4 * (sgi - 1) if sgi >= 2 else None
                ap, mark, bi = wslab_load(C, w1e, og, 512, wait_pe=wp)
                for j in range(4):
                    fc = 4 * sgi + j

                    def mm(t, ps, ap=ap, j=j):
                        for ts in range(2):
                            for dc in range(DC):
                                ins = t.matmul(
                                    ps[:, 512 * ts:512 * (ts + 1)],
                                    ap[:, dc, 128 * j:128 * (j + 1)],
                                    XB[:, dc, tcol + 512 * ts:tcol + 512 * (ts + 1)],
                                    start=(dc == 0), stop=(dc == DC - 1))
                        return ins

                    def evac(e_, ps, fc=fc):
                        return e_.activation(out=hslot[:, fc, :], in_=ps[:, 0:1024],
                                             func=AF.Gelu)

                    groups.append({"mm": mm, "evac": evac, "eng": "scalar",
                                   "wait": [mark]})
            run_pipe(C, groups, [PA[:, 0:1024], PB[:, 0:1024]])
            C.wb_last_use[0] = C.wb_last_use[1] = C.cnt["pe"]

            w2e = C.dr["L0f_w2"][e]
            groups = []
            pe_entry = C.val("pe")
            for hc in range(8):
                wp = pe_entry + (hc - 1) if hc >= 2 else None
                ap, mark, bi = wslab_load(C, w2e, 128 * hc, 128, wait_pe=wp)

                def mm(t, ps, ap=ap):
                    for ts in range(2):
                        for fc in range(32):
                            ins = t.matmul(ps[:, 512 * ts:512 * (ts + 1)],
                                           ap[:, fc, :],
                                           hslot[:, fc, 512 * ts:512 * (ts + 1)],
                                           start=(fc == 0), stop=(fc == 31))
                    return ins

                def evac(e_, ps, hc=hc):
                    e_.tensor_tensor(sb["HB"][:, 0:1024], ps[:, 0:1024],
                                     SM2[:, tcol:tcol + 1024], OP.mult)
                    return e_.tensor_tensor(S[:, hc, tcol:tcol + 1024],
                                            S[:, hc, tcol:tcol + 1024],
                                            sb["HB"][:, 0:1024], OP.add)

                groups.append({"mm": mm, "evac": evac, "eng": "vector",
                               "wait": [mark]})
            run_pipe(C, groups, [PA[:, 0:1024], PB[:, 0:1024]])
            C.wb_last_use[0] = C.wb_last_use[1] = C.cnt["pe"]

    postln_from_S(C)


def emit_l1_attn(C):
    block_barrier(C)
    sb = C.sb
    S, W, XB = sb["S"], sb["W"], sb["XB"]
    PA, PB = C.ps["PA"], C.ps["PB"]
    ones_row = sb["ONER"][0:1, :]
    ones_col = sb["ONE"][:, 0:1]
    BR = sb["BIASR"]
    SM = sb["SM"]
    SM2 = sb["SM2"]
    HB = sb["HB"]
    HBI = HB[:, :].bitcast(mybir.dt.int16)
    GAM = sb["STAT"][:, 0:1]
    LNG = sb["STAT"][:, 1:2]

    Wflat = W[:, :, :].rearrange("p c t -> p (c t)").bitcast(BF16)
    Qs = Wflat[:, 0:8192].rearrange("p (a b) -> p a b", b=1024)
    Ks = Wflat[:, 8192:16384].rearrange("p (a b) -> p a b", b=1024)
    KT = Wflat[:, 16384:24576].rearrange("p (a b) -> p a b", b=1024)
    VT = Wflat[:, 24576:32768].rearrange("p (a b) -> p a b", b=1024)

    load_rows_bf(C, C.dr["L1b_q"], 0, D, BR[0:1, 0:1024], wait_pe=C.val("pe"))
    load_rows_bf(C, C.dr["L1b_k"], 0, D, BR[0:1, 1024:2048])
    bm = load_rows_bf(C, C.dr["L1b_v"], 0, D, BR[0:1, 2048:3072])

    @C.G
    def _(g):
        ins = g.dma_start(out=GAM, in_=C.dr["L1_gamma"])
        C.inc(ins, "qg", 16)
        ins = g.dma_start(out=SM[0:1, 8:16], in_=C.dr["L1_dexp"])
        C.inc(ins, "qg", 16)
        ins = g.dma_start(out=SM[0:1, 16:24], in_=C.dr["L1_mask"])
        C.inc(ins, "qg", 16)
    gm = C.val("qg")

    @C.A
    def _(a):
        a.wait_ge(C.sem("qg"), gm)
        ins = a.activation(out=LNG, in_=GAM, func=AF.Ln)
        ins.then_inc(C.sem("act"), 1)
    C.cnt["act"] += 1
    lng_mark = C.cnt["act"]

    @C.G
    def _(g):
        g.wait_ge(C.sem("act"), lng_mark)
        ins = g.iota(HBI[:, 8:16], [[-128, 8]], base=L - 1,
                     channel_multiplier=-1, allow_small_or_imprecise_dtypes=True)
        C.inc(ins, "pool", 1)
    io_m = C.val("pool")

    @C.V
    def _(v):
        v.wait_ge(C.sem("pool"), io_m)
        v.tensor_scalar(sb["STAT"][:, 2:10], HBI[:, 8:16], LNG, None, OP.mult)
        ins = v.tensor_scalar(SM[0:1, 0:8], SM[0:1, 8:16], LNG[0:1, 0:1],
                              None, OP.mult)
        ins.then_inc(C.sem("dve"), 1)
    C.cnt["dve"] += 1
    dv_m = C.cnt["dve"]

    @C.A
    def _(a):
        a.wait_ge(C.sem("dve"), dv_m)
        a.activation(out=sb["STAT"][:, 2:10], in_=sb["STAT"][:, 2:10], func=AF.Exp)
        ins = a.activation(out=SM[0:1, 0:8], in_=SM[0:1, 0:8], func=AF.Exp)
        ins.then_inc(C.sem("act"), 1)
    C.cnt["act"] += 1
    dv_m2 = C.cnt["act"]

    @C.V
    def _(v):
        v.wait_ge(C.sem("act"), dv_m2)
        ins = v.tensor_tensor(SM[0:1, 0:8], SM[0:1, 0:8], SM[0:1, 16:24], OP.mult)
        ins.then_inc(C.sem("dve"), 1)
    C.cnt["dve"] += 1
    coef_m = C.cnt["dve"]

    @C.T
    def _(t):
        psum_guard(C, t)
        t.wait_ge(C.sem("dve"), coef_m)
        for k in range(NCORES):
            ins = t.matmul(PB[:, k:k + 1], sb["ONERF"][0:1, 0:128],
                           SM[0:1, k:k + 1], start=True, stop=True)
        ins.then_inc(C.sem("pe"), 1)
    C.cnt["pe"] += 1
    wk_m = C.cnt["pe"]

    @C.V
    def _(v):
        v.wait_ge(C.sem("pe"), wk_m)
        for k in range(NCORES):
            ins = v.scalar_tensor_tensor(SM2[:, 128 * k:128 * (k + 1)],
                                         sb["IDN"][:, :], 1.0,
                                         PB[:, k:k + 1].broadcast_to([128, 128]),
                                         OP.mult, OP.mult)
        ins.then_inc(C.sem("dve"), 1)
    C.cnt["dve"] += 1
    idist_m = C.cnt["dve"]

    ag_mark_prev = None
    for b in range(NB):
        bc = 1024 * b

        def ev_q(e, oc, ps):
            return e.activation(out=Qs[:, oc, :], in_=ps[:, :], func=AF.Relu)

        fm_proj(C, C.dr["L1W_q"], lambda dc: XB[:, dc, :], D, ev_q,
                evac_eng="scalar", bias_row=BR[0:1, 0:1024], tokens=(bc, bc + 1024),
                bias_mark=bm)

        def ev_k(e, oc, ps):
            return e.activation(out=Ks[:, oc, :], in_=ps[:, :], func=AF.Relu)

        fm_proj(C, C.dr["L1W_k"], lambda dc: XB[:, dc, :], D, ev_k,
                evac_eng="scalar", bias_row=BR[0:1, 1024:2048], tokens=(bc, bc + 1024),
                bias_mark=bm)

        groups = []
        for s in range(2):
            w_hbm = C.dr["L1W_k"] if s == 0 else C.dr["L1W_v"]
            brow = BR[0:1, 1024:2048] if s == 0 else BR[0:1, 2048:3072]
            dstsl = KT if s == 0 else VT
            for half in range(2):
                ap, mark, bi = wslab_load(C, w_hbm, 512 * half, 512)
                for tc in range(8):
                    def mm(t, ps, ap=ap, tc=tc, brow=brow, half=half):
                        for dc in range(DC):
                            t.matmul(ps[:, 0:512],
                                     XB[:, dc, bc + 128 * tc:bc + 128 * (tc + 1)],
                                     ap[:, dc, :], start=(dc == 0), stop=False)
                        return t.matmul(ps[:, 0:512], ones_row[:, 0:128],
                                        brow[:, 512 * half:512 * (half + 1)],
                                        start=False, stop=True)

                    def evac(e, ps, s=s, tc=tc, half=half, dstsl=dstsl):
                        if s == 0:
                            return e.activation(
                                out=dstsl[:, tc, 512 * half:512 * (half + 1)],
                                in_=ps[:, 0:512], func=AF.Relu)
                        return e.tensor_copy(
                            dstsl[:, tc, 512 * half:512 * (half + 1)], ps[:, 0:512])

                    groups.append({"mm": mm, "evac": evac,
                                   "eng": "scalar" if s == 0 else "vector",
                                   "wait": [mark, ("qgr", bm)]})
        bufs = [PA[:, 0:512], PA[:, 512:1024], PB[:, 0:512], PB[:, 512:1024]]
        run_pipe(C, groups, bufs)
        C.wb_last_use[0] = C.wb_last_use[1] = C.cnt["pe"]

        tap(C, "l1_proj", lambda: [(C.dr["xtap"][0:128, 0:1024], sb["SM"][:, 0:1024])])
        if C.tapped:
            return

        act_now = C.val("act")

        @C.V
        def _(v):
            v.wait_ge(C.sem("act"), act_now)
            for tc in range(8):
                ins = v.tensor_scalar(KT[:, tc, :], KT[:, tc, :],
                                      sb["STAT"][:, 2 + tc:3 + tc], None, OP.mult)
            ins.then_inc(C.sem("dve"), 1)
        C.cnt["dve"] += 1
        ktsc_m = C.cnt["dve"]

        pe_m0 = C.val("pe")
        evd_m0 = C.val("evd")
        qg_m0 = C.val("qg")

        @C.T
        def _(t):
            psum_guard(C, t)
            t.wait_ge(C.sem("dve"), ktsc_m)
            for dc in range(DC):
                if dc >= 2:
                    t.wait_ge(C.sem("evd"), evd_m0 + dc - 1)
                ps = [PA, PB][dc % 2]
                for jc in range(8):
                    for half in range(2):
                        t.matmul(ps[:, 512 * half:512 * (half + 1)],
                                 KT[:, jc, 128 * dc:128 * (dc + 1)],
                                 VT[:, jc, 512 * half:512 * (half + 1)],
                                 start=(jc == 0), stop=(jc == 7),
                                 skip_group_check=True)
                    ins = t.matmul(ps[:, 1024:1025],
                                   KT[:, jc, 128 * dc:128 * (dc + 1)],
                                   ones_col, start=(jc == 0), stop=(jc == 7),
                                   skip_group_check=True)
                ins.then_inc(C.sem("pe"), 1)
        C.cnt["pe"] += DC

        @C.V
        def _(v):
            for dc in range(DC):
                v.wait_ge(C.sem("pe"), pe_m0 + dc + 1)
                v.wait_ge(C.sem("qg"), qg_m0 + 16 * dc)
                ins = v.tensor_copy(HB[:, 0:1025], [PA, PB][dc % 2][:, 0:1025])
                ins.then_inc(C.sem("evd"), 1)
        C.cnt["evd"] += DC

        @C.G
        def _(g, b=b, ag_mark_prev=ag_mark_prev):
            if b == 1 and ag_mark_prev is not None:
                g.wait_ge(C.sem("cc"), ag_mark_prev)
            for dc in range(DC):
                g.wait_ge(C.sem("evd"), evd_m0 + dc + 1)
                ins = g.dma_start(out=C.dr["cc1_in"][128 * dc:128 * (dc + 1), 0:1025],
                                  in_=HB[:, 0:1025])
                C.inc(ins, "qg", 16)

        tap(C, "l1_m", lambda: [(C.dr["xtap"][0:128, 0:1024], sb["SM"][:, 0:1024])])
        if C.tapped:
            return

        if not DEBUG_NOAG:
            @C.G
            def _(g):
                g.wait_ge(C.sem("qg"), C.val("qg"))
                ins = g.collective_compute("AllGather", OP.bypass,
                                           replica_groups=[list(range(NCORES))],
                                           ins=[C.dr["cc1_in"].opt()],
                                           outs=[C.dr["cc1_out"].opt()])
                C.inc(ins, "cc", 1)
        ag_mark = C.val("cc")
        ag_mark_prev = ag_mark

        pe_a0 = C.val("pe")
        dve_a0 = C.val("dve")
        act_a0 = C.val("act")
        pool_a0 = C.val("pool")
        for jc in range(DEBUG_TRUNC):
            AGOPS = DEBUG_AGOPS

            @C.G
            def _(g, jc=jc):
                if jc == 0:
                    g.wait_ge(C.sem("evd"), C.val("evd"))
                else:
                    g.wait_ge(C.sem("dve"), dve_a0 + 2 * jc - 1)
                ins = g.iota(HBI[:, 0:1024], [[1, 1024]], base=-128 * jc,
                             channel_multiplier=-1,
                             allow_small_or_imprecise_dtypes=True)
                C.inc(ins, "pool", 1)

            if AGOPS >= 2:
                @C.V
                def _(v, jc=jc):
                    v.wait_ge(C.sem("pool"), pool_a0 + 2 * jc + 1)
                    ins = v.tensor_scalar(SM[:, 0:1024], HBI[:, 0:1024], LNG, 0.0,
                                          OP.mult, OP.min)
                    ins.then_inc(C.sem("dve"), 1)

            if AGOPS >= 3:
                @C.A
                def _(a, jc=jc):
                    a.wait_ge(C.sem("dve"), dve_a0 + 2 * jc + 1)
                    ins = a.activation(out=SM[:, 0:1024], in_=SM[:, 0:1024],
                                       func=AF.Exp)
                    ins.then_inc(C.sem("act"), 1)

            if AGOPS >= 4:
                @C.G
                def _(g, jc=jc):
                    g.wait_ge(C.sem("act"), act_a0 + jc + 1)
                    ins = g.affine_select(SM[:, 0:1024], SM[:, 0:1024], [[1, 1024]],
                                          OP.is_ge, 0.0, base=-128 * jc,
                                          channel_multiplier=-1)
                    C.inc(ins, "pool", 1)

            if AGOPS >= 5:
                @C.T
                def _(t, jc=jc):
                    if jc == 0:
                        psum_guard(C, t)
                    t.wait_ge(C.sem("pool"), pool_a0 + 2 * (jc + 1))
                    if jc >= 2:
                        t.wait_ge(C.sem("dve"), dve_a0 + 2 * jc)
                    ps = [PA, PB][jc % 2]
                    for dc in range(DC):
                        for half in range(2):
                            ins = t.matmul(ps[:, 512 * half:512 * (half + 1)],
                                           Ks[:, dc, 128 * jc:128 * (jc + 1)],
                                           Qs[:, dc, 512 * half:512 * (half + 1)],
                                           start=(dc == 0), stop=(dc == DC - 1),
                                           skip_group_check=True)
                    ins.then_inc(C.sem("pe"), 1)

            if AGOPS >= 6:
                @C.V
                def _(v, jc=jc):
                    v.wait_ge(C.sem("pe"), pe_a0 + jc + 1)
                    ins = v.tensor_tensor(KT[:, jc, :], [PA, PB][jc % 2][:, 0:1024],
                                          SM[:, 0:1024], OP.mult)
                    ins.then_inc(C.sem("dve"), 1)
            if AGOPS >= 2:
                C.cnt["dve"] += 1
            if AGOPS >= 6:
                C.cnt["dve"] += 1
            if AGOPS >= 3:
                C.cnt["act"] += 1
            if AGOPS >= 5:
                C.cnt["pe"] += 1
        agam_end_d = C.cnt["dve"]
        tap(C, "l1_agam", lambda: [(C.dr["xtap"][0:128, 0:1024], sb["SM"][:, 0:1024])])
        if C.tapped:
            return


        @C.G
        def _(g):
            g.wait_ge(C.sem("dve"), agam_end_d)
            ins = g.iota(HBI[0:1, 0:1024], [[1, 1024]], base=1,
                         channel_multiplier=0, allow_small_or_imprecise_dtypes=True)
            C.inc(ins, "pool", 1)
        gq_m = C.val("pool")

        @C.V
        def _(v):
            v.wait_ge(C.sem("pool"), gq_m)
            ins = v.tensor_scalar(SM[0:1, 0:1024], HBI[0:1, 0:1024],
                                  LNG[0:1, 0:1], None, OP.mult)
            ins.then_inc(C.sem("dve"), 1)
        C.cnt["dve"] += 1
        gq_m2 = C.cnt["dve"]

        @C.A
        def _(a):
            a.wait_ge(C.sem("dve"), gq_m2)
            ins = a.activation(out=sb["ABR"][0:1, 0:1024], in_=SM[0:1, 0:1024],
                               func=AF.Exp)
            ins.then_inc(C.sem("act"), 1)
        C.cnt["act"] += 1
        gq_m3 = C.cnt["act"]

        @C.T
        def _(t):
            t.wait_ge(C.sem("act"), gq_m3)
            t.wait_ge(C.sem("evd"), C.val("evd"))
            for half in range(2):
                ins = t.matmul(PA[:, 1024 + 512 * half:1024 + 512 * (half + 1)],
                               ones_row[:, 0:128],
                               sb["ABR"][0:1, 512 * half:512 * (half + 1)],
                               start=True, stop=True)
            ins.then_inc(C.sem("pe"), 1)
        C.cnt["pe"] += 1
        gq_m4 = C.cnt["pe"]

        @C.V
        def _(v):
            v.wait_ge(C.sem("pe"), gq_m4)
            for dc in range(DC):
                ins = v.tensor_tensor(Qs[:, dc, :], Qs[:, dc, :],
                                      PA[:, 1024:2048], OP.mult)
            ins.then_inc(C.sem("dve"), 1)
        C.cnt["dve"] += 1
        qt_m = C.cnt["dve"]
        tap(C, "l1_gq", lambda: [(C.dr["xtap"][0:128, 0:1024], sb["SM"][:, 0:1024])])
        if C.tapped:
            return


        pe_s0 = C.val("pe")
        dve_s0 = C.val("dve")
        sin_tiles = []
        for dc in range(DC):
            for ki in range(NCORES):
                gi = 8 * dc + ki
                i = C.wb_idx
                C.wb_idx = 1 - C.wb_idx
                buf = sb["WB0"] if i == 0 else sb["WB1"]
                qsem = "qga" if i == 0 else "qgb"
                lu = C.wb_last_use[i] if gi < 2 else pe_s0 + gi - 1

                @C.G
                def _(g, buf=buf, lu=lu, ki=ki, dc=dc, qsem=qsem,
                      first=(dc == 0 and ki == 0)):
                    if first:
                        g.wait_ge(C.sem("cc"), ag_mark)
                    if lu > 0:
                        g.wait_ge(C.sem("pe"), lu)
                    ins = g.dma_start(
                        out=buf[:, 0:1025],
                        in_=C.dr["cc1_out"][1024 * ki + 128 * dc:
                                            1024 * ki + 128 * (dc + 1), 0:1025])
                    C.inc(ins, qsem, 16)
                sin_tiles.append((dc, ki, buf, (qsem, C.val(qsem)), i))

        @C.T
        def _(t):
            t.wait_ge(C.sem("dve"), max(idist_m, qt_m))
            for dc in range(DC):
                if dc >= 2:
                    t.wait_ge(C.sem("dve"), dve_s0 + dc - 1)
                ps = [PA, PB][dc % 2]
                for (dc2, ki, buf, mark, bi) in sin_tiles:
                    if dc2 != dc:
                        continue
                    t.wait_ge(C.sem(mark[0]), mark[1])
                    for seg in range(3):
                        lo = [0, 512, 1024][seg]
                        hi = [512, 1024, 1025][seg]
                        ins = t.matmul(ps[:, lo:hi],
                                       SM2[:, 128 * ki:128 * (ki + 1)],
                                       buf[:, lo:hi], start=(ki == 0),
                                       stop=(ki == NCORES - 1),
                                       skip_group_check=True)
                    ins.then_inc(C.sem("pe"), 1)
        C.cnt["pe"] += DC * NCORES
        C.wb_last_use[0] = C.wb_last_use[1] = C.cnt["pe"]

        @C.V
        def _(v):
            for dc in range(DC):
                v.wait_ge(C.sem("pe"), pe_s0 + 8 * (dc + 1))
                v.tensor_copy(sb["ZCOL"][:, dc:dc + 1],
                              [PA, PB][dc % 2][:, 1024:1025])
                ins = v.tensor_copy(Ks[:, dc, :], [PA, PB][dc % 2][:, 0:1024])
                ins.then_inc(C.sem("dve"), 1)
        C.cnt["dve"] += DC
        sin_m = C.cnt["dve"]
        tap(C, "l1_sin", lambda: [(C.dr["xtap"][0:128, 0:1024], sb["SM"][:, 0:1024])])
        if C.tapped:
            return


        @C.T
        def _(t):
            t.wait_ge(C.sem("dve"), max(sin_m, qt_m, agam_end_d))
            for jc in range(8):
                for half in range(2):
                    ins = t.matmul(PB[0:1, 1024 + 512 * half:1024 + 512 * (half + 1)],
                                   ones_col, KT[:, jc, 512 * half:512 * (half + 1)],
                                   start=(jc == 0), stop=False,
                                   skip_group_check=True)
            for dc in range(DC):
                for half in range(2):
                    ins = t.matmul(PB[0:1, 1024 + 512 * half:1024 + 512 * (half + 1)],
                                   sb["ZCOL"][:, dc:dc + 1],
                                   Qs[:, dc, 512 * half:512 * (half + 1)],
                                   start=False, stop=(dc == DC - 1),
                                   skip_group_check=True)
            ins.then_inc(C.sem("pe"), 1)
        C.cnt["pe"] += 1
        den_m = C.cnt["pe"]

        @C.V
        def _(v):
            def edge(ins):
                ins.then_inc(C.sem("dve"), 1)
                C.cnt["dve"] += 1
                v.wait_ge(C.sem("dve"), C.cnt["dve"])

            v.wait_ge(C.sem("pe"), den_m)
            edge(v.tensor_scalar(SM[0:1, 0:1024], PB[0:1, 1024:2048], 1e-6,
                                 None, OP.add))
            edge(v.reciprocal(SM[0:1, 0:1024], SM[0:1, 0:1024]))
            ins = v.tensor_copy(sb["ABR"][0:1, 0:1024], SM[0:1, 0:1024])
            ins.then_inc(C.sem("dve"), 1)
        C.cnt["dve"] += 1
        rec_m = C.cnt["dve"]

        @C.T
        def _(t):
            t.wait_ge(C.sem("dve"), rec_m)
            for half in range(2):
                ins = t.matmul(PB[:, 1024 + 512 * half:1024 + 512 * (half + 1)],
                               ones_row[:, 0:128],
                               sb["ABR"][0:1, 512 * half:512 * (half + 1)],
                               start=True, stop=True)
            ins.then_inc(C.sem("pe"), 1)
        C.cnt["pe"] += 1
        recb_m = C.cnt["pe"]

        @C.V
        def _(v):
            v.wait_ge(C.sem("pe"), recb_m)
            ins = v.tensor_copy(SM2[:, 1024:2048], PB[:, 1024:2048])
            ins.then_inc(C.sem("dve"), 1)
        C.cnt["dve"] += 1
        recsb_m = C.cnt["dve"]
        tap(C, "l1_den", lambda: [(C.dr["xtap"][0:128, 0:1024], sb["SM"][:, 0:1024])])
        if C.tapped:
            return


        groups = []
        for ec in range(DC):
            def mm(t, ps, ec=ec):
                for jc in range(8):
                    for half in range(2):
                        t.matmul(ps[:, 512 * half:512 * (half + 1)],
                                 VT[:, jc, 128 * ec:128 * (ec + 1)],
                                 KT[:, jc, 512 * half:512 * (half + 1)],
                                 start=(jc == 0), stop=False)
                for dc in range(DC):
                    for half in range(2):
                        ins = t.matmul(ps[:, 512 * half:512 * (half + 1)],
                                       Ks[:, dc, 128 * ec:128 * (ec + 1)],
                                       Qs[:, dc, 512 * half:512 * (half + 1)],
                                       start=False, stop=(dc == DC - 1))
                return ins

            def evac(e, ps, ec=ec):
                e.tensor_tensor(HB[:, 0:1024], ps[:, 0:1024], SM2[:, 1024:2048],
                                OP.mult)
                return e.tensor_tensor(XB[:, ec, bc:bc + 1024], HB[:, 0:1024],
                                       S[:, ec, bc:bc + 1024], OP.add)

            groups.append({"mm": mm, "evac": evac, "eng": "vector",
                           "wait": [("dve", max(recsb_m, sin_m, qt_m))]})
        run_pipe(C, groups, [PA[:, 0:1024], PB[:, 0:1024]])

    postln_from_XB(C)


def postln_from_XB(C):
    sb = C.sb
    S, XB = sb["S"], sb["XB"]

    def apply(v, dc, a_ps, b_ps):
        v.tensor_tensor(S[:, dc, :], XB[:, dc, :], a_ps[:, :], OP.mult)
        v.tensor_tensor(S[:, dc, :], S[:, dc, :], b_ps[:, :], OP.add)
        return v.tensor_copy(XB[:, dc, :], S[:, dc, :])

    ln_fm(C, lambda dc: XB[:, dc, :], apply)


def _quantize_weight(C, w_hbm, spill, nume):
    sb = C.sb
    WB0F = sb["WB0"][:, :].bitcast(F32)
    WB1I = sb["WB1"][:, :].bitcast(I32)
    WB1B = sb["WB1"]
    din, dout = w_hbm.shape
    nkc = din // 128
    ocols = 2048 // nkc
    nslab = dout // ocols
    acc = sb["STAT"][:, 24:25]
    tmp = sb["STAT"][:, 25:26]
    scol = sb["STAT"][:, 26:27]

    @C.V
    def _(v):
        ins = v.memset(acc, 0.0)
        ins.then_inc(C.sem("dve"), 1)
    C.cnt["dve"] += 1

    for s in range(nslab):
        @C.G
        def _(g, s=s):
            g.wait_ge(C.sem("dve"), C.val("dve"))
            ins = g.dma_start(
                out=WB0F[:, 0:nkc * ocols].rearrange("p (c o) -> p c o", o=ocols),
                in_=w_hbm[:, s * ocols:(s + 1) * ocols]
                    .rearrange("(c p) o -> p c o", p=128))
            C.inc(ins, "qg", 16)
        mk = C.val("qg")

        @C.A
        def _(a):
            a.wait_ge(C.sem("qg"), mk)
            ins = a.activation(out=WB0F[:, :], in_=WB0F[:, :], func=AF.Abs,
                               accum_out=tmp)
            ins.then_inc(C.sem("act"), 1)
        C.cnt["act"] += 1
        mk2 = C.cnt["act"]

        @C.V
        def _(v):
            v.wait_ge(C.sem("act"), mk2)
            ins = v.tensor_tensor(acc, acc, tmp, OP.add)
            ins.then_inc(C.sem("dve"), 1)
        C.cnt["dve"] += 1
    acc_m = C.cnt["dve"]

    @C.T
    def _(t):
        psum_guard(C, t)
        t.wait_ge(C.sem("dve"), acc_m)
        ins = t.matmul(C.ps["PA"][0:1, 0:1], acc, sb["ONEF"][:, 0:1],
                       start=True, stop=True)
        ins.then_inc(C.sem("pe"), 1)
    C.cnt["pe"] += 1
    ts_m = C.cnt["pe"]

    @C.V
    def _(v):
        def edge(ins):
            ins.then_inc(C.sem("dve"), 1)
            C.cnt["dve"] += 1
            v.wait_ge(C.sem("dve"), C.cnt["dve"])

        v.wait_ge(C.sem("pe"), ts_m)
        edge(v.tensor_scalar(sb["STAT"][0:1, 27:28], C.ps["PA"][0:1, 0:1],
                             1.0 / nume, 1e-5, OP.mult, OP.add))
        edge(v.reciprocal(sb["STAT"][0:1, 28:29], sb["STAT"][0:1, 27:28]))
        v.nop()
    rs_m = C.cnt["dve"]

    @C.T
    def _(t):
        t.wait_ge(C.sem("dve"), rs_m)
        ins = t.matmul(C.ps["PA"][:, 1:2], sb["ONERF"][0:1, 0:128],
                       sb["STAT"][0:1, 28:29], start=True, stop=True)
        ins.then_inc(C.sem("pe"), 1)
    C.cnt["pe"] += 1
    sc_m = C.cnt["pe"]

    @C.V
    def _(v):
        v.wait_ge(C.sem("pe"), sc_m)
        ins = v.tensor_copy(scol, C.ps["PA"][:, 1:2])
        ins.then_inc(C.sem("dve"), 1)
    C.cnt["dve"] += 1
    scol_m = C.cnt["dve"]

    for s in range(nslab):
        @C.G
        def _(g, s=s):
            g.wait_ge(C.sem("dve"), C.val("dve"))
            ins = g.dma_start(
                out=WB0F[:, 0:nkc * ocols].rearrange("p (c o) -> p c o", o=ocols),
                in_=w_hbm[:, s * ocols:(s + 1) * ocols]
                    .rearrange("(c p) o -> p c o", p=128))
            C.inc(ins, "qg", 16)
        mk = C.val("qg")

        @C.V
        def _(v, s=s, mk=mk):
            def edge(ins):
                ins.then_inc(C.sem("dve"), 1)
                C.cnt["dve"] += 1
                v.wait_ge(C.sem("dve"), C.cnt["dve"])

            v.wait_ge(C.sem("qg"), mk)
            if s == 0:
                v.wait_ge(C.sem("dve"), scol_m)
            edge(v.tensor_scalar(WB0F[:, :], WB0F[:, :], scol, None, OP.mult))
            edge(v.tensor_copy(WB1I[:, :], WB0F[:, :]))
            edge(v.tensor_copy(WB0F[:, :], WB1I[:, :]))
            edge(v.tensor_scalar(WB0F[:, :], WB0F[:, :], 1.0, -1.0,
                                 OP.min, OP.max))
            ins = v.tensor_copy(WB1B[:, 0:2048], WB0F[:, :])
            ins.then_inc(C.sem("dve"), 1)
        C.cnt["dve"] += 1
        mk2 = C.cnt["dve"]

        @C.G
        def _(g, s=s):
            g.wait_ge(C.sem("dve"), mk2)
            ins = g.dma_start(
                out=spill[:, s * ocols:(s + 1) * ocols]
                    .rearrange("(c p) o -> p c o", p=128),
                in_=WB1B[:, 0:2048].rearrange("p (c o) -> p c o", o=ocols))
            C.inc(ins, "qg", 16)
    C.wb_last_use[0] = C.wb_last_use[1] = C.cnt["pe"]
    return C.val("qg")


def emit_l1_ffn(C):
    block_barrier(C)
    sb = C.sb
    S, W, XB = sb["S"], sb["W"], sb["XB"]
    BR = sb["BIASR"]

    q1_done = _quantize_weight(C, C.dr["L1fW1"], C.dr["qsp1"], D * 4096)
    q2_done = _quantize_weight(C, C.dr["L1fW2"], C.dr["qsp2"], D * 4096)

    Wflat = W[:, :, :].rearrange("p c t -> p (c t)").bitcast(BF16)
    hslot = Wflat[:, :].rearrange("p (a b) -> p a b", b=1024)

    @C.G
    def _(g):
        g.wait_ge(C.sem("qg"), max(q1_done, q2_done))

    for tt in range(2):
        tcol = 1024 * tt
        mb1a = load_rows_bf(C, C.dr["L1fb1"], 0, 2048, BR[0:1, 0:2048],
                            wait_pe=C.val("pe"))
        fm_proj(C, C.dr["qsp1"][:, 0:2048], lambda dc: XB[:, dc, :], 2048,
                lambda e, oc, ps: e.activation(out=hslot[:, oc, :],
                                               in_=ps[:, :], func=AF.Relu),
                evac_eng="scalar", bias_row=BR[0:1, 0:2048],
                tokens=(tcol, tcol + 1024), bias_mark=mb1a)
        mb1b = load_rows_bf(C, C.dr["L1fb1"], 2048, 4096, BR[0:1, 0:2048],
                            wait_pe=C.val("pe"))
        fm_proj(C, C.dr["qsp1"][:, 2048:4096], lambda dc: XB[:, dc, :], 2048,
                lambda e, oc, ps: e.activation(out=hslot[:, 16 + oc, :],
                                               in_=ps[:, :], func=AF.Relu),
                evac_eng="scalar", bias_row=BR[0:1, 0:2048],
                tokens=(tcol, tcol + 1024), bias_mark=mb1b)
        mb2 = load_rows_bf(C, C.dr["L1fb2"], 0, D, BR[0:1, 2048:3072],
                           wait_pe=C.val("pe"))

        def ev_o(e, oc, ps, tcol=tcol):
            return e.tensor_tensor(XB[:, oc, tcol:tcol + 1024], ps[:, :],
                                   S[:, oc, tcol:tcol + 1024], OP.add)

        fm_proj(C, C.dr["qsp2"], lambda fc: hslot[:, fc, :], D, ev_o,
                evac_eng="vector", bias_row=BR[0:1, 2048:3072],
                tokens=(0, 1024), bias_mark=mb2)

    postln_from_XB(C)


def emit_l2_attn(C):
    block_barrier(C)
    sb = C.sb
    S, W, XB = sb["S"], sb["W"], sb["XB"]
    PA, PB = C.ps["PA"], C.ps["PB"]
    ones_row = sb["ONER"][0:1, :]
    SM = sb["SM"]
    SM2 = sb["SM2"]
    HB = sb["HB"]
    RP = sb["BONES"]

    Wflat = W[:, :, :].rearrange("p c t -> p (c t)").bitcast(BF16)
    Qr = [Wflat[:, 0:8192].rearrange("p (a b) -> p a b", b=1024),
          Wflat[:, 8192:16384].rearrange("p (a b) -> p a b", b=1024)]
    KS = Wflat[:, 16384:24576].rearrange("p (a b) -> p a b", b=1024)
    VS = Wflat[:, 24576:32768].rearrange("p (a b) -> p a b", b=1024)
    ATN = Wflat[:, 16384:32768].rearrange("p (a b) -> p a b", b=2048)
    XBf = XB[:, :, :].rearrange("p c t -> p (c t)")

    @C.G
    def _(g):
        ins = g.dma_start(out=RP[:, :], in_=C.dr["L2_ropeP"])
        C.inc(ins, "qg", 16)
        ins = g.dma_start(out=SM2[:, 0:1024], in_=C.dr["L2_cos"])
        C.inc(ins, "qg", 16)
        ins = g.dma_start(out=SM2[:, 1024:2048], in_=C.dr["L2_sin"])
        C.inc(ins, "qg", 16)
    const_m = C.val("qg")
    CC = SM2[:, 0:1024]
    SN = SM2[:, 1024:2048]

    def rope_inplace(slot, wait_extra):
        pe0 = C.val("pe")
        dve0 = C.val("dve")

        @C.T
        def _(t):
            psum_guard(C, t)
            t.wait_ge(C.sem("qg"), const_m)
            for (sn, v) in wait_extra:
                t.wait_ge(C.sem(sn), v)
            for dc in range(DC):
                if dc >= 2:
                    t.wait_ge(C.sem("dve"), dve0 + 3 * (dc - 1))
                ps = [PA, PB][dc % 2]
                for half in range(2):
                    ins = t.matmul(ps[:, 512 * half:512 * (half + 1)], RP[:, :],
                                   slot[:, dc, 512 * half:512 * (half + 1)],
                                   start=True, stop=True)
                ins.then_inc(C.sem("pe"), 1)
        C.cnt["pe"] += DC

        @C.V
        def _(v):
            for dc in range(DC):
                v.wait_ge(C.sem("pe"), pe0 + dc + 1)
                v.tensor_tensor(HB[:, 0:1024], slot[:, dc, :], CC, OP.mult)
                v.tensor_tensor(HB[:, 1024:2048], [PA, PB][dc % 2][:, 0:1024],
                                SN, OP.mult)
                ins = v.tensor_tensor(slot[:, dc, :], HB[:, 0:1024],
                                      HB[:, 1024:2048], OP.add)
                ins.then_inc(C.sem("dve"), 3)
        C.cnt["dve"] += 3 * DC

    for b in range(NB):
        fm_proj(C, C.dr["L2W_q"], lambda dc: XB[:, dc, :], D,
                lambda e, oc, ps, b=b: e.tensor_copy(Qr[b][:, oc, :], ps[:, :]),
                evac_eng="vector", tokens=(1024 * b, 1024 * b + 1024))
        rope_inplace(Qr[b], [])

    tap(C, "l2_rope", lambda: [
        (C.dr["xtap"][0:D, 0:1024].rearrange("(c p) t -> p c t", p=128),
         Wflat[:, 0:8192].rearrange("p (c t) -> p c t", t=1024)),
        (C.dr["xtap"][D:2 * D, 0:1024].rearrange("(c p) t -> p c t", p=128),
         Wflat[:, 8192:16384].rearrange("p (c t) -> p c t", t=1024))])
    if C.tapped:
        return

    kv_marks = []
    for b in range(NB):
        stg_w = [("qg", kv_marks[-1])] if kv_marks else []
        fm_proj(C, C.dr["L2W_k"], lambda dc: XB[:, dc, :], D,
                lambda e, oc, ps: e.tensor_copy(KS[:, oc, :], ps[:, :]),
                evac_eng="vector", tokens=(1024 * b, 1024 * b + 1024),
                extra_wait=stg_w)
        rope_inplace(KS, [])
        krot_done = C.cnt["dve"]

        groups = []
        for half in range(2):
            ap, mark, bi = wslab_load(C, C.dr["L2W_v"], 512 * half, 512)
            for tc in range(8):
                def mm(t, ps, ap=ap, tc=tc, b=b):
                    for dc in range(DC):
                        ins = t.matmul(ps[:, 0:512],
                                       XB[:, dc, 1024 * b + 128 * tc:1024 * b + 128 * (tc + 1)],
                                       ap[:, dc, :], start=(dc == 0), stop=(dc == DC - 1))
                    return ins

                def evac(e, ps, tc=tc, half=half):
                    return e.tensor_copy(VS[:, tc, 512 * half:512 * (half + 1)],
                                         ps[:, 0:512])

                groups.append({"mm": mm, "evac": evac, "eng": "vector",
                               "wait": [mark] + stg_w})
        run_pipe(C, groups, [PA[:, 0:512], PA[:, 512:1024],
                             PB[:, 0:512], PB[:, 512:1024]])
        C.wb_last_use[0] = C.wb_last_use[1] = C.cnt["pe"]
        v_done = C.cnt["evd"]

        @C.G
        def _(g, b=b, krot_done=krot_done, v_done=v_done):
            g.wait_ge(C.sem("dve"), krot_done)
            g.wait_ge(C.sem("evd"), v_done)
            ins = g.dma_start(
                out=C.dr["cc2_in"][1024 * b:1024 * (b + 1), 0:1024]
                    .rearrange("(c p) t -> p c t", p=128),
                in_=KS[:, :, :])
            C.inc(ins, "qg", 16)
            ins = g.dma_start(
                out=C.dr["cc2_in"][2048 + 1024 * b:2048 + 1024 * (b + 1), 0:1024]
                    .rearrange("(c p) t -> p c t", p=128),
                in_=VS[:, :, :])
            C.inc(ins, "qg", 16)

        kv_marks.append(C.val("qg"))

    @C.G
    def _(g):
        g.wait_ge(C.sem("qg"), kv_marks[-1])
        ins = g.collective_compute("AllGather", OP.bypass,
                                   replica_groups=[list(range(NCORES))],
                                   ins=[C.dr["cc2_in"].opt()],
                                   outs=[C.dr["cc2_out"].opt()])
        C.inc(ins, "cc", 1)
    ag_m = C.val("cc")

    scale = 1.0 / 8.0
    for b in range(NB):
        for hp in range(8):
            h2 = [2 * hp, 2 * hp + 1]
            tiles = []
            pe_now = C.val("pe")

            @C.V
            def _(v):
                v.wait_ge(C.sem("pe"), pe_now)
                for bufm in (sb["WB0"], sb["WB1"]):
                    for j in range(2):
                        vv = bufm[:, 1024 + 520 * j:1024 + 520 * j + 520] \
                            .rearrange("p (c o) -> p c o", o=65)
                        ins = v.memset(vv[:, :, 64:65], 1.0)
                ins.then_inc(C.sem("dve"), 1)
            C.cnt["dve"] += 1
            ones_m = C.cnt["dve"]

            pe_f0 = C.val("pe")
            for r in range(NCORES):
                i = C.wb_idx
                C.wb_idx = 1 - C.wb_idx
                buf = sb["WB0"] if i == 0 else sb["WB1"]
                lu = C.wb_last_use[i] if r < 2 else pe_f0 + 32 * (r - 2) + 34

                qsem = "qga" if i == 0 else "qgb"

                @C.G
                def _(g, r=r, b=b, hp=hp, buf=buf, lu=lu, qsem=qsem,
                      first=(b == 0 and hp == 0 and r == 0)):
                    if first:
                        g.wait_ge(C.sem("cc"), ag_m)
                    if lu > 0:
                        g.wait_ge(C.sem("pe"), lu)
                    base = 4096 * r
                    ins = g.dma_start(
                        out=buf[:, 0:1024],
                        in_=C.dr["cc2_out"][base + 1024 * b + 128 * hp:
                                            base + 1024 * b + 128 * (hp + 1), 0:1024])
                    C.inc(ins, qsem, 16)
                    for j, h in enumerate([2 * hp, 2 * hp + 1]):
                        vv = buf[:, 1024 + 520 * j:1024 + 520 * j + 520] \
                            .rearrange("p (c o) -> p c o", o=65)
                        ins = g.dma_start(
                            out=vv[:, :, 0:64],
                            in_=C.dr["cc2_out"][base + 2048 + 1024 * b:
                                                base + 2048 + 1024 * (b + 1),
                                                64 * h:64 * (h + 1)]
                                .rearrange("(c p) o -> p c o", p=128))
                        C.inc(ins, qsem, 16)
                tiles.append((buf, (qsem, C.val(qsem)), i))

            dve_f0 = C.val("dve")
            eva_f0 = C.val("eva")
            pe0f = C.val("pe")
            ngrp = NCORES * 8 * 2
            stm = {}
            avm = {}
            run_pe = pe0f
            order = []
            for r in range(NCORES):
                for tkc in range(8):
                    for j, h in enumerate(h2):
                        order.append((r, tkc, j, h))

            @C.T
            def _(t):
                nonlocal_run = [pe0f]

                def emit_st(gi):
                    r, tkc, j, h = order[gi]
                    buf, mark, bi = tiles[r]
                    t.wait_ge(C.sem(mark[0]), mark[1])
                    if gi >= 2:
                        t.wait_ge(C.sem("eva"), eva_f0 + gi - 1)
                    ps = [PB[:, 0:1024], PB[:, 1024:2048]][gi % 2]
                    for half in range(2):
                        ins = t.matmul(
                            ps[:, 512 * half:512 * (half + 1)],
                            buf[64 * (h % 2):64 * (h % 2) + 64,
                                128 * tkc:128 * (tkc + 1)],
                            Qr[b][64 * (h % 2):64 * (h % 2) + 64, hp,
                                  512 * half:512 * (half + 1)],
                            start=True, stop=True, skip_group_check=True)
                    nonlocal_run[0] += 1
                    ins.then_inc(C.sem("pe"), 1)
                    stm[gi] = nonlocal_run[0]

                def emit_av(gi):
                    r, tkc, j, h = order[gi]
                    buf, mark, bi = tiles[r]
                    t.wait_ge(C.sem("eva"), eva_f0 + gi + 1)
                    if gi == 0:
                        t.wait_ge(C.sem("dve"), ones_m)
                    hbslot = XBf[:, 1024 * (gi % 2):1024 * (gi % 2) + 1024]
                    vv = buf[:, 1024 + 520 * j:1024 + 520 * j + 520] \
                        .rearrange("p (c o) -> p c o", o=65)
                    avp = PA[0:65, 1024 * j:1024 * j + 1024]
                    for half in range(2):
                        ins = t.matmul(
                            avp[:, 512 * half:512 * (half + 1)],
                            vv[:, tkc, :],
                            hbslot[:, 512 * half:512 * (half + 1)],
                            start=(gi < 2), stop=(gi >= ngrp - 2),
                            skip_group_check=True)
                    nonlocal_run[0] += 1
                    ins.then_inc(C.sem("pe"), 1)
                    avm[gi] = nonlocal_run[0]

                psum_guard(C, t)
                t.wait_ge(C.sem("dve"), dve_f0)
                for gi in range(ngrp):
                    emit_st(gi)
                    if gi >= 2:
                        emit_av(gi - 2)
                emit_av(ngrp - 2)
                emit_av(ngrp - 1)
            C.cnt["pe"] += 2 * ngrp

            @C.A
            def _(a):
                for gi in range(ngrp):
                    a.wait_ge(C.sem("pe"), stm[gi])
                    if gi >= 2:
                        a.wait_ge(C.sem("pe"), avm[gi - 2])
                    ps = [PB[:, 0:1024], PB[:, 1024:2048]][gi % 2]
                    hbslot = XBf[:, 1024 * (gi % 2):1024 * (gi % 2) + 1024]
                    ins = a.activation(out=hbslot, in_=ps[:, :], func=AF.Exp,
                                       scale=scale)
                    ins.then_inc(C.sem("eva"), 1)
            C.cnt["eva"] += ngrp
            av_done = C.cnt["pe"]
            C.wb_last_use[0] = C.wb_last_use[1] = av_done

            for j, h in enumerate(h2):
                avp = PA[0:65, 1024 * j:1024 * j + 1024]

                @C.V
                def _(v, j=j):
                    def edge(ins):
                        ins.then_inc(C.sem("dve"), 1)
                        C.cnt["dve"] += 1
                        v.wait_ge(C.sem("dve"), C.cnt["dve"])

                    v.wait_ge(C.sem("pe"), av_done)
                    avp2 = PA[0:65, 1024 * j:1024 * j + 1024]
                    edge(v.tensor_copy(SM[0:1, 0:1024], avp2[64:65, 0:1024]))
                    edge(v.reciprocal(SM[0:1, 0:1024], SM[0:1, 0:1024]))
                    ins = v.tensor_copy(sb["ABR"][0:1, 0:1024], SM[0:1, 0:1024])
                    ins.then_inc(C.sem("dve"), 1)
                C.cnt["dve"] += 1
                rc_m = C.cnt["dve"]

                @C.T
                def _(t):
                    t.wait_ge(C.sem("dve"), rc_m)
                    for half in range(2):
                        ins = t.matmul(
                            PB[0:64, 512 * half:512 * (half + 1)],
                            ones_row[:, 0:64],
                            sb["ABR"][0:1, 512 * half:512 * (half + 1)],
                            start=True, stop=True)
                    ins.then_inc(C.sem("pe"), 1)
                C.cnt["pe"] += 1
                rb_m = C.cnt["pe"]

                @C.V
                def _(v, j=j, h=h, b=b, hp=hp):
                    v.wait_ge(C.sem("pe"), rb_m)
                    v.tensor_copy(HB[0:64, 0:1024], PB[0:64, 0:1024])
                    avp2 = PA[0:65, 1024 * j:1024 * j + 1024]
                    ins = v.tensor_tensor(
                        ATN[64 * (h % 2):64 * (h % 2) + 64, hp,
                            1024 * b:1024 * (b + 1)],
                        avp2[0:64, 0:1024], HB[0:64, 0:1024], OP.mult)
                    ins.then_inc(C.sem("dve"), 1)
                C.cnt["dve"] += 1

    tap(C, "l2_atn", lambda: [
        (C.dr["xtap"][0:D, :].rearrange("(c p) t -> p c t", p=128),
         Wflat[:, 16384:32768].rearrange("p (c t) -> p c t", t=2048))])
    if C.tapped:
        return

    def ev_o(e, oc, ps):
        return e.tensor_tensor(XB[:, oc, :], ps[:, :], S[:, oc, :], OP.add)

    fm_proj(C, C.dr["L2W_o"], lambda dc: ATN[:, dc, :], D, ev_o,
            evac_eng="vector")

    postln_from_XB(C)


def emit_l2_ffn(C):
    block_barrier(C)
    sb = C.sb
    S, W, XB = sb["S"], sb["W"], sb["XB"]
    PA, PB = C.ps["PA"], C.ps["PB"]
    ones_row = sb["ONER"][0:1, :]
    SM = sb["SM"]
    SM2 = sb["SM2"]
    hslot = W[:, :, :].rearrange("p c t -> p (c t)").bitcast(BF16) \
        .rearrange("p (a b) -> p a b", b=1024)

    apr, mr, bir = wslab_load(C, C.dr["L2fW_rt"], 0, 8,
                              wait_pe=C.val("pe"))

    @C.T
    def _(t):
        psum_guard(C, t)
        t.wait_ge(C.sem(mr[0]), mr[1])
        for tc in range(TC):
            for dc in range(DC):
                ins = t.matmul(PA[:, 8 * tc:8 * (tc + 1)],
                               XB[:, dc, 128 * tc:128 * (tc + 1)],
                               apr[:, dc, :], start=(dc == 0), stop=(dc == DC - 1),
                               skip_group_check=True)
        ins.then_inc(C.sem("pe"), 1)
    C.cnt["pe"] += 1
    rt_mark = C.cnt["pe"]
    C.wb_last_use[bir] = C.cnt["pe"]

    RT = SM[:, 0:128].rearrange("p (a b) -> p a b", b=8)
    RX = SM[:, 128:256].rearrange("p (a b) -> p a b", b=8)
    RS = SM[:, 256:272].rearrange("p (a b) -> p a b", b=1)
    RS2 = SM[:, 272:288].rearrange("p (a b) -> p a b", b=1)

    @C.V
    def _(v):
        v.wait_ge(C.sem("pe"), rt_mark)
        v.tensor_copy(SM[:, 0:128], PA[:, 0:128])
        ins = v.reduce_max(RS[:, :, :], RT[:, :, :], axis=AX.X)
        ins.then_inc(C.sem("dve"), 1)
        v.wait_ge(C.sem("dve"), C.cnt["dve"] + 1)
        ins = v.tensor_tensor(RX[:, :, :], RT[:, :, :],
                              RS[:, :, :].broadcast_to([128, 16, 8]), OP.subtract)
        ins.then_inc(C.sem("dve"), 1)
    C.cnt["dve"] += 2
    r1_mark = C.cnt["dve"]

    @C.A
    def _(a):
        a.wait_ge(C.sem("dve"), r1_mark)
        ins = a.activation(out=SM[:, 128:256], in_=SM[:, 128:256], func=AF.Exp)
        ins.then_inc(C.sem("act"), 1)
    C.cnt["act"] += 1
    r2_mark = C.cnt["act"]

    @C.V
    def _(v):
        def edge(ins):
            ins.then_inc(C.sem("dve"), 1)
            C.cnt["dve"] += 1
            v.wait_ge(C.sem("dve"), C.cnt["dve"])

        v.wait_ge(C.sem("act"), r2_mark)
        edge(v.reduce_sum(RS[:, :, :], RX[:, :, :], axis=AX.X))
        edge(v.reciprocal(SM[:, 272:288], SM[:, 256:272]))
        v.tensor_tensor(RX[:, :, :], RX[:, :, :],
                        RS2[:, :, :].broadcast_to([128, 16, 8]), OP.mult)
        edge(v.reduce_max(RS[:, :, :], RX[:, :, :], axis=AX.X))
        v.tensor_tensor(RT[:, :, :], RX[:, :, :],
                        RS[:, :, :].broadcast_to([128, 16, 8]), OP.is_ge)
        v.scalar_tensor_tensor(RT[:, :, :], RT[:, :, :], -1e30, RX[:, :, :],
                               OP.mult, OP.add)
        edge(v.reduce_max(RS2[:, :, :], RT[:, :, :], axis=AX.X))
        v.tensor_tensor(RT[:, :, :], RX[:, :, :],
                        RS2[:, :, :].broadcast_to([128, 16, 8]), OP.is_ge)
        v.tensor_tensor(RT[:, :, :], RT[:, :, :], RX[:, :, :], OP.mult)
        ins = v.tensor_copy(SM[:, 288:416], SM[:, 0:128])
        ins.then_inc(C.sem("dve"), 1)
    C.cnt["dve"] += 1
    rw_mark = C.cnt["dve"]

    @C.T
    def _(t):
        psum_guard(C, t)
        t.wait_ge(C.sem("dve"), rw_mark)
        for tc in range(TC):
            ins = t.transpose(PB[0:8, 128 * tc:128 * (tc + 1)],
                              SM[:, 288 + 8 * tc:288 + 8 * (tc + 1)],
                              sb["IDN"][:, :])
        ins.then_inc(C.sem("pe"), 1)
    C.cnt["pe"] += 1
    wt_mark = C.cnt["pe"]

    WEB = SM[:, :].bitcast(BF16)

    @C.V
    def _(v):
        v.wait_ge(C.sem("pe"), wt_mark)
        ins = v.tensor_copy(WEB[0:8, 0:2048], PB[0:8, 0:2048])
        ins.then_inc(C.sem("dve"), 1)
    C.cnt["dve"] += 1
    we_mark = C.cnt["dve"]

    HBI = sb["HB"][:, :].bitcast(mybir.dt.int16)

    @C.G
    def _(g):
        g.wait_ge(C.sem("dve"), we_mark)
        ins = g.iota(HBI[0:8, 0:1024], [[1, 8], [0, 128]], base=0,
                     channel_multiplier=0, allow_small_or_imprecise_dtypes=True)
        C.inc(ins, "pool", 1)
    sel_mark = C.val("pool")

    @C.V
    def _(v):
        v.wait_ge(C.sem("pool"), sel_mark)
        ins = v.tensor_scalar(sb["BONEST"][0:8, 0:1024], HBI[0:8, 0:1024],
                              sb["PIDX"][0:8, 0:1], None, OP.is_equal)
        ins.then_inc(C.sem("dve"), 1)
    C.cnt["dve"] += 1
    selv_mark = C.cnt["dve"]

    for tt in range(2):
        tcol = 1024 * tt
        for e in range(8):
            @C.T
            def _(t, e=e):
                psum_guard(C, t)
                t.wait_ge(C.sem("dve"), max(we_mark, selv_mark))
                for ts in range(NTS):
                    ins = t.matmul(PA[:, 512 * ts:512 * (ts + 1)],
                                   sb["BONEST"][0:8, 128 * e:128 * (e + 1)],
                                   WEB[0:8, 512 * ts:512 * (ts + 1)],
                                   start=True, stop=True)
                ins.then_inc(C.sem("pe"), 1)
            C.cnt["pe"] += 1
            web_mark = C.cnt["pe"]

            @C.V
            def _(v):
                v.wait_ge(C.sem("pe"), web_mark)
                ins = v.tensor_copy(SM2[:, :], PA[:, :])
                ins.then_inc(C.sem("evd"), 1)
            C.cnt["evd"] += 1

            w1e = C.dr["L2f_w1"][e]
            groups = []
            pe_entry = C.val("pe")
            for sgi in range(8):
                og = 512 * sgi
                wp = pe_entry + 4 * (sgi - 1) if sgi >= 2 else None
                ap, mark, bi = wslab_load(C, w1e, og, 512, wait_pe=wp)
                for j in range(4):
                    fc = 4 * sgi + j

                    def mm(t, ps, ap=ap, j=j):
                        for ts in range(2):
                            for dc in range(DC):
                                ins = t.matmul(
                                    ps[:, 512 * ts:512 * (ts + 1)],
                                    ap[:, dc, 128 * j:128 * (j + 1)],
                                    XB[:, dc, tcol + 512 * ts:tcol + 512 * (ts + 1)],
                                    start=(dc == 0), stop=(dc == DC - 1))
                        return ins

                    def evac(e_, ps, fc=fc):
                        return e_.activation(out=hslot[:, fc, :], in_=ps[:, 0:1024],
                                             func=AF.Gelu)

                    groups.append({"mm": mm, "evac": evac, "eng": "scalar",
                                   "wait": [mark]})
            run_pipe(C, groups, [PA[:, 0:1024], PB[:, 0:1024]])
            C.wb_last_use[0] = C.wb_last_use[1] = C.cnt["pe"]

            w2e = C.dr["L2f_w2"][e]
            groups = []
            pe_entry = C.val("pe")
            for hc in range(8):
                wp = pe_entry + (hc - 1) if hc >= 2 else None
                ap, mark, bi = wslab_load(C, w2e, 128 * hc, 128, wait_pe=wp)

                def mm(t, ps, ap=ap):
                    for ts in range(2):
                        for fc in range(32):
                            ins = t.matmul(ps[:, 512 * ts:512 * (ts + 1)],
                                           ap[:, fc, :],
                                           hslot[:, fc, 512 * ts:512 * (ts + 1)],
                                           start=(fc == 0), stop=(fc == 31))
                    return ins

                def evac(e_, ps, hc=hc):
                    e_.tensor_tensor(sb["HB"][:, 0:1024], ps[:, 0:1024],
                                     SM2[:, tcol:tcol + 1024], OP.mult)
                    return e_.tensor_tensor(S[:, hc, tcol:tcol + 1024],
                                            S[:, hc, tcol:tcol + 1024],
                                            sb["HB"][:, 0:1024], OP.add)

                groups.append({"mm": mm, "evac": evac, "eng": "vector",
                               "wait": [mark]})
            run_pipe(C, groups, [PA[:, 0:1024], PB[:, 0:1024]])
            C.wb_last_use[0] = C.wb_last_use[1] = C.cnt["pe"]

    postln_from_S(C)


def emit_l3_attn(C):
    block_barrier(C)
    sb = C.sb
    S, W, XB = sb["S"], sb["W"], sb["XB"]
    PA, PB = C.ps["PA"], C.ps["PB"]
    ones_row = sb["ONER"][0:1, :]
    SM = sb["SM"]
    SM2 = sb["SM2"]
    HB = sb["HB"]
    BR = sb["BIASR"]
    HBI2 = SM2[:, :].bitcast(mybir.dt.int16)

    Wflat = W[:, :, :].rearrange("p c t -> p (c t)").bitcast(BF16)
    Qr = [Wflat[:, 0:8192].rearrange("p (a b) -> p a b", b=1024),
          Wflat[:, 8192:16384].rearrange("p (a b) -> p a b", b=1024)]
    KS = Wflat[:, 16384:24576].rearrange("p (a b) -> p a b", b=1024)
    VS = Wflat[:, 24576:32768].rearrange("p (a b) -> p a b", b=1024)
    ATN = Wflat[:, 16384:32768].rearrange("p (a b) -> p a b", b=2048)
    XBf = XB[:, :, :].rearrange("p c t -> p (c t)")

    mb_qkv = load_rows_bf(C, C.dr["L3b_qkv"], 0, 3 * D, BR[0:1, 0:3072],
                          wait_pe=C.val("pe"))

    @C.G
    def _(g):
        ins = g.dma_start(out=sb["STAT"][:, 0:16], in_=C.dr["L3_ab"])
        C.inc(ins, "qg", 16)
    ab_m = C.val("qg")

    for b in range(NB):
        fm_proj(C, C.dr["L3W_qkv"], lambda dc: XB[:, dc, :], D,
                lambda e, oc, ps, b=b: e.tensor_copy(Qr[b][:, oc, :], ps[:, :]),
                evac_eng="vector", bias_row=BR[0:1, 0:1024],
                tokens=(1024 * b, 1024 * b + 1024), bias_mark=mb_qkv)

    kv_marks = []
    for b in range(NB):
        stg_w = [("qg", kv_marks[-1])] if kv_marks else []
        fm_proj(C, C.dr["L3W_qkv"][:, D:2 * D],
                lambda dc: XB[:, dc, :], D,
                lambda e, oc, ps: e.tensor_copy(KS[:, oc, :], ps[:, :]),
                evac_eng="vector", bias_row=BR[0:1, 1024:2048],
                tokens=(1024 * b, 1024 * b + 1024), bias_mark=mb_qkv,
                extra_wait=stg_w)
        k_done = C.cnt["evd"]

        groups = []
        for half in range(2):
            ap, mark, bi = wslab_load(C, C.dr["L3W_qkv"], 2048 + 512 * half, 512)
            for tc in range(8):
                def mm(t, ps, ap=ap, tc=tc, b=b, half=half):
                    for dc in range(DC):
                        t.matmul(ps[:, 0:512],
                                 XB[:, dc, 1024 * b + 128 * tc:1024 * b + 128 * (tc + 1)],
                                 ap[:, dc, :], start=(dc == 0), stop=False)
                    return t.matmul(ps[:, 0:512], ones_row[:, 0:128],
                                    BR[0:1, 2048 + 512 * half:2048 + 512 * (half + 1)],
                                    start=False, stop=True)

                def evac(e, ps, tc=tc, half=half):
                    return e.tensor_copy(VS[:, tc, 512 * half:512 * (half + 1)],
                                         ps[:, 0:512])

                groups.append({"mm": mm, "evac": evac, "eng": "vector",
                               "wait": [mark, ("qg", ab_m), ("qgr", mb_qkv)]
                                       + stg_w})
        run_pipe(C, groups, [PA[:, 0:512], PA[:, 512:1024],
                             PB[:, 0:512], PB[:, 512:1024]])
        C.wb_last_use[0] = C.wb_last_use[1] = C.cnt["pe"]
        v_done = C.cnt["evd"]

        @C.G
        def _(g, b=b, k_done=k_done, v_done=v_done):
            g.wait_ge(C.sem("evd"), max(k_done, v_done))
            ins = g.dma_start(
                out=C.dr["cc3_in"][1024 * b:1024 * (b + 1), 0:1024]
                    .rearrange("(c p) t -> p c t", p=128),
                in_=KS[:, :, :])
            C.inc(ins, "qg", 16)
            ins = g.dma_start(
                out=C.dr["cc3_in"][2048 + 1024 * b:2048 + 1024 * (b + 1), 0:1024]
                    .rearrange("(c p) t -> p c t", p=128),
                in_=VS[:, :, :])
            C.inc(ins, "qg", 16)
        kv_marks.append(C.val("qg"))

    @C.G
    def _(g):
        g.wait_ge(C.sem("qg"), kv_marks[-1])
        ins = g.collective_compute("AllGather", OP.bypass,
                                   replica_groups=[list(range(NCORES))],
                                   ins=[C.dr["cc3_in"].opt()],
                                   outs=[C.dr["cc3_out"].opt()])
        C.inc(ins, "cc", 1)
    ag_m = C.val("cc")

    XBf_t = XB[:, :, :].rearrange("p c t -> p (c t)")
    TRI = [XBf_t[:, 2048 + 1024 * k:2048 + 1024 * (k + 1)] for k in range(8)]
    SMI = SM[:, :].bitcast(mybir.dt.int16)
    pe_proj = C.val("pe")
    pool_t0 = C.val("pool")
    dve_t0 = C.val("dve")

    @C.G
    def _(g):
        for tkc in range(8):
            if tkc >= 1:
                g.wait_ge(C.sem("dve"), dve_t0 + tkc)
            ins = g.iota(SMI[:, 0:1024], [[1, 1024]], base=-128 * tkc,
                         channel_multiplier=-1,
                         allow_small_or_imprecise_dtypes=True)
            C.inc(ins, "pool", 1)

    @C.V
    def _(v):
        v.wait_ge(C.sem("pe"), pe_proj)
        for tkc in range(8):
            v.wait_ge(C.sem("pool"), pool_t0 + tkc + 1)
            ins = v.tensor_scalar(TRI[tkc], SMI[:, 0:1024], 0.0, None, OP.is_ge)
            ins.then_inc(C.sem("dve"), 1)
    C.cnt["dve"] += 8
    tri_m = C.cnt["dve"]

    scale = 0.125
    npair = 0
    for b in range(NB):
        for hp in range(8):
            npair += 1
            if npair > DEBUG_L3PAIRS:
                continue
            h2 = [2 * hp, 2 * hp + 1]
            pe_now = C.val("pe")

            @C.V
            def _(v):
                v.wait_ge(C.sem("pe"), pe_now)
                for bufm in (sb["WB0"], sb["WB1"]):
                    for j in range(2):
                        vv = bufm[:, 1024 + 520 * j:1024 + 520 * j + 520] \
                            .rearrange("p (c o) -> p c o", o=65)
                        ins = v.memset(vv[:, :, 64:65], 1.0)
                ins.then_inc(C.sem("dve"), 1)
            C.cnt["dve"] += 1
            ones_m = C.cnt["dve"]

            tiles = []
            pe_f0 = C.val("pe")
            dve_f0 = C.val("dve")

            for r in range(NCORES):
                i = C.wb_idx
                C.wb_idx = 1 - C.wb_idx
                buf = sb["WB0"] if i == 0 else sb["WB1"]
                lu = C.wb_last_use[i] if r < 2 else pe_f0 + 32 * (r - 2) + 34

                qsem = "qga" if i == 0 else "qgb"

                @C.G
                def _(g, r=r, b=b, hp=hp, buf=buf, lu=lu, qsem=qsem,
                      first=(b == 0 and hp == 0 and r == 0)):
                    if first:
                        g.wait_ge(C.sem("cc"), ag_m)
                    if lu > 0:
                        g.wait_ge(C.sem("pe"), lu)
                    base = 4096 * r
                    ins = g.dma_start(
                        out=buf[:, 0:1024],
                        in_=C.dr["cc3_out"][base + 1024 * b + 128 * hp:
                                            base + 1024 * b + 128 * (hp + 1), 0:1024])
                    C.inc(ins, qsem, 16)
                    for j, h in enumerate([2 * hp, 2 * hp + 1]):
                        vv = buf[:, 1024 + 520 * j:1024 + 520 * j + 520] \
                            .rearrange("p (c o) -> p c o", o=65)
                        ins = g.dma_start(
                            out=vv[:, :, 0:64],
                            in_=C.dr["cc3_out"][base + 2048 + 1024 * b:
                                                base + 2048 + 1024 * (b + 1),
                                                64 * h:64 * (h + 1)]
                                .rearrange("(c p) o -> p c o", p=128))
                        C.inc(ins, qsem, 16)
                tiles.append((buf, (qsem, C.val(qsem)), i))

            eva_f0 = C.val("eva")
            pe0f = C.val("pe")
            act_f0 = C.val("act")
            ngrp = NCORES * 8 * 2
            stm = {}
            avm = {}
            mkm = {}
            order = []
            for r in range(NCORES):
                for tkc in range(8):
                    for j, h in enumerate(h2):
                        order.append((r, tkc, j, h))

            @C.T
            def _(t):
                run = [pe0f]

                def emit_st(gi):
                    r, tkc, j, h = order[gi]
                    buf, mark, bi = tiles[r]
                    t.wait_ge(C.sem(mark[0]), mark[1])
                    if gi >= 2:
                        t.wait_ge(C.sem("act"), act_f0 + gi - 1)
                    ps = [PB[:, 0:1024], PB[:, 1024:2048]][gi % 2]
                    for half in range(2):
                        ins = t.matmul(
                            ps[:, 512 * half:512 * (half + 1)],
                            buf[64 * (h % 2):64 * (h % 2) + 64,
                                128 * tkc:128 * (tkc + 1)],
                            Qr[b][64 * (h % 2):64 * (h % 2) + 64, hp,
                                  512 * half:512 * (half + 1)],
                            start=True, stop=True, skip_group_check=True)
                    run[0] += 1
                    ins.then_inc(C.sem("pe"), 1)
                    stm[gi] = run[0]

                def emit_av(gi):
                    r, tkc, j, h = order[gi]
                    buf, mark, bi = tiles[r]
                    t.wait_ge(C.sem("eva"), eva_f0 + gi + 1)
                    if gi == 0:
                        t.wait_ge(C.sem("dve"), ones_m)
                    hbslot = XBf[:, 1024 * (gi % 2):1024 * (gi % 2) + 1024]
                    vv = buf[:, 1024 + 520 * j:1024 + 520 * j + 520] \
                        .rearrange("p (c o) -> p c o", o=65)
                    avp = PA[0:65, 1024 * j:1024 * j + 1024]
                    for half in range(2):
                        ins = t.matmul(
                            avp[:, 512 * half:512 * (half + 1)],
                            vv[:, tkc, :],
                            hbslot[:, 512 * half:512 * (half + 1)],
                            start=(gi < 2), stop=(gi >= ngrp - 2),
                            skip_group_check=True)
                    run[0] += 1
                    ins.then_inc(C.sem("pe"), 1)
                    avm[gi] = run[0]

                psum_guard(C, t)
                t.wait_ge(C.sem("dve"), dve_f0)
                for gi in range(ngrp):
                    emit_st(gi)
                    if gi >= 2:
                        emit_av(gi - 2)
                emit_av(ngrp - 2)
                emit_av(ngrp - 1)
            C.cnt["pe"] += 2 * ngrp

            @C.A
            def _(a):
                for gi in range(ngrp):
                    a.wait_ge(C.sem("pe"), stm[gi])
                    if gi >= 2:
                        a.wait_ge(C.sem("pe"), avm[gi - 2])
                    ps = [PB[:, 0:1024], PB[:, 1024:2048]][gi % 2]
                    hbslot = XBf[:, 1024 * (gi % 2):1024 * (gi % 2) + 1024]
                    ins = a.activation(out=hbslot, in_=ps[:, :], func=AF.Exp,
                                       scale=scale)
                    ins.then_inc(C.sem("act"), 1)
            C.cnt["act"] += ngrp

            @C.V
            def _(v):
                if DEBUG_L3NOMASK or DEBUG_L3MASKLVL == 0:
                    for gi in range(ngrp):
                        v.wait_ge(C.sem("act"), act_f0 + gi + 1)
                        hbslot = XBf[:, 1024 * (gi % 2):1024 * (gi % 2) + 1024]
                        ins = v.tensor_scalar(hbslot, hbslot, 1.0, None, OP.mult)
                        ins.then_inc(C.sem("eva"), 1)
                    return
                v.wait_ge(C.sem("dve"), tri_m)
                for gi in range(ngrp):
                    r, tkc, j, h = order[gi]
                    gg = gi // 2
                    if gi % 2 == 0:
                        ins = v.tensor_scalar(
                            SM2[:, 1024 * (gg % 2):1024 * (gg % 2) + 1024],
                            TRI[tkc],
                            sb["STAT"][:, 2 * r + 1:2 * r + 2],
                            sb["STAT"][:, 2 * r:2 * r + 1], OP.mult, OP.add)
                        ins.then_inc(C.sem("dve"), 1)
                    v.wait_ge(C.sem("act"), act_f0 + gi + 1)
                    hbslot = XBf[:, 1024 * (gi % 2):1024 * (gi % 2) + 1024]
                    if DEBUG_L3MASKLVL >= 2:
                        ins = v.tensor_tensor(
                            hbslot, hbslot,
                            SM2[:, 1024 * (gg % 2):1024 * (gg % 2) + 1024], OP.mult)
                    else:
                        ins = v.tensor_scalar(hbslot, hbslot, 1.0, None, OP.mult)
                    ins.then_inc(C.sem("eva"), 1)
            if not DEBUG_L3NOMASK and DEBUG_L3MASKLVL > 0:
                C.cnt["dve"] += ngrp // 2
            C.cnt["eva"] += ngrp
            av_done = C.cnt["pe"]
            C.wb_last_use[0] = C.wb_last_use[1] = av_done

            for j, h in enumerate(h2):
                @C.V
                def _(v, j=j):
                    def edge(ins):
                        ins.then_inc(C.sem("dve"), 1)
                        C.cnt["dve"] += 1
                        v.wait_ge(C.sem("dve"), C.cnt["dve"])

                    v.wait_ge(C.sem("pe"), av_done)
                    avp2 = PA[0:65, 1024 * j:1024 * j + 1024]
                    edge(v.tensor_copy(SM[0:1, 0:1024], avp2[64:65, 0:1024]))
                    edge(v.reciprocal(SM[0:1, 0:1024], SM[0:1, 0:1024]))
                    ins = v.tensor_copy(sb["ABR"][0:1, 0:1024], SM[0:1, 0:1024])
                    ins.then_inc(C.sem("dve"), 1)
                C.cnt["dve"] += 1
                rc_m = C.cnt["dve"]

                @C.T
                def _(t):
                    t.wait_ge(C.sem("dve"), rc_m)
                    for half in range(2):
                        ins = t.matmul(
                            PB[0:64, 512 * half:512 * (half + 1)],
                            ones_row[:, 0:64],
                            sb["ABR"][0:1, 512 * half:512 * (half + 1)],
                            start=True, stop=True)
                    ins.then_inc(C.sem("pe"), 1)
                C.cnt["pe"] += 1
                rb_m = C.cnt["pe"]

                @C.V
                def _(v, j=j, h=h, b=b, hp=hp):
                    v.wait_ge(C.sem("pe"), rb_m)
                    v.tensor_copy(HB[0:64, 0:1024], PB[0:64, 0:1024])
                    avp2 = PA[0:65, 1024 * j:1024 * j + 1024]
                    ins = v.tensor_tensor(
                        ATN[64 * (h % 2):64 * (h % 2) + 64, hp,
                            1024 * b:1024 * (b + 1)],
                        avp2[0:64, 0:1024], HB[0:64, 0:1024], OP.mult)
                    ins.then_inc(C.sem("dve"), 1)
                C.cnt["dve"] += 1

    mb_out = load_rows_bf(C, C.dr["L3b_out"], 0, D, BR[0:1, 0:1024],
                          wait_pe=C.val("pe"))

    def ev_o(e, oc, ps):
        return e.tensor_tensor(XB[:, oc, :], ps[:, :], S[:, oc, :], OP.add)

    fm_proj(C, C.dr["L3W_out"], lambda dc: ATN[:, dc, :], D, ev_o,
            evac_eng="vector", bias_row=BR[0:1, 0:1024], bias_mark=mb_out)

    postln_from_XB(C)




def emit_l3_ffn(C):
    block_barrier(C)
    sb = C.sb
    S, W, XB = sb["S"], sb["W"], sb["XB"]
    PA, PB = C.ps["PA"], C.ps["PB"]
    SM = sb["SM"]
    HB = sb["HB"]

    Wflat = W[:, :, :].rearrange("p c t -> p (c t)").bitcast(BF16)
    xrslot = Wflat[:, 0:8192].rearrange("p (a b) -> p a b", b=1024)
    xkslot = Wflat[:, 8192:16384].rearrange("p (a b) -> p a b", b=1024)
    kslot = Wflat[:, 16384:24576].rearrange("p (a b) -> p a b", b=1024)
    rslot = Wflat[:, 24576:32768].rearrange("p (a b) -> p a b", b=1024)

    PUB = SM[:, 0:16]
    GATH = SM[:, 16:144].rearrange("p (r j) -> p r j", j=16)
    BND = SM[:, 144:160]
    MUR = SM[:, 160:168]
    MUK = SM[:, 168:176]
    SELC = SM[:, 176:184]

    @C.V
    def _(v):
        for b in range(NB):
            for dc in range(DC):
                ins = v.tensor_copy(PUB[:, 8 * b + dc:8 * b + dc + 1],
                                    S[:, dc, 1024 * b + 1023:1024 * b + 1024])
        ins.then_inc(C.sem("dve"), 1)
    C.cnt["dve"] += 1
    pub_m = C.cnt["dve"]

    @C.G
    def _(g):
        ins = g.dma_start(out=MUR[:, :], in_=C.dr["L3f_mur"])
        C.inc(ins, "qg", 16)
        ins = g.dma_start(out=MUK[:, :], in_=C.dr["L3f_muk"])
        C.inc(ins, "qg", 16)
        ins = g.dma_start(out=SM[:, 176:184], in_=C.dr["L3f_sel"])
        C.inc(ins, "qg", 16)
    const_m = C.val("qg")

    @C.G
    def _(g):
        g.wait_ge(C.sem("dve"), pub_m)
        ins = g.dma_start(out=C.dr["cc7_in"], in_=PUB[:, :])
        C.inc(ins, "qgc", 16)
        g.wait_ge(C.sem("qgc"), C.val("qgc"))
        ins = g.collective_compute("AllGather", OP.bypass,
                                   replica_groups=[list(range(NCORES))],
                                   ins=[C.dr["cc7_in"].opt()],
                                   outs=[C.dr["cc7_out"].opt()])
        C.inc(ins, "cc", 1)
    ag_m = C.val("cc")

    @C.G
    def _(g):
        g.wait_ge(C.sem("cc"), ag_m)
        ins = g.dma_start(out=GATH[:, :, :],
                          in_=C.dr["cc7_out"].rearrange("(r p) j -> p r j", p=128))
        C.inc(ins, "qgc", 16)
    rb_m = C.val("qgc")

    @C.V
    def _(v):
        v.wait_ge(C.sem("qgc"), rb_m)
        v.wait_ge(C.sem("qg"), const_m)
        v.tensor_scalar(BND[:, :], GATH[:, 0, :], SELC[:, 0:1], None, OP.mult)
        for r in range(1, NCORES):
            ins = v.scalar_tensor_tensor(BND[:, :], GATH[:, r, :],
                                         SELC[:, r:r + 1], BND[:, :],
                                         OP.mult, OP.add)
        ins.then_inc(C.sem("dve"), 1)
    C.cnt["dve"] += 1
    bnd_m = C.cnt["dve"]

    for b in range(NB):
        bc = 1024 * b
        pe_b_gate = C.val("pe")
        evd_b_gate = C.val("evd")

        @C.V
        def _(v, b=b, pe_b_gate=pe_b_gate):
            if b == 0:
                v.wait_ge(C.sem("dve"), bnd_m)
            else:
                v.wait_ge(C.sem("pe"), pe_b_gate)
            for dc in range(DC):
                v.tensor_tensor(HB[:, 1:1024], S[:, dc, bc + 1:bc + 1024],
                                S[:, dc, bc:bc + 1023], OP.subtract)
                v.tensor_tensor(HB[:, 0:1], S[:, dc, bc:bc + 1],
                                BND[:, 8 * b + dc:8 * b + dc + 1], OP.subtract)
                v.scalar_tensor_tensor(xrslot[:, dc, 1:1024], HB[:, 1:1024],
                                       MUR[:, dc:dc + 1], S[:, dc, bc:bc + 1023],
                                       OP.mult, OP.add)
                v.scalar_tensor_tensor(xrslot[:, dc, 0:1], HB[:, 0:1],
                                       MUR[:, dc:dc + 1],
                                       BND[:, 8 * b + dc:8 * b + dc + 1],
                                       OP.mult, OP.add)
                v.scalar_tensor_tensor(xkslot[:, dc, 1:1024], HB[:, 1:1024],
                                       MUK[:, dc:dc + 1], S[:, dc, bc:bc + 1023],
                                       OP.mult, OP.add)
                ins = v.scalar_tensor_tensor(xkslot[:, dc, 0:1], HB[:, 0:1],
                                             MUK[:, dc:dc + 1],
                                             BND[:, 8 * b + dc:8 * b + dc + 1],
                                             OP.mult, OP.add)
            ins.then_inc(C.sem("dve"), 1)
        C.cnt["dve"] += 1
        xr_m = C.cnt["dve"]

        rk_wait = [("dve", xr_m)]
        if b == 1:
            rk_wait.append(("evd", evd_b_gate))

        fm_proj(C, C.dr["L3fW_r"], lambda dc: xrslot[:, dc, :], D,
                lambda e, oc, ps: e.activation(out=rslot[:, oc, :], in_=ps[:, :],
                                               func=AF.Sigmoid),
                evac_eng="scalar", tokens=(0, 1024), extra_wait=rk_wait)

        def k_evac(e, oc, ps):
            e.activation(out=HB[:, 1024:2048], in_=ps[:, :], func=AF.Relu)
            return e.activation(out=kslot[:, oc, :], in_=HB[:, 1024:2048],
                                func=AF.Square)

        fm_proj(C, C.dr["L3fW_k"], lambda dc: xkslot[:, dc, :], D, k_evac,
                evac_eng="scalar", tokens=(0, 1024), extra_wait=rk_wait)
        eva_rk = C.val("eva")

        def v_evac(e, oc, ps, b=b):
            e.tensor_tensor(HB[:, 0:1024], ps[:, :], rslot[:, oc, :], OP.mult)
            return e.tensor_tensor(XB[:, oc, 1024 * b:1024 * (b + 1)],
                                   HB[:, 0:1024],
                                   S[:, oc, 1024 * b:1024 * (b + 1)], OP.add)

        fm_proj(C, C.dr["L3fW_v"], lambda dc: kslot[:, dc, :], D, v_evac,
                evac_eng="vector", tokens=(0, 1024),
                extra_wait=[("eva", eva_rk)])

    postln_from_XB(C)


BLOCK_DECLS = {
    0: [("L0W_qkv", [D, 3 * D], F32), ("L0b_qkv", [3 * D], F32),
        ("L0W_g1", [D, 64], F32), ("L0b_g1", [64], F32),
        ("L0W_g2", [64, D], F32), ("L0b_g2", [D], F32),
        ("L0W_out", [D, D], F32), ("L0b_out", [D], F32),
        ("L0_ln_g", [D], F32), ("L0_ln_b", [D], F32)],
    1: [("L0fW_qkv", [D, 1536], F32), ("L0fW_proj", [D, D], F32),
        ("L0fW_rt", [D, 8], F32),
        ("L0f_w1", [8, D, 4096], F32), ("L0f_w2", [8, 4096, D], F32),
        ("L0f_ln1g", [D], F32), ("L0f_ln1b", [D], F32),
        ("L0f_ln2g", [D], F32), ("L0f_ln2b", [D], F32)],
    2: [("L1W_q", [D, D], F32), ("L1b_q", [D], F32),
        ("L1W_k", [D, D], F32), ("L1b_k", [D], F32),
        ("L1W_v", [D, D], F32), ("L1b_v", [D], F32),
        ("L1_gamma", [128, 1], F32), ("L1_dexp", [1, 8], F32),
        ("L1_mask", [1, 8], F32)],
    3: [("L1fW1", [D, 4096], F32), ("L1fb1", [4096], F32),
        ("L1fW2", [4096, D], F32), ("L1fb2", [D], F32)],
    4: [("L2W_q", [D, D], F32), ("L2W_k", [D, D], F32),
        ("L2W_v", [D, D], F32), ("L2W_o", [D, D], F32),
        ("L2_ropeP", [128, 128], F32), ("L2_cos", [128, 1024], F32),
        ("L2_sin", [128, 1024], F32)],
    5: [("L2fW_rt", [D, 8], F32),
        ("L2f_w1", [8, D, 4096], F32), ("L2f_w2", [8, 4096, D], F32)],
    6: [("L3W_qkv", [D, 3 * D], F32), ("L3b_qkv", [3 * D], F32),
        ("L3W_out", [D, D], F32), ("L3b_out", [D], F32),
        ("L3_ab", [128, 16], F32)],
    7: [("L3fW_r", [D, D], F32), ("L3fW_k", [D, D], F32),
        ("L3fW_v", [D, D], F32), ("L3f_mur", [128, 8], F32),
        ("L3f_muk", [128, 8], F32), ("L3f_sel", [128, 8], F32)],
}
BLOCK_EMITTERS = {0: emit_l0_attn, 1: emit_l0_ffn, 2: emit_l1_attn,
                  3: emit_l1_ffn, 4: emit_l2_attn, 5: emit_l2_ffn,
                  6: emit_l3_attn, 7: emit_l3_ffn}


def build_program(block_ids):
    es = ExitStack()
    nc = bass.Bass("TRN2", target_bir_lowering=False)
    dram = {}

    def din(name, shape, dtype=F32):
        dram[name] = nc.dram_tensor(name, list(shape), dtype, kind="ExternalInput").ap()

    def dint(name, shape, dtype=F32):
        dram[name] = nc.dram_tensor(name, list(shape), dtype).ap()

    din("xin", [D, T])
    dram["xout"] = nc.dram_tensor("xout", [D, T], F32, kind="ExternalOutput").ap()
    tap_shape = [2 * D, T] if DEBUG_TAP else [1, 1]
    dram["xtap"] = nc.dram_tensor("xtap", tap_shape, F32, kind="ExternalOutput").ap()
    for b in block_ids:
        for (name, shape, dt) in BLOCK_DECLS[b]:
            din(name, shape, dt)
    if 0 in block_ids:
        dint("cc0_in", [130, 1024])
        dint("cc0_out", [130, 1024])
    if 2 in block_ids:
        dint("cc1_in", [1024, 1032], BF16)
        dint("cc1_out", [NCORES * 1024, 1032], BF16)
    if 3 in block_ids:
        dint("qsp1", [D, 4096], BF16)
        dint("qsp2", [4096, D], BF16)
    if 4 in block_ids:
        dint("cc2_in", [4096, 1032], BF16)
        dint("cc2_out", [NCORES * 4096, 1032], BF16)
    if 7 in block_ids:
        dint("cc7_in", [128, 16], F32)
        dint("cc7_out", [NCORES * 128, 16], F32)
    if 6 in block_ids:
        dint("cc3_in", [4096, 1024], BF16)
        dint("cc3_out", [NCORES * 4096, 1024], BF16)

    with es:
        with nc.Block() as blk:
            C = Ctx(nc, blk, es)
            C.dr = dram

            def SB(name, shape, dtype):
                C.sb[name] = es.enter_context(nc.sbuf_tensor(name, list(shape), dtype))

            def PSUM(name, shape, dtype=F32):
                C.ps[name] = es.enter_context(nc.psum_tensor(name, list(shape), dtype))

            SB("S", [128, DC, T], F32)
            SB("W", [128, DC, T], F32)
            SB("XB", [128, DC, T], BF16)
            SB("WB0", [128, 4096], BF16)
            SB("WB1", [128, 4096], BF16)
            SB("SM", [128, 1024], F32)
            SB("SM2", [128, 2048], BF16)
            SB("HB", [128, 2048], BF16)
            SB("IDN", [128, 128], F32)
            SB("ONER", [1, 512], BF16)
            SB("ABR", [1, 4096], BF16)
            SB("BIASR", [1, 3072], BF16)
            SB("TB", [1, 128], BF16)
            SB("BONES", [128, 128], BF16)
            SB("BONEST", [16, 1024], BF16)
            SB("PIDX", [128, 1], F32)
            SB("ONE", [128, 8], BF16)
            SB("ONERF", [1, 128], F32)
            SB("ONEF", [128, 1], F32)
            SB("ZCOL", [128, 8], BF16)
            SB("EPSC", [1, 8], F32)
            SB("STAT", [128, 32], F32)
            PSUM("PA", [128, 2048], F32)
            PSUM("PB", [128, 2048], F32)

            emit_prelude(C)
            for b in block_ids:
                BLOCK_EMITTERS[b](C)
                if C.tapped:
                    break
            emit_epilogue(C)
    return nc


def _weight_maps(inputs, block_ids):
    t = lambda a: np.ascontiguousarray(np.asarray(a).T)
    m = {}
    if 1 in block_ids:
        wq = np.asarray(inputs["L0_ffn_qkv_weight"]).T
        perm = np.empty(1536, np.int64)
        for g in range(4):
            for j in range(4):
                perm[(4 * g + j) * 64:(4 * g + j + 1) * 64] = \
                    np.arange(g * 384 + j * 64, g * 384 + (j + 1) * 64)
            perm[1024 + g * 64:1024 + (g + 1) * 64] = \
                np.arange(g * 384 + 256, g * 384 + 320)
            perm[1280 + g * 64:1280 + (g + 1) * 64] = \
                np.arange(g * 384 + 320, g * 384 + 384)
        m["L0fW_qkv"] = np.ascontiguousarray(wq[:, perm])
        m["L0fW_proj"] = t(inputs["L0_ffn_proj_weight"])
        m["L0fW_rt"] = np.ascontiguousarray(np.asarray(inputs["L0_ffn_router_weight"]))
        m["L0f_w1"] = np.asarray(inputs["L0_ffn_moe_w1"])
        m["L0f_w2"] = np.asarray(inputs["L0_ffn_moe_w2"])
        m["L0f_ln1g"] = np.asarray(inputs["L0_ffn_ln1_weight"])
        m["L0f_ln1b"] = np.asarray(inputs["L0_ffn_ln1_bias"])
        m["L0f_ln2g"] = np.asarray(inputs["L0_ffn_ln2_weight"])
        m["L0f_ln2b"] = np.asarray(inputs["L0_ffn_ln2_bias"])
    if 2 in block_ids:
        m["L1W_q"] = t(inputs["L1_attn_Wq"])
        m["L1b_q"] = np.asarray(inputs["L1_attn_bq"])
        m["L1W_k"] = t(inputs["L1_attn_Wk"])
        m["L1b_k"] = np.asarray(inputs["L1_attn_bk"])
        m["L1W_v"] = t(inputs["L1_attn_Wv"])
        m["L1b_v"] = np.asarray(inputs["L1_attn_bv"])
        m["L1_gamma"] = np.broadcast_to(
            np.asarray(inputs["L1_attn_gamma"], dtype=np.float32).reshape(1, 1),
            (128, 1)).copy()
    if 4 in block_ids:
        m["L2W_q"] = np.ascontiguousarray(np.asarray(inputs["L2_attn_Wq"]))
        m["L2W_k"] = np.ascontiguousarray(np.asarray(inputs["L2_attn_Wk"]))
        m["L2W_v"] = np.ascontiguousarray(np.asarray(inputs["L2_attn_Wv"]))
        m["L2W_o"] = np.ascontiguousarray(np.asarray(inputs["L2_attn_Wo"]))
        P = np.zeros((128, 128), np.float32)
        for i in range(128):
            P[i ^ 1, i] = 1.0
        m["L2_ropeP"] = P
    if 6 in block_ids:
        m["L3W_qkv"] = np.ascontiguousarray(np.asarray(inputs["L3_attn_w_qkv"]))
        m["L3b_qkv"] = np.asarray(inputs["L3_attn_b_qkv"])
        m["L3W_out"] = np.ascontiguousarray(np.asarray(inputs["L3_attn_w_out"]))
        m["L3b_out"] = np.asarray(inputs["L3_attn_b_out"])
    if 7 in block_ids:
        m["L3fW_r"] = t(inputs["L3_ffn_W_r"])
        m["L3fW_k"] = t(inputs["L3_ffn_W_k"])
        m["L3fW_v"] = t(inputs["L3_ffn_W_v"])
        m["L3f_mur"] = np.ascontiguousarray(
            np.asarray(inputs["L3_ffn_mu_r"], np.float32).reshape(8, 128).T)
        m["L3f_muk"] = np.ascontiguousarray(
            np.asarray(inputs["L3_ffn_mu_k"], np.float32).reshape(8, 128).T)
    if 5 in block_ids:
        m["L2fW_rt"] = t(inputs["L2_ffn_router_w"])
        m["L2f_w1"] = np.ascontiguousarray(
            np.transpose(np.asarray(inputs["L2_ffn_w1"]), (0, 2, 1)))
        m["L2f_w2"] = np.ascontiguousarray(
            np.transpose(np.asarray(inputs["L2_ffn_w2"]), (0, 2, 1)))
    if 3 in block_ids:
        m["L1fW1"] = t(inputs["L1_ffn_W1"])
        m["L1fb1"] = np.asarray(inputs["L1_ffn_b1"])
        m["L1fW2"] = t(inputs["L1_ffn_W2"])
        m["L1fb2"] = np.asarray(inputs["L1_ffn_b2"])
    if 0 in block_ids:
        m["L0W_qkv"] = t(inputs["L0_attn_w_qkv"])
        m["L0b_qkv"] = np.asarray(inputs["L0_attn_b_qkv"])
        m["L0W_g1"] = t(inputs["L0_attn_w_g1"])
        m["L0b_g1"] = np.asarray(inputs["L0_attn_b_g1"])
        m["L0W_g2"] = t(inputs["L0_attn_w_g2"])
        m["L0b_g2"] = np.asarray(inputs["L0_attn_b_g2"])
        m["L0W_out"] = t(inputs["L0_attn_w_out"])
        m["L0b_out"] = np.asarray(inputs["L0_attn_b_out"])
        m["L0_ln_g"] = np.asarray(inputs["L0_attn_ln_g"])
        m["L0_ln_b"] = np.asarray(inputs["L0_attn_ln_b"])
    return m


def shard_stream(x_full):
    out = []
    for c in range(NCORES):
        sl = x_full[:, c * L:(c + 1) * L, :]
        fm = np.transpose(sl, (2, 0, 1)).reshape(D, T)
        out.append(np.ascontiguousarray(fm, dtype=np.float32))
    return out


def unshard_stream(cores):
    x = np.empty((NB, SEQ, D), dtype=np.float32)
    for c in range(NCORES):
        fm = np.asarray(cores[c]).reshape(D, NB, L)
        x[:, c * L:(c + 1) * L, :] = np.transpose(fm, (1, 2, 0))
    return x


def _rope_tables(c):
    pos = np.arange(c * L, (c + 1) * L, dtype=np.float64)
    i = np.arange(32, dtype=np.float64)
    coss, sins = [], []
    for s in (1.0, 4.0, 16.0):
        freqs = 1.0 / (10000.0 * s) ** (2.0 * i / 64.0)
        ang = pos[:, None] * freqs[None, :]
        coss.append(np.cos(ang))
        sins.append(np.sin(ang))
    cos = np.mean(coss, 0)
    sin = np.mean(sins, 0)
    CC = np.zeros((128, L), np.float32)
    SN = np.zeros((128, L), np.float32)
    for p in range(128):
        ii = (p % 64) // 2
        CC[p] = cos[:, ii]
        SN[p] = sin[:, ii] * (-1.0 if p % 2 == 0 else 1.0)
    return CC, SN


def _percore_maps(block_ids, c):
    m = {}
    if 4 in block_ids:
        CC, SN = _rope_tables(c)
        m["L2_cos"] = CC
        m["L2_sin"] = SN
    if 7 in block_ids:
        sel = np.zeros((128, 8), np.float32)
        if c > 0:
            sel[:, c - 1] = 1.0
        m["L3f_sel"] = sel
    if 6 in block_ids:
        ab = np.zeros((128, 16), np.float32)
        for r in range(8):
            if r < c:
                ab[:, 2 * r] = 1.0
            elif r == c:
                ab[:, 2 * r + 1] = 1.0
        m["L3_ab"] = ab
    if 2 in block_ids:
        dexp = np.zeros((1, 8), np.float32)
        mask = np.zeros((1, 8), np.float32)
        for k in range(8):
            if k < c:
                dexp[0, k] = 1024.0 * (c - 1 - k)
                mask[0, k] = 1.0
        m["L1_dexp"] = dexp
        m["L1_mask"] = mask
    return m


def run_blocks(block_ids, stream_percore, inputs):
    nc = build_program(block_ids)
    wmap = _weight_maps(inputs, block_ids)
    in_maps = [dict(wmap, xin=stream_percore[c], **_percore_maps(block_ids, c))
               for c in range(NCORES)]
    res = run_bass_kernel_spmd(nc, in_maps, core_ids=list(range(NCORES)))
    return [res.results[c]["xout"] for c in range(NCORES)]


def kernel(**inputs):
    streams = shard_stream(np.asarray(inputs["L0_attn_x"], dtype=np.float32))
    streams = run_blocks(list(range(8)), streams, inputs)
    return unshard_stream(streams)
